# version 12
# baseline (speedup 1.0000x reference)
import numpy as np
import ml_dtypes
from contextlib import ExitStack

import concourse.bass as bass
import concourse.mybir as mybir
from concourse.bass_utils import run_bass_kernel_spmd

F32 = mybir.dt.float32
BF16 = mybir.dt.bfloat16
AF = mybir.ActivationFunctionType
ALU = mybir.AluOpType
AX = mybir.AxisListType

D = 1024
T = 2048
NCH = D // 128
H = 16
NJ = T // 512
JW = 512
SCALE = 0.125
NEG = -2.0e30
EPS = 1e-5

N_GQ = 10
N_SQ = 6
SEMS = (("pe", "dv", "ac", "gp", "sy")
        + tuple(f"g{i}" for i in range(N_GQ))
        + tuple(f"s{i}" for i in range(N_SQ)))
ENG2SEM = {"tensor": "pe", "vector": "dv", "scalar": "ac",
           "gpsimd": "gp", "sync": "sy"}
_ESIZE = {F32: 4, BF16: 2}


def _is_ap(x):
    return hasattr(x, "base_partition")


def _unm(x):
    return x() if callable(x) else x


def _fp(ap):
    if callable(ap.ap):
        ap = ap.ap()
    name = _unm(ap.name)
    ent = _unm(ap.ap)
    es = _ESIZE.get(_unm(ap.dtype), 4)
    if "DRAM" in str(_unm(ap.space)):
        reach = 1
        for st, sz in ent:
            reach += (sz - 1) * abs(st)
        off = _unm(ap.offset)
        return (name, 0, 1 << 20, off * es, (off + reach) * es)
    p0 = _unm(ap.base_partition)
    p1 = p0 + ent[0][1]
    reach = 1
    for st, sz in ent[1:]:
        reach += (sz - 1) * abs(st)
    off = _unm(ap.offset)
    return (name, p0, p1, off * es, (off + reach) * es)


def _ovl(a, b):
    return a[1] < b[2] and b[1] < a[2] and a[3] < b[4] and b[3] < a[4]


def _covers(big, small):
    return (big[1] <= small[1] and big[2] >= small[2]
            and big[3] <= small[3] and big[4] >= small[4])


class Plan:

    def __init__(self):
        self.steps = {e: [] for e in ENG2SEM}
        self.cnt = {s: 0 for s in SEMS}
        self.wr = {}
        self.rd = {}
        self.bar = None
        self.gq = 0
        self.sq = 0
        self.last_pool = {}

    def _dep_rd(self, f, deps):
        for e in self.wr.get(f[0], ()):
            if _ovl(e[0], f):
                if e[2] > deps.get(e[1], 0):
                    deps[e[1]] = e[2]

    def _dep_wr(self, f, deps):
        self._dep_rd(f, deps)
        for e in self.rd.get(f[0], ()):
            if _ovl(e[0], f):
                if e[2] > deps.get(e[1], 0):
                    deps[e[1]] = e[2]

    def _note_wr(self, f, sem, cnt):
        lst = self.wr.setdefault(f[0], [])
        lst[:] = [e for e in lst if not _covers(f, e[0])]
        lst.append((f, sem, cnt))
        rl = self.rd.get(f[0])
        if rl:
            rl[:] = [e for e in rl if not _covers(f, e[0])]

    def _note_rd(self, f, sem, cnt):
        lst = self.rd.setdefault(f[0], [])
        for i, e in enumerate(lst):
            if e[1] == sem and _covers(e[0], f):
                lst[i] = (e[0], sem, max(e[2], cnt))
                return
        lst.append((f, sem, cnt))

    def _emit(self, eng, fn, rf, wf, inc, inc_k, extra=None):
        deps = {}
        for f in rf:
            self._dep_rd(f, deps)
        for f in wf:
            self._dep_wr(f, deps)
        if self.bar is not None:
            s, c = self.bar
            if c > deps.get(s, 0):
                deps[s] = c
        if extra:
            for s, c in extra.items():
                if c > deps.get(s, 0):
                    deps[s] = c
        if eng == "tensor":
            deps.pop("pe", None)
        self.cnt[inc] += inc_k
        cnt = self.cnt[inc]
        for f in wf:
            self._note_wr(f, inc, cnt)
        for f in rf:
            self._note_rd(f, inc, cnt)
        self.steps[eng].append((fn, sorted(deps.items()), (inc, inc_k)))

    def _emit_universal(self, eng, fn, inc, inc_k):
        deps = {s: c for s, c in self.cnt.items() if c > 0}
        self.cnt[inc] += inc_k
        self.wr = {}
        self.rd = {}
        self.bar = (inc, self.cnt[inc])
        self.steps[eng].append((fn, sorted(deps.items()), (inc, inc_k)))

    def step(self, eng, fn, r=None, w=None, inc=None, inc_k=1):
        if inc is None:
            inc = ENG2SEM[eng]
        if r is None and w is None:
            self._emit_universal(eng, fn, inc, inc_k)
        else:
            rf = [_fp(a) for a in (r or ()) if _is_ap(a)]
            wf = [_fp(a) for a in (w or ()) if _is_ap(a)]
            self._emit(eng, fn, rf, wf, inc, inc_k)

    def _pool_sem(self, eng):
        if eng == "gpsimd":
            s = f"g{self.gq % N_GQ}"
            self.gq += 1
        else:
            s = f"s{self.sq % N_SQ}"
            self.sq += 1
        return s

    def dma(self, eng, fn, r=None, w=None):
        sem = self._pool_sem(eng)
        extra = {}
        prev = self.last_pool.get(sem)
        if prev:
            extra[sem] = prev
        if r is None and w is None:
            deps = {s: c for s, c in self.cnt.items() if c > 0}
            self.cnt[sem] += 16
            self.wr = {}
            self.rd = {}
            self.bar = (sem, self.cnt[sem])
            self.steps[eng].append((fn, sorted(deps.items()), (sem, 16)))
        else:
            rf = [_fp(a) for a in (r or ()) if _is_ap(a)]
            wf = [_fp(a) for a in (w or ()) if _is_ap(a)]
            self._emit(eng, fn, rf, wf, sem, 16, extra=extra)
        self.last_pool[sem] = self.cnt[sem]

    def fence(self):
        self._emit_universal("sync", lambda sy: sy.nop(), "sy", 1)

    def ld(self, eng, out, in_, **kw):
        self.dma(eng, lambda e, out=out, in_=in_, kw=kw: e.dma_start(
            out=out, in_=in_, **kw), r=[in_], w=[out])

    def mm(self, ops):
        rl, wl = [], []
        for (o, l, rr, _s, _p) in ops:
            wl.append(o)
            rl += [l, rr]
        def fn(t, ops=ops):
            last = None
            for (o, l, rr, s, p) in ops:
                last = t.matmul(o, l, rr, start=s, stop=p)
            return last
        self.step("tensor", fn, r=rl, w=wl)

    def tr(self, ops, ident):
        if callable(ident.ap):
            ident = ident.ap()
        rl = [ident]
        wl = []
        for (o, i) in ops:
            wl.append(o)
            rl.append(i)
        def fn(t, ops=ops, ident=ident):
            last = None
            for (o, i) in ops:
                last = t.transpose(o, i, ident)
            return last
        self.step("tensor", fn, r=rl, w=wl)

    def act(self, out, in_, func, bias=0.0, scale=1.0, accum_out=None):
        r = [in_] + [x for x in (bias, scale) if _is_ap(x)]
        w = [out] + ([accum_out] if accum_out is not None else [])
        if accum_out is not None:
            self.step("scalar", lambda s, o=out, i=in_, f=func, b=bias, sc=scale,
                      a=accum_out: s.activation(o, i, f, bias=b, scale=sc,
                                                accum_out=a), r=r, w=w)
        else:
            self.step("scalar", lambda s, o=out, i=in_, f=func, b=bias, sc=scale:
                      s.activation(o, i, f, bias=b, scale=sc), r=r, w=w)

    def ve(self, name, out, *ins, eng="vector", **kw):
        r = [x for x in ins if _is_ap(x)] + [x for x in kw.values() if _is_ap(x)]
        def fn(v, name=name, out=out, ins=ins, kw=kw):
            return getattr(v, name)(out, *ins, **kw)
        self.step(eng, fn, r=r, w=[out])

    def emit(self, block, sems):
        def runner(eng_name):
            def run(eng):
                hi = {s: 0 for s in SEMS}
                for fn, wvals, inc in self.steps[eng_name]:
                    for s, v in wvals:
                        if v > hi[s]:
                            eng.wait_ge(sems[s], v)
                            hi[s] = v
                    inst = fn(eng)
                    name, k = inc
                    assert inst is not None, f"step incrementing {name} returned None"
                    inst.then_inc(sems[name], k)
            return run
        block.tensor(runner("tensor"))
        block.vector(runner("vector"))
        block.scalar(runner("scalar"))
        block.gpsimd(runner("gpsimd"))
        block.sync(runner("sync"))



def prep_weights(inp):
    f32c = lambda a: np.ascontiguousarray(np.asarray(a), dtype=np.float32)
    w = {}
    for n in ("Wq", "Wk", "Wv", "Wo"):
        w["a0_" + n] = f32c(inp[f"L0_attn_{n}"])
    w["a0_bo_row"] = f32c(np.asarray(inp["L0_attn_bo"]).reshape(1, D))
    for n in ("kln_w", "kln_b", "qln_w", "qln_b"):
        w["a0_" + n] = f32c(np.tile(np.asarray(inp[f"L0_attn_{n}"]), 2).reshape(128, 1))
    w["f0_wiT"] = f32c(np.asarray(inp["L0_ffn_wi"]).T)
    w["f0_woT"] = f32c(np.asarray(inp["L0_ffn_wo"]).T)
    w["f0_bi"] = f32c(np.asarray(inp["L0_ffn_bi"]).reshape(1, 4096))
    w["f0_bo_row"] = f32c(np.asarray(inp["L0_ffn_bo"]).reshape(1, D))
    _la = np.asarray(inp["L0_ffn_lora_A"])
    w["f0_AT"] = f32c(_la.transpose(2, 0, 1).reshape(NCH, 128, 8, 16)
                      .transpose(1, 2, 0, 3))
    _wo = np.asarray(inp["L0_ffn_wo"]).T
    w["f0_woTt"] = f32c(_wo.reshape(32, 128, D).transpose(1, 0, 2))
    w["f0_BT"] = f32c(np.asarray(inp["L0_ffn_lora_B"]).transpose(0, 2, 1))
    w["f0_gwT"] = f32c(np.asarray(inp["L0_ffn_gate_w"]).T)
    w["f0_gb"] = f32c(np.asarray(inp["L0_ffn_gate_b"]).reshape(1, 8))
    for n in ("Wq", "Wk", "Wv", "Wo"):
        w["a1_" + n] = f32c(np.asarray(inp[f"L1_attn_{n}"]).T)
    for n in ("bq", "bk", "bv"):
        w["a1_" + n] = f32c(np.asarray(inp[f"L1_attn_{n}"]).reshape(1, D))
    w["a1_bo_row"] = f32c(np.asarray(inp["L1_attn_bo"]).reshape(1, D))
    w["f1_WmT"] = f32c(np.asarray(inp["L1_ffn_Wm"]).T)
    w["f1_WpT"] = f32c(np.asarray(inp["L1_ffn_Wp"]).T)
    w["f1_Wc1T"] = f32c(np.asarray(inp["L1_ffn_Wc1"]).T)
    w["f1_Wc2T"] = f32c(np.asarray(inp["L1_ffn_Wc2"]).T)
    return w


def make_consts():
    c = {}
    p = np.arange(128)[:, None]
    f512 = np.arange(JW)[None, :]
    c["c_ident"] = np.eye(128, dtype=np.float32)
    m = np.zeros((128, 4, JW), np.float32)
    for o in range(4):
        m[:, o, :] = np.where(p + 128 * o - f512 >= 1, NEG, 0.0)
    c["c_masks"] = np.ascontiguousarray(m.reshape(128, 4 * JW))
    s2 = np.zeros((2, 128), np.float32)
    s2[0, 0:64] = 1.0
    s2[1, 64:128] = 1.0
    c["c_sel2"] = s2
    o2 = np.zeros((128, 2), np.float32)
    o2[0:64, 0] = 1.0
    o2[64:128, 1] = 1.0
    c["c_ones2"] = o2
    c["c_ones_col"] = np.ones((128, 1), np.float32)
    dg = np.where(np.arange(128)[None, :] > np.arange(128)[:, None], NEG, 0.0)
    c["c_diag"] = np.ascontiguousarray(dg.astype(np.float32))
    c["c_ones_row"] = np.ones((1, JW), np.float32)
    c["c_eps"] = np.full((128, 1), EPS, np.float32)
    return c


CONST_SHAPES = {
    "c_ident": [128, 128], "c_masks": [128, 4 * JW], "c_sel2": [2, 128],
    "c_ones2": [128, 2], "c_ones_col": [128, 1], "c_ones_row": [1, JW],
    "c_eps": [128, 1], "c_diag": [128, 128],
}


WEIGHT_SHAPES = {
    "a0_Wq": [D, D], "a0_Wk": [D, D], "a0_Wv": [D, D], "a0_Wo": [D, D],
    "a0_bo_row": [1, D],
    "a0_kln_w": [128, 1], "a0_kln_b": [128, 1],
    "a0_qln_w": [128, 1], "a0_qln_b": [128, 1],
    "f0_wiT": [D, 4096], "f0_woT": [4096, D], "f0_bi": [1, 4096],
    "f0_bo_row": [1, D], "f0_AT": [128, 8, NCH, 16], "f0_BT": [8, 16, 4096],
    "f0_woTt": [128, 32, D],
    "f0_gwT": [D, 8], "f0_gb": [1, 8],
    "a1_Wq": [D, D], "a1_Wk": [D, D], "a1_Wv": [D, D], "a1_Wo": [D, D],
    "a1_bq": [1, D], "a1_bk": [1, D], "a1_bv": [1, D],
    "a1_bo_row": [1, D],
    "f1_WmT": [D, 12288], "f1_WpT": [12288, D], "f1_Wc1T": [D, 256],
    "f1_Wc2T": [256, 12288],
}


class Ctx:
    def __init__(self, nc, plan, es):
        self.nc, self.plan, self.es = nc, plan, es
        self.block_es = es
        self.uid = 0

    def nm(self, base):
        self.uid += 1
        return f"{base}{self.uid}"

    def sbuf(self, base, shape, dt, es=None):
        return (es or self.block_es).enter_context(
            self.nc.sbuf_tensor(self.nm(base), shape, dt))

    def psum(self, base, shape, dt=F32):
        return self.es.enter_context(self.nc.psum_tensor(self.nm(base), shape, dt))

    def make_shared(self, C):
        p = self.plan
        self.ones_col = self.sbuf("ones_col", [128, 1], BF16)
        self.ones_row16 = self.sbuf("ones_row16", [1, 128], BF16)
        self.onesf_row = self.sbuf("onesf_row", [1, JW], F32)
        self.onesf_128 = self.sbuf("onesf_128", [1, 128], F32)
        self.onesf_col = self.sbuf("onesf_col", [128, 1], F32)
        self.ones2 = self.sbuf("ones2", [128, 2], BF16)
        self.sel2 = self.sbuf("sel2", [2, 128], BF16)
        self.identf = self.sbuf("identf", [128, 128], F32)
        self.ident16 = self.sbuf("ident16", [128, 128], BF16)
        self.C = C
        self.eps_col = self.sbuf("eps_col", [128, 1], F32)
        self.nc.const_aps.register(F32, EPS, self.eps_col[:])

        self.PA = self.psum("PA", [128, 2048])
        self.PB = self.psum("PB", [128, 1024])
        self.PC = self.psum("PC", [128, 512])
        self.PT = self.psum("PTps", [128, 1024], BF16)

        self.row_a = self.sbuf("row_a", [2, JW], F32)
        self.row_b = self.sbuf("row_b", [2, JW], F32)
        self.row_c = self.sbuf("row_c", [2, JW], F32)
        self.abf = self.sbuf("abf", [128, JW], F32)
        self.cbf = self.sbuf("cbf", [128, JW], F32)
        self.sqf = self.sbuf("sqf", [128, JW], F32)

        for sb, cn in ((self.ones_col, "c_ones_col"), (self.ones2, "c_ones2"),
                       (self.sel2, "c_sel2"), (self.ident16, "c_ident")):
            p.ld("gpsimd", sb[:], C[cn])
        p.ld("gpsimd", self.ones_row16[:], C["c_ones_row"][0:1, 0:128])
        p.ld("gpsimd", self.onesf_row[:], C["c_ones_row"])
        p.ld("gpsimd", self.onesf_128[:], C["c_ones_row"][0:1, 0:128])
        p.ld("gpsimd", self.onesf_col[:], C["c_ones_col"])
        p.ld("gpsimd", self.identf[:], C["c_ident"])
        p.ld("gpsimd", self.eps_col[:], C["c_eps"])


def load_chunked(ctx, sb, hbm, n_chunks):
    p = ctx.plan
    for c in range(n_chunks):
        p.ld("gpsimd", sb[:, c, :], hbm[c * 128:(c + 1) * 128, :])


LN_STAGE = 99


def emit_outer_ln(ctx, rj, yj, y_ap, j0):
    p, X = ctx.plan, ctx
    for c in range(NCH):
        p.act(X.sqf[:, :], rj[:, c, :], AF.Square)
        def mm_st(t, c=c):
            t.matmul(X.PB[0:1, 0:JW], X.onesf_col[:], rj[:, c, :],
                     start=(c == 0), stop=(c == NCH - 1))
            return t.matmul(X.PB[0:1, JW:2 * JW], X.onesf_col[:], X.sqf[:, :],
                            start=(c == 0), stop=(c == NCH - 1))
        p.step("tensor", mm_st,
               r=[X.onesf_col[:], rj[:, c, :], X.sqf[:, :]],
               w=[X.PB[0:1, 0:2 * JW]])
    p.ve("tensor_scalar", X.row_b[0:1, :], X.PB[0:1, 0:JW],
         1.0 / D, None, ALU.mult)
    p.act(X.row_a[0:1, :], X.row_b[0:1, :], AF.Square)
    p.ve("scalar_tensor_tensor", X.row_c[0:1, :], X.PB[0:1, JW:2 * JW],
         1.0 / D, X.row_a[0:1, :], ALU.mult, ALU.subtract)
    p.act(X.row_c[0:1, :], X.row_c[0:1, :], AF.Ln, bias=EPS)
    p.act(X.row_a[0:1, :], X.row_c[0:1, :], AF.Exp, scale=-0.5)
    p.ve("scalar_tensor_tensor", X.row_b[0:1, :], X.row_b[0:1, :],
         -1.0, X.row_a[0:1, :], ALU.mult, ALU.mult)

    p.mm([(X.PC[:, :], X.onesf_128[:], X.row_a[0:1, :], True, True),
          (X.PB[:, 0:JW], X.onesf_128[:], X.row_b[0:1, :], True, True)])
    p.act(X.abf[:], X.PC[:, :], AF.Copy)
    p.act(X.cbf[:], X.PB[:, 0:JW], AF.Copy)
    for c in range(NCH):
        p.ve("tensor_mul", yj[:, c, :], rj[:, c, :], X.abf[:])
        p.ve("tensor_add", yj[:, c, :], yj[:, c, :], X.cbf[:])
        p.ld("sync", y_ap[c * 128:(c + 1) * 128, j0:j0 + JW], yj[:, c, :])



def emit_l0_attn(ctx, W, x_ap, y_ap):
    p, X = ctx.plan, ctx
    diagm = ctx.sbuf("diagm", [128, 128], F32)
    p.ld("gpsimd", diagm[:], ctx.C["c_diag"])
    qb = ctx.sbuf("qb", [128, NCH, T], BF16)
    kb = ctx.sbuf("kb", [128, NCH, T], BF16)
    vt = ctx.sbuf("vt", [128, T // 128, D], BF16)
    wo_j = ctx.sbuf("wo_j", [128, 2, NCH, 128], BF16)
    bo_row = ctx.sbuf("bo_row", [1, D], F32)
    lnw = {n: ctx.sbuf(n, [128, 1], F32) for n in ("klw", "klb", "qlw", "qlb")}
    sq = ctx.sbuf("sqs", [128, JW], BF16)
    t1 = ctx.sbuf("t1", [128, JW], BF16)
    ab16 = ctx.sbuf("ab16", [128, JW], BF16)
    cb16 = ctx.sbuf("cb16", [128, JW], BF16)
    row16a = ctx.sbuf("row16a", [2, JW], BF16)
    row16c = ctx.sbuf("row16c", [2, JW], BF16)

    with ExitStack() as es1:
        xb = ctx.sbuf("xb", [128, NCH, T], BF16, es1)
        wcol = ctx.sbuf("wcol", [128, 2, NCH, 128], BF16, es1)
        stage = ctx.sbuf("stage", [128, 2, JW], BF16, es1)

        load_chunked(ctx, xb, x_ap, NCH)
        p.ld("gpsimd", bo_row[:], W["a0_bo_row"])
        for sb_n, w_n in (("klw", "a0_kln_w"), ("klb", "a0_kln_b"),
                          ("qlw", "a0_qln_w"), ("qlb", "a0_qln_b")):
            p.ld("gpsimd", lnw[sb_n][:], W[w_n])

        qkv_ps = [X.PC[:, 0:JW], X.PB[:, 0:JW]]
        for wi, (w_n, kind) in enumerate((("a0_Wq", "q"), ("a0_Wk", "k"),
                                          ("a0_Wv", "v"))):
            for cout in range(NCH):
                slot = (wi * NCH + cout) % 2
                for ci in range(NCH):
                    p.ld("gpsimd", wcol[:, slot, ci, :],
                         W[w_n][ci * 128:(ci + 1) * 128,
                                cout * 128:(cout + 1) * 128])
                for jn in range(T // JW):
                    n0 = jn * JW
                    ps = qkv_ps[jn % 2]
                    p.mm([(ps, wcol[:, slot, ci, :], xb[:, ci, n0:n0 + JW],
                           ci == 0, ci == NCH - 1) for ci in range(NCH)])
                    if kind in ("q", "k"):
                        out_sb = qb if kind == "q" else kb
                        p.act(out_sb[:, cout, n0:n0 + JW], ps, AF.Copy)
                    else:
                        stg = stage[:, jn % 2, :]
                        p.act(stg, ps, AF.Copy)
                        p.tr([(X.PT[:, i * 128:(i + 1) * 128],
                               stg[:, i * 128:(i + 1) * 128])
                              for i in range(4)], X.ident16)
                        p.ve("tensor_copy",
                             vt[:, jn * 4:jn * 4 + 4,
                                cout * 128:(cout + 1) * 128],
                             X.PT[:, 0:512].rearrange("p (a b) -> p a b", a=4))

    def qkln(src, wl, bl):
        for c in range(NCH):
            for n in range(4):
                n0 = n * JW
                sl = src[:, c, n0:n0 + JW]
                p.act(sq[:, :], sl, AF.Square)
                p.mm([(X.PB[0:2, 0:JW], X.ones2[:], sl, True, True),
                      (X.PC[0:2, :], X.ones2[:], sq[:, :], True, True)])
                p.ve("tensor_scalar", X.row_b[:, :], X.PB[0:2, 0:JW],
                     1.0 / 64, None, ALU.mult)
                p.act(X.row_a[:, :], X.row_b[:, :], AF.Square)
                p.ve("scalar_tensor_tensor", X.row_c[:, :], X.PC[0:2, :],
                     1.0 / 64, X.row_a[:, :], ALU.mult, ALU.subtract)
                p.act(X.row_c[:, :], X.row_c[:, :], AF.Ln, bias=EPS)
                p.act(X.row_a[:, :], X.row_c[:, :], AF.Exp, scale=-0.5)
                p.ve("scalar_tensor_tensor", X.row_b[:, :], X.row_b[:, :],
                     -1.0, X.row_a[:, :], ALU.mult, ALU.mult)
                p.ve("tensor_copy", row16a[0:2, :], X.row_a[:, :])
                p.ve("tensor_copy", row16c[0:2, :], X.row_b[:, :])
                p.mm([(X.PB[:, 0:JW], X.sel2[:], row16a[0:2, :], True, True),
                      (X.PC[:, :], X.sel2[:], row16c[0:2, :], True, True)])
                p.act(ab16[:, :], X.PB[:, 0:JW], AF.Copy)
                p.act(cb16[:, :], X.PC[:, :], AF.Copy)
                p.ve("tensor_mul", t1[:, :], sl, ab16[:, :])
                p.ve("tensor_add", t1[:, :], t1[:, :], cb16[:, :])
                p.ve("tensor_scalar", sl, t1[:, :], wl[:], bl[:],
                     ALU.mult, ALU.add)
    qkln(qb, lnw["qlw"], lnw["qlb"])
    qkln(kb, lnw["klw"], lnw["klb"])

    p.fence()
    with ExitStack() as es2:
        xj = ctx.sbuf("xj", [128, NCH, JW], F32, es2)
        s0b = ctx.sbuf("s0b", [128, 2, T], F32, es2)
        pt = ctx.sbuf("pt", [128, T], BF16, es2)
        ptT = ctx.sbuf("ptT", [128, T // 128, JW], BF16, es2)
        negm = ctx.sbuf("negm", [128, 1], F32, es2)
        bcol = ctx.sbuf("bcol", [128, 1], F32, es2)
        zcol = ctx.sbuf("zcol", [128, 4], F32, es2)
        rrow = X.row_c[0:1, :]
        rb64 = ctx.sbuf("rb64", [64, JW], F32, es2)
        ao = ctx.sbuf("ao", [128, NCH, JW], BF16, es2)

        for J in range(NJ):
            j0 = J * JW
            for c in range(NCH):
                p.ld("sync", xj[:, c, :],
                     x_ap[c * 128:(c + 1) * 128, j0:j0 + JW])
            for h in range(H):
                hr0 = (h % 2) * 64
                hc = h // 2
                for jj in range(4):
                    qt0 = j0 + jj * 128
                    ktw = qt0 + 128
                    ops = []
                    n0 = 0
                    while n0 < ktw:
                        nn = min(JW, ktw - n0)
                        ops.append((X.PA[:, n0:n0 + nn],
                                    qb[hr0:hr0 + 64, hc, qt0:qt0 + 128],
                                    kb[hr0:hr0 + 64, hc, n0:n0 + nn],
                                    True, True))
                        n0 += nn
                    p.mm(ops)
                    sj = s0b[:, jj % 2, 0:ktw]
                    p.ve("tensor_copy", sj, X.PA[:, 0:ktw])
                    p.ve("tensor_add", sj[:, ktw - 128:ktw],
                         sj[:, ktw - 128:ktw], diagm[:])
                    p.ve("reduce_max", negm[:], sj, axis=AX.X,
                         negate=True)
                    p.ve("tensor_scalar", bcol[:], negm[:], SCALE, None,
                         ALU.mult)
                    p.act(pt[:, 0:ktw], sj, AF.Exp, bias=bcol[:],
                          scale=SCALE, accum_out=zcol[:, jj:jj + 1])
                    nch_k = ktw // 128
                    for g0 in range(0, nch_k, 8):
                        gn = min(8, nch_k - g0)
                        p.tr([(X.PT[:, i * 128:(i + 1) * 128],
                               pt[:, (g0 + i) * 128:(g0 + i + 1) * 128])
                              for i in range(gn)], X.ident16)
                        p.act(ptT[:, g0:g0 + gn, jj * 128:(jj + 1) * 128],
                              X.PT[:, 0:gn * 128].rearrange(
                                  "p (a b) -> p a b", b=128), AF.Copy)

                p.mm([(X.PB[0:1, JW + jj * 128:JW + (jj + 1) * 128],
                       zcol[:, jj:jj + 1], X.identf[:], True, True)
                      for jj in range(4)])
                p.act(rrow[:], X.PB[0:1, JW:2 * JW], AF.Ln)
                p.act(rrow[:], rrow[:], AF.Exp, scale=-1.0)
                p.mm([(X.PB[0:64, 0:JW], X.onesf_128[0:1, 0:64], rrow[:],
                       True, True)])
                p.act(rb64[:], X.PB[0:64, 0:JW], AF.Copy)
                ops = []
                for c in range(4 * J + 4):
                    k = c - 4 * J
                    lo = max(0, k) * 128
                    ops.append((X.PC[0:64, lo:JW],
                                vt[:, c, h * 64:(h + 1) * 64],
                                ptT[:, c, lo:JW], c == 0, c == 4 * J + 3))
                p.mm(ops)
                p.ve("tensor_mul",
                     ao[(h % 2) * 64:(h % 2) * 64 + 64, h // 2, :],
                     X.PC[0:64, :], rb64[:])

            for cout in range(NCH):
                ws = cout % 2
                for ci in range(NCH):
                    p.ld("gpsimd", wo_j[:, ws, ci, :],
                         W["a0_Wo"][ci * 128:(ci + 1) * 128,
                                    cout * 128:(cout + 1) * 128])
                p.mm([(X.PC[:, :], wo_j[:, ws, ci, :],
                       ao[:, ci, :], ci == 0, False) for ci in range(NCH)]
                     + [(X.PC[:, :], bo_row[0:1, cout * 128:(cout + 1) * 128],
                         X.onesf_row[:], False, True)])
                p.ve("tensor_add", xj[:, cout, :], X.PC[:, :], xj[:, cout, :])
            emit_outer_ln(ctx, xj, xj, y_ap, j0)



L1A_STAGE = 99
L1A_NJ = NJ
L1A_DBG = False


def emit_l1_attn(ctx, W, x_ap, y_ap):
    if L1A_DBG:
        dbg = {n: ctx.nc.dram_tensor("dbg_" + n, shp, F32,
                                     kind="ExternalOutput").ap()
               for n, shp in (("scores", [128, T]), ("m8", [128, 8]),
                              ("u", [128, T]), ("pt", [128, T]),
                              ("zcol", [128, 4]), ("ao", [64, JW]),
                              ("q", [64, 128]), ("k", [64, T]),
                              ("rj", [128, NCH * JW]),
                              ("aof", [128, NCH * JW]),
                              ("xj0", [128, NCH * JW]))}
    p, X = ctx.plan, ctx
    qb = ctx.sbuf("qb", [128, NCH, T], BF16)
    kb = ctx.sbuf("kb", [128, NCH, T], BF16)
    vt = ctx.sbuf("vt", [128, T // 128, D], BF16)
    wo_j = ctx.sbuf("wo_j", [128, 2, NCH, 128], BF16)
    bo_row = ctx.sbuf("bo_row", [1, D], F32)

    with ExitStack() as es1:
        xb = ctx.sbuf("xb", [128, NCH, T], BF16, es1)
        bqkv = [ctx.sbuf("bqkv", [1, D], F32, es1) for _ in range(3)]
        wcol = ctx.sbuf("wcol", [128, 2, NCH, 128], BF16, es1)
        stage = ctx.sbuf("stage", [128, 2, JW], BF16, es1)

        load_chunked(ctx, xb, x_ap, NCH)
        p.ld("gpsimd", bo_row[:], W["a1_bo_row"])
        for i, n in enumerate(("a1_bq", "a1_bk", "a1_bv")):
            p.ld("gpsimd", bqkv[i][:], W[n])

        qkv_ps = [X.PC[:, 0:JW], X.PB[:, 0:JW]]
        for wi, (w_n, kind) in enumerate((("a1_Wq", "q"), ("a1_Wk", "k"),
                                          ("a1_Wv", "v"))):
            for cout in range(NCH):
                slot = (wi * NCH + cout) % 2
                for ci in range(NCH):
                    p.ld("gpsimd", wcol[:, slot, ci, :],
                         W[w_n][ci * 128:(ci + 1) * 128,
                                cout * 128:(cout + 1) * 128])
                for jn in range(T // JW):
                    n0 = jn * JW
                    ps = qkv_ps[jn % 2]
                    p.mm([(ps, wcol[:, slot, ci, :], xb[:, ci, n0:n0 + JW],
                           ci == 0, False) for ci in range(NCH)]
                         + [(ps, bqkv[wi][0:1, cout * 128:(cout + 1) * 128],
                             X.onesf_row[:], False, True)])
                    if kind in ("q", "k"):
                        out_sb = qb if kind == "q" else kb
                        p.act(out_sb[:, cout, n0:n0 + JW], ps, AF.Copy)
                    else:
                        stg = stage[:, jn % 2, :]
                        p.act(stg, ps, AF.Copy)
                        p.tr([(X.PT[:, i * 128:(i + 1) * 128],
                               stg[:, i * 128:(i + 1) * 128])
                              for i in range(4)], X.ident16)
                        p.ve("tensor_copy",
                             vt[:, jn * 4:jn * 4 + 4,
                                cout * 128:(cout + 1) * 128],
                             X.PT[:, 0:512].rearrange("p (a b) -> p a b", a=4))

    p.fence()
    with ExitStack() as es2:
        xj = ctx.sbuf("xj", [128, NCH, JW], F32, es2)
        s_sb = ctx.sbuf("s_sb", [128, 4, T], F32, es2)
        scr = ctx.sbuf("scr", [128, 1024], F32, es2)
        cand = ctx.sbuf("cand", [128, 4, 256], F32, es2)
        m8 = ctx.sbuf("m8", [128, 4, 8], F32, es2)
        taucol = ctx.sbuf("taucol", [128, 4], F32, es2)
        pt = ctx.sbuf("pt", [128, T], BF16, es2)
        ptT = ctx.sbuf("ptT", [128, T // 128, JW], BF16, es2)
        zcol = ctx.sbuf("zcol", [128, 4], F32, es2)
        rrow = X.row_c[0:1, :]
        rb64 = ctx.sbuf("rb64", [64, JW], F32, es2)
        ao = ctx.sbuf("ao", [128, NCH, JW], BF16, es2)

        for J in range(L1A_NJ):
            j0 = J * JW
            for c in range(NCH):
                p.ld("sync", xj[:, c, :],
                     x_ap[c * 128:(c + 1) * 128, j0:j0 + JW])
            for h in range(H if L1A_STAGE >= 2 else 0):
                hr0 = (h % 2) * 64
                hc = h // 2
                for jj in range(4):
                    qt0 = j0 + jj * 128
                    p.mm([(X.PA[:, n * JW:(n + 1) * JW],
                           qb[hr0:hr0 + 64, hc, qt0:qt0 + 128],
                           kb[hr0:hr0 + 64, hc, n * JW:(n + 1) * JW],
                           True, True) for n in range(T // JW)])
                    sj = s_sb[:, jj, :]
                    p.ve("tensor_copy", sj, X.PA[:, :])
                    cnd = cand[:, jj, :]
                    def tk_a(v, sj=sj, cnd=cnd):
                        last = None
                        for c in range(16):
                            last = v.max(out=cnd[:, c * 8:(c + 1) * 8],
                                         in_=sj[:, c * 128:(c + 1) * 128])
                        return last
                    p.step("vector", tk_a, r=[sj], w=[cnd])
                    for half in range(2):
                        def tk_b(v, sj=sj, cnd=cnd, half=half):
                            last = None
                            for c8 in range(8):
                                c = half * 8 + c8
                                last = v.match_replace(
                                    out=scr[:, c8 * 128:(c8 + 1) * 128],
                                    in_to_replace=cnd[:, c * 8:(c + 1) * 8],
                                    in_values=sj[:, c * 128:(c + 1) * 128],
                                    imm_value=-1e30)
                            return last
                        p.step("vector", tk_b, r=[sj, cnd], w=[scr[:]])
                        def tk_c(v, cnd=cnd, half=half):
                            last = None
                            for c8 in range(8):
                                c = half * 8 + c8
                                last = v.max(
                                    out=cnd[:, 128 + c * 8:136 + c * 8],
                                    in_=scr[:, c8 * 128:(c8 + 1) * 128])
                            return last
                        p.step("vector", tk_c, r=[scr[:]], w=[cnd])
                def dwa(jj):
                    return scr[:, jj * 256:(jj + 1) * 256]
                def dwb(jj):
                    return cand[:, jj, :]
                def mstep(fn, r, w):
                    p.step("vector", fn, r=r, w=w)
                mstep(lambda v: [v.max(out=m8[:, jj, :], in_=cand[:, jj, :])
                                 for jj in range(4)][-1],
                      r=[cand[:]], w=[m8[:]])
                mstep(lambda v: [v.match_replace(
                    out=dwa(jj), in_to_replace=m8[:, jj, :],
                    in_values=cand[:, jj, :], imm_value=-1e30)
                    for jj in range(4)][-1],
                      r=[cand[:], m8[:]], w=[scr[:]])
                mstep(lambda v: [v.max(out=m8[:, jj, :], in_=dwa(jj))
                                 for jj in range(4)][-1],
                      r=[scr[:]], w=[m8[:]])
                mstep(lambda v: [v.match_replace(
                    out=dwb(jj), in_to_replace=m8[:, jj, :],
                    in_values=dwa(jj), imm_value=-1e30)
                    for jj in range(4)][-1],
                      r=[scr[:], m8[:]], w=[cand[:]])
                mstep(lambda v: [v.max(out=m8[:, jj, :], in_=dwb(jj))
                                 for jj in range(4)][-1],
                      r=[cand[:]], w=[m8[:]])
                mstep(lambda v: [v.match_replace(
                    out=dwa(jj), in_to_replace=m8[:, jj, :],
                    in_values=dwb(jj), imm_value=-1e30)
                    for jj in range(4)][-1],
                      r=[cand[:], m8[:]], w=[scr[:]])
                mstep(lambda v: [v.max(out=m8[:, jj, :], in_=dwa(jj))
                                 for jj in range(4)][-1],
                      r=[scr[:]], w=[m8[:]])
                mstep(lambda v: [v.tensor_scalar(
                    taucol[:, jj:jj + 1], m8[:, jj, 7:8], -SCALE, None,
                    ALU.mult) for jj in range(4)][-1],
                      r=[m8[:]], w=[taucol[:]])
                for jj in range(4):
                    sj = s_sb[:, jj, :]
                    p.act(sj, sj, AF.Exp, bias=taucol[:, jj:jj + 1],
                          scale=SCALE)
                    p.ve("scalar_tensor_tensor", pt[:, :], sj, 0.99995, sj,
                         ALU.is_ge, ALU.mult)
                    p.act(pt[:, :], pt[:, :], AF.Copy,
                          accum_out=zcol[:, jj:jj + 1])
                    for g in range(2):
                        p.tr([(X.PT[:, i * 128:(i + 1) * 128],
                               pt[:, (g * 8 + i) * 128:(g * 8 + i + 1) * 128])
                              for i in range(8)], X.ident16)
                        p.act(ptT[:, g * 8:g * 8 + 8, jj * 128:(jj + 1) * 128],
                              X.PT[:].rearrange("p (a b) -> p a b", a=8),
                              AF.Copy)

                if L1A_STAGE < 5:
                    continue
                p.mm([(X.PB[0:1, JW + jj * 128:JW + (jj + 1) * 128],
                       zcol[:, jj:jj + 1], X.identf[:], True, True)
                      for jj in range(4)])
                p.act(rrow[:], X.PB[0:1, JW:2 * JW], AF.Ln)
                p.act(rrow[:], rrow[:], AF.Exp, scale=-1.0)
                p.mm([(X.PB[0:64, 0:JW], X.onesf_128[0:1, 0:64], rrow[:],
                       True, True)])
                p.act(rb64[:], X.PB[0:64, 0:JW], AF.Copy)
                p.mm([(X.PC[0:64, :], vt[:, c, h * 64:(h + 1) * 64],
                       ptT[:, c, :], c == 0, c == T // 128 - 1)
                      for c in range(T // 128)])
                p.ve("tensor_mul",
                     ao[(h % 2) * 64:(h % 2) * 64 + 64, h // 2, :],
                     X.PC[0:64, :], rb64[:])
                if L1A_DBG and J == 0 and h == 0:
                    p.ld("sync", dbg["zcol"], zcol[:])
                    p.ld("gpsimd", dbg["ao"], ao[0:64, 0, :])

            if L1A_DBG and J == 0:
                p.ld("gpsimd", dbg["xj0"],
                     xj[:].rearrange("p a b -> p (a b)"))
            for cout in range(NCH if L1A_STAGE >= 6 else 0):
                ws = cout % 2
                for ci in range(NCH):
                    p.ld("gpsimd", wo_j[:, ws, ci, :],
                         W["a1_Wo"][ci * 128:(ci + 1) * 128,
                                    cout * 128:(cout + 1) * 128])
                p.mm([(X.PC[:, :], wo_j[:, ws, ci, :],
                       ao[:, ci, :], ci == 0, False) for ci in range(NCH)]
                     + [(X.PC[:, :], bo_row[0:1, cout * 128:(cout + 1) * 128],
                         X.onesf_row[:], False, True)])
                p.ve("tensor_add", xj[:, cout, :], X.PC[:, :], xj[:, cout, :])
            if L1A_DBG and J == 0:
                p.ld("gpsimd", dbg["aof"],
                     ao[:].rearrange("p a b -> p (a b)"))
                p.ld("gpsimd", dbg["rj"],
                     xj[:].rearrange("p a b -> p (a b)"))
            if L1A_STAGE >= 7:
                emit_outer_ln(ctx, xj, xj, y_ap, j0)



def emit_l0_ffn(ctx, W, x_ap, y_ap):
    p, X = ctx.plan, ctx
    DFF = 4096
    NFC = DFF // JW
    gw_sb = ctx.sbuf("gw_sb", [128, NCH, 8], BF16)
    gb_row = ctx.sbuf("gb_row", [1, 8], F32)
    bi_row = ctx.sbuf("bi_row", [1, DFF], F32)
    bo_row = ctx.sbuf("bo_row", [1, D], F32)
    at_sb = ctx.sbuf("at_sb", [128, 8, NCH, 16], BF16)
    u16 = ctx.sbuf("u16", [16, 8, JW], BF16)

    for ci in range(NCH):
        p.ld("gpsimd", gw_sb[:, ci, :], W["f0_gwT"][ci * 128:(ci + 1) * 128, :])
    p.ld("gpsimd", gb_row[:], W["f0_gb"])
    p.ld("gpsimd", bi_row[:], W["f0_bi"])
    p.ld("gpsimd", bo_row[:], W["f0_bo_row"])
    p.ld("gpsimd", at_sb[:], W["f0_AT"][:])

    with ExitStack() as es2:
        xbJ = ctx.sbuf("xbJ", [128, NCH, JW], BF16, es2)
        xj = ctx.sbuf("xj", [128, NCH, JW], F32, es2)
        lg = ctx.sbuf("lg", [128, 4, 8], F32, es2)
        srt = ctx.sbuf("srt", [128, 4, 8], F32, es2)
        ge = ctx.sbuf("ge", [128, 4, 8], F32, es2)
        vs = ctx.sbuf("vs", [128, 4], F32, es2)
        n1c = ctx.sbuf("n1c", [128, 1], F32, es2)
        e2c = ctx.sbuf("e2c", [128, 1], F32, es2)
        rzc = ctx.sbuf("rzc", [128, 1], F32, es2)
        zc = ctx.sbuf("zc", [128, 1], F32, es2)
        jk8 = ctx.sbuf("jk8", [128, 8], F32, es2)
        vs2row = ctx.sbuf("vs2row", [1, JW], F32, es2)
        wi_fc = ctx.sbuf("wi_fc", [128, 2, NCH, JW], BF16, es2)
        bt_fc = ctx.sbuf("bt_fc", [16, 2, 8, JW], BF16, es2)
        base_sb = ctx.sbuf("base_sb", [128, 4, JW], F32, es2)
        tmp_f = ctx.sbuf("tmp_f", [128, 2, 4, JW], F32, es2)
        rsum = ctx.sbuf("rsum", [128, 2, 4, JW], BF16, es2)
        rk = ctx.sbuf("rk", [128, 2, 4, JW], BF16, es2)
        rt_sb = ctx.sbuf("rt_sb", [128, 4 * NFC, JW], BF16, es2)
        wo_co = ctx.sbuf("wo_co", [128, 2, 4 * NFC, 128], BF16, es2)

        p.fence()

        for J in range(NJ):
            j0 = J * JW
            for c in range(NCH):
                p.ld("gpsimd", xbJ[:, c, :],
                     x_ap[c * 128:(c + 1) * 128, j0:j0 + JW])
                p.ld("sync", xj[:, c, :],
                     x_ap[c * 128:(c + 1) * 128, j0:j0 + JW])

            for ts in range(4):
                p.mm([(X.PC[:, 0:8], xbJ[:, ci, ts * 128:(ts + 1) * 128],
                       gw_sb[:, ci, :], ci == 0, False) for ci in range(NCH)]
                     + [(X.PC[:, 0:8], X.onesf_128[:], gb_row[:], False, True)])
                p.act(lg[:, ts, :], X.PC[:, 0:8], AF.Copy)
                p.step("vector", (lambda ts=ts: lambda v: v.max(
                    out=srt[:, ts, :], in_=lg[:, ts, :]))(),
                    r=[lg[:, ts, :]], w=[srt[:, ts, :]])
                p.ve("tensor_scalar", n1c[:], srt[:, ts, 0:1], -1.0, None,
                     ALU.mult)
                p.act(jk8[:], lg[:, ts, :], AF.Exp, bias=n1c[:],
                      accum_out=zc[:])
                p.act(e2c[:], srt[:, ts, 1:2], AF.Exp, bias=n1c[:])
                p.ve("reciprocal", rzc[:], zc[:])
                p.ve("scalar_tensor_tensor", vs[:, ts:ts + 1], e2c[:], 1.0,
                     rzc[:], ALU.add, ALU.mult)
                def mk_ge(v, ts=ts):
                    last = None
                    for e in range(8):
                        last = v.scalar_tensor_tensor(
                            ge[:, ts, e:e + 1], lg[:, ts, e:e + 1],
                            srt[:, ts, 1:2], vs[:, ts:ts + 1],
                            ALU.is_ge, ALU.mult)
                    return last
                p.step("vector", mk_ge,
                       r=[lg[:, ts, :], srt[:, ts, :], vs[:, ts:ts + 1]],
                       w=[ge[:, ts, :]])
            p.ve("tensor_scalar", vs[:, :], vs[:, :], 2.0, None, ALU.mult)
            p.mm([(X.PB[0:1, ts * 128:(ts + 1) * 128], vs[:, ts:ts + 1],
                   X.identf[:], True, True) for ts in range(4)])
            p.act(vs2row[:], X.PB[0:1, 0:JW], AF.Copy)

            for e in range(8):
                p.mm([(X.PB[0:16, 0:JW], at_sb[:, e, ci, :], xbJ[:, ci, :],
                       ci == 0, ci == NCH - 1) for ci in range(NCH)])
                p.act(u16[0:16, e, :], X.PB[0:16, 0:JW], AF.Copy)

            for fc in range(NFC):
                slot = fc % 2
                for ci in range(NCH):
                    p.ld("gpsimd", wi_fc[:, slot, ci, :],
                         W["f0_wiT"][ci * 128:(ci + 1) * 128,
                                     fc * JW:(fc + 1) * JW])
                for e in range(8):
                    p.ld("gpsimd", bt_fc[0:16, slot, e, :],
                         W["f0_BT"][e][:, fc * JW:(fc + 1) * JW])
                for ts in range(4):
                    p.mm([(X.PA[:, ts * JW:(ts + 1) * JW],
                           xbJ[:, ci, ts * 128:(ts + 1) * 128],
                           wi_fc[:, slot, ci, :], ci == 0, False)
                          for ci in range(NCH)]
                         + [(X.PA[:, ts * JW:(ts + 1) * JW], X.onesf_128[:],
                             bi_row[0:1, fc * JW:(fc + 1) * JW], False, True)])
                for hp in range(2):
                    p.act(base_sb[:, 2 * hp:2 * hp + 2, :].rearrange(
                        "p a b -> p (a b)"),
                        X.PA[:, hp * 1024:(hp + 1) * 1024], AF.Copy)
                for e in range(8):
                    tpar = e % 2
                    for hp in range(2):
                        p.mm([(X.PA[:, ts * JW:(ts + 1) * JW],
                               u16[0:16, e, ts * 128:(ts + 1) * 128],
                               bt_fc[0:16, slot, e, :], True, True)
                              for ts in (2 * hp, 2 * hp + 1)])
                    for hp in range(2):
                        p.ve("scalar_tensor_tensor",
                             tmp_f[:, tpar, 2 * hp:2 * hp + 2, :].rearrange(
                                 "p a b -> p (a b)"),
                             X.PA[:, hp * 1024:(hp + 1) * 1024], 1.0,
                             base_sb[:, 2 * hp:2 * hp + 2, :].rearrange(
                                 "p a b -> p (a b)"),
                             ALU.mult, ALU.add)
                    dst = rsum if e < 2 else rk
                    for hp in range(2):
                        def mk_relu(s, e=e, dst=dst, tpar=tpar, hp=hp):
                            last = None
                            for ts in (2 * hp, 2 * hp + 1):
                                last = s.activation(dst[:, tpar, ts, :],
                                                    tmp_f[:, tpar, ts, :],
                                                    AF.Relu, bias=0.0,
                                                    scale=ge[:, ts, e:e + 1])
                            return last
                        p.step("scalar", mk_relu,
                               r=[tmp_f[:, tpar, 2 * hp:2 * hp + 2], ge[:]],
                               w=[dst[:, tpar, 2 * hp:2 * hp + 2]])
                    if e >= 2:
                        for hp in range(2):
                            p.ve("tensor_tensor",
                                 rsum[:, tpar, 2 * hp:2 * hp + 2].rearrange(
                                     "p a b -> p (a b)"),
                                 rsum[:, tpar, 2 * hp:2 * hp + 2].rearrange(
                                     "p a b -> p (a b)"),
                                 rk[:, tpar, 2 * hp:2 * hp + 2].rearrange(
                                     "p a b -> p (a b)"),
                                 eng=("vector" if tpar == 0 else "gpsimd"),
                                 op=ALU.add)
                p.ve("tensor_tensor",
                     rsum[:, 0].rearrange("p a b -> p (a b)"),
                     rsum[:, 0].rearrange("p a b -> p (a b)"),
                     rsum[:, 1].rearrange("p a b -> p (a b)"),
                     op=ALU.add)
                for half in range(2):
                    ops = []
                    for i in range(8):
                        ts, fs = (half * 8 + i) // 4, (half * 8 + i) % 4
                        ops.append((X.PT[:, i * 128:(i + 1) * 128],
                                    rsum[:, 0, ts, fs * 128:(fs + 1) * 128]))
                    p.tr(ops, X.ident16)
                    for i in range(8):
                        ts, fs = (half * 8 + i) // 4, (half * 8 + i) % 4
                        p.act(rt_sb[:, fc * 4 + fs, ts * 128:(ts + 1) * 128],
                              X.PT[:, i * 128:(i + 1) * 128], AF.Copy)

            for cout in range(NCH):
                ws = cout % 2
                p.ld("gpsimd", wo_co[:, ws, :, :],
                     W["f0_woTt"][:, :, cout * 128:(cout + 1) * 128])
                p.mm([(X.PB[:, 0:JW], wo_co[:, ws, fcc, :], rt_sb[:, fcc, :],
                       fcc == 0, False) for fcc in range(4 * NFC)]
                     + [(X.PB[:, 0:JW],
                         bo_row[0:1, cout * 128:(cout + 1) * 128],
                         vs2row[:], False, True)])
                p.ve("tensor_add", xj[:, cout, :], X.PB[:, 0:JW],
                     xj[:, cout, :])
            emit_outer_ln(ctx, xj, xj, y_ap, j0)



def emit_l1_ffn(ctx, W, x_ap, y_ap):
    p, X = ctx.plan, ctx
    nc = ctx.nc
    FULL = 12288
    NF128 = FULL // 128
    NSC = 12
    R_THRESH = 0.7 * 4096.0 * 8.0

    carry = [nc.dram_tensor(ctx.nm("carry"), [D, T], BF16, kind="Internal").ap()
             for _ in range(2)]
    nov_in = nc.dram_tensor(ctx.nm("nov_in"), [1, 512], F32, kind="Internal").ap()
    nov_out = nc.dram_tensor(ctx.nm("nov_out"), [1, 512], F32, kind="Internal").ap()
    nov_stage = ctx.sbuf("nov_stage", [1, 512], F32)

    xnew = ctx.sbuf("xnew", [128, NCH, T], F32)
    active = ctx.sbuf("active", [128, NF128], F32)
    hist = ctx.sbuf("hist", [128, NF128], F32)
    msk = ctx.sbuf("msk", [128, NF128], F32)
    comb = ctx.sbuf("comb", [128, NF128], F32)
    d96 = ctx.sbuf("d96", [128, NF128], F32)
    ck_sb = ctx.sbuf("ck_sb", [128, NF128], F32)
    jk96 = ctx.sbuf("jk96", [128, NF128], F32)
    pooled = ctx.sbuf("pooled", [128, NCH], F32)
    pooled16 = ctx.sbuf("pooled16", [128, NCH], BF16)
    cnt_c = ctx.sbuf("cnt_c", [128, 1], F32)
    cnt16 = ctx.sbuf("cnt16", [128, 1], BF16)
    rr16 = ctx.sbuf("rr16", [128, 2], BF16)
    wc1_sb = ctx.sbuf("wc1_sb", [128, NCH, 256], BF16)
    srow = ctx.sbuf("srow", [1, 16], F32)
    dcol = ctx.sbuf("dcol", [128, 1], F32)
    omdcol = ctx.sbuf("omdcol", [128, 1], F32)
    midcol = ctx.sbuf("midcol", [128, 1], F32)
    colr = ctx.sbuf("colr", [128, 1], F32)

    SL = {n: i for i, n in enumerate(
        ("lo", "hi", "mid", "tot", "ind", "tmp", "nov", "done", "dnew"))}

    def sl(n):
        return srow[0:1, SL[n]:SL[n] + 1]

    for ci in range(NCH):
        p.dma("gpsimd", (lambda ci=ci: lambda g: g.dma_start(
            out=wc1_sb[:, ci, :], in_=W["f1_Wc1T"][ci * 128:(ci + 1) * 128, :]))())

    def init_v(v):
        v.memset(active[:], 0.0)
        v.memset(active[:, 0:32], 1.0)
        v.memset(hist[:], 0.0)
        v.memset(hist[:, 0:32], 1.0)
        return v.memset(srow[:], 0.0)
    p.step("vector", init_v)

    wm_sc = ctx.sbuf("wm_sc", [128, 2, NCH, 1024], BF16)
    wp_sc = ctx.sbuf("wp_sc", [128, 2, 8, 1024], BF16)
    hb = ctx.sbuf("hb", [128, 8, JW], BF16)
    xcJ = ctx.sbuf("xcJ", [128, NCH, T], BF16)
    bl_t1f = ctx.sbuf("bl_t1f", [128, 2, JW], F32)
    bl_st16 = ctx.sbuf("bl_st16", [128, 2, JW], BF16)

    def load_mwp(sc, slot):
        f0 = sc * 1024
        for ci in range(NCH):
            p.ld("gpsimd", wm_sc[:, slot, ci, :],
                 W["f1_WmT"][ci * 128:(ci + 1) * 128, f0:f0 + 1024])
        for fi in range(8):
            p.ld("gpsimd", wp_sc[:, slot, fi, :],
                 W["f1_WpT"][f0 + fi * 128:f0 + (fi + 1) * 128, :])

    for st in range(3):
        src_ap = x_ap if st == 0 else carry[(st + 1) % 2]
        dst_ap = carry[st % 2]

        for ci in range(NCH):
            p.ld("gpsimd", xcJ[:, ci, :], src_ap[ci * 128:(ci + 1) * 128, :])
        if st == 0:
            load_mwp(0, 0)

        for sc in range(NSC):
            slot = sc % 2
            if sc + 1 < NSC:
                load_mwp(sc + 1, (sc + 1) % 2)
            elif st < 2:
                load_mwp(0, 0)
            for t4 in range(4):
                t0 = t4 * JW
                t4p = t4 % 2
                for fi in range(8):
                    s0 = (fi % 2) * JW
                    p.mm([(X.PB[:, s0:s0 + JW],
                           wm_sc[:, slot, ci, fi * 128:(fi + 1) * 128],
                           xcJ[:, ci, t0:t0 + JW],
                           ci == 0, ci == NCH - 1) for ci in range(NCH)])
                    acol = active[:, sc * 8 + fi:sc * 8 + fi + 1]
                    p.act(hb[:, fi, :], X.PB[:, s0:s0 + JW], AF.Relu,
                          bias=0.0, scale=acol)
                for half in range(2):
                    for cc in range(4):
                        c = half * 4 + cc
                        p.mm([(X.PA[:, cc * JW:(cc + 1) * JW],
                               wp_sc[:, slot, fi, c * 128:(c + 1) * 128],
                               hb[:, fi, :], fi == 0, fi == 7)
                              for fi in range(8)])
                    for cc in range(4):
                        c = half * 4 + cc
                        if sc == 0:
                            p.ve("tensor_copy", xnew[:, c, t0:t0 + JW],
                                 X.PA[:, cc * JW:(cc + 1) * JW])
                        else:
                            p.ve("tensor_add", xnew[:, c, t0:t0 + JW],
                                 X.PA[:, cc * JW:(cc + 1) * JW],
                                 xnew[:, c, t0:t0 + JW])

        for c in range(NCH):
            p.step("vector", (lambda c=c: lambda v: v.reduce_sum(
                pooled[:, c:c + 1], xnew[:, c, :], axis=AX.X))())
        p.step("vector", lambda v: v.tensor_copy(pooled16[:], pooled[:]))
        for rc in range(2):
            def mm_r(t, rc=rc):
                last = None
                for ci in range(NCH):
                    last = t.matmul(X.PB[:, rc:rc + 1],
                                    wc1_sb[:, ci, rc * 128:(rc + 1) * 128],
                                    pooled16[:, ci:ci + 1], start=(ci == 0),
                                    stop=(ci == NCH - 1))
                return last
            p.step("tensor", mm_r)
            p.step("scalar", (lambda rc=rc: lambda s: s.activation(
                rr16[:, rc:rc + 1], X.PB[:, rc:rc + 1], AF.Relu,
                bias=0.0, scale=1.0 / T))())
        wc2a = wm_sc[:, 1, :, :].rearrange("p a b -> p (a b)")
        wc2b = wp_sc[:, 1, 0:4, :].rearrange("p a b -> p (a b)")
        for rc in range(2):
            p.ld("gpsimd", wc2a, W["f1_Wc2T"][rc * 128:(rc + 1) * 128, 0:8192])
            p.ld("gpsimd", wc2b, W["f1_Wc2T"][rc * 128:(rc + 1) * 128, 8192:FULL])
            def mm_ck(t, rc=rc, wc2a=wc2a, wc2b=wc2b):
                last = None
                for f in range(NF128):
                    src = (wc2a[:, f * 128:(f + 1) * 128] if f < 64 else
                           wc2b[:, (f - 64) * 128:(f - 63) * 128])
                    last = t.matmul(X.PC[:, f:f + 1], src,
                                    rr16[:, rc:rc + 1], start=(rc == 0),
                                    stop=(rc == 1))
                return last
            p.step("tensor", mm_ck,
                   r=[wc2a, wc2b, rr16[:, rc:rc + 1]], w=[X.PC[:, 0:NF128]])
        p.step("scalar", lambda s: s.activation(ck_sb[:], X.PC[:, 0:NF128],
                                                AF.Copy, bias=0.0, scale=1.0))

        p.step("vector", lambda v: v.reduce_max(colr[:], ck_sb[:], axis=AX.X))
        p.step("tensor", lambda t: t.matmul(X.PB[0:1, 0:128], colr[:], X.identf[:],
                                            start=True, stop=True))
        p.step("vector", lambda v: v.reduce_max(sl("hi"), X.PB[0:1, 0:128], axis=AX.X))
        p.step("vector", lambda v: v.tensor_reduce(colr[:], ck_sb[:], axis=AX.X,
                                                   op=ALU.min))
        p.step("tensor", lambda t: t.matmul(X.PB[0:1, 0:128], colr[:], X.identf[:],
                                            start=True, stop=True))
        p.step("vector", lambda v: v.tensor_reduce(sl("lo"), X.PB[0:1, 0:128],
                                                   axis=AX.X, op=ALU.min))
        p.step("vector", lambda v: v.tensor_scalar(sl("hi"), sl("hi"), 1e-6, None,
                                                   ALU.add))
        p.step("vector", lambda v: v.tensor_scalar(sl("lo"), sl("lo"), 1e-6, None,
                                                   ALU.subtract))

        for it in range(22):
            p.step("vector", lambda v: v.tensor_add(sl("mid"), sl("lo"), sl("hi")))
            p.step("vector", lambda v: v.tensor_scalar(sl("mid"), sl("mid"), 0.5,
                                                       None, ALU.mult))
            p.step("tensor", lambda t: t.matmul(X.PB[:, 2:3], X.onesf_128[:],
                                                sl("mid"), start=True, stop=True))
            p.step("scalar", lambda s: s.activation(midcol[:], X.PB[:, 2:3],
                                                    AF.Copy, bias=0.0, scale=1.0))
            p.step("vector", lambda v: v.tensor_scalar(jk96[:], ck_sb[:], midcol[:],
                                                       None, ALU.is_ge))
            p.step("vector", lambda v: v.reduce_sum(cnt_c[:], jk96[:], axis=AX.X))
            p.step("vector", lambda v: v.tensor_copy(cnt16[:], cnt_c[:]))
            p.step("tensor", lambda t: t.matmul(X.PB[0:1, 3:4], X.ones_col[:],
                                                cnt16[:], start=True, stop=True))
            p.step("scalar", lambda s: s.activation(sl("tot"), X.PB[0:1, 3:4],
                                                    AF.Copy, bias=0.0, scale=1.0))
            p.step("vector", lambda v: v.tensor_scalar(sl("ind"), sl("tot"), 4096.0,
                                                       None, ALU.is_ge))
            p.step("vector", lambda v: v.tensor_sub(sl("tmp"), sl("mid"), sl("lo")))
            p.step("vector", lambda v: v.tensor_mul(sl("tmp"), sl("tmp"), sl("ind")))
            p.step("vector", lambda v: v.tensor_add(sl("lo"), sl("lo"), sl("tmp")))
            p.step("vector", lambda v: v.tensor_sub(sl("tmp"), sl("mid"), sl("hi")))
            p.step("vector", lambda v: v.tensor_scalar(sl("ind"), sl("ind"), -1.0,
                                                       1.0, ALU.mult, ALU.add))
            p.step("vector", lambda v: v.tensor_mul(sl("tmp"), sl("tmp"), sl("ind")))
            p.step("vector", lambda v: v.tensor_add(sl("hi"), sl("hi"), sl("tmp")))

        p.step("tensor", lambda t: t.matmul(X.PB[:, 2:3], X.onesf_128[:], sl("lo"),
                                            start=True, stop=True))
        p.step("scalar", lambda s: s.activation(midcol[:], X.PB[:, 2:3], AF.Copy,
                                                bias=0.0, scale=1.0))
        p.step("vector", lambda v: v.tensor_scalar(msk[:], ck_sb[:], midcol[:],
                                                   None, ALU.is_ge))
        p.step("vector", lambda v: v.tensor_add(comb[:], hist[:], msk[:]))
        p.step("vector", lambda v: v.tensor_scalar(comb[:], comb[:], 1.0, None,
                                                   ALU.min))
        p.step("vector", lambda v: v.tensor_sub(d96[:], comb[:], hist[:]))
        p.step("vector", lambda v: v.reduce_sum(cnt_c[:], d96[:], axis=AX.X))
        p.step("vector", lambda v: v.tensor_copy(cnt16[:], cnt_c[:]))
        p.step("tensor", lambda t: t.matmul(X.PB[0:1, 3:4], X.ones_col[:], cnt16[:],
                                            start=True, stop=True))
        p.step("scalar", lambda s: s.activation(sl("nov"), X.PB[0:1, 3:4], AF.Copy,
                                                bias=0.0, scale=1.0))
        p.step("vector", lambda v: v.memset(nov_stage[:], 0.0))
        p.step("vector", lambda v: v.tensor_copy(nov_stage[0:1, 0:1], sl("nov")))
        p.dma("gpsimd", lambda g: g.dma_start(out=nov_in, in_=nov_stage[:]))
        p.step("gpsimd", lambda g: g.collective_compute(
            "AllReduce", ALU.add, [[0, 1, 2, 3, 4, 5, 6, 7]],
            [nov_in.opt()], [nov_out.opt()]), inc="gp", inc_k=1)
        p.dma("gpsimd", lambda g: g.dma_start(out=sl("nov"),
                                              in_=nov_out[0:1, 0:1]))
        p.step("vector", (lambda: lambda v: v.tensor_scalar(
            sl("dnew"), sl("nov"), R_THRESH, None, ALU.is_lt))())
        p.step("tensor", lambda t: t.matmul(X.PB[:, 2:3], X.onesf_128[:], sl("done"),
                                            start=True, stop=True))
        p.step("scalar", lambda s: s.activation(dcol[:], X.PB[:, 2:3], AF.Copy,
                                                bias=0.0, scale=1.0))
        p.step("vector", lambda v: v.tensor_scalar(omdcol[:], dcol[:], -1.0, 1.0,
                                                   ALU.mult, ALU.add))
        p.step("vector", lambda v: v.tensor_scalar(jk96[:], msk[:], omdcol[:], None,
                                                   ALU.mult))
        p.step("vector", lambda v: v.tensor_scalar(active[:], active[:], dcol[:],
                                                   None, ALU.mult))
        p.step("vector", lambda v: v.tensor_add(active[:], active[:], jk96[:]))
        p.step("vector", lambda v: v.tensor_scalar(jk96[:], comb[:], omdcol[:], None,
                                                   ALU.mult))
        p.step("vector", lambda v: v.tensor_scalar(hist[:], hist[:], dcol[:], None,
                                                   ALU.mult))
        p.step("vector", lambda v: v.tensor_add(hist[:], hist[:], jk96[:]))
        p.step("vector", lambda v: v.tensor_max(sl("done"), sl("done"), sl("dnew")))

        if True:
            t1f = bl_t1f
            st16 = bl_st16
            for t4 in range(4):
                t0 = t4 * JW
                for c in range(NCH):
                    par = c % 2
                    xc16 = xcJ[:, c, t0:t0 + JW]
                    xn = xnew[:, c, t0:t0 + JW]
                    if st < 2:
                        p.ve("tensor_scalar", t1f[:, par, :], xn,
                             omdcol[:], None, ALU.mult)
                        p.ve("scalar_tensor_tensor", st16[:, par, :], xc16,
                             dcol[:], t1f[:, par, :], ALU.mult, ALU.add)
                        p.ld("sync", dst_ap[c * 128:(c + 1) * 128, t0:t0 + JW],
                             st16[:, par, :])
                    else:
                        p.ve("tensor_scalar", xn, xn, omdcol[:], None, ALU.mult)
                        p.ve("scalar_tensor_tensor", xn, xc16,
                             dcol[:], xn, ALU.mult, ALU.add)
                        p.ld("sync", t1f[:, par, :],
                             x_ap[c * 128:(c + 1) * 128, t0:t0 + JW])
                        p.ve("tensor_add", xn, t1f[:, par, :], xn)
                if st == 2:
                    emit_outer_ln(ctx, xnew[:, :, t0:t0 + JW],
                                  xnew[:, :, t0:t0 + JW], y_ap, t0)



def build_block_program(block_name, wnames):
    nc = bass.Bass("TRN2", target_bir_lowering=False)
    x_ap = nc.dram_tensor("x", [D, T], F32, kind="ExternalInput").ap()
    y_ap = nc.dram_tensor("y", [D, T], F32, kind="ExternalOutput").ap()
    W = {n: nc.dram_tensor(n, WEIGHT_SHAPES[n], F32, kind="ExternalInput").ap()
         for n in wnames}
    C = {n: nc.dram_tensor(n, CONST_SHAPES[n], F32, kind="ExternalInput").ap()
         for n in CONST_SHAPES}
    plan = Plan()
    es = ExitStack()
    with es:
        ctx = Ctx(nc, plan, es)
        ctx.make_shared(C)
        if block_name == "l0_attn":
            emit_l0_attn(ctx, W, x_ap, y_ap)
        elif block_name == "l1_attn":
            emit_l1_attn(ctx, W, x_ap, y_ap)
        elif block_name == "l0_ffn":
            emit_l0_ffn(ctx, W, x_ap, y_ap)
        elif block_name == "l1_ffn":
            emit_l1_ffn(ctx, W, x_ap, y_ap)
        else:
            raise ValueError(block_name)
        plan.step("sync", lambda sy: sy.nop())
        sems = {s: es.enter_context(nc.semaphore(s)) for s in SEMS}
        block = es.enter_context(nc.Block())
        plan.emit(block, sems)
    return nc


BLOCK_WNAMES = {
    "l0_attn": ["a0_Wq", "a0_Wk", "a0_Wv", "a0_Wo", "a0_bo_row",
                "a0_kln_w", "a0_kln_b", "a0_qln_w", "a0_qln_b"],
    "l1_attn": ["a1_Wq", "a1_Wk", "a1_Wv", "a1_Wo", "a1_bo_row",
                "a1_bq", "a1_bk", "a1_bv"],
    "l0_ffn": ["f0_wiT", "f0_woTt", "f0_bi", "f0_bo_row", "f0_AT", "f0_BT",
               "f0_gwT", "f0_gb"],
    "l1_ffn": ["f1_WmT", "f1_WpT", "f1_Wc1T", "f1_Wc2T"],
}

_prog_cache = {}


EMITTERS = {}


def build_full_program():
    nc = bass.Bass("TRN2", target_bir_lowering=False)
    x_ap = nc.dram_tensor("x", [D, T], F32, kind="ExternalInput").ap()
    y_ap = nc.dram_tensor("y", [D, T], F32, kind="ExternalOutput").ap()
    W = {}
    for names in BLOCK_WNAMES.values():
        for n in names:
            W[n] = nc.dram_tensor(n, WEIGHT_SHAPES[n], F32, kind="ExternalInput").ap()
    C = {n: nc.dram_tensor(n, CONST_SHAPES[n], F32, kind="ExternalInput").ap()
         for n in CONST_SHAPES}
    xb1 = nc.dram_tensor("xbuf1", [D, T], F32, kind="Internal").ap()
    xb2 = nc.dram_tensor("xbuf2", [D, T], F32, kind="Internal").ap()
    xb3 = nc.dram_tensor("xbuf3", [D, T], F32, kind="Internal").ap()
    plan = Plan()
    es = ExitStack()
    with es:
        ctx = Ctx(nc, plan, es)
        ctx.make_shared(C)
        chain = [(emit_l0_attn, x_ap, xb1), (emit_l0_ffn, xb1, xb2),
                 (emit_l1_attn, xb2, xb3), (emit_l1_ffn, xb3, y_ap)]
        for emit, xin, xout in chain:
            plan.fence()
            with ExitStack() as bes:
                ctx.block_es = bes
                emit(ctx, W, xin, xout)
            ctx.block_es = es
        plan.step("sync", lambda sy: sy.nop())
        sems = {s: es.enter_context(nc.semaphore(s)) for s in SEMS}
        block = es.enter_context(nc.Block())
        plan.emit(block, sems)
    return nc


_consts_cache = {}


def run_block(block_name, x_fm_cores, W):
    if block_name not in _prog_cache:
        _prog_cache[block_name] = build_block_program(block_name,
                                                      BLOCK_WNAMES[block_name])
    if "c" not in _consts_cache:
        _consts_cache["c"] = make_consts()
    CONSTS = _consts_cache["c"]
    nc = _prog_cache[block_name]
    in_maps = []
    for c in range(8):
        m = {"x": x_fm_cores[c]}
        for n in BLOCK_WNAMES[block_name]:
            m[n] = W[n]
        m.update(CONSTS)
        in_maps.append(m)
    res = run_bass_kernel_spmd(nc, in_maps, core_ids=list(range(8)))
    return [res.results[c]["y"] for c in range(8)]


def kernel(**inputs):
    W = prep_weights(inputs)
    x = np.asarray(inputs["L0_attn_x"], np.float32)
    xs = [np.ascontiguousarray(x[c].T) for c in range(8)]
    if "full" not in _prog_cache:
        _prog_cache["full"] = build_full_program()
    if "c" not in _consts_cache:
        _consts_cache["c"] = make_consts()
    nc = _prog_cache["full"]
    in_maps = []
    for c in range(8):
        m = {"x": xs[c]}
        for names in BLOCK_WNAMES.values():
            for n in names:
                m[n] = W[n]
        m.update(_consts_cache["c"])
        in_maps.append(m)
    res = run_bass_kernel_spmd(nc, in_maps, core_ids=list(range(8)))
    out = np.stack([np.asarray(res.results[c]["y"]).T for c in range(8)], axis=0)
    return np.ascontiguousarray(out, np.float32)


# version 13
# speedup vs baseline: 1.0298x; 1.0298x over previous
import numpy as np
import ml_dtypes
from contextlib import ExitStack

import concourse.bass as bass
import concourse.mybir as mybir
from concourse.bass_utils import run_bass_kernel_spmd

F32 = mybir.dt.float32
BF16 = mybir.dt.bfloat16
AF = mybir.ActivationFunctionType
ALU = mybir.AluOpType
AX = mybir.AxisListType

D = 1024
T = 2048
NCH = D // 128
H = 16
NJ = T // 512
JW = 512
SCALE = 0.125
NEG = -2.0e30
EPS = 1e-5

N_GQ = 10
N_SQ = 6
SEMS = (("pe", "dv", "ac", "gp", "sy")
        + tuple(f"g{i}" for i in range(N_GQ))
        + tuple(f"s{i}" for i in range(N_SQ)))
ENG2SEM = {"tensor": "pe", "vector": "dv", "scalar": "ac",
           "gpsimd": "gp", "sync": "sy"}
_ESIZE = {F32: 4, BF16: 2}


def _is_ap(x):
    return hasattr(x, "base_partition")


def _unm(x):
    return x() if callable(x) else x


def _fp(ap):
    if callable(ap.ap):
        ap = ap.ap()
    name = _unm(ap.name)
    ent = _unm(ap.ap)
    es = _ESIZE.get(_unm(ap.dtype), 4)
    if "DRAM" in str(_unm(ap.space)):
        reach = 1
        for st, sz in ent:
            reach += (sz - 1) * abs(st)
        off = _unm(ap.offset)
        return (name, 0, 1 << 20, off * es, (off + reach) * es)
    p0 = _unm(ap.base_partition)
    p1 = p0 + ent[0][1]
    reach = 1
    for st, sz in ent[1:]:
        reach += (sz - 1) * abs(st)
    off = _unm(ap.offset)
    return (name, p0, p1, off * es, (off + reach) * es)


def _ovl(a, b):
    return a[1] < b[2] and b[1] < a[2] and a[3] < b[4] and b[3] < a[4]


def _covers(big, small):
    return (big[1] <= small[1] and big[2] >= small[2]
            and big[3] <= small[3] and big[4] >= small[4])


class Plan:

    def __init__(self):
        self.steps = {e: [] for e in ENG2SEM}
        self.cnt = {s: 0 for s in SEMS}
        self.wr = {}
        self.rd = {}
        self.bar = None
        self.gq = 0
        self.sq = 0
        self.last_pool = {}

    def _dep_rd(self, f, deps):
        for e in self.wr.get(f[0], ()):
            if _ovl(e[0], f):
                if e[2] > deps.get(e[1], 0):
                    deps[e[1]] = e[2]

    def _dep_wr(self, f, deps):
        self._dep_rd(f, deps)
        for e in self.rd.get(f[0], ()):
            if _ovl(e[0], f):
                if e[2] > deps.get(e[1], 0):
                    deps[e[1]] = e[2]

    def _note_wr(self, f, sem, cnt):
        lst = self.wr.setdefault(f[0], [])
        lst[:] = [e for e in lst if not _covers(f, e[0])]
        lst.append((f, sem, cnt))
        rl = self.rd.get(f[0])
        if rl:
            rl[:] = [e for e in rl if not _covers(f, e[0])]

    def _note_rd(self, f, sem, cnt):
        lst = self.rd.setdefault(f[0], [])
        for i, e in enumerate(lst):
            if e[1] == sem and _covers(e[0], f):
                lst[i] = (e[0], sem, max(e[2], cnt))
                return
        lst.append((f, sem, cnt))

    def _emit(self, eng, fn, rf, wf, inc, inc_k, extra=None):
        deps = {}
        for f in rf:
            self._dep_rd(f, deps)
        for f in wf:
            self._dep_wr(f, deps)
        if self.bar is not None:
            s, c = self.bar
            if c > deps.get(s, 0):
                deps[s] = c
        if extra:
            for s, c in extra.items():
                if c > deps.get(s, 0):
                    deps[s] = c
        if eng == "tensor":
            deps.pop("pe", None)
        self.cnt[inc] += inc_k
        cnt = self.cnt[inc]
        for f in wf:
            self._note_wr(f, inc, cnt)
        for f in rf:
            self._note_rd(f, inc, cnt)
        self.steps[eng].append((fn, sorted(deps.items()), (inc, inc_k)))

    def _emit_universal(self, eng, fn, inc, inc_k):
        deps = {s: c for s, c in self.cnt.items() if c > 0}
        self.cnt[inc] += inc_k
        self.wr = {}
        self.rd = {}
        self.bar = (inc, self.cnt[inc])
        self.steps[eng].append((fn, sorted(deps.items()), (inc, inc_k)))

    def step(self, eng, fn, r=None, w=None, inc=None, inc_k=1):
        if inc is None:
            inc = ENG2SEM[eng]
        if r is None and w is None:
            self._emit_universal(eng, fn, inc, inc_k)
        else:
            rf = [_fp(a) for a in (r or ()) if _is_ap(a)]
            wf = [_fp(a) for a in (w or ()) if _is_ap(a)]
            self._emit(eng, fn, rf, wf, inc, inc_k)

    def _pool_sem(self, eng):
        if eng == "gpsimd":
            s = f"g{self.gq % N_GQ}"
            self.gq += 1
        else:
            s = f"s{self.sq % N_SQ}"
            self.sq += 1
        return s

    def dma(self, eng, fn, r=None, w=None):
        sem = self._pool_sem(eng)
        extra = {}
        prev = self.last_pool.get(sem)
        if prev:
            extra[sem] = prev
        if r is None and w is None:
            deps = {s: c for s, c in self.cnt.items() if c > 0}
            self.cnt[sem] += 16
            self.wr = {}
            self.rd = {}
            self.bar = (sem, self.cnt[sem])
            self.steps[eng].append((fn, sorted(deps.items()), (sem, 16)))
        else:
            rf = [_fp(a) for a in (r or ()) if _is_ap(a)]
            wf = [_fp(a) for a in (w or ()) if _is_ap(a)]
            self._emit(eng, fn, rf, wf, sem, 16, extra=extra)
        self.last_pool[sem] = self.cnt[sem]

    def fence(self):
        self._emit_universal("sync", lambda sy: sy.nop(), "sy", 1)

    def ld(self, eng, out, in_, **kw):
        self.dma(eng, lambda e, out=out, in_=in_, kw=kw: e.dma_start(
            out=out, in_=in_, **kw), r=[in_], w=[out])

    def mm(self, ops):
        rl, wl = [], []
        for (o, l, rr, _s, _p) in ops:
            wl.append(o)
            rl += [l, rr]
        def fn(t, ops=ops):
            last = None
            for (o, l, rr, s, p) in ops:
                last = t.matmul(o, l, rr, start=s, stop=p)
            return last
        self.step("tensor", fn, r=rl, w=wl)

    def tr(self, ops, ident):
        if callable(ident.ap):
            ident = ident.ap()
        rl = [ident]
        wl = []
        for (o, i) in ops:
            wl.append(o)
            rl.append(i)
        def fn(t, ops=ops, ident=ident):
            last = None
            for (o, i) in ops:
                last = t.transpose(o, i, ident)
            return last
        self.step("tensor", fn, r=rl, w=wl)

    def act(self, out, in_, func, bias=0.0, scale=1.0, accum_out=None):
        r = [in_] + [x for x in (bias, scale) if _is_ap(x)]
        w = [out] + ([accum_out] if accum_out is not None else [])
        if accum_out is not None:
            self.step("scalar", lambda s, o=out, i=in_, f=func, b=bias, sc=scale,
                      a=accum_out: s.activation(o, i, f, bias=b, scale=sc,
                                                accum_out=a), r=r, w=w)
        else:
            self.step("scalar", lambda s, o=out, i=in_, f=func, b=bias, sc=scale:
                      s.activation(o, i, f, bias=b, scale=sc), r=r, w=w)

    def ve(self, name, out, *ins, eng="vector", **kw):
        r = [x for x in ins if _is_ap(x)] + [x for x in kw.values() if _is_ap(x)]
        def fn(v, name=name, out=out, ins=ins, kw=kw):
            return getattr(v, name)(out, *ins, **kw)
        self.step(eng, fn, r=r, w=[out])

    def emit(self, block, sems):
        def runner(eng_name):
            def run(eng):
                hi = {s: 0 for s in SEMS}
                for fn, wvals, inc in self.steps[eng_name]:
                    for s, v in wvals:
                        if v > hi[s]:
                            eng.wait_ge(sems[s], v)
                            hi[s] = v
                    inst = fn(eng)
                    name, k = inc
                    assert inst is not None, f"step incrementing {name} returned None"
                    inst.then_inc(sems[name], k)
            return run
        block.tensor(runner("tensor"))
        block.vector(runner("vector"))
        block.scalar(runner("scalar"))
        block.gpsimd(runner("gpsimd"))
        block.sync(runner("sync"))



def prep_weights(inp):
    f32c = lambda a: np.ascontiguousarray(np.asarray(a), dtype=np.float32)
    w = {}
    for n in ("Wq", "Wk", "Wv", "Wo"):
        w["a0_" + n] = f32c(inp[f"L0_attn_{n}"])
    w["a0_bo_row"] = f32c(np.asarray(inp["L0_attn_bo"]).reshape(1, D))
    for n in ("kln_w", "kln_b", "qln_w", "qln_b"):
        w["a0_" + n] = f32c(np.tile(np.asarray(inp[f"L0_attn_{n}"]), 2).reshape(128, 1))
    w["f0_wiT"] = f32c(np.asarray(inp["L0_ffn_wi"]).T)
    w["f0_woT"] = f32c(np.asarray(inp["L0_ffn_wo"]).T)
    w["f0_bi"] = f32c(np.asarray(inp["L0_ffn_bi"]).reshape(1, 4096))
    w["f0_bo_row"] = f32c(np.asarray(inp["L0_ffn_bo"]).reshape(1, D))
    _la = np.asarray(inp["L0_ffn_lora_A"])
    w["f0_AT"] = f32c(_la.transpose(2, 0, 1).reshape(NCH, 128, 8, 16)
                      .transpose(1, 2, 0, 3))
    _wo = np.asarray(inp["L0_ffn_wo"]).T
    w["f0_woTt"] = f32c(_wo.reshape(32, 128, D).transpose(1, 0, 2))
    w["f0_BT"] = f32c(np.asarray(inp["L0_ffn_lora_B"]).transpose(0, 2, 1))
    w["f0_gwT"] = f32c(np.asarray(inp["L0_ffn_gate_w"]).T)
    w["f0_gb"] = f32c(np.asarray(inp["L0_ffn_gate_b"]).reshape(1, 8))
    for n in ("Wq", "Wk", "Wv", "Wo"):
        w["a1_" + n] = f32c(np.asarray(inp[f"L1_attn_{n}"]).T)
    for n in ("bq", "bk", "bv"):
        w["a1_" + n] = f32c(np.asarray(inp[f"L1_attn_{n}"]).reshape(1, D))
    w["a1_bo_row"] = f32c(np.asarray(inp["L1_attn_bo"]).reshape(1, D))
    w["f1_WmT"] = f32c(np.asarray(inp["L1_ffn_Wm"]).T)
    w["f1_WpT"] = f32c(np.asarray(inp["L1_ffn_Wp"]).T)
    w["f1_Wc1T"] = f32c(np.asarray(inp["L1_ffn_Wc1"]).T)
    w["f1_Wc2T"] = f32c(np.asarray(inp["L1_ffn_Wc2"]).T)
    return w


def make_consts():
    c = {}
    p = np.arange(128)[:, None]
    f512 = np.arange(JW)[None, :]
    c["c_ident"] = np.eye(128, dtype=np.float32)
    m = np.zeros((128, 4, JW), np.float32)
    for o in range(4):
        m[:, o, :] = np.where(p + 128 * o - f512 >= 1, NEG, 0.0)
    c["c_masks"] = np.ascontiguousarray(m.reshape(128, 4 * JW))
    s2 = np.zeros((2, 128), np.float32)
    s2[0, 0:64] = 1.0
    s2[1, 64:128] = 1.0
    c["c_sel2"] = s2
    o2 = np.zeros((128, 2), np.float32)
    o2[0:64, 0] = 1.0
    o2[64:128, 1] = 1.0
    c["c_ones2"] = o2
    c["c_ones_col"] = np.ones((128, 1), np.float32)
    dg = np.where(np.arange(128)[None, :] > np.arange(128)[:, None], NEG, 0.0)
    c["c_diag"] = np.ascontiguousarray(dg.astype(np.float32))
    c["c_ones_row"] = np.ones((1, JW), np.float32)
    c["c_eps"] = np.full((128, 1), EPS, np.float32)
    return c


CONST_SHAPES = {
    "c_ident": [128, 128], "c_masks": [128, 4 * JW], "c_sel2": [2, 128],
    "c_ones2": [128, 2], "c_ones_col": [128, 1], "c_ones_row": [1, JW],
    "c_eps": [128, 1], "c_diag": [128, 128],
}


WEIGHT_SHAPES = {
    "a0_Wq": [D, D], "a0_Wk": [D, D], "a0_Wv": [D, D], "a0_Wo": [D, D],
    "a0_bo_row": [1, D],
    "a0_kln_w": [128, 1], "a0_kln_b": [128, 1],
    "a0_qln_w": [128, 1], "a0_qln_b": [128, 1],
    "f0_wiT": [D, 4096], "f0_woT": [4096, D], "f0_bi": [1, 4096],
    "f0_bo_row": [1, D], "f0_AT": [128, 8, NCH, 16], "f0_BT": [8, 16, 4096],
    "f0_woTt": [128, 32, D],
    "f0_gwT": [D, 8], "f0_gb": [1, 8],
    "a1_Wq": [D, D], "a1_Wk": [D, D], "a1_Wv": [D, D], "a1_Wo": [D, D],
    "a1_bq": [1, D], "a1_bk": [1, D], "a1_bv": [1, D],
    "a1_bo_row": [1, D],
    "f1_WmT": [D, 12288], "f1_WpT": [12288, D], "f1_Wc1T": [D, 256],
    "f1_Wc2T": [256, 12288],
}


class Ctx:
    def __init__(self, nc, plan, es):
        self.nc, self.plan, self.es = nc, plan, es
        self.block_es = es
        self.uid = 0

    def nm(self, base):
        self.uid += 1
        return f"{base}{self.uid}"

    def sbuf(self, base, shape, dt, es=None):
        return (es or self.block_es).enter_context(
            self.nc.sbuf_tensor(self.nm(base), shape, dt))

    def psum(self, base, shape, dt=F32):
        return self.es.enter_context(self.nc.psum_tensor(self.nm(base), shape, dt))

    def make_shared(self, C):
        p = self.plan
        self.ones_col = self.sbuf("ones_col", [128, 1], BF16)
        self.ones_row16 = self.sbuf("ones_row16", [1, 128], BF16)
        self.onesf_row = self.sbuf("onesf_row", [1, JW], F32)
        self.onesf_128 = self.sbuf("onesf_128", [1, 128], F32)
        self.onesf_col = self.sbuf("onesf_col", [128, 1], F32)
        self.ones2 = self.sbuf("ones2", [128, 2], BF16)
        self.sel2 = self.sbuf("sel2", [2, 128], BF16)
        self.identf = self.sbuf("identf", [128, 128], F32)
        self.ident16 = self.sbuf("ident16", [128, 128], BF16)
        self.C = C
        self.eps_col = self.sbuf("eps_col", [128, 1], F32)
        self.nc.const_aps.register(F32, EPS, self.eps_col[:])

        self.PA = self.psum("PA", [128, 2048])
        self.PB = self.psum("PB", [128, 1024])
        self.PC = self.psum("PC", [128, 512])
        self.PT = self.psum("PTps", [128, 1024], BF16)

        self.row_a = self.sbuf("row_a", [2, JW], F32)
        self.row_b = self.sbuf("row_b", [2, JW], F32)
        self.row_c = self.sbuf("row_c", [2, JW], F32)
        self.abf = self.sbuf("abf", [128, JW], F32)
        self.cbf = self.sbuf("cbf", [128, JW], F32)
        self.sqf = self.sbuf("sqf", [128, JW], F32)

        for sb, cn in ((self.ones_col, "c_ones_col"), (self.ones2, "c_ones2"),
                       (self.sel2, "c_sel2"), (self.ident16, "c_ident")):
            p.ld("gpsimd", sb[:], C[cn])
        p.ld("gpsimd", self.ones_row16[:], C["c_ones_row"][0:1, 0:128])
        p.ld("gpsimd", self.onesf_row[:], C["c_ones_row"])
        p.ld("gpsimd", self.onesf_128[:], C["c_ones_row"][0:1, 0:128])
        p.ld("gpsimd", self.onesf_col[:], C["c_ones_col"])
        p.ld("gpsimd", self.identf[:], C["c_ident"])
        p.ld("gpsimd", self.eps_col[:], C["c_eps"])


def load_chunked(ctx, sb, hbm, n_chunks):
    p = ctx.plan
    for c in range(n_chunks):
        p.ld("gpsimd", sb[:, c, :], hbm[c * 128:(c + 1) * 128, :])


LN_STAGE = 99


def emit_outer_ln(ctx, rj, yj, y_ap, j0):
    p, X = ctx.plan, ctx
    for c in range(NCH):
        p.act(X.sqf[:, :], rj[:, c, :], AF.Square)
        def mm_st(t, c=c):
            t.matmul(X.PB[0:1, 0:JW], X.onesf_col[:], rj[:, c, :],
                     start=(c == 0), stop=(c == NCH - 1))
            return t.matmul(X.PB[0:1, JW:2 * JW], X.onesf_col[:], X.sqf[:, :],
                            start=(c == 0), stop=(c == NCH - 1))
        p.step("tensor", mm_st,
               r=[X.onesf_col[:], rj[:, c, :], X.sqf[:, :]],
               w=[X.PB[0:1, 0:2 * JW]])
    p.ve("tensor_scalar", X.row_b[0:1, :], X.PB[0:1, 0:JW],
         1.0 / D, None, ALU.mult)
    p.act(X.row_a[0:1, :], X.row_b[0:1, :], AF.Square)
    p.ve("scalar_tensor_tensor", X.row_c[0:1, :], X.PB[0:1, JW:2 * JW],
         1.0 / D, X.row_a[0:1, :], ALU.mult, ALU.subtract)
    p.act(X.row_c[0:1, :], X.row_c[0:1, :], AF.Ln, bias=EPS)
    p.act(X.row_a[0:1, :], X.row_c[0:1, :], AF.Exp, scale=-0.5)
    p.ve("scalar_tensor_tensor", X.row_b[0:1, :], X.row_b[0:1, :],
         -1.0, X.row_a[0:1, :], ALU.mult, ALU.mult)

    p.mm([(X.PC[:, :], X.onesf_128[:], X.row_a[0:1, :], True, True),
          (X.PB[:, 0:JW], X.onesf_128[:], X.row_b[0:1, :], True, True)])
    p.act(X.abf[:], X.PC[:, :], AF.Copy)
    p.act(X.cbf[:], X.PB[:, 0:JW], AF.Copy)
    for c in range(NCH):
        p.ve("tensor_mul", yj[:, c, :], rj[:, c, :], X.abf[:])
        p.ve("tensor_add", yj[:, c, :], yj[:, c, :], X.cbf[:])
        p.ld("sync", y_ap[c * 128:(c + 1) * 128, j0:j0 + JW], yj[:, c, :])



def emit_l0_attn(ctx, W, x_ap, y_ap):
    p, X = ctx.plan, ctx
    diagm = ctx.sbuf("diagm", [128, 128], F32)
    p.ld("gpsimd", diagm[:], ctx.C["c_diag"])
    qb = ctx.sbuf("qb", [128, NCH, T], BF16)
    kb = ctx.sbuf("kb", [128, NCH, T], BF16)
    vt = ctx.sbuf("vt", [128, T // 128, D], BF16)
    wo_j = ctx.sbuf("wo_j", [128, 2, NCH, 128], BF16)
    bo_row = ctx.sbuf("bo_row", [1, D], F32)
    lnw = {n: ctx.sbuf(n, [128, 1], F32) for n in ("klw", "klb", "qlw", "qlb")}
    sq = ctx.sbuf("sqs", [128, JW], BF16)
    t1 = ctx.sbuf("t1", [128, JW], BF16)
    ab16 = ctx.sbuf("ab16", [128, JW], BF16)
    cb16 = ctx.sbuf("cb16", [128, JW], BF16)
    row16a = ctx.sbuf("row16a", [2, JW], BF16)
    row16c = ctx.sbuf("row16c", [2, JW], BF16)

    with ExitStack() as es1:
        xb = ctx.sbuf("xb", [128, NCH, T], BF16, es1)
        wcol = ctx.sbuf("wcol", [128, 2, NCH, 128], BF16, es1)
        stage = ctx.sbuf("stage", [128, 2, JW], BF16, es1)

        load_chunked(ctx, xb, x_ap, NCH)
        p.ld("gpsimd", bo_row[:], W["a0_bo_row"])
        for sb_n, w_n in (("klw", "a0_kln_w"), ("klb", "a0_kln_b"),
                          ("qlw", "a0_qln_w"), ("qlb", "a0_qln_b")):
            p.ld("gpsimd", lnw[sb_n][:], W[w_n])

        qkv_ps = [X.PC[:, 0:JW], X.PB[:, 0:JW]]
        for wi, (w_n, kind) in enumerate((("a0_Wq", "q"), ("a0_Wk", "k"),
                                          ("a0_Wv", "v"))):
            for cout in range(NCH):
                slot = (wi * NCH + cout) % 2
                for ci in range(NCH):
                    p.ld("gpsimd", wcol[:, slot, ci, :],
                         W[w_n][ci * 128:(ci + 1) * 128,
                                cout * 128:(cout + 1) * 128])
                for jn in range(T // JW):
                    n0 = jn * JW
                    ps = qkv_ps[jn % 2]
                    p.mm([(ps, wcol[:, slot, ci, :], xb[:, ci, n0:n0 + JW],
                           ci == 0, ci == NCH - 1) for ci in range(NCH)])
                    if kind in ("q", "k"):
                        out_sb = qb if kind == "q" else kb
                        p.act(out_sb[:, cout, n0:n0 + JW], ps, AF.Copy)
                    else:
                        stg = stage[:, jn % 2, :]
                        p.act(stg, ps, AF.Copy)
                        p.tr([(X.PT[:, i * 128:(i + 1) * 128],
                               stg[:, i * 128:(i + 1) * 128])
                              for i in range(4)], X.ident16)
                        p.ve("tensor_copy",
                             vt[:, jn * 4:jn * 4 + 4,
                                cout * 128:(cout + 1) * 128],
                             X.PT[:, 0:512].rearrange("p (a b) -> p a b", a=4))

    def qkln(src, wl, bl):
        for c in range(NCH):
            for n in range(4):
                n0 = n * JW
                sl = src[:, c, n0:n0 + JW]
                p.act(sq[:, :], sl, AF.Square)
                p.mm([(X.PB[0:2, 0:JW], X.ones2[:], sl, True, True),
                      (X.PC[0:2, :], X.ones2[:], sq[:, :], True, True)])
                p.ve("tensor_scalar", X.row_b[:, :], X.PB[0:2, 0:JW],
                     1.0 / 64, None, ALU.mult)
                p.act(X.row_a[:, :], X.row_b[:, :], AF.Square)
                p.ve("scalar_tensor_tensor", X.row_c[:, :], X.PC[0:2, :],
                     1.0 / 64, X.row_a[:, :], ALU.mult, ALU.subtract)
                p.act(X.row_c[:, :], X.row_c[:, :], AF.Ln, bias=EPS)
                p.act(X.row_a[:, :], X.row_c[:, :], AF.Exp, scale=-0.5)
                p.ve("scalar_tensor_tensor", X.row_b[:, :], X.row_b[:, :],
                     -1.0, X.row_a[:, :], ALU.mult, ALU.mult)
                p.ve("tensor_copy", row16a[0:2, :], X.row_a[:, :])
                p.ve("tensor_copy", row16c[0:2, :], X.row_b[:, :])
                p.mm([(X.PB[:, 0:JW], X.sel2[:], row16a[0:2, :], True, True),
                      (X.PC[:, :], X.sel2[:], row16c[0:2, :], True, True)])
                p.act(ab16[:, :], X.PB[:, 0:JW], AF.Copy)
                p.act(cb16[:, :], X.PC[:, :], AF.Copy)
                p.ve("tensor_mul", t1[:, :], sl, ab16[:, :])
                p.ve("tensor_add", t1[:, :], t1[:, :], cb16[:, :])
                p.ve("tensor_scalar", sl, t1[:, :], wl[:], bl[:],
                     ALU.mult, ALU.add)
    qkln(qb, lnw["qlw"], lnw["qlb"])
    qkln(kb, lnw["klw"], lnw["klb"])

    p.fence()
    with ExitStack() as es2:
        xj = ctx.sbuf("xj", [128, NCH, JW], F32, es2)
        s0b = ctx.sbuf("s0b", [128, 2, T], F32, es2)
        pt = ctx.sbuf("pt", [128, T], BF16, es2)
        ptT = ctx.sbuf("ptT", [128, T // 128, JW], BF16, es2)
        negm = ctx.sbuf("negm", [128, 1], F32, es2)
        bcol = ctx.sbuf("bcol", [128, 1], F32, es2)
        zcol = ctx.sbuf("zcol", [128, 4], F32, es2)
        rrow = X.row_c[0:1, :]
        rb64 = ctx.sbuf("rb64", [64, JW], F32, es2)
        ao = ctx.sbuf("ao", [128, NCH, JW], BF16, es2)

        for J in range(NJ):
            j0 = J * JW
            for c in range(NCH):
                p.ld("sync", xj[:, c, :],
                     x_ap[c * 128:(c + 1) * 128, j0:j0 + JW])
            for h in range(H):
                hr0 = (h % 2) * 64
                hc = h // 2
                for jj in range(4):
                    qt0 = j0 + jj * 128
                    ktw = qt0 + 128
                    ops = []
                    n0 = 0
                    while n0 < ktw:
                        nn = min(JW, ktw - n0)
                        ops.append((X.PA[:, n0:n0 + nn],
                                    qb[hr0:hr0 + 64, hc, qt0:qt0 + 128],
                                    kb[hr0:hr0 + 64, hc, n0:n0 + nn],
                                    True, True))
                        n0 += nn
                    p.mm(ops)
                    sj = s0b[:, jj % 2, 0:ktw]
                    p.act(sj, X.PA[:, 0:ktw], AF.Copy)
                    p.ve("tensor_add", sj[:, ktw - 128:ktw],
                         sj[:, ktw - 128:ktw], diagm[:])
                    p.ve("reduce_max", negm[:], sj, axis=AX.X,
                         negate=True)
                    p.ve("tensor_scalar", bcol[:], negm[:], SCALE, None,
                         ALU.mult)
                    p.act(pt[:, 0:ktw], sj, AF.Exp, bias=bcol[:],
                          scale=SCALE, accum_out=zcol[:, jj:jj + 1])
                    nch_k = ktw // 128
                    for g0 in range(0, nch_k, 8):
                        gn = min(8, nch_k - g0)
                        p.tr([(X.PT[:, i * 128:(i + 1) * 128],
                               pt[:, (g0 + i) * 128:(g0 + i + 1) * 128])
                              for i in range(gn)], X.ident16)
                        p.act(ptT[:, g0:g0 + gn, jj * 128:(jj + 1) * 128],
                              X.PT[:, 0:gn * 128].rearrange(
                                  "p (a b) -> p a b", b=128), AF.Copy)

                p.mm([(X.PB[0:1, JW + jj * 128:JW + (jj + 1) * 128],
                       zcol[:, jj:jj + 1], X.identf[:], True, True)
                      for jj in range(4)])
                p.act(rrow[:], X.PB[0:1, JW:2 * JW], AF.Ln)
                p.act(rrow[:], rrow[:], AF.Exp, scale=-1.0)
                p.mm([(X.PB[0:64, 0:JW], X.onesf_128[0:1, 0:64], rrow[:],
                       True, True)])
                p.act(rb64[:], X.PB[0:64, 0:JW], AF.Copy)
                ops = []
                for c in range(4 * J + 4):
                    k = c - 4 * J
                    lo = max(0, k) * 128
                    ops.append((X.PC[0:64, lo:JW],
                                vt[:, c, h * 64:(h + 1) * 64],
                                ptT[:, c, lo:JW], c == 0, c == 4 * J + 3))
                p.mm(ops)
                p.ve("tensor_mul",
                     ao[(h % 2) * 64:(h % 2) * 64 + 64, h // 2, :],
                     X.PC[0:64, :], rb64[:])

            for cout in range(NCH):
                ws = cout % 2
                for ci in range(NCH):
                    p.ld("gpsimd", wo_j[:, ws, ci, :],
                         W["a0_Wo"][ci * 128:(ci + 1) * 128,
                                    cout * 128:(cout + 1) * 128])
                p.mm([(X.PC[:, :], wo_j[:, ws, ci, :],
                       ao[:, ci, :], ci == 0, False) for ci in range(NCH)]
                     + [(X.PC[:, :], bo_row[0:1, cout * 128:(cout + 1) * 128],
                         X.onesf_row[:], False, True)])
                p.ve("tensor_add", xj[:, cout, :], X.PC[:, :], xj[:, cout, :])
            emit_outer_ln(ctx, xj, xj, y_ap, j0)



L1A_STAGE = 99
L1A_NJ = NJ
L1A_DBG = False


def emit_l1_attn(ctx, W, x_ap, y_ap):
    if L1A_DBG:
        dbg = {n: ctx.nc.dram_tensor("dbg_" + n, shp, F32,
                                     kind="ExternalOutput").ap()
               for n, shp in (("scores", [128, T]), ("m8", [128, 8]),
                              ("u", [128, T]), ("pt", [128, T]),
                              ("zcol", [128, 4]), ("ao", [64, JW]),
                              ("q", [64, 128]), ("k", [64, T]),
                              ("rj", [128, NCH * JW]),
                              ("aof", [128, NCH * JW]),
                              ("xj0", [128, NCH * JW]))}
    p, X = ctx.plan, ctx
    qb = ctx.sbuf("qb", [128, NCH, T], BF16)
    kb = ctx.sbuf("kb", [128, NCH, T], BF16)
    vt = ctx.sbuf("vt", [128, T // 128, D], BF16)
    wo_j = ctx.sbuf("wo_j", [128, 2, NCH, 128], BF16)
    bo_row = ctx.sbuf("bo_row", [1, D], F32)

    with ExitStack() as es1:
        xb = ctx.sbuf("xb", [128, NCH, T], BF16, es1)
        bqkv = [ctx.sbuf("bqkv", [1, D], F32, es1) for _ in range(3)]
        wcol = ctx.sbuf("wcol", [128, 2, NCH, 128], BF16, es1)
        stage = ctx.sbuf("stage", [128, 2, JW], BF16, es1)

        load_chunked(ctx, xb, x_ap, NCH)
        p.ld("gpsimd", bo_row[:], W["a1_bo_row"])
        for i, n in enumerate(("a1_bq", "a1_bk", "a1_bv")):
            p.ld("gpsimd", bqkv[i][:], W[n])

        qkv_ps = [X.PC[:, 0:JW], X.PB[:, 0:JW]]
        for wi, (w_n, kind) in enumerate((("a1_Wq", "q"), ("a1_Wk", "k"),
                                          ("a1_Wv", "v"))):
            for cout in range(NCH):
                slot = (wi * NCH + cout) % 2
                for ci in range(NCH):
                    p.ld("gpsimd", wcol[:, slot, ci, :],
                         W[w_n][ci * 128:(ci + 1) * 128,
                                cout * 128:(cout + 1) * 128])
                for jn in range(T // JW):
                    n0 = jn * JW
                    ps = qkv_ps[jn % 2]
                    p.mm([(ps, wcol[:, slot, ci, :], xb[:, ci, n0:n0 + JW],
                           ci == 0, False) for ci in range(NCH)]
                         + [(ps, bqkv[wi][0:1, cout * 128:(cout + 1) * 128],
                             X.onesf_row[:], False, True)])
                    if kind in ("q", "k"):
                        out_sb = qb if kind == "q" else kb
                        p.act(out_sb[:, cout, n0:n0 + JW], ps, AF.Copy)
                    else:
                        stg = stage[:, jn % 2, :]
                        p.act(stg, ps, AF.Copy)
                        p.tr([(X.PT[:, i * 128:(i + 1) * 128],
                               stg[:, i * 128:(i + 1) * 128])
                              for i in range(4)], X.ident16)
                        p.ve("tensor_copy",
                             vt[:, jn * 4:jn * 4 + 4,
                                cout * 128:(cout + 1) * 128],
                             X.PT[:, 0:512].rearrange("p (a b) -> p a b", a=4))

    p.fence()
    with ExitStack() as es2:
        xj = ctx.sbuf("xj", [128, NCH, JW], F32, es2)
        s_sb = ctx.sbuf("s_sb", [128, 4, T], F32, es2)
        scr = ctx.sbuf("scr", [128, 1024], F32, es2)
        cand = ctx.sbuf("cand", [128, 4, 256], F32, es2)
        m8 = ctx.sbuf("m8", [128, 4, 8], F32, es2)
        taucol = ctx.sbuf("taucol", [128, 4], F32, es2)
        pt = ctx.sbuf("pt", [128, T], BF16, es2)
        ptT = ctx.sbuf("ptT", [128, T // 128, JW], BF16, es2)
        zcol = ctx.sbuf("zcol", [128, 4], F32, es2)
        rrow = X.row_c[0:1, :]
        rb64 = ctx.sbuf("rb64", [64, JW], F32, es2)
        ao = ctx.sbuf("ao", [128, NCH, JW], BF16, es2)

        for J in range(L1A_NJ):
            j0 = J * JW
            for c in range(NCH):
                p.ld("sync", xj[:, c, :],
                     x_ap[c * 128:(c + 1) * 128, j0:j0 + JW])
            for h in range(H if L1A_STAGE >= 2 else 0):
                hr0 = (h % 2) * 64
                hc = h // 2
                for jj in range(4):
                    qt0 = j0 + jj * 128
                    p.mm([(X.PA[:, n * JW:(n + 1) * JW],
                           qb[hr0:hr0 + 64, hc, qt0:qt0 + 128],
                           kb[hr0:hr0 + 64, hc, n * JW:(n + 1) * JW],
                           True, True) for n in range(T // JW)])
                    sj = s_sb[:, jj, :]
                    p.act(sj, X.PA[:, :], AF.Copy)
                    cnd = cand[:, jj, :]
                    def tk_a(v, sj=sj, cnd=cnd):
                        last = None
                        for c in range(16):
                            last = v.max(out=cnd[:, c * 8:(c + 1) * 8],
                                         in_=sj[:, c * 128:(c + 1) * 128])
                        return last
                    p.step("vector", tk_a, r=[sj], w=[cnd])
                    for half in range(2):
                        def tk_b(v, sj=sj, cnd=cnd, half=half):
                            last = None
                            for c8 in range(8):
                                c = half * 8 + c8
                                last = v.match_replace(
                                    out=scr[:, c8 * 128:(c8 + 1) * 128],
                                    in_to_replace=cnd[:, c * 8:(c + 1) * 8],
                                    in_values=sj[:, c * 128:(c + 1) * 128],
                                    imm_value=-1e30)
                            return last
                        p.step("vector", tk_b, r=[sj, cnd], w=[scr[:]])
                        def tk_c(v, cnd=cnd, half=half):
                            last = None
                            for c8 in range(8):
                                c = half * 8 + c8
                                last = v.max(
                                    out=cnd[:, 128 + c * 8:136 + c * 8],
                                    in_=scr[:, c8 * 128:(c8 + 1) * 128])
                            return last
                        p.step("vector", tk_c, r=[scr[:]], w=[cnd])
                def dwa(jj):
                    return scr[:, jj * 256:(jj + 1) * 256]
                def dwb(jj):
                    return cand[:, jj, :]
                def mstep(fn, r, w):
                    p.step("vector", fn, r=r, w=w)
                mstep(lambda v: [v.max(out=m8[:, jj, :], in_=cand[:, jj, :])
                                 for jj in range(4)][-1],
                      r=[cand[:]], w=[m8[:]])
                mstep(lambda v: [v.match_replace(
                    out=dwa(jj), in_to_replace=m8[:, jj, :],
                    in_values=cand[:, jj, :], imm_value=-1e30)
                    for jj in range(4)][-1],
                      r=[cand[:], m8[:]], w=[scr[:]])
                mstep(lambda v: [v.max(out=m8[:, jj, :], in_=dwa(jj))
                                 for jj in range(4)][-1],
                      r=[scr[:]], w=[m8[:]])
                mstep(lambda v: [v.match_replace(
                    out=dwb(jj), in_to_replace=m8[:, jj, :],
                    in_values=dwa(jj), imm_value=-1e30)
                    for jj in range(4)][-1],
                      r=[scr[:], m8[:]], w=[cand[:]])
                mstep(lambda v: [v.max(out=m8[:, jj, :], in_=dwb(jj))
                                 for jj in range(4)][-1],
                      r=[cand[:]], w=[m8[:]])
                mstep(lambda v: [v.match_replace(
                    out=dwa(jj), in_to_replace=m8[:, jj, :],
                    in_values=dwb(jj), imm_value=-1e30)
                    for jj in range(4)][-1],
                      r=[cand[:], m8[:]], w=[scr[:]])
                mstep(lambda v: [v.max(out=m8[:, jj, :], in_=dwa(jj))
                                 for jj in range(4)][-1],
                      r=[scr[:]], w=[m8[:]])
                mstep(lambda v: [v.tensor_scalar(
                    taucol[:, jj:jj + 1], m8[:, jj, 7:8], -SCALE, None,
                    ALU.mult) for jj in range(4)][-1],
                      r=[m8[:]], w=[taucol[:]])
                for jj in range(4):
                    sj = s_sb[:, jj, :]
                    p.act(sj, sj, AF.Exp, bias=taucol[:, jj:jj + 1],
                          scale=SCALE)
                    p.ve("scalar_tensor_tensor", pt[:, :], sj, 0.99995, sj,
                         ALU.is_ge, ALU.mult)
                    p.act(pt[:, :], pt[:, :], AF.Copy,
                          accum_out=zcol[:, jj:jj + 1])
                    for g in range(2):
                        p.tr([(X.PT[:, i * 128:(i + 1) * 128],
                               pt[:, (g * 8 + i) * 128:(g * 8 + i + 1) * 128])
                              for i in range(8)], X.ident16)
                        p.act(ptT[:, g * 8:g * 8 + 8, jj * 128:(jj + 1) * 128],
                              X.PT[:].rearrange("p (a b) -> p a b", a=8),
                              AF.Copy)

                if L1A_STAGE < 5:
                    continue
                p.mm([(X.PB[0:1, JW + jj * 128:JW + (jj + 1) * 128],
                       zcol[:, jj:jj + 1], X.identf[:], True, True)
                      for jj in range(4)])
                p.act(rrow[:], X.PB[0:1, JW:2 * JW], AF.Ln)
                p.act(rrow[:], rrow[:], AF.Exp, scale=-1.0)
                p.mm([(X.PB[0:64, 0:JW], X.onesf_128[0:1, 0:64], rrow[:],
                       True, True)])
                p.act(rb64[:], X.PB[0:64, 0:JW], AF.Copy)
                p.mm([(X.PC[0:64, :], vt[:, c, h * 64:(h + 1) * 64],
                       ptT[:, c, :], c == 0, c == T // 128 - 1)
                      for c in range(T // 128)])
                p.ve("tensor_mul",
                     ao[(h % 2) * 64:(h % 2) * 64 + 64, h // 2, :],
                     X.PC[0:64, :], rb64[:])
                if L1A_DBG and J == 0 and h == 0:
                    p.ld("sync", dbg["zcol"], zcol[:])
                    p.ld("gpsimd", dbg["ao"], ao[0:64, 0, :])

            if L1A_DBG and J == 0:
                p.ld("gpsimd", dbg["xj0"],
                     xj[:].rearrange("p a b -> p (a b)"))
            for cout in range(NCH if L1A_STAGE >= 6 else 0):
                ws = cout % 2
                for ci in range(NCH):
                    p.ld("gpsimd", wo_j[:, ws, ci, :],
                         W["a1_Wo"][ci * 128:(ci + 1) * 128,
                                    cout * 128:(cout + 1) * 128])
                p.mm([(X.PC[:, :], wo_j[:, ws, ci, :],
                       ao[:, ci, :], ci == 0, False) for ci in range(NCH)]
                     + [(X.PC[:, :], bo_row[0:1, cout * 128:(cout + 1) * 128],
                         X.onesf_row[:], False, True)])
                p.ve("tensor_add", xj[:, cout, :], X.PC[:, :], xj[:, cout, :])
            if L1A_DBG and J == 0:
                p.ld("gpsimd", dbg["aof"],
                     ao[:].rearrange("p a b -> p (a b)"))
                p.ld("gpsimd", dbg["rj"],
                     xj[:].rearrange("p a b -> p (a b)"))
            if L1A_STAGE >= 7:
                emit_outer_ln(ctx, xj, xj, y_ap, j0)



def emit_l0_ffn(ctx, W, x_ap, y_ap):
    p, X = ctx.plan, ctx
    DFF = 4096
    NFC = DFF // JW
    gw_sb = ctx.sbuf("gw_sb", [128, NCH, 8], BF16)
    gb_row = ctx.sbuf("gb_row", [1, 8], F32)
    bi_row = ctx.sbuf("bi_row", [1, DFF], F32)
    bo_row = ctx.sbuf("bo_row", [1, D], F32)
    at_sb = ctx.sbuf("at_sb", [128, 8, NCH, 16], BF16)
    u16 = ctx.sbuf("u16", [16, 8, JW], BF16)

    for ci in range(NCH):
        p.ld("gpsimd", gw_sb[:, ci, :], W["f0_gwT"][ci * 128:(ci + 1) * 128, :])
    p.ld("gpsimd", gb_row[:], W["f0_gb"])
    p.ld("gpsimd", bi_row[:], W["f0_bi"])
    p.ld("gpsimd", bo_row[:], W["f0_bo_row"])
    p.ld("gpsimd", at_sb[:], W["f0_AT"][:])

    with ExitStack() as es2:
        xbJ = ctx.sbuf("xbJ", [128, NCH, JW], BF16, es2)
        xj = ctx.sbuf("xj", [128, NCH, JW], F32, es2)
        lg = ctx.sbuf("lg", [128, 4, 8], F32, es2)
        srt = ctx.sbuf("srt", [128, 4, 8], F32, es2)
        ge = ctx.sbuf("ge", [128, 4, 8], F32, es2)
        vs = ctx.sbuf("vs", [128, 4], F32, es2)
        n1c = ctx.sbuf("n1c", [128, 1], F32, es2)
        e2c = ctx.sbuf("e2c", [128, 1], F32, es2)
        rzc = ctx.sbuf("rzc", [128, 1], F32, es2)
        zc = ctx.sbuf("zc", [128, 1], F32, es2)
        jk8 = ctx.sbuf("jk8", [128, 8], F32, es2)
        vs2row = ctx.sbuf("vs2row", [1, JW], F32, es2)
        wi_fc = ctx.sbuf("wi_fc", [128, 2, NCH, JW], BF16, es2)
        bt_fc = ctx.sbuf("bt_fc", [16, 2, 8, JW], BF16, es2)
        base_sb = ctx.sbuf("base_sb", [128, 4, JW], F32, es2)
        tmp_f = ctx.sbuf("tmp_f", [128, 2, 4, JW], F32, es2)
        rsum = ctx.sbuf("rsum", [128, 2, 4, JW], BF16, es2)
        rk = ctx.sbuf("rk", [128, 2, 4, JW], BF16, es2)
        rt_sb = ctx.sbuf("rt_sb", [128, 4 * NFC, JW], BF16, es2)
        wo_co = ctx.sbuf("wo_co", [128, 2, 4 * NFC, 128], BF16, es2)

        p.fence()

        for J in range(NJ):
            j0 = J * JW
            for c in range(NCH):
                p.ld("gpsimd", xbJ[:, c, :],
                     x_ap[c * 128:(c + 1) * 128, j0:j0 + JW])
                p.ld("sync", xj[:, c, :],
                     x_ap[c * 128:(c + 1) * 128, j0:j0 + JW])

            for ts in range(4):
                p.mm([(X.PC[:, 0:8], xbJ[:, ci, ts * 128:(ts + 1) * 128],
                       gw_sb[:, ci, :], ci == 0, False) for ci in range(NCH)]
                     + [(X.PC[:, 0:8], X.onesf_128[:], gb_row[:], False, True)])
                p.act(lg[:, ts, :], X.PC[:, 0:8], AF.Copy)
                p.step("vector", (lambda ts=ts: lambda v: v.max(
                    out=srt[:, ts, :], in_=lg[:, ts, :]))(),
                    r=[lg[:, ts, :]], w=[srt[:, ts, :]])
                p.ve("tensor_scalar", n1c[:], srt[:, ts, 0:1], -1.0, None,
                     ALU.mult)
                p.act(jk8[:], lg[:, ts, :], AF.Exp, bias=n1c[:],
                      accum_out=zc[:])
                p.act(e2c[:], srt[:, ts, 1:2], AF.Exp, bias=n1c[:])
                p.ve("reciprocal", rzc[:], zc[:])
                p.ve("scalar_tensor_tensor", vs[:, ts:ts + 1], e2c[:], 1.0,
                     rzc[:], ALU.add, ALU.mult)
                def mk_ge(v, ts=ts):
                    last = None
                    for e in range(8):
                        last = v.scalar_tensor_tensor(
                            ge[:, ts, e:e + 1], lg[:, ts, e:e + 1],
                            srt[:, ts, 1:2], vs[:, ts:ts + 1],
                            ALU.is_ge, ALU.mult)
                    return last
                p.step("vector", mk_ge,
                       r=[lg[:, ts, :], srt[:, ts, :], vs[:, ts:ts + 1]],
                       w=[ge[:, ts, :]])
            p.ve("tensor_scalar", vs[:, :], vs[:, :], 2.0, None, ALU.mult)
            p.mm([(X.PB[0:1, ts * 128:(ts + 1) * 128], vs[:, ts:ts + 1],
                   X.identf[:], True, True) for ts in range(4)])
            p.act(vs2row[:], X.PB[0:1, 0:JW], AF.Copy)

            for e in range(8):
                p.mm([(X.PB[0:16, 0:JW], at_sb[:, e, ci, :], xbJ[:, ci, :],
                       ci == 0, ci == NCH - 1) for ci in range(NCH)])
                p.act(u16[0:16, e, :], X.PB[0:16, 0:JW], AF.Copy)

            for fc in range(NFC):
                slot = fc % 2
                for ci in range(NCH):
                    p.ld("gpsimd", wi_fc[:, slot, ci, :],
                         W["f0_wiT"][ci * 128:(ci + 1) * 128,
                                     fc * JW:(fc + 1) * JW])
                for e in range(8):
                    p.ld("gpsimd", bt_fc[0:16, slot, e, :],
                         W["f0_BT"][e][:, fc * JW:(fc + 1) * JW])
                for ts in range(4):
                    p.mm([(X.PA[:, ts * JW:(ts + 1) * JW],
                           xbJ[:, ci, ts * 128:(ts + 1) * 128],
                           wi_fc[:, slot, ci, :], ci == 0, False)
                          for ci in range(NCH)]
                         + [(X.PA[:, ts * JW:(ts + 1) * JW], X.onesf_128[:],
                             bi_row[0:1, fc * JW:(fc + 1) * JW], False, True)])
                for hp in range(2):
                    p.act(base_sb[:, 2 * hp:2 * hp + 2, :].rearrange(
                        "p a b -> p (a b)"),
                        X.PA[:, hp * 1024:(hp + 1) * 1024], AF.Copy)
                for e in range(8):
                    tpar = e % 2
                    for hp in range(2):
                        p.mm([(X.PA[:, ts * JW:(ts + 1) * JW],
                               u16[0:16, e, ts * 128:(ts + 1) * 128],
                               bt_fc[0:16, slot, e, :], True, True)
                              for ts in (2 * hp, 2 * hp + 1)])
                    for hp in range(2):
                        p.ve("scalar_tensor_tensor",
                             tmp_f[:, tpar, 2 * hp:2 * hp + 2, :].rearrange(
                                 "p a b -> p (a b)"),
                             X.PA[:, hp * 1024:(hp + 1) * 1024], 1.0,
                             base_sb[:, 2 * hp:2 * hp + 2, :].rearrange(
                                 "p a b -> p (a b)"),
                             ALU.mult, ALU.add)
                    dst = rsum if e < 2 else rk
                    for hp in range(2):
                        def mk_relu(s, e=e, dst=dst, tpar=tpar, hp=hp):
                            last = None
                            for ts in (2 * hp, 2 * hp + 1):
                                last = s.activation(dst[:, tpar, ts, :],
                                                    tmp_f[:, tpar, ts, :],
                                                    AF.Relu, bias=0.0,
                                                    scale=ge[:, ts, e:e + 1])
                            return last
                        p.step("scalar", mk_relu,
                               r=[tmp_f[:, tpar, 2 * hp:2 * hp + 2], ge[:]],
                               w=[dst[:, tpar, 2 * hp:2 * hp + 2]])
                    if e >= 2:
                        for hp in range(2):
                            p.ve("tensor_tensor",
                                 rsum[:, tpar, 2 * hp:2 * hp + 2].rearrange(
                                     "p a b -> p (a b)"),
                                 rsum[:, tpar, 2 * hp:2 * hp + 2].rearrange(
                                     "p a b -> p (a b)"),
                                 rk[:, tpar, 2 * hp:2 * hp + 2].rearrange(
                                     "p a b -> p (a b)"),
                                 eng=("vector" if tpar == 0 else "gpsimd"),
                                 op=ALU.add)
                p.ve("tensor_tensor",
                     rsum[:, 0].rearrange("p a b -> p (a b)"),
                     rsum[:, 0].rearrange("p a b -> p (a b)"),
                     rsum[:, 1].rearrange("p a b -> p (a b)"),
                     op=ALU.add)
                for half in range(2):
                    ops = []
                    for i in range(8):
                        ts, fs = (half * 8 + i) // 4, (half * 8 + i) % 4
                        ops.append((X.PT[:, i * 128:(i + 1) * 128],
                                    rsum[:, 0, ts, fs * 128:(fs + 1) * 128]))
                    p.tr(ops, X.ident16)
                    for i in range(8):
                        ts, fs = (half * 8 + i) // 4, (half * 8 + i) % 4
                        p.act(rt_sb[:, fc * 4 + fs, ts * 128:(ts + 1) * 128],
                              X.PT[:, i * 128:(i + 1) * 128], AF.Copy)

            for cout in range(NCH):
                ws = cout % 2
                p.ld("gpsimd", wo_co[:, ws, :, :],
                     W["f0_woTt"][:, :, cout * 128:(cout + 1) * 128])
                p.mm([(X.PB[:, 0:JW], wo_co[:, ws, fcc, :], rt_sb[:, fcc, :],
                       fcc == 0, False) for fcc in range(4 * NFC)]
                     + [(X.PB[:, 0:JW],
                         bo_row[0:1, cout * 128:(cout + 1) * 128],
                         vs2row[:], False, True)])
                p.ve("tensor_add", xj[:, cout, :], X.PB[:, 0:JW],
                     xj[:, cout, :])
            emit_outer_ln(ctx, xj, xj, y_ap, j0)



def emit_l1_ffn(ctx, W, x_ap, y_ap):
    p, X = ctx.plan, ctx
    nc = ctx.nc
    FULL = 12288
    NF128 = FULL // 128
    NSC = 12
    R_THRESH = 0.7 * 4096.0 * 8.0

    carry = [nc.dram_tensor(ctx.nm("carry"), [D, T], BF16, kind="Internal").ap()
             for _ in range(2)]
    nov_in = nc.dram_tensor(ctx.nm("nov_in"), [1, 512], F32, kind="Internal").ap()
    nov_out = nc.dram_tensor(ctx.nm("nov_out"), [1, 512], F32, kind="Internal").ap()
    nov_stage = ctx.sbuf("nov_stage", [1, 512], F32)

    xnew = ctx.sbuf("xnew", [128, NCH, T], F32)
    active = ctx.sbuf("active", [128, NF128], F32)
    hist = ctx.sbuf("hist", [128, NF128], F32)
    msk = ctx.sbuf("msk", [128, NF128], F32)
    comb = ctx.sbuf("comb", [128, NF128], F32)
    d96 = ctx.sbuf("d96", [128, NF128], F32)
    ck_sb = ctx.sbuf("ck_sb", [128, NF128], F32)
    jk96 = ctx.sbuf("jk96", [128, NF128], F32)
    pooled = ctx.sbuf("pooled", [128, NCH], F32)
    pooled16 = ctx.sbuf("pooled16", [128, NCH], BF16)
    cnt_c = ctx.sbuf("cnt_c", [128, 1], F32)
    cnt16 = ctx.sbuf("cnt16", [128, 1], BF16)
    rr16 = ctx.sbuf("rr16", [128, 2], BF16)
    wc1_sb = ctx.sbuf("wc1_sb", [128, NCH, 256], BF16)
    srow = ctx.sbuf("srow", [1, 16], F32)
    dcol = ctx.sbuf("dcol", [128, 1], F32)
    omdcol = ctx.sbuf("omdcol", [128, 1], F32)
    midcol = ctx.sbuf("midcol", [128, 1], F32)
    colr = ctx.sbuf("colr", [128, 1], F32)

    SL = {n: i for i, n in enumerate(
        ("lo", "hi", "mid", "tot", "ind", "tmp", "nov", "done", "dnew"))}

    def sl(n):
        return srow[0:1, SL[n]:SL[n] + 1]

    for ci in range(NCH):
        p.dma("gpsimd", (lambda ci=ci: lambda g: g.dma_start(
            out=wc1_sb[:, ci, :], in_=W["f1_Wc1T"][ci * 128:(ci + 1) * 128, :]))())

    def init_v(v):
        v.memset(active[:], 0.0)
        v.memset(active[:, 0:32], 1.0)
        v.memset(hist[:], 0.0)
        v.memset(hist[:, 0:32], 1.0)
        return v.memset(srow[:], 0.0)
    p.step("vector", init_v)

    wm_sc = ctx.sbuf("wm_sc", [128, 2, NCH, 1024], BF16)
    wp_sc = ctx.sbuf("wp_sc", [128, 2, 8, 1024], BF16)
    hb = ctx.sbuf("hb", [128, 8, JW], BF16)
    xcJ = ctx.sbuf("xcJ", [128, NCH, T], BF16)
    bl_t1f = ctx.sbuf("bl_t1f", [128, 2, JW], F32)
    bl_st16 = ctx.sbuf("bl_st16", [128, 2, JW], BF16)

    def load_mwp(sc, slot):
        f0 = sc * 1024
        for ci in range(NCH):
            p.ld("gpsimd", wm_sc[:, slot, ci, :],
                 W["f1_WmT"][ci * 128:(ci + 1) * 128, f0:f0 + 1024])
        for fi in range(8):
            p.ld("gpsimd", wp_sc[:, slot, fi, :],
                 W["f1_WpT"][f0 + fi * 128:f0 + (fi + 1) * 128, :])

    for st in range(3):
        src_ap = x_ap if st == 0 else carry[(st + 1) % 2]
        dst_ap = carry[st % 2]

        for ci in range(NCH):
            p.ld("gpsimd", xcJ[:, ci, :], src_ap[ci * 128:(ci + 1) * 128, :])
        if st == 0:
            load_mwp(0, 0)

        for sc in range(NSC):
            slot = sc % 2
            if sc + 1 < NSC:
                load_mwp(sc + 1, (sc + 1) % 2)
            elif st < 2:
                load_mwp(0, 0)
            for t4 in range(4):
                t0 = t4 * JW
                t4p = t4 % 2
                for fi in range(8):
                    s0 = (fi % 2) * JW
                    p.mm([(X.PB[:, s0:s0 + JW],
                           wm_sc[:, slot, ci, fi * 128:(fi + 1) * 128],
                           xcJ[:, ci, t0:t0 + JW],
                           ci == 0, ci == NCH - 1) for ci in range(NCH)])
                    acol = active[:, sc * 8 + fi:sc * 8 + fi + 1]
                    p.act(hb[:, fi, :], X.PB[:, s0:s0 + JW], AF.Relu,
                          bias=0.0, scale=acol)
                for half in range(2):
                    for cc in range(4):
                        c = half * 4 + cc
                        p.mm([(X.PA[:, cc * JW:(cc + 1) * JW],
                               wp_sc[:, slot, fi, c * 128:(c + 1) * 128],
                               hb[:, fi, :], fi == 0, fi == 7)
                              for fi in range(8)])
                    for cc in range(4):
                        c = half * 4 + cc
                        if sc == 0:
                            p.ve("tensor_copy", xnew[:, c, t0:t0 + JW],
                                 X.PA[:, cc * JW:(cc + 1) * JW])
                        else:
                            p.ve("tensor_add", xnew[:, c, t0:t0 + JW],
                                 X.PA[:, cc * JW:(cc + 1) * JW],
                                 xnew[:, c, t0:t0 + JW])

        for c in range(NCH):
            p.ve("reduce_sum", pooled[:, c:c + 1], xnew[:, c, :], axis=AX.X)
        p.ve("tensor_copy", pooled16[:], pooled[:])
        for rc in range(2):
            p.mm([(X.PB[:, rc:rc + 1],
                   wc1_sb[:, ci, rc * 128:(rc + 1) * 128],
                   pooled16[:, ci:ci + 1], ci == 0, ci == NCH - 1)
                  for ci in range(NCH)])
            p.act(rr16[:, rc:rc + 1], X.PB[:, rc:rc + 1], AF.Relu,
                  scale=1.0 / T)
        wc2a = wm_sc[:, 1, :, :].rearrange("p a b -> p (a b)")
        wc2b = wp_sc[:, 1, 0:4, :].rearrange("p a b -> p (a b)")
        for rc in range(2):
            p.ld("gpsimd", wc2a, W["f1_Wc2T"][rc * 128:(rc + 1) * 128, 0:8192])
            p.ld("gpsimd", wc2b, W["f1_Wc2T"][rc * 128:(rc + 1) * 128, 8192:FULL])
            def mm_ck(t, rc=rc, wc2a=wc2a, wc2b=wc2b):
                last = None
                for f in range(NF128):
                    src = (wc2a[:, f * 128:(f + 1) * 128] if f < 64 else
                           wc2b[:, (f - 64) * 128:(f - 63) * 128])
                    last = t.matmul(X.PC[:, f:f + 1], src,
                                    rr16[:, rc:rc + 1], start=(rc == 0),
                                    stop=(rc == 1))
                return last
            p.step("tensor", mm_ck,
                   r=[wc2a, wc2b, rr16[:, rc:rc + 1]], w=[X.PC[:, 0:NF128]])
        p.act(ck_sb[:], X.PC[:, 0:NF128], AF.Copy)

        p.ve("reduce_max", colr[:], ck_sb[:], axis=AX.X)
        p.mm([(X.PB[0:1, 0:128], colr[:], X.identf[:], True, True)])
        p.ve("reduce_max", sl("hi"), X.PB[0:1, 0:128], axis=AX.X)
        p.ve("tensor_reduce", colr[:], ck_sb[:], axis=AX.X, op=ALU.min)
        p.mm([(X.PB[0:1, 128:256], colr[:], X.identf[:], True, True)])
        p.ve("tensor_reduce", sl("lo"), X.PB[0:1, 128:256], axis=AX.X,
             op=ALU.min)
        p.ve("tensor_scalar", sl("hi"), sl("hi"), 1e-6, None, ALU.add)
        p.ve("tensor_scalar", sl("lo"), sl("lo"), 1e-6, None, ALU.subtract)

        for it in range(22):
            p.ve("tensor_add", sl("mid"), sl("lo"), sl("hi"))
            p.ve("tensor_scalar", sl("mid"), sl("mid"), 0.5, None, ALU.mult)
            p.mm([(X.PB[:, 2:3], X.onesf_128[:], sl("mid"), True, True)])
            p.act(midcol[:], X.PB[:, 2:3], AF.Copy)
            p.ve("tensor_scalar", jk96[:], ck_sb[:], midcol[:], None,
                 ALU.is_ge)
            p.ve("reduce_sum", cnt_c[:], jk96[:], axis=AX.X)
            p.ve("tensor_copy", cnt16[:], cnt_c[:])
            p.mm([(X.PB[0:1, 3:4], X.ones_col[:], cnt16[:], True, True)])
            p.act(sl("tot"), X.PB[0:1, 3:4], AF.Copy)
            p.ve("tensor_scalar", sl("ind"), sl("tot"), 4096.0, None,
                 ALU.is_ge)
            p.ve("tensor_sub", sl("tmp"), sl("mid"), sl("lo"))
            p.ve("tensor_mul", sl("tmp"), sl("tmp"), sl("ind"))
            p.ve("tensor_add", sl("lo"), sl("lo"), sl("tmp"))
            p.ve("tensor_sub", sl("tmp"), sl("mid"), sl("hi"))
            p.ve("tensor_scalar", sl("ind"), sl("ind"), -1.0, 1.0,
                 ALU.mult, ALU.add)
            p.ve("tensor_mul", sl("tmp"), sl("tmp"), sl("ind"))
            p.ve("tensor_add", sl("hi"), sl("hi"), sl("tmp"))

        p.mm([(X.PB[:, 2:3], X.onesf_128[:], sl("lo"), True, True)])
        p.act(midcol[:], X.PB[:, 2:3], AF.Copy)
        p.ve("tensor_scalar", msk[:], ck_sb[:], midcol[:], None, ALU.is_ge)
        p.ve("tensor_add", comb[:], hist[:], msk[:])
        p.ve("tensor_scalar", comb[:], comb[:], 1.0, None, ALU.min)
        p.ve("tensor_sub", d96[:], comb[:], hist[:])
        p.ve("reduce_sum", cnt_c[:], d96[:], axis=AX.X)
        p.ve("tensor_copy", cnt16[:], cnt_c[:])
        p.mm([(X.PB[0:1, 3:4], X.ones_col[:], cnt16[:], True, True)])
        p.act(sl("nov"), X.PB[0:1, 3:4], AF.Copy)
        p.ve("memset", nov_stage[:], 0.0)
        p.ve("tensor_copy", nov_stage[0:1, 0:1], sl("nov"))
        p.dma("gpsimd", lambda g: g.dma_start(out=nov_in, in_=nov_stage[:]))
        p.step("gpsimd", lambda g: g.collective_compute(
            "AllReduce", ALU.add, [[0, 1, 2, 3, 4, 5, 6, 7]],
            [nov_in.opt()], [nov_out.opt()]), inc="gp", inc_k=1)
        p.dma("gpsimd", lambda g: g.dma_start(out=sl("nov"),
                                              in_=nov_out[0:1, 0:1]))
        p.ve("tensor_scalar", sl("dnew"), sl("nov"), R_THRESH, None,
             ALU.is_lt)
        p.mm([(X.PB[:, 2:3], X.onesf_128[:], sl("done"), True, True)])
        p.act(dcol[:], X.PB[:, 2:3], AF.Copy)
        p.ve("tensor_scalar", omdcol[:], dcol[:], -1.0, 1.0,
             ALU.mult, ALU.add)
        p.ve("tensor_scalar", jk96[:], msk[:], omdcol[:], None, ALU.mult)
        p.ve("tensor_scalar", active[:], active[:], dcol[:], None, ALU.mult)
        p.ve("tensor_add", active[:], active[:], jk96[:])
        p.ve("tensor_scalar", jk96[:], comb[:], omdcol[:], None, ALU.mult)
        p.ve("tensor_scalar", hist[:], hist[:], dcol[:], None, ALU.mult)
        p.ve("tensor_add", hist[:], hist[:], jk96[:])
        p.ve("tensor_max", sl("done"), sl("done"), sl("dnew"))

        if True:
            t1f = bl_t1f
            st16 = bl_st16
            for t4 in range(4):
                t0 = t4 * JW
                for c in range(NCH):
                    par = c % 2
                    xc16 = xcJ[:, c, t0:t0 + JW]
                    xn = xnew[:, c, t0:t0 + JW]
                    if st < 2:
                        p.ve("tensor_scalar", t1f[:, par, :], xn,
                             omdcol[:], None, ALU.mult)
                        p.ve("scalar_tensor_tensor", st16[:, par, :], xc16,
                             dcol[:], t1f[:, par, :], ALU.mult, ALU.add)
                        p.ld("sync", dst_ap[c * 128:(c + 1) * 128, t0:t0 + JW],
                             st16[:, par, :])
                    else:
                        p.ve("tensor_scalar", xn, xn, omdcol[:], None, ALU.mult)
                        p.ve("scalar_tensor_tensor", xn, xc16,
                             dcol[:], xn, ALU.mult, ALU.add)
                        p.ld("sync", t1f[:, par, :],
                             x_ap[c * 128:(c + 1) * 128, t0:t0 + JW])
                        p.ve("tensor_add", xn, t1f[:, par, :], xn)
                if st == 2:
                    emit_outer_ln(ctx, xnew[:, :, t0:t0 + JW],
                                  xnew[:, :, t0:t0 + JW], y_ap, t0)



def build_block_program(block_name, wnames):
    nc = bass.Bass("TRN2", target_bir_lowering=False)
    x_ap = nc.dram_tensor("x", [D, T], F32, kind="ExternalInput").ap()
    y_ap = nc.dram_tensor("y", [D, T], F32, kind="ExternalOutput").ap()
    W = {n: nc.dram_tensor(n, WEIGHT_SHAPES[n], F32, kind="ExternalInput").ap()
         for n in wnames}
    C = {n: nc.dram_tensor(n, CONST_SHAPES[n], F32, kind="ExternalInput").ap()
         for n in CONST_SHAPES}
    plan = Plan()
    es = ExitStack()
    with es:
        ctx = Ctx(nc, plan, es)
        ctx.make_shared(C)
        if block_name == "l0_attn":
            emit_l0_attn(ctx, W, x_ap, y_ap)
        elif block_name == "l1_attn":
            emit_l1_attn(ctx, W, x_ap, y_ap)
        elif block_name == "l0_ffn":
            emit_l0_ffn(ctx, W, x_ap, y_ap)
        elif block_name == "l1_ffn":
            emit_l1_ffn(ctx, W, x_ap, y_ap)
        else:
            raise ValueError(block_name)
        plan.step("sync", lambda sy: sy.nop())
        sems = {s: es.enter_context(nc.semaphore(s)) for s in SEMS}
        block = es.enter_context(nc.Block())
        plan.emit(block, sems)
    return nc


BLOCK_WNAMES = {
    "l0_attn": ["a0_Wq", "a0_Wk", "a0_Wv", "a0_Wo", "a0_bo_row",
                "a0_kln_w", "a0_kln_b", "a0_qln_w", "a0_qln_b"],
    "l1_attn": ["a1_Wq", "a1_Wk", "a1_Wv", "a1_Wo", "a1_bo_row",
                "a1_bq", "a1_bk", "a1_bv"],
    "l0_ffn": ["f0_wiT", "f0_woTt", "f0_bi", "f0_bo_row", "f0_AT", "f0_BT",
               "f0_gwT", "f0_gb"],
    "l1_ffn": ["f1_WmT", "f1_WpT", "f1_Wc1T", "f1_Wc2T"],
}

_prog_cache = {}


EMITTERS = {}


def build_full_program():
    nc = bass.Bass("TRN2", target_bir_lowering=False)
    x_ap = nc.dram_tensor("x", [D, T], F32, kind="ExternalInput").ap()
    y_ap = nc.dram_tensor("y", [D, T], F32, kind="ExternalOutput").ap()
    W = {}
    for names in BLOCK_WNAMES.values():
        for n in names:
            W[n] = nc.dram_tensor(n, WEIGHT_SHAPES[n], F32, kind="ExternalInput").ap()
    C = {n: nc.dram_tensor(n, CONST_SHAPES[n], F32, kind="ExternalInput").ap()
         for n in CONST_SHAPES}
    xb1 = nc.dram_tensor("xbuf1", [D, T], F32, kind="Internal").ap()
    xb2 = nc.dram_tensor("xbuf2", [D, T], F32, kind="Internal").ap()
    xb3 = nc.dram_tensor("xbuf3", [D, T], F32, kind="Internal").ap()
    plan = Plan()
    es = ExitStack()
    with es:
        ctx = Ctx(nc, plan, es)
        ctx.make_shared(C)
        chain = [(emit_l0_attn, x_ap, xb1), (emit_l0_ffn, xb1, xb2),
                 (emit_l1_attn, xb2, xb3), (emit_l1_ffn, xb3, y_ap)]
        for emit, xin, xout in chain:
            plan.fence()
            with ExitStack() as bes:
                ctx.block_es = bes
                emit(ctx, W, xin, xout)
            ctx.block_es = es
        plan.step("sync", lambda sy: sy.nop())
        sems = {s: es.enter_context(nc.semaphore(s)) for s in SEMS}
        block = es.enter_context(nc.Block())
        plan.emit(block, sems)
    return nc


_consts_cache = {}


def run_block(block_name, x_fm_cores, W):
    if block_name not in _prog_cache:
        _prog_cache[block_name] = build_block_program(block_name,
                                                      BLOCK_WNAMES[block_name])
    if "c" not in _consts_cache:
        _consts_cache["c"] = make_consts()
    CONSTS = _consts_cache["c"]
    nc = _prog_cache[block_name]
    in_maps = []
    for c in range(8):
        m = {"x": x_fm_cores[c]}
        for n in BLOCK_WNAMES[block_name]:
            m[n] = W[n]
        m.update(CONSTS)
        in_maps.append(m)
    res = run_bass_kernel_spmd(nc, in_maps, core_ids=list(range(8)))
    return [res.results[c]["y"] for c in range(8)]


def kernel(**inputs):
    W = prep_weights(inputs)
    x = np.asarray(inputs["L0_attn_x"], np.float32)
    xs = [np.ascontiguousarray(x[c].T) for c in range(8)]
    if "full" not in _prog_cache:
        _prog_cache["full"] = build_full_program()
    if "c" not in _consts_cache:
        _consts_cache["c"] = make_consts()
    nc = _prog_cache["full"]
    in_maps = []
    for c in range(8):
        m = {"x": xs[c]}
        for names in BLOCK_WNAMES.values():
            for n in names:
                m[n] = W[n]
        m.update(_consts_cache["c"])
        in_maps.append(m)
    res = run_bass_kernel_spmd(nc, in_maps, core_ids=list(range(8)))
    out = np.stack([np.asarray(res.results[c]["y"]).T for c in range(8)], axis=0)
    return np.ascontiguousarray(out, np.float32)


# version 14
# speedup vs baseline: 1.0624x; 1.0316x over previous
import numpy as np
import ml_dtypes
from contextlib import ExitStack

import concourse.bass as bass
import concourse.mybir as mybir
from concourse.bass_utils import run_bass_kernel_spmd

F32 = mybir.dt.float32
BF16 = mybir.dt.bfloat16
AF = mybir.ActivationFunctionType
ALU = mybir.AluOpType
AX = mybir.AxisListType

D = 1024
T = 2048
NCH = D // 128
H = 16
NJ = T // 512
JW = 512
SCALE = 0.125
NEG = -2.0e30
EPS = 1e-5

N_GQ = 10
N_SQ = 6
SEMS = (("pe", "dv", "ac", "gp", "sy")
        + tuple(f"g{i}" for i in range(N_GQ))
        + tuple(f"s{i}" for i in range(N_SQ)))
ENG2SEM = {"tensor": "pe", "vector": "dv", "scalar": "ac",
           "gpsimd": "gp", "sync": "sy"}
_ESIZE = {F32: 4, BF16: 2}


def _is_ap(x):
    return hasattr(x, "base_partition")


def _unm(x):
    return x() if callable(x) else x


def _fp(ap):
    if callable(ap.ap):
        ap = ap.ap()
    name = _unm(ap.name)
    ent = _unm(ap.ap)
    es = _ESIZE.get(_unm(ap.dtype), 4)
    if "DRAM" in str(_unm(ap.space)):
        reach = 1
        for st, sz in ent:
            reach += (sz - 1) * abs(st)
        off = _unm(ap.offset)
        return (name, 0, 1 << 20, off * es, (off + reach) * es)
    p0 = _unm(ap.base_partition)
    p1 = p0 + ent[0][1]
    reach = 1
    for st, sz in ent[1:]:
        reach += (sz - 1) * abs(st)
    off = _unm(ap.offset)
    return (name, p0, p1, off * es, (off + reach) * es)


def _ovl(a, b):
    return a[1] < b[2] and b[1] < a[2] and a[3] < b[4] and b[3] < a[4]


def _covers(big, small):
    return (big[1] <= small[1] and big[2] >= small[2]
            and big[3] <= small[3] and big[4] >= small[4])


class Plan:

    def __init__(self):
        self.steps = {e: [] for e in ENG2SEM}
        self.cnt = {s: 0 for s in SEMS}
        self.wr = {}
        self.rd = {}
        self.bar = None
        self.gq = 0
        self.sq = 0
        self.last_pool = {}

    def _dep_rd(self, f, deps):
        for e in self.wr.get(f[0], ()):
            if _ovl(e[0], f):
                if e[2] > deps.get(e[1], 0):
                    deps[e[1]] = e[2]

    def _dep_wr(self, f, deps):
        self._dep_rd(f, deps)
        for e in self.rd.get(f[0], ()):
            if _ovl(e[0], f):
                if e[2] > deps.get(e[1], 0):
                    deps[e[1]] = e[2]

    def _note_wr(self, f, sem, cnt):
        lst = self.wr.setdefault(f[0], [])
        lst[:] = [e for e in lst if not _covers(f, e[0])]
        lst.append((f, sem, cnt))
        rl = self.rd.get(f[0])
        if rl:
            rl[:] = [e for e in rl if not _covers(f, e[0])]

    def _note_rd(self, f, sem, cnt):
        lst = self.rd.setdefault(f[0], [])
        for i, e in enumerate(lst):
            if e[1] == sem and _covers(e[0], f):
                lst[i] = (e[0], sem, max(e[2], cnt))
                return
        lst.append((f, sem, cnt))

    def _emit(self, eng, fn, rf, wf, inc, inc_k, extra=None):
        deps = {}
        for f in rf:
            self._dep_rd(f, deps)
        for f in wf:
            self._dep_wr(f, deps)
        if self.bar is not None:
            s, c = self.bar
            if c > deps.get(s, 0):
                deps[s] = c
        if extra:
            for s, c in extra.items():
                if c > deps.get(s, 0):
                    deps[s] = c
        if eng == "tensor":
            deps.pop("pe", None)
        self.cnt[inc] += inc_k
        cnt = self.cnt[inc]
        for f in wf:
            self._note_wr(f, inc, cnt)
        for f in rf:
            self._note_rd(f, inc, cnt)
        self.steps[eng].append((fn, sorted(deps.items()), (inc, inc_k)))

    def _emit_universal(self, eng, fn, inc, inc_k):
        deps = {s: c for s, c in self.cnt.items() if c > 0}
        self.cnt[inc] += inc_k
        self.wr = {}
        self.rd = {}
        self.bar = (inc, self.cnt[inc])
        self.steps[eng].append((fn, sorted(deps.items()), (inc, inc_k)))

    def step(self, eng, fn, r=None, w=None, inc=None, inc_k=1):
        if inc is None:
            inc = ENG2SEM[eng]
        if r is None and w is None:
            self._emit_universal(eng, fn, inc, inc_k)
        else:
            rf = [_fp(a) for a in (r or ()) if _is_ap(a)]
            wf = [_fp(a) for a in (w or ()) if _is_ap(a)]
            self._emit(eng, fn, rf, wf, inc, inc_k)

    def _pool_sem(self, eng):
        if eng == "gpsimd":
            s = f"g{self.gq % N_GQ}"
            self.gq += 1
        else:
            s = f"s{self.sq % N_SQ}"
            self.sq += 1
        return s

    def dma(self, eng, fn, r=None, w=None):
        sem = self._pool_sem(eng)
        extra = {}
        prev = self.last_pool.get(sem)
        if prev:
            extra[sem] = prev
        if r is None and w is None:
            deps = {s: c for s, c in self.cnt.items() if c > 0}
            self.cnt[sem] += 16
            self.wr = {}
            self.rd = {}
            self.bar = (sem, self.cnt[sem])
            self.steps[eng].append((fn, sorted(deps.items()), (sem, 16)))
        else:
            rf = [_fp(a) for a in (r or ()) if _is_ap(a)]
            wf = [_fp(a) for a in (w or ()) if _is_ap(a)]
            self._emit(eng, fn, rf, wf, sem, 16, extra=extra)
        self.last_pool[sem] = self.cnt[sem]

    def fence(self):
        self._emit_universal("sync", lambda sy: sy.nop(), "sy", 1)

    def ld(self, eng, out, in_, **kw):
        self.dma(eng, lambda e, out=out, in_=in_, kw=kw: e.dma_start(
            out=out, in_=in_, **kw), r=[in_], w=[out])

    def mm(self, ops):
        rl, wl = [], []
        for (o, l, rr, _s, _p) in ops:
            wl.append(o)
            rl += [l, rr]
        def fn(t, ops=ops):
            last = None
            for (o, l, rr, s, p) in ops:
                last = t.matmul(o, l, rr, start=s, stop=p)
            return last
        self.step("tensor", fn, r=rl, w=wl)

    def tr(self, ops, ident):
        if callable(ident.ap):
            ident = ident.ap()
        rl = [ident]
        wl = []
        for (o, i) in ops:
            wl.append(o)
            rl.append(i)
        def fn(t, ops=ops, ident=ident):
            last = None
            for (o, i) in ops:
                last = t.transpose(o, i, ident)
            return last
        self.step("tensor", fn, r=rl, w=wl)

    def act(self, out, in_, func, bias=0.0, scale=1.0, accum_out=None):
        r = [in_] + [x for x in (bias, scale) if _is_ap(x)]
        w = [out] + ([accum_out] if accum_out is not None else [])
        if accum_out is not None:
            self.step("scalar", lambda s, o=out, i=in_, f=func, b=bias, sc=scale,
                      a=accum_out: s.activation(o, i, f, bias=b, scale=sc,
                                                accum_out=a), r=r, w=w)
        else:
            self.step("scalar", lambda s, o=out, i=in_, f=func, b=bias, sc=scale:
                      s.activation(o, i, f, bias=b, scale=sc), r=r, w=w)

    def ve(self, name, out, *ins, eng="vector", **kw):
        r = [x for x in ins if _is_ap(x)] + [x for x in kw.values() if _is_ap(x)]
        def fn(v, name=name, out=out, ins=ins, kw=kw):
            return getattr(v, name)(out, *ins, **kw)
        self.step(eng, fn, r=r, w=[out])

    def emit(self, block, sems):
        def runner(eng_name):
            def run(eng):
                hi = {s: 0 for s in SEMS}
                for fn, wvals, inc in self.steps[eng_name]:
                    for s, v in wvals:
                        if v > hi[s]:
                            eng.wait_ge(sems[s], v)
                            hi[s] = v
                    inst = fn(eng)
                    name, k = inc
                    assert inst is not None, f"step incrementing {name} returned None"
                    inst.then_inc(sems[name], k)
            return run
        block.tensor(runner("tensor"))
        block.vector(runner("vector"))
        block.scalar(runner("scalar"))
        block.gpsimd(runner("gpsimd"))
        block.sync(runner("sync"))



def prep_weights(inp):
    f32c = lambda a: np.ascontiguousarray(np.asarray(a), dtype=np.float32)
    w = {}
    for n in ("Wq", "Wk", "Wv", "Wo"):
        w["a0_" + n] = f32c(inp[f"L0_attn_{n}"])
    w["a0_bo_row"] = f32c(np.asarray(inp["L0_attn_bo"]).reshape(1, D))
    for n in ("kln_w", "kln_b", "qln_w", "qln_b"):
        w["a0_" + n] = f32c(np.tile(np.asarray(inp[f"L0_attn_{n}"]), 2).reshape(128, 1))
    w["f0_wiT"] = f32c(np.asarray(inp["L0_ffn_wi"]).T)
    w["f0_woT"] = f32c(np.asarray(inp["L0_ffn_wo"]).T)
    w["f0_bi"] = f32c(np.asarray(inp["L0_ffn_bi"]).reshape(1, 4096))
    w["f0_bo_row"] = f32c(np.asarray(inp["L0_ffn_bo"]).reshape(1, D))
    _la = np.asarray(inp["L0_ffn_lora_A"])
    w["f0_AT"] = f32c(_la.transpose(2, 0, 1).reshape(NCH, 128, 8, 16)
                      .transpose(1, 2, 0, 3))
    _wo = np.asarray(inp["L0_ffn_wo"]).T
    w["f0_woTt"] = f32c(_wo.reshape(32, 128, D).transpose(1, 0, 2))
    w["f0_BT"] = f32c(np.asarray(inp["L0_ffn_lora_B"]).transpose(0, 2, 1))
    w["f0_gwT"] = f32c(np.asarray(inp["L0_ffn_gate_w"]).T)
    w["f0_gb"] = f32c(np.asarray(inp["L0_ffn_gate_b"]).reshape(1, 8))
    for n in ("Wq", "Wk", "Wv", "Wo"):
        w["a1_" + n] = f32c(np.asarray(inp[f"L1_attn_{n}"]).T)
    for n in ("bq", "bk", "bv"):
        w["a1_" + n] = f32c(np.asarray(inp[f"L1_attn_{n}"]).reshape(1, D))
    w["a1_bo_row"] = f32c(np.asarray(inp["L1_attn_bo"]).reshape(1, D))
    w["f1_WmT"] = f32c(np.asarray(inp["L1_ffn_Wm"]).T)
    w["f1_WpT"] = f32c(np.asarray(inp["L1_ffn_Wp"]).T)
    w["f1_Wc1T"] = f32c(np.asarray(inp["L1_ffn_Wc1"]).T)
    w["f1_Wc2T"] = f32c(np.asarray(inp["L1_ffn_Wc2"]).T)
    return w


def make_consts():
    c = {}
    p = np.arange(128)[:, None]
    f512 = np.arange(JW)[None, :]
    c["c_ident"] = np.eye(128, dtype=np.float32)
    m = np.zeros((128, 4, JW), np.float32)
    for o in range(4):
        m[:, o, :] = np.where(p + 128 * o - f512 >= 1, NEG, 0.0)
    c["c_masks"] = np.ascontiguousarray(m.reshape(128, 4 * JW))
    s2 = np.zeros((2, 128), np.float32)
    s2[0, 0:64] = 1.0
    s2[1, 64:128] = 1.0
    c["c_sel2"] = s2
    o2 = np.zeros((128, 2), np.float32)
    o2[0:64, 0] = 1.0
    o2[64:128, 1] = 1.0
    c["c_ones2"] = o2
    c["c_ones_col"] = np.ones((128, 1), np.float32)
    dg = np.where(np.arange(128)[None, :] > np.arange(128)[:, None], NEG, 0.0)
    c["c_diag"] = np.ascontiguousarray(dg.astype(np.float32))
    c["c_ones_row"] = np.ones((1, JW), np.float32)
    c["c_eps"] = np.full((128, 1), EPS, np.float32)
    return c


CONST_SHAPES = {
    "c_ident": [128, 128], "c_masks": [128, 4 * JW], "c_sel2": [2, 128],
    "c_ones2": [128, 2], "c_ones_col": [128, 1], "c_ones_row": [1, JW],
    "c_eps": [128, 1], "c_diag": [128, 128],
}


WEIGHT_SHAPES = {
    "a0_Wq": [D, D], "a0_Wk": [D, D], "a0_Wv": [D, D], "a0_Wo": [D, D],
    "a0_bo_row": [1, D],
    "a0_kln_w": [128, 1], "a0_kln_b": [128, 1],
    "a0_qln_w": [128, 1], "a0_qln_b": [128, 1],
    "f0_wiT": [D, 4096], "f0_woT": [4096, D], "f0_bi": [1, 4096],
    "f0_bo_row": [1, D], "f0_AT": [128, 8, NCH, 16], "f0_BT": [8, 16, 4096],
    "f0_woTt": [128, 32, D],
    "f0_gwT": [D, 8], "f0_gb": [1, 8],
    "a1_Wq": [D, D], "a1_Wk": [D, D], "a1_Wv": [D, D], "a1_Wo": [D, D],
    "a1_bq": [1, D], "a1_bk": [1, D], "a1_bv": [1, D],
    "a1_bo_row": [1, D],
    "f1_WmT": [D, 12288], "f1_WpT": [12288, D], "f1_Wc1T": [D, 256],
    "f1_Wc2T": [256, 12288],
}


class Ctx:
    def __init__(self, nc, plan, es):
        self.nc, self.plan, self.es = nc, plan, es
        self.block_es = es
        self.uid = 0

    def nm(self, base):
        self.uid += 1
        return f"{base}{self.uid}"

    def sbuf(self, base, shape, dt, es=None):
        return (es or self.block_es).enter_context(
            self.nc.sbuf_tensor(self.nm(base), shape, dt))

    def psum(self, base, shape, dt=F32):
        return self.es.enter_context(self.nc.psum_tensor(self.nm(base), shape, dt))

    def make_shared(self, C):
        p = self.plan
        self.ones_col = self.sbuf("ones_col", [128, 1], BF16)
        self.ones_row16 = self.sbuf("ones_row16", [1, 128], BF16)
        self.onesf_row = self.sbuf("onesf_row", [1, JW], F32)
        self.onesf_128 = self.sbuf("onesf_128", [1, 128], F32)
        self.onesf_col = self.sbuf("onesf_col", [128, 1], F32)
        self.ones2 = self.sbuf("ones2", [128, 2], BF16)
        self.sel2 = self.sbuf("sel2", [2, 128], BF16)
        self.identf = self.sbuf("identf", [128, 128], F32)
        self.ident16 = self.sbuf("ident16", [128, 128], BF16)
        self.C = C
        self.eps_col = self.sbuf("eps_col", [128, 1], F32)
        self.nc.const_aps.register(F32, EPS, self.eps_col[:])

        self.PA = self.psum("PA", [128, 2048])
        self.PB = self.psum("PB", [128, 1024])
        self.PC = self.psum("PC", [128, 512])
        self.PT = self.psum("PTps", [128, 1024], BF16)

        self.row_a = self.sbuf("row_a", [2, JW], F32)
        self.row_b = self.sbuf("row_b", [2, JW], F32)
        self.row_c = self.sbuf("row_c", [2, JW], F32)
        self.abf = self.sbuf("abf", [128, JW], F32)
        self.cbf = self.sbuf("cbf", [128, JW], F32)
        self.sqf = self.sbuf("sqf", [128, JW], F32)

        for sb, cn in ((self.ones_col, "c_ones_col"), (self.ones2, "c_ones2"),
                       (self.sel2, "c_sel2"), (self.ident16, "c_ident")):
            p.ld("gpsimd", sb[:], C[cn])
        p.ld("gpsimd", self.ones_row16[:], C["c_ones_row"][0:1, 0:128])
        p.ld("gpsimd", self.onesf_row[:], C["c_ones_row"])
        p.ld("gpsimd", self.onesf_128[:], C["c_ones_row"][0:1, 0:128])
        p.ld("gpsimd", self.onesf_col[:], C["c_ones_col"])
        p.ld("gpsimd", self.identf[:], C["c_ident"])
        p.ld("gpsimd", self.eps_col[:], C["c_eps"])


def load_chunked(ctx, sb, hbm, n_chunks):
    p = ctx.plan
    for c in range(n_chunks):
        p.ld("gpsimd", sb[:, c, :], hbm[c * 128:(c + 1) * 128, :])


LN_STAGE = 99


def emit_outer_ln(ctx, rj, yj, y_ap, j0):
    p, X = ctx.plan, ctx
    for c in range(NCH):
        p.act(X.sqf[:, :], rj[:, c, :], AF.Square)
        def mm_st(t, c=c):
            t.matmul(X.PB[0:1, 0:JW], X.onesf_col[:], rj[:, c, :],
                     start=(c == 0), stop=(c == NCH - 1))
            return t.matmul(X.PB[0:1, JW:2 * JW], X.onesf_col[:], X.sqf[:, :],
                            start=(c == 0), stop=(c == NCH - 1))
        p.step("tensor", mm_st,
               r=[X.onesf_col[:], rj[:, c, :], X.sqf[:, :]],
               w=[X.PB[0:1, 0:2 * JW]])
    p.ve("tensor_scalar", X.row_b[0:1, :], X.PB[0:1, 0:JW],
         1.0 / D, None, ALU.mult)
    p.act(X.row_a[0:1, :], X.row_b[0:1, :], AF.Square)
    p.ve("scalar_tensor_tensor", X.row_c[0:1, :], X.PB[0:1, JW:2 * JW],
         1.0 / D, X.row_a[0:1, :], ALU.mult, ALU.subtract)
    p.act(X.row_c[0:1, :], X.row_c[0:1, :], AF.Ln, bias=EPS)
    p.act(X.row_a[0:1, :], X.row_c[0:1, :], AF.Exp, scale=-0.5)
    p.ve("scalar_tensor_tensor", X.row_b[0:1, :], X.row_b[0:1, :],
         -1.0, X.row_a[0:1, :], ALU.mult, ALU.mult)

    p.mm([(X.PC[:, :], X.onesf_128[:], X.row_a[0:1, :], True, True),
          (X.PB[:, 0:JW], X.onesf_128[:], X.row_b[0:1, :], True, True)])
    p.act(X.abf[:], X.PC[:, :], AF.Copy)
    p.act(X.cbf[:], X.PB[:, 0:JW], AF.Copy)
    for c in range(NCH):
        p.ve("tensor_mul", yj[:, c, :], rj[:, c, :], X.abf[:])
        p.ve("tensor_add", yj[:, c, :], yj[:, c, :], X.cbf[:])
        p.ld("sync", y_ap[c * 128:(c + 1) * 128, j0:j0 + JW], yj[:, c, :])



def emit_l0_attn(ctx, W, x_ap, y_ap):
    p, X = ctx.plan, ctx
    diagm = ctx.sbuf("diagm", [128, 128], F32)
    p.ld("gpsimd", diagm[:], ctx.C["c_diag"])
    qb = ctx.sbuf("qb", [128, NCH, T], BF16)
    kb = ctx.sbuf("kb", [128, NCH, T], BF16)
    vt = ctx.sbuf("vt", [128, T // 128, D], BF16)
    wo_j = ctx.sbuf("wo_j", [128, 2, NCH, 128], BF16)
    bo_row = ctx.sbuf("bo_row", [1, D], F32)
    lnw = {n: ctx.sbuf(n, [128, 1], F32) for n in ("klw", "klb", "qlw", "qlb")}
    sq = ctx.sbuf("sqs", [128, JW], BF16)
    t1 = ctx.sbuf("t1", [128, JW], BF16)
    ab16 = ctx.sbuf("ab16", [128, JW], BF16)
    cb16 = ctx.sbuf("cb16", [128, JW], BF16)
    row16a = ctx.sbuf("row16a", [2, JW], BF16)
    row16c = ctx.sbuf("row16c", [2, JW], BF16)

    with ExitStack() as es1:
        xb = ctx.sbuf("xb", [128, NCH, T], BF16, es1)
        wcol = ctx.sbuf("wcol", [128, 2, NCH, 128], BF16, es1)
        stage = ctx.sbuf("stage", [128, 2, JW], BF16, es1)

        load_chunked(ctx, xb, x_ap, NCH)
        p.ld("gpsimd", bo_row[:], W["a0_bo_row"])
        for sb_n, w_n in (("klw", "a0_kln_w"), ("klb", "a0_kln_b"),
                          ("qlw", "a0_qln_w"), ("qlb", "a0_qln_b")):
            p.ld("gpsimd", lnw[sb_n][:], W[w_n])

        qkv_ps = [X.PC[:, 0:JW], X.PB[:, 0:JW]]
        for wi, (w_n, kind) in enumerate((("a0_Wq", "q"), ("a0_Wk", "k"),
                                          ("a0_Wv", "v"))):
            for cout in range(NCH):
                slot = (wi * NCH + cout) % 2
                for ci in range(NCH):
                    p.ld("gpsimd", wcol[:, slot, ci, :],
                         W[w_n][ci * 128:(ci + 1) * 128,
                                cout * 128:(cout + 1) * 128])
                for jn in range(T // JW):
                    n0 = jn * JW
                    ps = qkv_ps[jn % 2]
                    p.mm([(ps, wcol[:, slot, ci, :], xb[:, ci, n0:n0 + JW],
                           ci == 0, ci == NCH - 1) for ci in range(NCH)])
                    if kind in ("q", "k"):
                        out_sb = qb if kind == "q" else kb
                        p.act(out_sb[:, cout, n0:n0 + JW], ps, AF.Copy)
                    else:
                        stg = stage[:, jn % 2, :]
                        p.act(stg, ps, AF.Copy)
                        p.tr([(X.PT[:, i * 128:(i + 1) * 128],
                               stg[:, i * 128:(i + 1) * 128])
                              for i in range(4)], X.ident16)
                        p.ve("tensor_copy",
                             vt[:, jn * 4:jn * 4 + 4,
                                cout * 128:(cout + 1) * 128],
                             X.PT[:, 0:512].rearrange("p (a b) -> p a b", a=4))

    def qkln(src, wl, bl):
        for c in range(NCH):
            for n in range(4):
                n0 = n * JW
                sl = src[:, c, n0:n0 + JW]
                p.act(sq[:, :], sl, AF.Square)
                p.mm([(X.PB[0:2, 0:JW], X.ones2[:], sl, True, True),
                      (X.PC[0:2, :], X.ones2[:], sq[:, :], True, True)])
                p.ve("tensor_scalar", X.row_b[:, :], X.PB[0:2, 0:JW],
                     1.0 / 64, None, ALU.mult)
                p.act(X.row_a[:, :], X.row_b[:, :], AF.Square)
                p.ve("scalar_tensor_tensor", X.row_c[:, :], X.PC[0:2, :],
                     1.0 / 64, X.row_a[:, :], ALU.mult, ALU.subtract)
                p.act(X.row_c[:, :], X.row_c[:, :], AF.Ln, bias=EPS)
                p.act(X.row_a[:, :], X.row_c[:, :], AF.Exp, scale=-0.5)
                p.ve("scalar_tensor_tensor", X.row_b[:, :], X.row_b[:, :],
                     -1.0, X.row_a[:, :], ALU.mult, ALU.mult)
                p.ve("tensor_copy", row16a[0:2, :], X.row_a[:, :])
                p.ve("tensor_copy", row16c[0:2, :], X.row_b[:, :])
                p.mm([(X.PB[:, 0:JW], X.sel2[:], row16a[0:2, :], True, True),
                      (X.PC[:, :], X.sel2[:], row16c[0:2, :], True, True)])
                p.act(ab16[:, :], X.PB[:, 0:JW], AF.Copy)
                p.act(cb16[:, :], X.PC[:, :], AF.Copy)
                p.ve("tensor_mul", t1[:, :], sl, ab16[:, :])
                p.ve("tensor_add", t1[:, :], t1[:, :], cb16[:, :])
                p.ve("tensor_scalar", sl, t1[:, :], wl[:], bl[:],
                     ALU.mult, ALU.add)
    qkln(qb, lnw["qlw"], lnw["qlb"])
    qkln(kb, lnw["klw"], lnw["klb"])

    p.fence()
    with ExitStack() as es2:
        xj = ctx.sbuf("xj", [128, NCH, JW], F32, es2)
        s0b = ctx.sbuf("s0b", [128, 2, T], F32, es2)
        pt = ctx.sbuf("pt", [128, T], BF16, es2)
        ptT = ctx.sbuf("ptT", [128, T // 128, JW], BF16, es2)
        negm = ctx.sbuf("negm", [128, 1], F32, es2)
        bcol = ctx.sbuf("bcol", [128, 1], F32, es2)
        zcol = ctx.sbuf("zcol", [128, 4], F32, es2)
        rrow = X.row_c[0:1, :]
        rb64 = ctx.sbuf("rb64", [64, JW], F32, es2)
        ao = ctx.sbuf("ao", [128, NCH, JW], BF16, es2)

        for J in range(NJ):
            j0 = J * JW
            for c in range(NCH):
                p.ld("sync", xj[:, c, :],
                     x_ap[c * 128:(c + 1) * 128, j0:j0 + JW])
            for h in range(H):
                hr0 = (h % 2) * 64
                hc = h // 2
                for jj in range(4):
                    qt0 = j0 + jj * 128
                    ktw = qt0 + 128
                    ops = []
                    n0 = 0
                    while n0 < ktw:
                        nn = min(JW, ktw - n0)
                        ops.append((X.PA[:, n0:n0 + nn],
                                    qb[hr0:hr0 + 64, hc, qt0:qt0 + 128],
                                    kb[hr0:hr0 + 64, hc, n0:n0 + nn],
                                    True, True))
                        n0 += nn
                    p.mm(ops)
                    sj = s0b[:, jj % 2, 0:ktw]
                    p.act(sj, X.PA[:, 0:ktw], AF.Copy)
                    p.ve("tensor_add", sj[:, ktw - 128:ktw],
                         sj[:, ktw - 128:ktw], diagm[:])
                    p.ve("reduce_max", negm[:], sj, axis=AX.X,
                         negate=True)
                    p.ve("tensor_scalar", bcol[:], negm[:], SCALE, None,
                         ALU.mult)
                    p.act(pt[:, 0:ktw], sj, AF.Exp, bias=bcol[:],
                          scale=SCALE, accum_out=zcol[:, jj:jj + 1])
                    nch_k = ktw // 128
                    for g0 in range(0, nch_k, 8):
                        gn = min(8, nch_k - g0)
                        p.tr([(X.PT[:, i * 128:(i + 1) * 128],
                               pt[:, (g0 + i) * 128:(g0 + i + 1) * 128])
                              for i in range(gn)], X.ident16)
                        p.act(ptT[:, g0:g0 + gn, jj * 128:(jj + 1) * 128],
                              X.PT[:, 0:gn * 128].rearrange(
                                  "p (a b) -> p a b", b=128), AF.Copy)

                p.mm([(X.PB[0:1, JW + jj * 128:JW + (jj + 1) * 128],
                       zcol[:, jj:jj + 1], X.identf[:], True, True)
                      for jj in range(4)])
                p.act(rrow[:], X.PB[0:1, JW:2 * JW], AF.Ln)
                p.act(rrow[:], rrow[:], AF.Exp, scale=-1.0)
                p.mm([(X.PB[0:64, 0:JW], X.onesf_128[0:1, 0:64], rrow[:],
                       True, True)])
                p.act(rb64[:], X.PB[0:64, 0:JW], AF.Copy)
                ops = []
                for c in range(4 * J + 4):
                    k = c - 4 * J
                    lo = max(0, k) * 128
                    ops.append((X.PC[0:64, lo:JW],
                                vt[:, c, h * 64:(h + 1) * 64],
                                ptT[:, c, lo:JW], c == 0, c == 4 * J + 3))
                p.mm(ops)
                p.ve("tensor_mul",
                     ao[(h % 2) * 64:(h % 2) * 64 + 64, h // 2, :],
                     X.PC[0:64, :], rb64[:])

            for cout in range(NCH):
                ws = cout % 2
                for ci in range(NCH):
                    p.ld("gpsimd", wo_j[:, ws, ci, :],
                         W["a0_Wo"][ci * 128:(ci + 1) * 128,
                                    cout * 128:(cout + 1) * 128])
                p.mm([(X.PC[:, :], wo_j[:, ws, ci, :],
                       ao[:, ci, :], ci == 0, False) for ci in range(NCH)]
                     + [(X.PC[:, :], bo_row[0:1, cout * 128:(cout + 1) * 128],
                         X.onesf_row[:], False, True)])
                p.ve("tensor_add", xj[:, cout, :], X.PC[:, :], xj[:, cout, :])
            emit_outer_ln(ctx, xj, xj, y_ap, j0)



L1A_STAGE = 99
L1A_NJ = NJ
L1A_DBG = False


def emit_l1_attn(ctx, W, x_ap, y_ap):
    if L1A_DBG:
        dbg = {n: ctx.nc.dram_tensor("dbg_" + n, shp, F32,
                                     kind="ExternalOutput").ap()
               for n, shp in (("scores", [128, T]), ("m8", [128, 8]),
                              ("u", [128, T]), ("pt", [128, T]),
                              ("zcol", [128, 4]), ("ao", [64, JW]),
                              ("q", [64, 128]), ("k", [64, T]),
                              ("rj", [128, NCH * JW]),
                              ("aof", [128, NCH * JW]),
                              ("xj0", [128, NCH * JW]))}
    p, X = ctx.plan, ctx
    qb = ctx.sbuf("qb", [128, NCH, T], BF16)
    kb = ctx.sbuf("kb", [128, NCH, T], BF16)
    vt = ctx.sbuf("vt", [128, T // 128, D], BF16)
    wo_j = ctx.sbuf("wo_j", [128, 2, NCH, 128], BF16)
    bo_row = ctx.sbuf("bo_row", [1, D], F32)

    with ExitStack() as es1:
        xb = ctx.sbuf("xb", [128, NCH, T], BF16, es1)
        bqkv = [ctx.sbuf("bqkv", [1, D], F32, es1) for _ in range(3)]
        wcol = ctx.sbuf("wcol", [128, 2, NCH, 128], BF16, es1)
        stage = ctx.sbuf("stage", [128, 2, JW], BF16, es1)

        load_chunked(ctx, xb, x_ap, NCH)
        p.ld("gpsimd", bo_row[:], W["a1_bo_row"])
        for i, n in enumerate(("a1_bq", "a1_bk", "a1_bv")):
            p.ld("gpsimd", bqkv[i][:], W[n])

        qkv_ps = [X.PC[:, 0:JW], X.PB[:, 0:JW]]
        for wi, (w_n, kind) in enumerate((("a1_Wq", "q"), ("a1_Wk", "k"),
                                          ("a1_Wv", "v"))):
            for cout in range(NCH):
                slot = (wi * NCH + cout) % 2
                for ci in range(NCH):
                    p.ld("gpsimd", wcol[:, slot, ci, :],
                         W[w_n][ci * 128:(ci + 1) * 128,
                                cout * 128:(cout + 1) * 128])
                for jn in range(T // JW):
                    n0 = jn * JW
                    ps = qkv_ps[jn % 2]
                    p.mm([(ps, wcol[:, slot, ci, :], xb[:, ci, n0:n0 + JW],
                           ci == 0, False) for ci in range(NCH)]
                         + [(ps, bqkv[wi][0:1, cout * 128:(cout + 1) * 128],
                             X.onesf_row[:], False, True)])
                    if kind in ("q", "k"):
                        out_sb = qb if kind == "q" else kb
                        p.act(out_sb[:, cout, n0:n0 + JW], ps, AF.Copy)
                    else:
                        stg = stage[:, jn % 2, :]
                        p.act(stg, ps, AF.Copy)
                        p.tr([(X.PT[:, i * 128:(i + 1) * 128],
                               stg[:, i * 128:(i + 1) * 128])
                              for i in range(4)], X.ident16)
                        p.ve("tensor_copy",
                             vt[:, jn * 4:jn * 4 + 4,
                                cout * 128:(cout + 1) * 128],
                             X.PT[:, 0:512].rearrange("p (a b) -> p a b", a=4))

    p.fence()
    with ExitStack() as es2:
        xj = ctx.sbuf("xj", [128, NCH, JW], F32, es2)
        s_sb = ctx.sbuf("s_sb", [128, 4, T], F32, es2)
        scr = ctx.sbuf("scr", [128, 1024], F32, es2)
        cand = ctx.sbuf("cand", [128, 4, 256], F32, es2)
        m8 = ctx.sbuf("m8", [128, 4, 8], F32, es2)
        taucol = ctx.sbuf("taucol", [128, 4], F32, es2)
        pt = ctx.sbuf("pt", [128, T], BF16, es2)
        ptT = ctx.sbuf("ptT", [128, T // 128, JW], BF16, es2)
        zcol = ctx.sbuf("zcol", [128, 4], F32, es2)
        rrow = X.row_c[0:1, :]
        rb64 = ctx.sbuf("rb64", [64, JW], F32, es2)
        ao = ctx.sbuf("ao", [128, NCH, JW], BF16, es2)

        for J in range(L1A_NJ):
            j0 = J * JW
            for c in range(NCH):
                p.ld("sync", xj[:, c, :],
                     x_ap[c * 128:(c + 1) * 128, j0:j0 + JW])
            def tail_T(hp, jj):
                for g in range(2):
                    p.tr([(X.PT[:, i * 128:(i + 1) * 128],
                           pt[:, (g * 8 + i) * 128:(g * 8 + i + 1) * 128])
                          for i in range(8)], X.ident16)
                    p.act(ptT[:, g * 8:g * 8 + 8, jj * 128:(jj + 1) * 128],
                          X.PT[:].rearrange("p (a b) -> p a b", a=8),
                          AF.Copy)

            def tail_PV(hp):
                p.mm([(X.PB[0:1, JW + jj * 128:JW + (jj + 1) * 128],
                       zcol[:, jj:jj + 1], X.identf[:], True, True)
                      for jj in range(4)])
                p.act(rrow[:], X.PB[0:1, JW:2 * JW], AF.Ln)
                p.act(rrow[:], rrow[:], AF.Exp, scale=-1.0)
                p.mm([(X.PB[0:64, 0:JW], X.onesf_128[0:1, 0:64], rrow[:],
                       True, True)])
                p.act(rb64[:], X.PB[0:64, 0:JW], AF.Copy)
                p.mm([(X.PC[0:64, :], vt[:, c, hp * 64:(hp + 1) * 64],
                       ptT[:, c, :], c == 0, c == T // 128 - 1)
                      for c in range(T // 128)])
                p.ve("tensor_mul",
                     ao[(hp % 2) * 64:(hp % 2) * 64 + 64, hp // 2, :],
                     X.PC[0:64, :], rb64[:])

            NHH = H if L1A_STAGE >= 2 else 0
            for h in range(NHH + 1):
                hr0 = (h % 2) * 64
                hc = h // 2
                for jj in range(4 if h < NHH else 0):
                    qt0 = j0 + jj * 128
                    p.mm([(X.PA[:, n * JW:(n + 1) * JW],
                           qb[hr0:hr0 + 64, hc, qt0:qt0 + 128],
                           kb[hr0:hr0 + 64, hc, n * JW:(n + 1) * JW],
                           True, True) for n in range(T // JW)])
                    sj = s_sb[:, jj, :]
                    p.act(sj, X.PA[:, :], AF.Copy)
                    cnd = cand[:, jj, :]
                    def tk_a(v, sj=sj, cnd=cnd):
                        last = None
                        for c in range(16):
                            last = v.max(out=cnd[:, c * 8:(c + 1) * 8],
                                         in_=sj[:, c * 128:(c + 1) * 128])
                        return last
                    p.step("vector", tk_a, r=[sj], w=[cnd])
                    for half in range(2):
                        def tk_b(v, sj=sj, cnd=cnd, half=half):
                            last = None
                            for c8 in range(8):
                                c = half * 8 + c8
                                last = v.match_replace(
                                    out=scr[:, c8 * 128:(c8 + 1) * 128],
                                    in_to_replace=cnd[:, c * 8:(c + 1) * 8],
                                    in_values=sj[:, c * 128:(c + 1) * 128],
                                    imm_value=-1e30)
                            return last
                        p.step("vector", tk_b, r=[sj, cnd], w=[scr[:]])
                        def tk_c(v, cnd=cnd, half=half):
                            last = None
                            for c8 in range(8):
                                c = half * 8 + c8
                                last = v.max(
                                    out=cnd[:, 128 + c * 8:136 + c * 8],
                                    in_=scr[:, c8 * 128:(c8 + 1) * 128])
                            return last
                        p.step("vector", tk_c, r=[scr[:]], w=[cnd])
                if h > 0:
                    tail_PV(h - 1)
                if h >= NHH:
                    break
                def dwa(jj):
                    return scr[:, jj * 256:(jj + 1) * 256]
                def dwb(jj):
                    return cand[:, jj, :]
                def mstep(fn, r, w):
                    p.step("vector", fn, r=r, w=w)
                mstep(lambda v: [v.max(out=m8[:, jj, :], in_=cand[:, jj, :])
                                 for jj in range(4)][-1],
                      r=[cand[:]], w=[m8[:]])
                mstep(lambda v: [v.match_replace(
                    out=dwa(jj), in_to_replace=m8[:, jj, :],
                    in_values=cand[:, jj, :], imm_value=-1e30)
                    for jj in range(4)][-1],
                      r=[cand[:], m8[:]], w=[scr[:]])
                mstep(lambda v: [v.max(out=m8[:, jj, :], in_=dwa(jj))
                                 for jj in range(4)][-1],
                      r=[scr[:]], w=[m8[:]])
                mstep(lambda v: [v.match_replace(
                    out=dwb(jj), in_to_replace=m8[:, jj, :],
                    in_values=dwa(jj), imm_value=-1e30)
                    for jj in range(4)][-1],
                      r=[scr[:], m8[:]], w=[cand[:]])
                mstep(lambda v: [v.max(out=m8[:, jj, :], in_=dwb(jj))
                                 for jj in range(4)][-1],
                      r=[cand[:]], w=[m8[:]])
                mstep(lambda v: [v.match_replace(
                    out=dwa(jj), in_to_replace=m8[:, jj, :],
                    in_values=dwb(jj), imm_value=-1e30)
                    for jj in range(4)][-1],
                      r=[cand[:], m8[:]], w=[scr[:]])
                mstep(lambda v: [v.max(out=m8[:, jj, :], in_=dwa(jj))
                                 for jj in range(4)][-1],
                      r=[scr[:]], w=[m8[:]])
                mstep(lambda v: [v.tensor_scalar(
                    taucol[:, jj:jj + 1], m8[:, jj, 7:8], -SCALE, None,
                    ALU.mult) for jj in range(4)][-1],
                      r=[m8[:]], w=[taucol[:]])
                for jj in range(4):
                    sj = s_sb[:, jj, :]
                    p.act(sj, sj, AF.Exp, bias=taucol[:, jj:jj + 1],
                          scale=SCALE)
                    p.ve("scalar_tensor_tensor", pt[:, :], sj, 0.99995, sj,
                         ALU.is_ge, ALU.mult)
                    p.act(pt[:, :], pt[:, :], AF.Copy,
                          accum_out=zcol[:, jj:jj + 1])
                    tail_T(h, jj)

            for cout in range(NCH if L1A_STAGE >= 6 else 0):
                ws = cout % 2
                for ci in range(NCH):
                    p.ld("gpsimd", wo_j[:, ws, ci, :],
                         W["a1_Wo"][ci * 128:(ci + 1) * 128,
                                    cout * 128:(cout + 1) * 128])
                p.mm([(X.PC[:, :], wo_j[:, ws, ci, :],
                       ao[:, ci, :], ci == 0, False) for ci in range(NCH)]
                     + [(X.PC[:, :], bo_row[0:1, cout * 128:(cout + 1) * 128],
                         X.onesf_row[:], False, True)])
                p.ve("tensor_add", xj[:, cout, :], X.PC[:, :], xj[:, cout, :])
            if L1A_DBG and J == 0:
                p.ld("gpsimd", dbg["aof"],
                     ao[:].rearrange("p a b -> p (a b)"))
                p.ld("gpsimd", dbg["rj"],
                     xj[:].rearrange("p a b -> p (a b)"))
            if L1A_STAGE >= 7:
                emit_outer_ln(ctx, xj, xj, y_ap, j0)



def emit_l0_ffn(ctx, W, x_ap, y_ap):
    p, X = ctx.plan, ctx
    DFF = 4096
    NFC = DFF // JW
    gw_sb = ctx.sbuf("gw_sb", [128, NCH, 8], BF16)
    gb_row = ctx.sbuf("gb_row", [1, 8], F32)
    bi_row = ctx.sbuf("bi_row", [1, DFF], F32)
    bo_row = ctx.sbuf("bo_row", [1, D], F32)
    at_sb = ctx.sbuf("at_sb", [128, 8, NCH, 16], BF16)
    u16 = ctx.sbuf("u16", [16, 8, JW], BF16)

    for ci in range(NCH):
        p.ld("gpsimd", gw_sb[:, ci, :], W["f0_gwT"][ci * 128:(ci + 1) * 128, :])
    p.ld("gpsimd", gb_row[:], W["f0_gb"])
    p.ld("gpsimd", bi_row[:], W["f0_bi"])
    p.ld("gpsimd", bo_row[:], W["f0_bo_row"])
    p.ld("gpsimd", at_sb[:], W["f0_AT"][:])

    with ExitStack() as es2:
        xbJ = ctx.sbuf("xbJ", [128, NCH, JW], BF16, es2)
        xj = ctx.sbuf("xj", [128, NCH, JW], F32, es2)
        lg = ctx.sbuf("lg", [128, 4, 8], F32, es2)
        srt = ctx.sbuf("srt", [128, 4, 8], F32, es2)
        ge = ctx.sbuf("ge", [128, 4, 8], F32, es2)
        vs = ctx.sbuf("vs", [128, 4], F32, es2)
        n1c = ctx.sbuf("n1c", [128, 1], F32, es2)
        e2c = ctx.sbuf("e2c", [128, 1], F32, es2)
        rzc = ctx.sbuf("rzc", [128, 1], F32, es2)
        zc = ctx.sbuf("zc", [128, 1], F32, es2)
        jk8 = ctx.sbuf("jk8", [128, 8], F32, es2)
        vs2row = ctx.sbuf("vs2row", [1, JW], F32, es2)
        wi_fc = ctx.sbuf("wi_fc", [128, 2, NCH, JW], BF16, es2)
        bt_fc = ctx.sbuf("bt_fc", [16, 2, 8, JW], BF16, es2)
        base_sb = ctx.sbuf("base_sb", [128, 4, JW], F32, es2)
        tmp_f = ctx.sbuf("tmp_f", [128, 2, 4, JW], F32, es2)
        rsum = ctx.sbuf("rsum", [128, 2, 4, JW], BF16, es2)
        rk = ctx.sbuf("rk", [128, 2, 4, JW], BF16, es2)
        rt_sb = ctx.sbuf("rt_sb", [128, 4 * NFC, JW], BF16, es2)
        wo_co = ctx.sbuf("wo_co", [128, 2, 4 * NFC, 128], BF16, es2)

        p.fence()

        for J in range(NJ):
            j0 = J * JW
            for c in range(NCH):
                p.ld("gpsimd", xbJ[:, c, :],
                     x_ap[c * 128:(c + 1) * 128, j0:j0 + JW])
                p.ld("sync", xj[:, c, :],
                     x_ap[c * 128:(c + 1) * 128, j0:j0 + JW])

            for ts in range(4):
                p.mm([(X.PC[:, 0:8], xbJ[:, ci, ts * 128:(ts + 1) * 128],
                       gw_sb[:, ci, :], ci == 0, False) for ci in range(NCH)]
                     + [(X.PC[:, 0:8], X.onesf_128[:], gb_row[:], False, True)])
                p.act(lg[:, ts, :], X.PC[:, 0:8], AF.Copy)
                p.step("vector", (lambda ts=ts: lambda v: v.max(
                    out=srt[:, ts, :], in_=lg[:, ts, :]))(),
                    r=[lg[:, ts, :]], w=[srt[:, ts, :]])
                p.ve("tensor_scalar", n1c[:], srt[:, ts, 0:1], -1.0, None,
                     ALU.mult)
                p.act(jk8[:], lg[:, ts, :], AF.Exp, bias=n1c[:],
                      accum_out=zc[:])
                p.act(e2c[:], srt[:, ts, 1:2], AF.Exp, bias=n1c[:])
                p.ve("reciprocal", rzc[:], zc[:])
                p.ve("scalar_tensor_tensor", vs[:, ts:ts + 1], e2c[:], 1.0,
                     rzc[:], ALU.add, ALU.mult)
                def mk_ge(v, ts=ts):
                    last = None
                    for e in range(8):
                        last = v.scalar_tensor_tensor(
                            ge[:, ts, e:e + 1], lg[:, ts, e:e + 1],
                            srt[:, ts, 1:2], vs[:, ts:ts + 1],
                            ALU.is_ge, ALU.mult)
                    return last
                p.step("vector", mk_ge,
                       r=[lg[:, ts, :], srt[:, ts, :], vs[:, ts:ts + 1]],
                       w=[ge[:, ts, :]])
            p.ve("tensor_scalar", vs[:, :], vs[:, :], 2.0, None, ALU.mult)
            p.mm([(X.PB[0:1, ts * 128:(ts + 1) * 128], vs[:, ts:ts + 1],
                   X.identf[:], True, True) for ts in range(4)])
            p.act(vs2row[:], X.PB[0:1, 0:JW], AF.Copy)

            for e in range(8):
                p.mm([(X.PB[0:16, 0:JW], at_sb[:, e, ci, :], xbJ[:, ci, :],
                       ci == 0, ci == NCH - 1) for ci in range(NCH)])
                p.act(u16[0:16, e, :], X.PB[0:16, 0:JW], AF.Copy)

            for fc in range(NFC):
                slot = fc % 2
                for ci in range(NCH):
                    p.ld("gpsimd", wi_fc[:, slot, ci, :],
                         W["f0_wiT"][ci * 128:(ci + 1) * 128,
                                     fc * JW:(fc + 1) * JW])
                for e in range(8):
                    p.ld("gpsimd", bt_fc[0:16, slot, e, :],
                         W["f0_BT"][e][:, fc * JW:(fc + 1) * JW])
                for ts in range(4):
                    p.mm([(X.PA[:, ts * JW:(ts + 1) * JW],
                           xbJ[:, ci, ts * 128:(ts + 1) * 128],
                           wi_fc[:, slot, ci, :], ci == 0, False)
                          for ci in range(NCH)]
                         + [(X.PA[:, ts * JW:(ts + 1) * JW], X.onesf_128[:],
                             bi_row[0:1, fc * JW:(fc + 1) * JW], False, True)])
                for hp in range(2):
                    p.act(base_sb[:, 2 * hp:2 * hp + 2, :].rearrange(
                        "p a b -> p (a b)"),
                        X.PA[:, hp * 1024:(hp + 1) * 1024], AF.Copy)
                for e in range(8):
                    tpar = e % 2
                    for hp in range(2):
                        p.mm([(X.PA[:, ts * JW:(ts + 1) * JW],
                               u16[0:16, e, ts * 128:(ts + 1) * 128],
                               bt_fc[0:16, slot, e, :], True, True)
                              for ts in (2 * hp, 2 * hp + 1)])
                    for hp in range(2):
                        p.ve("scalar_tensor_tensor",
                             tmp_f[:, tpar, 2 * hp:2 * hp + 2, :].rearrange(
                                 "p a b -> p (a b)"),
                             X.PA[:, hp * 1024:(hp + 1) * 1024], 1.0,
                             base_sb[:, 2 * hp:2 * hp + 2, :].rearrange(
                                 "p a b -> p (a b)"),
                             ALU.mult, ALU.add)
                    dst = rsum if e < 2 else rk
                    for hp in range(2):
                        def mk_relu(s, e=e, dst=dst, tpar=tpar, hp=hp):
                            last = None
                            for ts in (2 * hp, 2 * hp + 1):
                                last = s.activation(dst[:, tpar, ts, :],
                                                    tmp_f[:, tpar, ts, :],
                                                    AF.Relu, bias=0.0,
                                                    scale=ge[:, ts, e:e + 1])
                            return last
                        p.step("scalar", mk_relu,
                               r=[tmp_f[:, tpar, 2 * hp:2 * hp + 2], ge[:]],
                               w=[dst[:, tpar, 2 * hp:2 * hp + 2]])
                    if e >= 2:
                        for hp in range(2):
                            p.ve("tensor_tensor",
                                 rsum[:, tpar, 2 * hp:2 * hp + 2].rearrange(
                                     "p a b -> p (a b)"),
                                 rsum[:, tpar, 2 * hp:2 * hp + 2].rearrange(
                                     "p a b -> p (a b)"),
                                 rk[:, tpar, 2 * hp:2 * hp + 2].rearrange(
                                     "p a b -> p (a b)"),
                                 eng=("vector" if tpar == 0 else "gpsimd"),
                                 op=ALU.add)
                p.ve("tensor_tensor",
                     rsum[:, 0].rearrange("p a b -> p (a b)"),
                     rsum[:, 0].rearrange("p a b -> p (a b)"),
                     rsum[:, 1].rearrange("p a b -> p (a b)"),
                     op=ALU.add)
                for half in range(2):
                    ops = []
                    for i in range(8):
                        ts, fs = (half * 8 + i) // 4, (half * 8 + i) % 4
                        ops.append((X.PT[:, i * 128:(i + 1) * 128],
                                    rsum[:, 0, ts, fs * 128:(fs + 1) * 128]))
                    p.tr(ops, X.ident16)
                    for i in range(8):
                        ts, fs = (half * 8 + i) // 4, (half * 8 + i) % 4
                        p.act(rt_sb[:, fc * 4 + fs, ts * 128:(ts + 1) * 128],
                              X.PT[:, i * 128:(i + 1) * 128], AF.Copy)

            for cout in range(NCH):
                ws = cout % 2
                p.ld("gpsimd", wo_co[:, ws, :, :],
                     W["f0_woTt"][:, :, cout * 128:(cout + 1) * 128])
                p.mm([(X.PB[:, 0:JW], wo_co[:, ws, fcc, :], rt_sb[:, fcc, :],
                       fcc == 0, False) for fcc in range(4 * NFC)]
                     + [(X.PB[:, 0:JW],
                         bo_row[0:1, cout * 128:(cout + 1) * 128],
                         vs2row[:], False, True)])
                p.ve("tensor_add", xj[:, cout, :], X.PB[:, 0:JW],
                     xj[:, cout, :])
            emit_outer_ln(ctx, xj, xj, y_ap, j0)



def emit_l1_ffn(ctx, W, x_ap, y_ap):
    p, X = ctx.plan, ctx
    nc = ctx.nc
    FULL = 12288
    NF128 = FULL // 128
    NSC = 12
    R_THRESH = 0.7 * 4096.0 * 8.0

    carry = [nc.dram_tensor(ctx.nm("carry"), [D, T], BF16, kind="Internal").ap()
             for _ in range(2)]
    nov_in = nc.dram_tensor(ctx.nm("nov_in"), [1, 512], F32, kind="Internal").ap()
    nov_out = nc.dram_tensor(ctx.nm("nov_out"), [1, 512], F32, kind="Internal").ap()
    nov_stage = ctx.sbuf("nov_stage", [1, 512], F32)

    xnew = ctx.sbuf("xnew", [128, NCH, T], F32)
    active = ctx.sbuf("active", [128, NF128], F32)
    hist = ctx.sbuf("hist", [128, NF128], F32)
    msk = ctx.sbuf("msk", [128, NF128], F32)
    comb = ctx.sbuf("comb", [128, NF128], F32)
    d96 = ctx.sbuf("d96", [128, NF128], F32)
    ck_sb = ctx.sbuf("ck_sb", [128, NF128], F32)
    jk96 = ctx.sbuf("jk96", [128, NF128], F32)
    pooled = ctx.sbuf("pooled", [128, NCH], F32)
    pooled16 = ctx.sbuf("pooled16", [128, NCH], BF16)
    cnt_c = ctx.sbuf("cnt_c", [128, 1], F32)
    cnt16 = ctx.sbuf("cnt16", [128, 1], BF16)
    rr16 = ctx.sbuf("rr16", [128, 2], BF16)
    wc1_sb = ctx.sbuf("wc1_sb", [128, NCH, 256], BF16)
    srow = ctx.sbuf("srow", [1, 16], F32)
    dcol = ctx.sbuf("dcol", [128, 1], F32)
    omdcol = ctx.sbuf("omdcol", [128, 1], F32)
    midcol = ctx.sbuf("midcol", [128, 1], F32)
    colr = ctx.sbuf("colr", [128, 1], F32)

    SL = {n: i for i, n in enumerate(
        ("lo", "hi", "mid", "tot", "ind", "tmp", "nov", "done", "dnew"))}

    def sl(n):
        return srow[0:1, SL[n]:SL[n] + 1]

    for ci in range(NCH):
        p.dma("gpsimd", (lambda ci=ci: lambda g: g.dma_start(
            out=wc1_sb[:, ci, :], in_=W["f1_Wc1T"][ci * 128:(ci + 1) * 128, :]))())

    def init_v(v):
        v.memset(active[:], 0.0)
        v.memset(active[:, 0:32], 1.0)
        v.memset(hist[:], 0.0)
        v.memset(hist[:, 0:32], 1.0)
        return v.memset(srow[:], 0.0)
    p.step("vector", init_v)

    wm_sc = ctx.sbuf("wm_sc", [128, 2, NCH, 1024], BF16)
    wp_sc = ctx.sbuf("wp_sc", [128, 2, 8, 1024], BF16)
    hb = ctx.sbuf("hb", [128, 8, JW], BF16)
    xcJ = ctx.sbuf("xcJ", [128, NCH, T], BF16)
    bl_t1f = ctx.sbuf("bl_t1f", [128, 2, JW], F32)
    bl_st16 = ctx.sbuf("bl_st16", [128, 2, JW], BF16)

    def load_mwp(sc, slot):
        f0 = sc * 1024
        for ci in range(NCH):
            p.ld("gpsimd", wm_sc[:, slot, ci, :],
                 W["f1_WmT"][ci * 128:(ci + 1) * 128, f0:f0 + 1024])
        for fi in range(8):
            p.ld("gpsimd", wp_sc[:, slot, fi, :],
                 W["f1_WpT"][f0 + fi * 128:f0 + (fi + 1) * 128, :])

    for st in range(3):
        src_ap = x_ap if st == 0 else carry[(st + 1) % 2]
        dst_ap = carry[st % 2]

        for ci in range(NCH):
            p.ld("gpsimd", xcJ[:, ci, :], src_ap[ci * 128:(ci + 1) * 128, :])
        if st == 0:
            load_mwp(0, 0)

        for sc in range(NSC):
            slot = sc % 2
            if sc + 1 < NSC:
                load_mwp(sc + 1, (sc + 1) % 2)
            elif st < 2:
                load_mwp(0, 0)
            for t4 in range(4):
                t0 = t4 * JW
                t4p = t4 % 2
                for fi in range(8):
                    s0 = (fi % 2) * JW
                    p.mm([(X.PB[:, s0:s0 + JW],
                           wm_sc[:, slot, ci, fi * 128:(fi + 1) * 128],
                           xcJ[:, ci, t0:t0 + JW],
                           ci == 0, ci == NCH - 1) for ci in range(NCH)])
                    acol = active[:, sc * 8 + fi:sc * 8 + fi + 1]
                    p.act(hb[:, fi, :], X.PB[:, s0:s0 + JW], AF.Relu,
                          bias=0.0, scale=acol)
                for half in range(2):
                    for cc in range(4):
                        c = half * 4 + cc
                        p.mm([(X.PA[:, cc * JW:(cc + 1) * JW],
                               wp_sc[:, slot, fi, c * 128:(c + 1) * 128],
                               hb[:, fi, :], fi == 0, fi == 7)
                              for fi in range(8)])
                    for cc in range(4):
                        c = half * 4 + cc
                        if sc == 0:
                            p.ve("tensor_copy", xnew[:, c, t0:t0 + JW],
                                 X.PA[:, cc * JW:(cc + 1) * JW])
                        else:
                            p.ve("tensor_add", xnew[:, c, t0:t0 + JW],
                                 X.PA[:, cc * JW:(cc + 1) * JW],
                                 xnew[:, c, t0:t0 + JW])

        for c in range(NCH):
            p.ve("reduce_sum", pooled[:, c:c + 1], xnew[:, c, :], axis=AX.X)
        p.ve("tensor_copy", pooled16[:], pooled[:])
        for rc in range(2):
            p.mm([(X.PB[:, rc:rc + 1],
                   wc1_sb[:, ci, rc * 128:(rc + 1) * 128],
                   pooled16[:, ci:ci + 1], ci == 0, ci == NCH - 1)
                  for ci in range(NCH)])
            p.act(rr16[:, rc:rc + 1], X.PB[:, rc:rc + 1], AF.Relu,
                  scale=1.0 / T)
        wc2a = wm_sc[:, 1, :, :].rearrange("p a b -> p (a b)")
        wc2b = wp_sc[:, 1, 0:4, :].rearrange("p a b -> p (a b)")
        for rc in range(2):
            p.ld("gpsimd", wc2a, W["f1_Wc2T"][rc * 128:(rc + 1) * 128, 0:8192])
            p.ld("gpsimd", wc2b, W["f1_Wc2T"][rc * 128:(rc + 1) * 128, 8192:FULL])
            def mm_ck(t, rc=rc, wc2a=wc2a, wc2b=wc2b):
                last = None
                for f in range(NF128):
                    src = (wc2a[:, f * 128:(f + 1) * 128] if f < 64 else
                           wc2b[:, (f - 64) * 128:(f - 63) * 128])
                    last = t.matmul(X.PC[:, f:f + 1], src,
                                    rr16[:, rc:rc + 1], start=(rc == 0),
                                    stop=(rc == 1))
                return last
            p.step("tensor", mm_ck,
                   r=[wc2a, wc2b, rr16[:, rc:rc + 1]], w=[X.PC[:, 0:NF128]])
        p.act(ck_sb[:], X.PC[:, 0:NF128], AF.Copy)

        p.ve("reduce_max", colr[:], ck_sb[:], axis=AX.X)
        p.mm([(X.PB[0:1, 0:128], colr[:], X.identf[:], True, True)])
        p.ve("reduce_max", sl("hi"), X.PB[0:1, 0:128], axis=AX.X)
        p.ve("tensor_reduce", colr[:], ck_sb[:], axis=AX.X, op=ALU.min)
        p.mm([(X.PB[0:1, 128:256], colr[:], X.identf[:], True, True)])
        p.ve("tensor_reduce", sl("lo"), X.PB[0:1, 128:256], axis=AX.X,
             op=ALU.min)
        p.ve("tensor_scalar", sl("hi"), sl("hi"), 1e-6, None, ALU.add)
        p.ve("tensor_scalar", sl("lo"), sl("lo"), 1e-6, None, ALU.subtract)

        for it in range(22):
            p.ve("tensor_add", sl("mid"), sl("lo"), sl("hi"))
            p.ve("tensor_scalar", sl("mid"), sl("mid"), 0.5, None, ALU.mult)
            p.mm([(X.PB[:, 2:3], X.onesf_128[:], sl("mid"), True, True)])
            p.act(midcol[:], X.PB[:, 2:3], AF.Copy)
            p.ve("tensor_scalar", jk96[:], ck_sb[:], midcol[:], None,
                 ALU.is_ge)
            p.ve("reduce_sum", cnt_c[:], jk96[:], axis=AX.X)
            p.ve("tensor_copy", cnt16[:], cnt_c[:])
            p.mm([(X.PB[0:1, 3:4], X.ones_col[:], cnt16[:], True, True)])
            p.act(sl("tot"), X.PB[0:1, 3:4], AF.Copy)
            p.ve("tensor_scalar", sl("ind"), sl("tot"), 4096.0, None,
                 ALU.is_ge)
            p.ve("tensor_sub", sl("tmp"), sl("mid"), sl("lo"))
            p.ve("tensor_mul", sl("tmp"), sl("tmp"), sl("ind"))
            p.ve("tensor_add", sl("lo"), sl("lo"), sl("tmp"))
            p.ve("tensor_sub", sl("tmp"), sl("mid"), sl("hi"))
            p.ve("tensor_scalar", sl("ind"), sl("ind"), -1.0, 1.0,
                 ALU.mult, ALU.add)
            p.ve("tensor_mul", sl("tmp"), sl("tmp"), sl("ind"))
            p.ve("tensor_add", sl("hi"), sl("hi"), sl("tmp"))

        p.mm([(X.PB[:, 2:3], X.onesf_128[:], sl("lo"), True, True)])
        p.act(midcol[:], X.PB[:, 2:3], AF.Copy)
        p.ve("tensor_scalar", msk[:], ck_sb[:], midcol[:], None, ALU.is_ge)
        p.ve("tensor_add", comb[:], hist[:], msk[:])
        p.ve("tensor_scalar", comb[:], comb[:], 1.0, None, ALU.min)
        p.ve("tensor_sub", d96[:], comb[:], hist[:])
        p.ve("reduce_sum", cnt_c[:], d96[:], axis=AX.X)
        p.ve("tensor_copy", cnt16[:], cnt_c[:])
        p.mm([(X.PB[0:1, 3:4], X.ones_col[:], cnt16[:], True, True)])
        p.act(sl("nov"), X.PB[0:1, 3:4], AF.Copy)
        p.ve("memset", nov_stage[:], 0.0)
        p.ve("tensor_copy", nov_stage[0:1, 0:1], sl("nov"))
        p.dma("gpsimd", lambda g: g.dma_start(out=nov_in, in_=nov_stage[:]))
        p.step("gpsimd", lambda g: g.collective_compute(
            "AllReduce", ALU.add, [[0, 1, 2, 3, 4, 5, 6, 7]],
            [nov_in.opt()], [nov_out.opt()]), inc="gp", inc_k=1)
        p.dma("gpsimd", lambda g: g.dma_start(out=sl("nov"),
                                              in_=nov_out[0:1, 0:1]))
        p.ve("tensor_scalar", sl("dnew"), sl("nov"), R_THRESH, None,
             ALU.is_lt)
        p.mm([(X.PB[:, 2:3], X.onesf_128[:], sl("done"), True, True)])
        p.act(dcol[:], X.PB[:, 2:3], AF.Copy)
        p.ve("tensor_scalar", omdcol[:], dcol[:], -1.0, 1.0,
             ALU.mult, ALU.add)
        p.ve("tensor_scalar", jk96[:], msk[:], omdcol[:], None, ALU.mult)
        p.ve("tensor_scalar", active[:], active[:], dcol[:], None, ALU.mult)
        p.ve("tensor_add", active[:], active[:], jk96[:])
        p.ve("tensor_scalar", jk96[:], comb[:], omdcol[:], None, ALU.mult)
        p.ve("tensor_scalar", hist[:], hist[:], dcol[:], None, ALU.mult)
        p.ve("tensor_add", hist[:], hist[:], jk96[:])
        p.ve("tensor_max", sl("done"), sl("done"), sl("dnew"))

        if True:
            t1f = bl_t1f
            st16 = bl_st16
            for t4 in range(4):
                t0 = t4 * JW
                for c in range(NCH):
                    par = c % 2
                    xc16 = xcJ[:, c, t0:t0 + JW]
                    xn = xnew[:, c, t0:t0 + JW]
                    if st < 2:
                        p.ve("tensor_scalar", t1f[:, par, :], xn,
                             omdcol[:], None, ALU.mult)
                        p.ve("scalar_tensor_tensor", st16[:, par, :], xc16,
                             dcol[:], t1f[:, par, :], ALU.mult, ALU.add)
                        p.ld("sync", dst_ap[c * 128:(c + 1) * 128, t0:t0 + JW],
                             st16[:, par, :])
                    else:
                        p.ve("tensor_scalar", xn, xn, omdcol[:], None, ALU.mult)
                        p.ve("scalar_tensor_tensor", xn, xc16,
                             dcol[:], xn, ALU.mult, ALU.add)
                        p.ld("sync", t1f[:, par, :],
                             x_ap[c * 128:(c + 1) * 128, t0:t0 + JW])
                        p.ve("tensor_add", xn, t1f[:, par, :], xn)
                if st == 2:
                    emit_outer_ln(ctx, xnew[:, :, t0:t0 + JW],
                                  xnew[:, :, t0:t0 + JW], y_ap, t0)



def build_block_program(block_name, wnames):
    nc = bass.Bass("TRN2", target_bir_lowering=False)
    x_ap = nc.dram_tensor("x", [D, T], F32, kind="ExternalInput").ap()
    y_ap = nc.dram_tensor("y", [D, T], F32, kind="ExternalOutput").ap()
    W = {n: nc.dram_tensor(n, WEIGHT_SHAPES[n], F32, kind="ExternalInput").ap()
         for n in wnames}
    C = {n: nc.dram_tensor(n, CONST_SHAPES[n], F32, kind="ExternalInput").ap()
         for n in CONST_SHAPES}
    plan = Plan()
    es = ExitStack()
    with es:
        ctx = Ctx(nc, plan, es)
        ctx.make_shared(C)
        if block_name == "l0_attn":
            emit_l0_attn(ctx, W, x_ap, y_ap)
        elif block_name == "l1_attn":
            emit_l1_attn(ctx, W, x_ap, y_ap)
        elif block_name == "l0_ffn":
            emit_l0_ffn(ctx, W, x_ap, y_ap)
        elif block_name == "l1_ffn":
            emit_l1_ffn(ctx, W, x_ap, y_ap)
        else:
            raise ValueError(block_name)
        plan.step("sync", lambda sy: sy.nop())
        sems = {s: es.enter_context(nc.semaphore(s)) for s in SEMS}
        block = es.enter_context(nc.Block())
        plan.emit(block, sems)
    return nc


BLOCK_WNAMES = {
    "l0_attn": ["a0_Wq", "a0_Wk", "a0_Wv", "a0_Wo", "a0_bo_row",
                "a0_kln_w", "a0_kln_b", "a0_qln_w", "a0_qln_b"],
    "l1_attn": ["a1_Wq", "a1_Wk", "a1_Wv", "a1_Wo", "a1_bo_row",
                "a1_bq", "a1_bk", "a1_bv"],
    "l0_ffn": ["f0_wiT", "f0_woTt", "f0_bi", "f0_bo_row", "f0_AT", "f0_BT",
               "f0_gwT", "f0_gb"],
    "l1_ffn": ["f1_WmT", "f1_WpT", "f1_Wc1T", "f1_Wc2T"],
}

_prog_cache = {}


EMITTERS = {}


def build_full_program():
    nc = bass.Bass("TRN2", target_bir_lowering=False)
    x_ap = nc.dram_tensor("x", [D, T], F32, kind="ExternalInput").ap()
    y_ap = nc.dram_tensor("y", [D, T], F32, kind="ExternalOutput").ap()
    W = {}
    for names in BLOCK_WNAMES.values():
        for n in names:
            W[n] = nc.dram_tensor(n, WEIGHT_SHAPES[n], F32, kind="ExternalInput").ap()
    C = {n: nc.dram_tensor(n, CONST_SHAPES[n], F32, kind="ExternalInput").ap()
         for n in CONST_SHAPES}
    xb1 = nc.dram_tensor("xbuf1", [D, T], F32, kind="Internal").ap()
    xb2 = nc.dram_tensor("xbuf2", [D, T], F32, kind="Internal").ap()
    xb3 = nc.dram_tensor("xbuf3", [D, T], F32, kind="Internal").ap()
    plan = Plan()
    es = ExitStack()
    with es:
        ctx = Ctx(nc, plan, es)
        ctx.make_shared(C)
        chain = [(emit_l0_attn, x_ap, xb1), (emit_l0_ffn, xb1, xb2),
                 (emit_l1_attn, xb2, xb3), (emit_l1_ffn, xb3, y_ap)]
        for emit, xin, xout in chain:
            plan.fence()
            with ExitStack() as bes:
                ctx.block_es = bes
                emit(ctx, W, xin, xout)
            ctx.block_es = es
        plan.step("sync", lambda sy: sy.nop())
        sems = {s: es.enter_context(nc.semaphore(s)) for s in SEMS}
        block = es.enter_context(nc.Block())
        plan.emit(block, sems)
    return nc


_consts_cache = {}


def run_block(block_name, x_fm_cores, W):
    if block_name not in _prog_cache:
        _prog_cache[block_name] = build_block_program(block_name,
                                                      BLOCK_WNAMES[block_name])
    if "c" not in _consts_cache:
        _consts_cache["c"] = make_consts()
    CONSTS = _consts_cache["c"]
    nc = _prog_cache[block_name]
    in_maps = []
    for c in range(8):
        m = {"x": x_fm_cores[c]}
        for n in BLOCK_WNAMES[block_name]:
            m[n] = W[n]
        m.update(CONSTS)
        in_maps.append(m)
    res = run_bass_kernel_spmd(nc, in_maps, core_ids=list(range(8)))
    return [res.results[c]["y"] for c in range(8)]


def kernel(**inputs):
    W = prep_weights(inputs)
    x = np.asarray(inputs["L0_attn_x"], np.float32)
    xs = [np.ascontiguousarray(x[c].T) for c in range(8)]
    if "full" not in _prog_cache:
        _prog_cache["full"] = build_full_program()
    if "c" not in _consts_cache:
        _consts_cache["c"] = make_consts()
    nc = _prog_cache["full"]
    in_maps = []
    for c in range(8):
        m = {"x": xs[c]}
        for names in BLOCK_WNAMES.values():
            for n in names:
                m[n] = W[n]
        m.update(_consts_cache["c"])
        in_maps.append(m)
    res = run_bass_kernel_spmd(nc, in_maps, core_ids=list(range(8)))
    out = np.stack([np.asarray(res.results[c]["y"]).T for c in range(8)], axis=0)
    return np.ascontiguousarray(out, np.float32)


# version 15
# speedup vs baseline: 1.1206x; 1.0547x over previous
import numpy as np
import ml_dtypes
from contextlib import ExitStack

import concourse.bass as bass
import concourse.mybir as mybir
from concourse.bass_utils import run_bass_kernel_spmd

F32 = mybir.dt.float32
BF16 = mybir.dt.bfloat16
AF = mybir.ActivationFunctionType
ALU = mybir.AluOpType
AX = mybir.AxisListType

D = 1024
T = 2048
NCH = D // 128
H = 16
NJ = T // 512
JW = 512
SCALE = 0.125
NEG = -2.0e30
EPS = 1e-5

N_GQ = 10
N_SQ = 6
SEMS = (("pe", "dv", "ac", "gp", "sy")
        + tuple(f"g{i}" for i in range(N_GQ))
        + tuple(f"s{i}" for i in range(N_SQ)))
ENG2SEM = {"tensor": "pe", "vector": "dv", "scalar": "ac",
           "gpsimd": "gp", "sync": "sy"}
_ESIZE = {F32: 4, BF16: 2}


def _is_ap(x):
    return hasattr(x, "base_partition")


def _unm(x):
    return x() if callable(x) else x


def _fp(ap):
    if callable(ap.ap):
        ap = ap.ap()
    name = _unm(ap.name)
    ent = _unm(ap.ap)
    es = _ESIZE.get(_unm(ap.dtype), 4)
    if "DRAM" in str(_unm(ap.space)):
        reach = 1
        for st, sz in ent:
            reach += (sz - 1) * abs(st)
        off = _unm(ap.offset)
        return (name, 0, 1 << 20, off * es, (off + reach) * es)
    p0 = _unm(ap.base_partition)
    p1 = p0 + ent[0][1]
    reach = 1
    for st, sz in ent[1:]:
        reach += (sz - 1) * abs(st)
    off = _unm(ap.offset)
    return (name, p0, p1, off * es, (off + reach) * es)


def _ovl(a, b):
    return a[1] < b[2] and b[1] < a[2] and a[3] < b[4] and b[3] < a[4]


def _covers(big, small):
    return (big[1] <= small[1] and big[2] >= small[2]
            and big[3] <= small[3] and big[4] >= small[4])


class Plan:

    def __init__(self):
        self.steps = {e: [] for e in ENG2SEM}
        self.cnt = {s: 0 for s in SEMS}
        self.wr = {}
        self.rd = {}
        self.bar = None
        self.gq = 0
        self.sq = 0
        self.last_pool = {}

    def _dep_rd(self, f, deps):
        for e in self.wr.get(f[0], ()):
            if _ovl(e[0], f):
                if e[2] > deps.get(e[1], 0):
                    deps[e[1]] = e[2]

    def _dep_wr(self, f, deps):
        self._dep_rd(f, deps)
        for e in self.rd.get(f[0], ()):
            if _ovl(e[0], f):
                if e[2] > deps.get(e[1], 0):
                    deps[e[1]] = e[2]

    def _note_wr(self, f, sem, cnt):
        lst = self.wr.setdefault(f[0], [])
        lst[:] = [e for e in lst if not _covers(f, e[0])]
        lst.append((f, sem, cnt))
        rl = self.rd.get(f[0])
        if rl:
            rl[:] = [e for e in rl if not _covers(f, e[0])]

    def _note_rd(self, f, sem, cnt):
        lst = self.rd.setdefault(f[0], [])
        for i, e in enumerate(lst):
            if e[1] == sem and _covers(e[0], f):
                lst[i] = (e[0], sem, max(e[2], cnt))
                return
        lst.append((f, sem, cnt))

    def _emit(self, eng, fn, rf, wf, inc, inc_k, extra=None):
        deps = {}
        for f in rf:
            self._dep_rd(f, deps)
        for f in wf:
            self._dep_wr(f, deps)
        if self.bar is not None:
            s, c = self.bar
            if c > deps.get(s, 0):
                deps[s] = c
        if extra:
            for s, c in extra.items():
                if c > deps.get(s, 0):
                    deps[s] = c
        if eng == "tensor":
            deps.pop("pe", None)
        self.cnt[inc] += inc_k
        cnt = self.cnt[inc]
        for f in wf:
            self._note_wr(f, inc, cnt)
        for f in rf:
            self._note_rd(f, inc, cnt)
        self.steps[eng].append((fn, sorted(deps.items()), (inc, inc_k)))

    def _emit_universal(self, eng, fn, inc, inc_k):
        deps = {s: c for s, c in self.cnt.items() if c > 0}
        self.cnt[inc] += inc_k
        self.wr = {}
        self.rd = {}
        self.bar = (inc, self.cnt[inc])
        self.steps[eng].append((fn, sorted(deps.items()), (inc, inc_k)))

    def step(self, eng, fn, r=None, w=None, inc=None, inc_k=1):
        if inc is None:
            inc = ENG2SEM[eng]
        if r is None and w is None:
            self._emit_universal(eng, fn, inc, inc_k)
        else:
            rf = [_fp(a) for a in (r or ()) if _is_ap(a)]
            wf = [_fp(a) for a in (w or ()) if _is_ap(a)]
            self._emit(eng, fn, rf, wf, inc, inc_k)

    def _pool_sem(self, eng):
        if eng == "gpsimd":
            s = f"g{self.gq % N_GQ}"
            self.gq += 1
        else:
            s = f"s{self.sq % N_SQ}"
            self.sq += 1
        return s

    def dma(self, eng, fn, r=None, w=None):
        sem = self._pool_sem(eng)
        extra = {}
        prev = self.last_pool.get(sem)
        if prev:
            extra[sem] = prev
        if r is None and w is None:
            deps = {s: c for s, c in self.cnt.items() if c > 0}
            self.cnt[sem] += 16
            self.wr = {}
            self.rd = {}
            self.bar = (sem, self.cnt[sem])
            self.steps[eng].append((fn, sorted(deps.items()), (sem, 16)))
        else:
            rf = [_fp(a) for a in (r or ()) if _is_ap(a)]
            wf = [_fp(a) for a in (w or ()) if _is_ap(a)]
            self._emit(eng, fn, rf, wf, sem, 16, extra=extra)
        self.last_pool[sem] = self.cnt[sem]

    def fence(self):
        self._emit_universal("sync", lambda sy: sy.nop(), "sy", 1)

    def ld(self, eng, out, in_, **kw):
        self.dma(eng, lambda e, out=out, in_=in_, kw=kw: e.dma_start(
            out=out, in_=in_, **kw), r=[in_], w=[out])

    def mm(self, ops):
        rl, wl = [], []
        for (o, l, rr, _s, _p) in ops:
            wl.append(o)
            rl += [l, rr]
        def fn(t, ops=ops):
            last = None
            for (o, l, rr, s, p) in ops:
                last = t.matmul(o, l, rr, start=s, stop=p)
            return last
        self.step("tensor", fn, r=rl, w=wl)

    def tr(self, ops, ident):
        if callable(ident.ap):
            ident = ident.ap()
        rl = [ident]
        wl = []
        for (o, i) in ops:
            wl.append(o)
            rl.append(i)
        def fn(t, ops=ops, ident=ident):
            last = None
            for (o, i) in ops:
                last = t.transpose(o, i, ident)
            return last
        self.step("tensor", fn, r=rl, w=wl)

    def act(self, out, in_, func, bias=0.0, scale=1.0, accum_out=None):
        r = [in_] + [x for x in (bias, scale) if _is_ap(x)]
        w = [out] + ([accum_out] if accum_out is not None else [])
        if accum_out is not None:
            self.step("scalar", lambda s, o=out, i=in_, f=func, b=bias, sc=scale,
                      a=accum_out: s.activation(o, i, f, bias=b, scale=sc,
                                                accum_out=a), r=r, w=w)
        else:
            self.step("scalar", lambda s, o=out, i=in_, f=func, b=bias, sc=scale:
                      s.activation(o, i, f, bias=b, scale=sc), r=r, w=w)

    def ve(self, name, out, *ins, eng="vector", **kw):
        r = [x for x in ins if _is_ap(x)] + [x for x in kw.values() if _is_ap(x)]
        def fn(v, name=name, out=out, ins=ins, kw=kw):
            return getattr(v, name)(out, *ins, **kw)
        self.step(eng, fn, r=r, w=[out])

    def emit(self, block, sems):
        def runner(eng_name):
            def run(eng):
                hi = {s: 0 for s in SEMS}
                for fn, wvals, inc in self.steps[eng_name]:
                    for s, v in wvals:
                        if v > hi[s]:
                            eng.wait_ge(sems[s], v)
                            hi[s] = v
                    inst = fn(eng)
                    name, k = inc
                    assert inst is not None, f"step incrementing {name} returned None"
                    inst.then_inc(sems[name], k)
            return run
        block.tensor(runner("tensor"))
        block.vector(runner("vector"))
        block.scalar(runner("scalar"))
        block.gpsimd(runner("gpsimd"))
        block.sync(runner("sync"))



def prep_weights(inp):
    f32c = lambda a: np.ascontiguousarray(np.asarray(a), dtype=np.float32)
    w = {}
    for n in ("Wq", "Wk", "Wv", "Wo"):
        w["a0_" + n] = f32c(inp[f"L0_attn_{n}"])
    w["a0_bo_row"] = f32c(np.asarray(inp["L0_attn_bo"]).reshape(1, D))
    for n in ("kln_w", "kln_b", "qln_w", "qln_b"):
        w["a0_" + n] = f32c(np.tile(np.asarray(inp[f"L0_attn_{n}"]), 2).reshape(128, 1))
    w["f0_wiT"] = f32c(np.asarray(inp["L0_ffn_wi"]).T)
    w["f0_woT"] = f32c(np.asarray(inp["L0_ffn_wo"]).T)
    w["f0_bi"] = f32c(np.asarray(inp["L0_ffn_bi"]).reshape(1, 4096))
    w["f0_bo_row"] = f32c(np.asarray(inp["L0_ffn_bo"]).reshape(1, D))
    _la = np.asarray(inp["L0_ffn_lora_A"])
    w["f0_AT"] = f32c(_la.transpose(2, 0, 1).reshape(NCH, 128, 8, 16)
                      .transpose(1, 2, 0, 3))
    _wo = np.asarray(inp["L0_ffn_wo"]).T
    w["f0_woTt"] = f32c(_wo.reshape(32, 128, D).transpose(1, 0, 2))
    w["f0_BT"] = f32c(np.asarray(inp["L0_ffn_lora_B"]).transpose(0, 2, 1))
    w["f0_gwT"] = f32c(np.asarray(inp["L0_ffn_gate_w"]).T)
    w["f0_gb"] = f32c(np.asarray(inp["L0_ffn_gate_b"]).reshape(1, 8))
    for n in ("Wq", "Wk", "Wv", "Wo"):
        w["a1_" + n] = f32c(np.asarray(inp[f"L1_attn_{n}"]).T)
    for n in ("bq", "bk", "bv"):
        w["a1_" + n] = f32c(np.asarray(inp[f"L1_attn_{n}"]).reshape(1, D))
    w["a1_bo_row"] = f32c(np.asarray(inp["L1_attn_bo"]).reshape(1, D))
    w["f1_WmT"] = f32c(np.asarray(inp["L1_ffn_Wm"]).T)
    w["f1_WpT"] = f32c(np.asarray(inp["L1_ffn_Wp"]).T)
    w["f1_Wc1T"] = f32c(np.asarray(inp["L1_ffn_Wc1"]).T)
    w["f1_Wc2T"] = f32c(np.asarray(inp["L1_ffn_Wc2"]).T)
    return w


def make_consts():
    c = {}
    p = np.arange(128)[:, None]
    f512 = np.arange(JW)[None, :]
    c["c_ident"] = np.eye(128, dtype=np.float32)
    m = np.zeros((128, 4, JW), np.float32)
    for o in range(4):
        m[:, o, :] = np.where(p + 128 * o - f512 >= 1, NEG, 0.0)
    c["c_masks"] = np.ascontiguousarray(m.reshape(128, 4 * JW))
    s2 = np.zeros((2, 128), np.float32)
    s2[0, 0:64] = 1.0
    s2[1, 64:128] = 1.0
    c["c_sel2"] = s2
    o2 = np.zeros((128, 2), np.float32)
    o2[0:64, 0] = 1.0
    o2[64:128, 1] = 1.0
    c["c_ones2"] = o2
    c["c_ones_col"] = np.ones((128, 1), np.float32)
    dg = np.where(np.arange(128)[None, :] > np.arange(128)[:, None], NEG, 0.0)
    c["c_diag"] = np.ascontiguousarray(dg.astype(np.float32))
    c["c_ones_row"] = np.ones((1, JW), np.float32)
    c["c_eps"] = np.full((128, 1), EPS, np.float32)
    return c


CONST_SHAPES = {
    "c_ident": [128, 128], "c_masks": [128, 4 * JW], "c_sel2": [2, 128],
    "c_ones2": [128, 2], "c_ones_col": [128, 1], "c_ones_row": [1, JW],
    "c_eps": [128, 1], "c_diag": [128, 128],
}


WEIGHT_SHAPES = {
    "a0_Wq": [D, D], "a0_Wk": [D, D], "a0_Wv": [D, D], "a0_Wo": [D, D],
    "a0_bo_row": [1, D],
    "a0_kln_w": [128, 1], "a0_kln_b": [128, 1],
    "a0_qln_w": [128, 1], "a0_qln_b": [128, 1],
    "f0_wiT": [D, 4096], "f0_woT": [4096, D], "f0_bi": [1, 4096],
    "f0_bo_row": [1, D], "f0_AT": [128, 8, NCH, 16], "f0_BT": [8, 16, 4096],
    "f0_woTt": [128, 32, D],
    "f0_gwT": [D, 8], "f0_gb": [1, 8],
    "a1_Wq": [D, D], "a1_Wk": [D, D], "a1_Wv": [D, D], "a1_Wo": [D, D],
    "a1_bq": [1, D], "a1_bk": [1, D], "a1_bv": [1, D],
    "a1_bo_row": [1, D],
    "f1_WmT": [D, 12288], "f1_WpT": [12288, D], "f1_Wc1T": [D, 256],
    "f1_Wc2T": [256, 12288],
}


class Ctx:
    def __init__(self, nc, plan, es):
        self.nc, self.plan, self.es = nc, plan, es
        self.block_es = es
        self.uid = 0

    def nm(self, base):
        self.uid += 1
        return f"{base}{self.uid}"

    def sbuf(self, base, shape, dt, es=None):
        return (es or self.block_es).enter_context(
            self.nc.sbuf_tensor(self.nm(base), shape, dt))

    def psum(self, base, shape, dt=F32):
        return self.es.enter_context(self.nc.psum_tensor(self.nm(base), shape, dt))

    def make_shared(self, C):
        p = self.plan
        self.ones_col = self.sbuf("ones_col", [128, 1], BF16)
        self.ones_row16 = self.sbuf("ones_row16", [1, 128], BF16)
        self.onesf_row = self.sbuf("onesf_row", [1, JW], F32)
        self.onesf_128 = self.sbuf("onesf_128", [1, 128], F32)
        self.onesf_col = self.sbuf("onesf_col", [128, 1], F32)
        self.ones2 = self.sbuf("ones2", [128, 2], BF16)
        self.sel2 = self.sbuf("sel2", [2, 128], BF16)
        self.identf = self.sbuf("identf", [128, 128], F32)
        self.ident16 = self.sbuf("ident16", [128, 128], BF16)
        self.C = C
        self.eps_col = self.sbuf("eps_col", [128, 1], F32)
        self.nc.const_aps.register(F32, EPS, self.eps_col[:])

        self.PA = self.psum("PA", [128, 2048])
        self.PB = self.psum("PB", [128, 1024])
        self.PC = self.psum("PC", [128, 512])
        self.PT = self.psum("PTps", [128, 1024], BF16)

        self.row_a = self.sbuf("row_a", [2, JW], F32)
        self.row_b = self.sbuf("row_b", [2, JW], F32)
        self.row_c = self.sbuf("row_c", [2, JW], F32)
        self.abf = self.sbuf("abf", [128, JW], F32)
        self.cbf = self.sbuf("cbf", [128, JW], F32)
        self.sqf = self.sbuf("sqf", [128, JW], F32)

        for sb, cn in ((self.ones_col, "c_ones_col"), (self.ones2, "c_ones2"),
                       (self.sel2, "c_sel2"), (self.ident16, "c_ident")):
            p.ld("gpsimd", sb[:], C[cn])
        p.ld("gpsimd", self.ones_row16[:], C["c_ones_row"][0:1, 0:128])
        p.ld("gpsimd", self.onesf_row[:], C["c_ones_row"])
        p.ld("gpsimd", self.onesf_128[:], C["c_ones_row"][0:1, 0:128])
        p.ld("gpsimd", self.onesf_col[:], C["c_ones_col"])
        p.ld("gpsimd", self.identf[:], C["c_ident"])
        p.ld("gpsimd", self.eps_col[:], C["c_eps"])


def load_chunked(ctx, sb, hbm, n_chunks):
    p = ctx.plan
    for c in range(n_chunks):
        p.ld("gpsimd", sb[:, c, :], hbm[c * 128:(c + 1) * 128, :])


LN_STAGE = 99


def emit_outer_ln(ctx, rj, yj, y_ap, j0):
    p, X = ctx.plan, ctx
    for c in range(NCH):
        p.act(X.sqf[:, :], rj[:, c, :], AF.Square)
        def mm_st(t, c=c):
            t.matmul(X.PB[0:1, 0:JW], X.onesf_col[:], rj[:, c, :],
                     start=(c == 0), stop=(c == NCH - 1))
            return t.matmul(X.PB[0:1, JW:2 * JW], X.onesf_col[:], X.sqf[:, :],
                            start=(c == 0), stop=(c == NCH - 1))
        p.step("tensor", mm_st,
               r=[X.onesf_col[:], rj[:, c, :], X.sqf[:, :]],
               w=[X.PB[0:1, 0:2 * JW]])
    p.ve("tensor_scalar", X.row_b[0:1, :], X.PB[0:1, 0:JW],
         1.0 / D, None, ALU.mult)
    p.act(X.row_a[0:1, :], X.row_b[0:1, :], AF.Square)
    p.ve("scalar_tensor_tensor", X.row_c[0:1, :], X.PB[0:1, JW:2 * JW],
         1.0 / D, X.row_a[0:1, :], ALU.mult, ALU.subtract)
    p.act(X.row_c[0:1, :], X.row_c[0:1, :], AF.Ln, bias=EPS)
    p.act(X.row_a[0:1, :], X.row_c[0:1, :], AF.Exp, scale=-0.5)
    p.ve("scalar_tensor_tensor", X.row_b[0:1, :], X.row_b[0:1, :],
         -1.0, X.row_a[0:1, :], ALU.mult, ALU.mult)

    p.mm([(X.PC[:, :], X.onesf_128[:], X.row_a[0:1, :], True, True),
          (X.PB[:, 0:JW], X.onesf_128[:], X.row_b[0:1, :], True, True)])
    p.act(X.abf[:], X.PC[:, :], AF.Copy)
    p.act(X.cbf[:], X.PB[:, 0:JW], AF.Copy)
    for c in range(NCH):
        p.ve("tensor_mul", yj[:, c, :], rj[:, c, :], X.abf[:])
        p.ve("tensor_add", yj[:, c, :], yj[:, c, :], X.cbf[:])
        p.ld("sync", y_ap[c * 128:(c + 1) * 128, j0:j0 + JW], yj[:, c, :])



def emit_l0_attn(ctx, W, x_ap, y_ap):
    p, X = ctx.plan, ctx
    diagm = ctx.sbuf("diagm", [128, 128], F32)
    p.ld("gpsimd", diagm[:], ctx.C["c_diag"])
    qb = ctx.sbuf("qb", [128, NCH, T], BF16)
    kb = ctx.sbuf("kb", [128, NCH, T], BF16)
    vt = ctx.sbuf("vt", [128, T // 128, D], BF16)
    wo_j = ctx.sbuf("wo_j", [128, 2, NCH, 128], BF16)
    bo_row = ctx.sbuf("bo_row", [1, D], F32)
    lnw = {n: ctx.sbuf(n, [128, 1], F32) for n in ("klw", "klb", "qlw", "qlb")}
    sq = ctx.sbuf("sqs", [128, JW], BF16)
    t1 = ctx.sbuf("t1", [128, JW], BF16)
    ab16 = ctx.sbuf("ab16", [128, JW], BF16)
    cb16 = ctx.sbuf("cb16", [128, JW], BF16)
    row16a = ctx.sbuf("row16a", [2, JW], BF16)
    row16c = ctx.sbuf("row16c", [2, JW], BF16)

    with ExitStack() as es1:
        xb = ctx.sbuf("xb", [128, NCH, T], BF16, es1)
        wcol = ctx.sbuf("wcol", [128, 2, NCH, 128], BF16, es1)
        stage = ctx.sbuf("stage", [128, 2, JW], BF16, es1)

        load_chunked(ctx, xb, x_ap, NCH)
        p.ld("gpsimd", bo_row[:], W["a0_bo_row"])
        for sb_n, w_n in (("klw", "a0_kln_w"), ("klb", "a0_kln_b"),
                          ("qlw", "a0_qln_w"), ("qlb", "a0_qln_b")):
            p.ld("gpsimd", lnw[sb_n][:], W[w_n])

        qkv_ps = [X.PC[:, 0:JW], X.PB[:, 0:JW]]
        for wi, (w_n, kind) in enumerate((("a0_Wq", "q"), ("a0_Wk", "k"),
                                          ("a0_Wv", "v"))):
            for cout in range(NCH):
                slot = (wi * NCH + cout) % 2
                for ci in range(NCH):
                    p.ld("gpsimd", wcol[:, slot, ci, :],
                         W[w_n][ci * 128:(ci + 1) * 128,
                                cout * 128:(cout + 1) * 128])
                for jn in range(T // JW):
                    n0 = jn * JW
                    ps = qkv_ps[jn % 2]
                    p.mm([(ps, wcol[:, slot, ci, :], xb[:, ci, n0:n0 + JW],
                           ci == 0, ci == NCH - 1) for ci in range(NCH)])
                    if kind in ("q", "k"):
                        out_sb = qb if kind == "q" else kb
                        p.act(out_sb[:, cout, n0:n0 + JW], ps, AF.Copy)
                    else:
                        stg = stage[:, jn % 2, :]
                        p.act(stg, ps, AF.Copy)
                        p.tr([(X.PT[:, i * 128:(i + 1) * 128],
                               stg[:, i * 128:(i + 1) * 128])
                              for i in range(4)], X.ident16)
                        p.ve("tensor_copy",
                             vt[:, jn * 4:jn * 4 + 4,
                                cout * 128:(cout + 1) * 128],
                             X.PT[:, 0:512].rearrange("p (a b) -> p a b", a=4))

    def qkln(src, wl, bl):
        for c in range(NCH):
            for n in range(4):
                n0 = n * JW
                sl = src[:, c, n0:n0 + JW]
                p.act(sq[:, :], sl, AF.Square)
                p.mm([(X.PB[0:2, 0:JW], X.ones2[:], sl, True, True),
                      (X.PC[0:2, :], X.ones2[:], sq[:, :], True, True)])
                p.ve("tensor_scalar", X.row_b[:, :], X.PB[0:2, 0:JW],
                     1.0 / 64, None, ALU.mult)
                p.act(X.row_a[:, :], X.row_b[:, :], AF.Square)
                p.ve("scalar_tensor_tensor", X.row_c[:, :], X.PC[0:2, :],
                     1.0 / 64, X.row_a[:, :], ALU.mult, ALU.subtract)
                p.act(X.row_c[:, :], X.row_c[:, :], AF.Ln, bias=EPS)
                p.act(X.row_a[:, :], X.row_c[:, :], AF.Exp, scale=-0.5)
                p.ve("scalar_tensor_tensor", X.row_b[:, :], X.row_b[:, :],
                     -1.0, X.row_a[:, :], ALU.mult, ALU.mult)
                p.ve("tensor_copy", row16a[0:2, :], X.row_a[:, :])
                p.ve("tensor_copy", row16c[0:2, :], X.row_b[:, :])
                p.mm([(X.PB[:, 0:JW], X.sel2[:], row16a[0:2, :], True, True),
                      (X.PC[:, :], X.sel2[:], row16c[0:2, :], True, True)])
                p.act(ab16[:, :], X.PB[:, 0:JW], AF.Copy)
                p.act(cb16[:, :], X.PC[:, :], AF.Copy)
                p.ve("tensor_mul", t1[:, :], sl, ab16[:, :])
                p.ve("tensor_add", t1[:, :], t1[:, :], cb16[:, :])
                p.ve("tensor_scalar", sl, t1[:, :], wl[:], bl[:],
                     ALU.mult, ALU.add)
    qkln(qb, lnw["qlw"], lnw["qlb"])
    qkln(kb, lnw["klw"], lnw["klb"])

    p.fence()
    with ExitStack() as es2:
        xj = ctx.sbuf("xj", [128, NCH, JW], F32, es2)
        s0b = ctx.sbuf("s0b", [128, 2, T], F32, es2)
        pt = ctx.sbuf("pt", [128, T], BF16, es2)
        ptT = ctx.sbuf("ptT", [128, T // 128, JW], BF16, es2)
        negm = ctx.sbuf("negm", [128, 1], F32, es2)
        bcol = ctx.sbuf("bcol", [128, 1], F32, es2)
        zcol = ctx.sbuf("zcol", [128, 4], F32, es2)
        rrow = X.row_c[0:1, :]
        rb64 = ctx.sbuf("rb64", [64, JW], F32, es2)
        ao = ctx.sbuf("ao", [128, NCH, JW], BF16, es2)

        for J in range(NJ):
            j0 = J * JW
            for c in range(NCH):
                p.ld("sync", xj[:, c, :],
                     x_ap[c * 128:(c + 1) * 128, j0:j0 + JW])
            for h in range(H):
                hr0 = (h % 2) * 64
                hc = h // 2
                for jj in range(4):
                    qt0 = j0 + jj * 128
                    ktw = qt0 + 128
                    ops = []
                    n0 = 0
                    while n0 < ktw:
                        nn = min(JW, ktw - n0)
                        ops.append((X.PA[:, n0:n0 + nn],
                                    qb[hr0:hr0 + 64, hc, qt0:qt0 + 128],
                                    kb[hr0:hr0 + 64, hc, n0:n0 + nn],
                                    True, True))
                        n0 += nn
                    p.mm(ops)
                    sj = s0b[:, jj % 2, 0:ktw]
                    p.act(sj, X.PA[:, 0:ktw], AF.Copy)
                    p.ve("tensor_add", sj[:, ktw - 128:ktw],
                         sj[:, ktw - 128:ktw], diagm[:])
                    p.ve("reduce_max", negm[:], sj, axis=AX.X,
                         negate=True)
                    p.ve("tensor_scalar", bcol[:], negm[:], SCALE, None,
                         ALU.mult)
                    p.act(pt[:, 0:ktw], sj, AF.Exp, bias=bcol[:],
                          scale=SCALE, accum_out=zcol[:, jj:jj + 1])
                    nch_k = ktw // 128
                    for g0 in range(0, nch_k, 8):
                        gn = min(8, nch_k - g0)
                        p.tr([(X.PT[:, i * 128:(i + 1) * 128],
                               pt[:, (g0 + i) * 128:(g0 + i + 1) * 128])
                              for i in range(gn)], X.ident16)
                        p.act(ptT[:, g0:g0 + gn, jj * 128:(jj + 1) * 128],
                              X.PT[:, 0:gn * 128].rearrange(
                                  "p (a b) -> p a b", b=128), AF.Copy)

                p.mm([(X.PB[0:1, JW + jj * 128:JW + (jj + 1) * 128],
                       zcol[:, jj:jj + 1], X.identf[:], True, True)
                      for jj in range(4)])
                p.act(rrow[:], X.PB[0:1, JW:2 * JW], AF.Ln)
                p.act(rrow[:], rrow[:], AF.Exp, scale=-1.0)
                p.mm([(X.PB[0:64, 0:JW], X.onesf_128[0:1, 0:64], rrow[:],
                       True, True)])
                p.act(rb64[:], X.PB[0:64, 0:JW], AF.Copy)
                ops = []
                for c in range(4 * J + 4):
                    k = c - 4 * J
                    lo = max(0, k) * 128
                    ops.append((X.PC[0:64, lo:JW],
                                vt[:, c, h * 64:(h + 1) * 64],
                                ptT[:, c, lo:JW], c == 0, c == 4 * J + 3))
                p.mm(ops)
                p.ve("tensor_mul",
                     ao[(h % 2) * 64:(h % 2) * 64 + 64, h // 2, :],
                     X.PC[0:64, :], rb64[:])

            for cout in range(NCH):
                ws = cout % 2
                for ci in range(NCH):
                    p.ld("gpsimd", wo_j[:, ws, ci, :],
                         W["a0_Wo"][ci * 128:(ci + 1) * 128,
                                    cout * 128:(cout + 1) * 128])
                p.mm([(X.PC[:, :], wo_j[:, ws, ci, :],
                       ao[:, ci, :], ci == 0, False) for ci in range(NCH)]
                     + [(X.PC[:, :], bo_row[0:1, cout * 128:(cout + 1) * 128],
                         X.onesf_row[:], False, True)])
                p.ve("tensor_add", xj[:, cout, :], X.PC[:, :], xj[:, cout, :])
            emit_outer_ln(ctx, xj, xj, y_ap, j0)



L1A_STAGE = 99
L1A_NJ = NJ
L1A_DBG = False


def emit_l1_attn(ctx, W, x_ap, y_ap):
    if L1A_DBG:
        dbg = {n: ctx.nc.dram_tensor("dbg_" + n, shp, F32,
                                     kind="ExternalOutput").ap()
               for n, shp in (("scores", [128, T]), ("m8", [128, 8]),
                              ("u", [128, T]), ("pt", [128, T]),
                              ("zcol", [128, 4]), ("ao", [64, JW]),
                              ("q", [64, 128]), ("k", [64, T]),
                              ("rj", [128, NCH * JW]),
                              ("aof", [128, NCH * JW]),
                              ("xj0", [128, NCH * JW]))}
    p, X = ctx.plan, ctx
    qb = ctx.sbuf("qb", [128, NCH, T], BF16)
    kb = ctx.sbuf("kb", [128, NCH, T], BF16)
    vt = ctx.sbuf("vt", [128, T // 128, D], BF16)
    wo_j = ctx.sbuf("wo_j", [128, 2, NCH, 128], BF16)
    bo_row = ctx.sbuf("bo_row", [1, D], F32)

    with ExitStack() as es1:
        xb = ctx.sbuf("xb", [128, NCH, T], BF16, es1)
        bqkv = [ctx.sbuf("bqkv", [1, D], F32, es1) for _ in range(3)]
        wcol = ctx.sbuf("wcol", [128, 2, NCH, 128], BF16, es1)
        stage = ctx.sbuf("stage", [128, 2, JW], BF16, es1)

        load_chunked(ctx, xb, x_ap, NCH)
        p.ld("gpsimd", bo_row[:], W["a1_bo_row"])
        for i, n in enumerate(("a1_bq", "a1_bk", "a1_bv")):
            p.ld("gpsimd", bqkv[i][:], W[n])

        qkv_ps = [X.PC[:, 0:JW], X.PB[:, 0:JW]]
        for wi, (w_n, kind) in enumerate((("a1_Wq", "q"), ("a1_Wk", "k"),
                                          ("a1_Wv", "v"))):
            for cout in range(NCH):
                slot = (wi * NCH + cout) % 2
                for ci in range(NCH):
                    p.ld("gpsimd", wcol[:, slot, ci, :],
                         W[w_n][ci * 128:(ci + 1) * 128,
                                cout * 128:(cout + 1) * 128])
                for jn in range(T // JW):
                    n0 = jn * JW
                    ps = qkv_ps[jn % 2]
                    p.mm([(ps, wcol[:, slot, ci, :], xb[:, ci, n0:n0 + JW],
                           ci == 0, False) for ci in range(NCH)]
                         + [(ps, bqkv[wi][0:1, cout * 128:(cout + 1) * 128],
                             X.onesf_row[:], False, True)])
                    if kind in ("q", "k"):
                        out_sb = qb if kind == "q" else kb
                        p.act(out_sb[:, cout, n0:n0 + JW], ps, AF.Copy)
                    else:
                        stg = stage[:, jn % 2, :]
                        p.act(stg, ps, AF.Copy)
                        p.tr([(X.PT[:, i * 128:(i + 1) * 128],
                               stg[:, i * 128:(i + 1) * 128])
                              for i in range(4)], X.ident16)
                        p.ve("tensor_copy",
                             vt[:, jn * 4:jn * 4 + 4,
                                cout * 128:(cout + 1) * 128],
                             X.PT[:, 0:512].rearrange("p (a b) -> p a b", a=4))

    p.fence()
    with ExitStack() as es2:
        xj = ctx.sbuf("xj", [128, NCH, JW], F32, es2)
        s_sb = ctx.sbuf("s_sb", [128, 4, T], F32, es2)
        scr = ctx.sbuf("scr", [128, 1024], F32, es2)
        cand = ctx.sbuf("cand", [128, 4, 256], F32, es2)
        m8 = ctx.sbuf("m8", [128, 4, 8], F32, es2)
        taucol = ctx.sbuf("taucol", [128, 4], F32, es2)
        pt = ctx.sbuf("pt", [128, T], BF16, es2)
        ptT = ctx.sbuf("ptT", [128, T // 128, JW], BF16, es2)
        zcol = ctx.sbuf("zcol", [128, 4], F32, es2)
        rrow = X.row_c[0:1, :]
        rb64 = ctx.sbuf("rb64", [64, JW], F32, es2)
        ao = ctx.sbuf("ao", [128, NCH, JW], BF16, es2)

        for J in range(L1A_NJ):
            j0 = J * JW
            for c in range(NCH):
                p.ld("sync", xj[:, c, :],
                     x_ap[c * 128:(c + 1) * 128, j0:j0 + JW])
            def tail_T(hp, jj):
                for g in range(2):
                    p.tr([(X.PT[:, i * 128:(i + 1) * 128],
                           pt[:, (g * 8 + i) * 128:(g * 8 + i + 1) * 128])
                          for i in range(8)], X.ident16)
                    p.act(ptT[:, g * 8:g * 8 + 8, jj * 128:(jj + 1) * 128],
                          X.PT[:].rearrange("p (a b) -> p a b", a=8),
                          AF.Copy)

            def tail_PV(hp):
                p.mm([(X.PB[0:1, JW + jj * 128:JW + (jj + 1) * 128],
                       zcol[:, jj:jj + 1], X.identf[:], True, True)
                      for jj in range(4)])
                p.act(rrow[:], X.PB[0:1, JW:2 * JW], AF.Ln)
                p.act(rrow[:], rrow[:], AF.Exp, scale=-1.0)
                p.mm([(X.PB[0:64, 0:JW], X.onesf_128[0:1, 0:64], rrow[:],
                       True, True)])
                p.act(rb64[:], X.PB[0:64, 0:JW], AF.Copy)
                p.mm([(X.PC[0:64, :], vt[:, c, hp * 64:(hp + 1) * 64],
                       ptT[:, c, :], c == 0, c == T // 128 - 1)
                      for c in range(T // 128)])
                p.ve("tensor_mul",
                     ao[(hp % 2) * 64:(hp % 2) * 64 + 64, hp // 2, :],
                     X.PC[0:64, :], rb64[:])

            NHH = H if L1A_STAGE >= 2 else 0
            for h in range(NHH + 1):
                hr0 = (h % 2) * 64
                hc = h // 2
                for jj in range(4 if h < NHH else 0):
                    qt0 = j0 + jj * 128
                    p.mm([(X.PA[:, n * JW:(n + 1) * JW],
                           qb[hr0:hr0 + 64, hc, qt0:qt0 + 128],
                           kb[hr0:hr0 + 64, hc, n * JW:(n + 1) * JW],
                           True, True) for n in range(T // JW)])
                    sj = s_sb[:, jj, :]
                    p.act(sj, X.PA[:, :], AF.Copy)
                    cnd = cand[:, jj, :]
                    def tk_a(v, sj=sj, cnd=cnd):
                        last = None
                        for c in range(16):
                            last = v.max(out=cnd[:, c * 8:(c + 1) * 8],
                                         in_=sj[:, c * 128:(c + 1) * 128])
                        return last
                    p.step("vector", tk_a, r=[sj], w=[cnd])
                    for half in range(2):
                        def tk_b(v, sj=sj, cnd=cnd, half=half):
                            last = None
                            for c8 in range(8):
                                c = half * 8 + c8
                                last = v.match_replace(
                                    out=scr[:, c8 * 128:(c8 + 1) * 128],
                                    in_to_replace=cnd[:, c * 8:(c + 1) * 8],
                                    in_values=sj[:, c * 128:(c + 1) * 128],
                                    imm_value=-1e30)
                            return last
                        p.step("vector", tk_b, r=[sj, cnd], w=[scr[:]])
                        def tk_c(v, cnd=cnd, half=half):
                            last = None
                            for c8 in range(8):
                                c = half * 8 + c8
                                last = v.max(
                                    out=cnd[:, 128 + c * 8:136 + c * 8],
                                    in_=scr[:, c8 * 128:(c8 + 1) * 128])
                            return last
                        p.step("vector", tk_c, r=[scr[:]], w=[cnd])
                if h > 0:
                    tail_PV(h - 1)
                if h >= NHH:
                    break
                def dwa(jj):
                    return scr[:, jj * 256:(jj + 1) * 256]
                def dwb(jj):
                    return cand[:, jj, :]
                def mstep(fn, r, w):
                    p.step("vector", fn, r=r, w=w)
                mstep(lambda v: [v.max(out=m8[:, jj, :], in_=cand[:, jj, :])
                                 for jj in range(4)][-1],
                      r=[cand[:]], w=[m8[:]])
                mstep(lambda v: [v.match_replace(
                    out=dwa(jj), in_to_replace=m8[:, jj, :],
                    in_values=cand[:, jj, :], imm_value=-1e30)
                    for jj in range(4)][-1],
                      r=[cand[:], m8[:]], w=[scr[:]])
                mstep(lambda v: [v.max(out=m8[:, jj, :], in_=dwa(jj))
                                 for jj in range(4)][-1],
                      r=[scr[:]], w=[m8[:]])
                mstep(lambda v: [v.match_replace(
                    out=dwb(jj), in_to_replace=m8[:, jj, :],
                    in_values=dwa(jj), imm_value=-1e30)
                    for jj in range(4)][-1],
                      r=[scr[:], m8[:]], w=[cand[:]])
                mstep(lambda v: [v.max(out=m8[:, jj, :], in_=dwb(jj))
                                 for jj in range(4)][-1],
                      r=[cand[:]], w=[m8[:]])
                mstep(lambda v: [v.match_replace(
                    out=dwa(jj), in_to_replace=m8[:, jj, :],
                    in_values=dwb(jj), imm_value=-1e30)
                    for jj in range(4)][-1],
                      r=[cand[:], m8[:]], w=[scr[:]])
                mstep(lambda v: [v.max(out=m8[:, jj, :], in_=dwa(jj))
                                 for jj in range(4)][-1],
                      r=[scr[:]], w=[m8[:]])
                mstep(lambda v: [v.tensor_scalar(
                    taucol[:, jj:jj + 1], m8[:, jj, 7:8], -SCALE, None,
                    ALU.mult) for jj in range(4)][-1],
                      r=[m8[:]], w=[taucol[:]])
                for jj in range(4):
                    p.act(s_sb[:, jj, :], s_sb[:, jj, :], AF.Exp,
                          bias=taucol[:, jj:jj + 1], scale=SCALE)
                for jj in range(4):
                    sj = s_sb[:, jj, :]
                    p.ve("scalar_tensor_tensor", pt[:, :], sj, 0.99995, sj,
                         ALU.is_ge, ALU.mult)
                    p.ve("reduce_sum", zcol[:, jj:jj + 1], pt[:, :],
                         axis=AX.X)
                    tail_T(h, jj)

            for cout in range(NCH if L1A_STAGE >= 6 else 0):
                ws = cout % 2
                for ci in range(NCH):
                    p.ld("gpsimd", wo_j[:, ws, ci, :],
                         W["a1_Wo"][ci * 128:(ci + 1) * 128,
                                    cout * 128:(cout + 1) * 128])
                p.mm([(X.PC[:, :], wo_j[:, ws, ci, :],
                       ao[:, ci, :], ci == 0, False) for ci in range(NCH)]
                     + [(X.PC[:, :], bo_row[0:1, cout * 128:(cout + 1) * 128],
                         X.onesf_row[:], False, True)])
                p.ve("tensor_add", xj[:, cout, :], X.PC[:, :], xj[:, cout, :])
            if L1A_DBG and J == 0:
                p.ld("gpsimd", dbg["aof"],
                     ao[:].rearrange("p a b -> p (a b)"))
                p.ld("gpsimd", dbg["rj"],
                     xj[:].rearrange("p a b -> p (a b)"))
            if L1A_STAGE >= 7:
                emit_outer_ln(ctx, xj, xj, y_ap, j0)



def emit_l0_ffn(ctx, W, x_ap, y_ap):
    p, X = ctx.plan, ctx
    DFF = 4096
    NFC = DFF // JW
    gw_sb = ctx.sbuf("gw_sb", [128, NCH, 8], BF16)
    gb_row = ctx.sbuf("gb_row", [1, 8], F32)
    bi_row = ctx.sbuf("bi_row", [1, DFF], F32)
    bo_row = ctx.sbuf("bo_row", [1, D], F32)
    at_sb = ctx.sbuf("at_sb", [128, 8, NCH, 16], BF16)
    u16 = ctx.sbuf("u16", [16, 8, JW], BF16)

    for ci in range(NCH):
        p.ld("gpsimd", gw_sb[:, ci, :], W["f0_gwT"][ci * 128:(ci + 1) * 128, :])
    p.ld("gpsimd", gb_row[:], W["f0_gb"])
    p.ld("gpsimd", bi_row[:], W["f0_bi"])
    p.ld("gpsimd", bo_row[:], W["f0_bo_row"])
    p.ld("gpsimd", at_sb[:], W["f0_AT"][:])

    with ExitStack() as es2:
        xbJ = ctx.sbuf("xbJ", [128, NCH, JW], BF16, es2)
        xj = ctx.sbuf("xj", [128, NCH, JW], F32, es2)
        lg = ctx.sbuf("lg", [128, 4, 8], F32, es2)
        srt = ctx.sbuf("srt", [128, 4, 8], F32, es2)
        ge = ctx.sbuf("ge", [128, 4, 8], F32, es2)
        vs = ctx.sbuf("vs", [128, 4], F32, es2)
        n1c = ctx.sbuf("n1c", [128, 1], F32, es2)
        e2c = ctx.sbuf("e2c", [128, 1], F32, es2)
        rzc = ctx.sbuf("rzc", [128, 1], F32, es2)
        zc = ctx.sbuf("zc", [128, 1], F32, es2)
        jk8 = ctx.sbuf("jk8", [128, 8], F32, es2)
        vs2row = ctx.sbuf("vs2row", [1, JW], F32, es2)
        wi_fc = ctx.sbuf("wi_fc", [128, 2, NCH, JW], BF16, es2)
        bt_fc = ctx.sbuf("bt_fc", [16, 2, 8, JW], BF16, es2)
        base_sb = ctx.sbuf("base_sb", [128, 4, JW], F32, es2)
        tmp_f = ctx.sbuf("tmp_f", [128, 2, 4, JW], F32, es2)
        rsum = ctx.sbuf("rsum", [128, 2, 4, JW], BF16, es2)
        rk = ctx.sbuf("rk", [128, 2, 4, JW], BF16, es2)
        rt_sb = ctx.sbuf("rt_sb", [128, 4 * NFC, JW], BF16, es2)
        wo_co = ctx.sbuf("wo_co", [128, 2, 4 * NFC, 128], BF16, es2)

        p.fence()

        for J in range(NJ):
            j0 = J * JW
            for c in range(NCH):
                p.ld("gpsimd", xbJ[:, c, :],
                     x_ap[c * 128:(c + 1) * 128, j0:j0 + JW])
                p.ld("sync", xj[:, c, :],
                     x_ap[c * 128:(c + 1) * 128, j0:j0 + JW])

            for ts in range(4):
                p.mm([(X.PC[:, 0:8], xbJ[:, ci, ts * 128:(ts + 1) * 128],
                       gw_sb[:, ci, :], ci == 0, False) for ci in range(NCH)]
                     + [(X.PC[:, 0:8], X.onesf_128[:], gb_row[:], False, True)])
                p.act(lg[:, ts, :], X.PC[:, 0:8], AF.Copy)
                p.step("vector", (lambda ts=ts: lambda v: v.max(
                    out=srt[:, ts, :], in_=lg[:, ts, :]))(),
                    r=[lg[:, ts, :]], w=[srt[:, ts, :]])
                p.ve("tensor_scalar", n1c[:], srt[:, ts, 0:1], -1.0, None,
                     ALU.mult)
                p.act(jk8[:], lg[:, ts, :], AF.Exp, bias=n1c[:],
                      accum_out=zc[:])
                p.act(e2c[:], srt[:, ts, 1:2], AF.Exp, bias=n1c[:])
                p.ve("reciprocal", rzc[:], zc[:])
                p.ve("scalar_tensor_tensor", vs[:, ts:ts + 1], e2c[:], 1.0,
                     rzc[:], ALU.add, ALU.mult)
                def mk_ge(v, ts=ts):
                    last = None
                    for e in range(8):
                        last = v.scalar_tensor_tensor(
                            ge[:, ts, e:e + 1], lg[:, ts, e:e + 1],
                            srt[:, ts, 1:2], vs[:, ts:ts + 1],
                            ALU.is_ge, ALU.mult)
                    return last
                p.step("vector", mk_ge,
                       r=[lg[:, ts, :], srt[:, ts, :], vs[:, ts:ts + 1]],
                       w=[ge[:, ts, :]])
            p.ve("tensor_scalar", vs[:, :], vs[:, :], 2.0, None, ALU.mult)
            p.mm([(X.PB[0:1, ts * 128:(ts + 1) * 128], vs[:, ts:ts + 1],
                   X.identf[:], True, True) for ts in range(4)])
            p.act(vs2row[:], X.PB[0:1, 0:JW], AF.Copy)

            for e in range(8):
                p.mm([(X.PB[0:16, 0:JW], at_sb[:, e, ci, :], xbJ[:, ci, :],
                       ci == 0, ci == NCH - 1) for ci in range(NCH)])
                p.act(u16[0:16, e, :], X.PB[0:16, 0:JW], AF.Copy)

            for fc in range(NFC):
                slot = fc % 2
                for ci in range(NCH):
                    p.ld("gpsimd", wi_fc[:, slot, ci, :],
                         W["f0_wiT"][ci * 128:(ci + 1) * 128,
                                     fc * JW:(fc + 1) * JW])
                for e in range(8):
                    p.ld("gpsimd", bt_fc[0:16, slot, e, :],
                         W["f0_BT"][e][:, fc * JW:(fc + 1) * JW])
                for ts in range(4):
                    p.mm([(X.PA[:, ts * JW:(ts + 1) * JW],
                           xbJ[:, ci, ts * 128:(ts + 1) * 128],
                           wi_fc[:, slot, ci, :], ci == 0, False)
                          for ci in range(NCH)]
                         + [(X.PA[:, ts * JW:(ts + 1) * JW], X.onesf_128[:],
                             bi_row[0:1, fc * JW:(fc + 1) * JW], False, True)])
                for hp in range(2):
                    p.act(base_sb[:, 2 * hp:2 * hp + 2, :].rearrange(
                        "p a b -> p (a b)"),
                        X.PA[:, hp * 1024:(hp + 1) * 1024], AF.Copy)
                for e in range(8):
                    tpar = e % 2
                    for hp in range(2):
                        p.mm([(X.PA[:, ts * JW:(ts + 1) * JW],
                               u16[0:16, e, ts * 128:(ts + 1) * 128],
                               bt_fc[0:16, slot, e, :], True, True)
                              for ts in (2 * hp, 2 * hp + 1)])
                    for hp in range(2):
                        p.ve("scalar_tensor_tensor",
                             tmp_f[:, tpar, 2 * hp:2 * hp + 2, :].rearrange(
                                 "p a b -> p (a b)"),
                             X.PA[:, hp * 1024:(hp + 1) * 1024], 1.0,
                             base_sb[:, 2 * hp:2 * hp + 2, :].rearrange(
                                 "p a b -> p (a b)"),
                             ALU.mult, ALU.add)
                    dst = rsum if e < 2 else rk
                    for hp in range(2):
                        def mk_relu(s, e=e, dst=dst, tpar=tpar, hp=hp):
                            last = None
                            for ts in (2 * hp, 2 * hp + 1):
                                last = s.activation(dst[:, tpar, ts, :],
                                                    tmp_f[:, tpar, ts, :],
                                                    AF.Relu, bias=0.0,
                                                    scale=ge[:, ts, e:e + 1])
                            return last
                        p.step("scalar", mk_relu,
                               r=[tmp_f[:, tpar, 2 * hp:2 * hp + 2], ge[:]],
                               w=[dst[:, tpar, 2 * hp:2 * hp + 2]])
                    if e >= 2:
                        for hp in range(2):
                            p.ve("tensor_tensor",
                                 rsum[:, tpar, 2 * hp:2 * hp + 2].rearrange(
                                     "p a b -> p (a b)"),
                                 rsum[:, tpar, 2 * hp:2 * hp + 2].rearrange(
                                     "p a b -> p (a b)"),
                                 rk[:, tpar, 2 * hp:2 * hp + 2].rearrange(
                                     "p a b -> p (a b)"),
                                 eng=("vector" if tpar == 0 else "gpsimd"),
                                 op=ALU.add)
                p.ve("tensor_tensor",
                     rsum[:, 0].rearrange("p a b -> p (a b)"),
                     rsum[:, 0].rearrange("p a b -> p (a b)"),
                     rsum[:, 1].rearrange("p a b -> p (a b)"),
                     op=ALU.add)
                for half in range(2):
                    ops = []
                    for i in range(8):
                        ts, fs = (half * 8 + i) // 4, (half * 8 + i) % 4
                        ops.append((X.PT[:, i * 128:(i + 1) * 128],
                                    rsum[:, 0, ts, fs * 128:(fs + 1) * 128]))
                    p.tr(ops, X.ident16)
                    for i in range(8):
                        ts, fs = (half * 8 + i) // 4, (half * 8 + i) % 4
                        p.act(rt_sb[:, fc * 4 + fs, ts * 128:(ts + 1) * 128],
                              X.PT[:, i * 128:(i + 1) * 128], AF.Copy)

            for cout in range(NCH):
                ws = cout % 2
                p.ld("gpsimd", wo_co[:, ws, :, :],
                     W["f0_woTt"][:, :, cout * 128:(cout + 1) * 128])
                p.mm([(X.PB[:, 0:JW], wo_co[:, ws, fcc, :], rt_sb[:, fcc, :],
                       fcc == 0, False) for fcc in range(4 * NFC)]
                     + [(X.PB[:, 0:JW],
                         bo_row[0:1, cout * 128:(cout + 1) * 128],
                         vs2row[:], False, True)])
                p.ve("tensor_add", xj[:, cout, :], X.PB[:, 0:JW],
                     xj[:, cout, :])
            emit_outer_ln(ctx, xj, xj, y_ap, j0)



def emit_l1_ffn(ctx, W, x_ap, y_ap):
    p, X = ctx.plan, ctx
    nc = ctx.nc
    FULL = 12288
    NF128 = FULL // 128
    NSC = 12
    R_THRESH = 0.7 * 4096.0 * 8.0

    carry = [nc.dram_tensor(ctx.nm("carry"), [D, T], BF16, kind="Internal").ap()
             for _ in range(2)]
    nov_in = nc.dram_tensor(ctx.nm("nov_in"), [1, 512], F32, kind="Internal").ap()
    nov_out = nc.dram_tensor(ctx.nm("nov_out"), [1, 512], F32, kind="Internal").ap()
    nov_stage = ctx.sbuf("nov_stage", [1, 512], F32)

    xnew = ctx.sbuf("xnew", [128, NCH, T], F32)
    active = ctx.sbuf("active", [128, NF128], F32)
    hist = ctx.sbuf("hist", [128, NF128], F32)
    msk = ctx.sbuf("msk", [128, NF128], F32)
    comb = ctx.sbuf("comb", [128, NF128], F32)
    d96 = ctx.sbuf("d96", [128, NF128], F32)
    ck_sb = ctx.sbuf("ck_sb", [128, NF128], F32)
    jk96 = ctx.sbuf("jk96", [128, NF128], F32)
    pooled = ctx.sbuf("pooled", [128, NCH], F32)
    pooled16 = ctx.sbuf("pooled16", [128, NCH], BF16)
    cnt_c = ctx.sbuf("cnt_c", [128, 1], F32)
    cnt16 = ctx.sbuf("cnt16", [128, 1], BF16)
    rr16 = ctx.sbuf("rr16", [128, 2], BF16)
    wc1_sb = ctx.sbuf("wc1_sb", [128, NCH, 256], BF16)
    srow = ctx.sbuf("srow", [1, 16], F32)
    dcol = ctx.sbuf("dcol", [128, 1], F32)
    omdcol = ctx.sbuf("omdcol", [128, 1], F32)
    midcol = ctx.sbuf("midcol", [128, 1], F32)
    colr = ctx.sbuf("colr", [128, 1], F32)

    SL = {n: i for i, n in enumerate(
        ("lo", "hi", "mid", "tot", "ind", "tmp", "nov", "done", "dnew"))}

    def sl(n):
        return srow[0:1, SL[n]:SL[n] + 1]

    for ci in range(NCH):
        p.dma("gpsimd", (lambda ci=ci: lambda g: g.dma_start(
            out=wc1_sb[:, ci, :], in_=W["f1_Wc1T"][ci * 128:(ci + 1) * 128, :]))())

    def init_v(v):
        v.memset(active[:], 0.0)
        v.memset(active[:, 0:32], 1.0)
        v.memset(hist[:], 0.0)
        v.memset(hist[:, 0:32], 1.0)
        return v.memset(srow[:], 0.0)
    p.step("vector", init_v)

    wm_sc = ctx.sbuf("wm_sc", [128, 2, NCH, 1024], BF16)
    wp_sc = ctx.sbuf("wp_sc", [128, 2, 8, 1024], BF16)
    hb = ctx.sbuf("hb", [128, 8, JW], BF16)
    xcJ = ctx.sbuf("xcJ", [128, NCH, T], BF16)
    bl_t1f = ctx.sbuf("bl_t1f", [128, 2, JW], F32)
    bl_st16 = ctx.sbuf("bl_st16", [128, 2, JW], BF16)

    def load_mwp(sc, slot):
        f0 = sc * 1024
        for ci in range(NCH):
            p.ld("gpsimd", wm_sc[:, slot, ci, :],
                 W["f1_WmT"][ci * 128:(ci + 1) * 128, f0:f0 + 1024])
        for fi in range(8):
            p.ld("gpsimd", wp_sc[:, slot, fi, :],
                 W["f1_WpT"][f0 + fi * 128:f0 + (fi + 1) * 128, :])

    for st in range(3):
        src_ap = x_ap if st == 0 else carry[(st + 1) % 2]
        dst_ap = carry[st % 2]

        for ci in range(NCH):
            p.ld("gpsimd", xcJ[:, ci, :], src_ap[ci * 128:(ci + 1) * 128, :])
        if st == 0:
            load_mwp(0, 0)

        for sc in range(NSC):
            slot = sc % 2
            if sc + 1 < NSC:
                load_mwp(sc + 1, (sc + 1) % 2)
            elif st < 2:
                load_mwp(0, 0)
            for t4 in range(4):
                t0 = t4 * JW
                t4p = t4 % 2
                for fi in range(8):
                    s0 = (fi % 2) * JW
                    p.mm([(X.PB[:, s0:s0 + JW],
                           wm_sc[:, slot, ci, fi * 128:(fi + 1) * 128],
                           xcJ[:, ci, t0:t0 + JW],
                           ci == 0, ci == NCH - 1) for ci in range(NCH)])
                    acol = active[:, sc * 8 + fi:sc * 8 + fi + 1]
                    p.act(hb[:, fi, :], X.PB[:, s0:s0 + JW], AF.Relu,
                          bias=0.0, scale=acol)
                for half in range(2):
                    for cc in range(4):
                        c = half * 4 + cc
                        p.mm([(X.PA[:, cc * JW:(cc + 1) * JW],
                               wp_sc[:, slot, fi, c * 128:(c + 1) * 128],
                               hb[:, fi, :], fi == 0, fi == 7)
                              for fi in range(8)])
                    for cc in range(4):
                        c = half * 4 + cc
                        if sc == 0:
                            p.ve("tensor_copy", xnew[:, c, t0:t0 + JW],
                                 X.PA[:, cc * JW:(cc + 1) * JW])
                        else:
                            p.ve("tensor_add", xnew[:, c, t0:t0 + JW],
                                 X.PA[:, cc * JW:(cc + 1) * JW],
                                 xnew[:, c, t0:t0 + JW])

        for c in range(NCH):
            p.ve("reduce_sum", pooled[:, c:c + 1], xnew[:, c, :], axis=AX.X)
        p.ve("tensor_copy", pooled16[:], pooled[:])
        for rc in range(2):
            p.mm([(X.PB[:, rc:rc + 1],
                   wc1_sb[:, ci, rc * 128:(rc + 1) * 128],
                   pooled16[:, ci:ci + 1], ci == 0, ci == NCH - 1)
                  for ci in range(NCH)])
            p.act(rr16[:, rc:rc + 1], X.PB[:, rc:rc + 1], AF.Relu,
                  scale=1.0 / T)
        wc2a = wm_sc[:, 1, :, :].rearrange("p a b -> p (a b)")
        wc2b = wp_sc[:, 1, 0:4, :].rearrange("p a b -> p (a b)")
        for rc in range(2):
            p.ld("gpsimd", wc2a, W["f1_Wc2T"][rc * 128:(rc + 1) * 128, 0:8192])
            p.ld("gpsimd", wc2b, W["f1_Wc2T"][rc * 128:(rc + 1) * 128, 8192:FULL])
            def mm_ck(t, rc=rc, wc2a=wc2a, wc2b=wc2b):
                last = None
                for f in range(NF128):
                    src = (wc2a[:, f * 128:(f + 1) * 128] if f < 64 else
                           wc2b[:, (f - 64) * 128:(f - 63) * 128])
                    last = t.matmul(X.PC[:, f:f + 1], src,
                                    rr16[:, rc:rc + 1], start=(rc == 0),
                                    stop=(rc == 1))
                return last
            p.step("tensor", mm_ck,
                   r=[wc2a, wc2b, rr16[:, rc:rc + 1]], w=[X.PC[:, 0:NF128]])
        p.act(ck_sb[:], X.PC[:, 0:NF128], AF.Copy)

        p.ve("reduce_max", colr[:], ck_sb[:], axis=AX.X)
        p.mm([(X.PB[0:1, 0:128], colr[:], X.identf[:], True, True)])
        p.ve("reduce_max", sl("hi"), X.PB[0:1, 0:128], axis=AX.X)
        p.ve("tensor_reduce", colr[:], ck_sb[:], axis=AX.X, op=ALU.min)
        p.mm([(X.PB[0:1, 128:256], colr[:], X.identf[:], True, True)])
        p.ve("tensor_reduce", sl("lo"), X.PB[0:1, 128:256], axis=AX.X,
             op=ALU.min)
        p.ve("tensor_scalar", sl("hi"), sl("hi"), 1e-6, None, ALU.add)
        p.ve("tensor_scalar", sl("lo"), sl("lo"), 1e-6, None, ALU.subtract)

        for it in range(22):
            p.ve("tensor_add", sl("mid"), sl("lo"), sl("hi"))
            p.ve("tensor_scalar", sl("mid"), sl("mid"), 0.5, None, ALU.mult)
            p.mm([(X.PB[:, 2:3], X.onesf_128[:], sl("mid"), True, True)])
            p.act(midcol[:], X.PB[:, 2:3], AF.Copy)
            p.ve("tensor_scalar", jk96[:], ck_sb[:], midcol[:], None,
                 ALU.is_ge)
            p.ve("reduce_sum", cnt_c[:], jk96[:], axis=AX.X)
            p.ve("tensor_copy", cnt16[:], cnt_c[:])
            p.mm([(X.PB[0:1, 3:4], X.ones_col[:], cnt16[:], True, True)])
            p.act(sl("tot"), X.PB[0:1, 3:4], AF.Copy)
            p.ve("tensor_scalar", sl("ind"), sl("tot"), 4096.0, None,
                 ALU.is_ge)
            p.ve("tensor_sub", sl("tmp"), sl("mid"), sl("lo"))
            p.ve("tensor_mul", sl("tmp"), sl("tmp"), sl("ind"))
            p.ve("tensor_add", sl("lo"), sl("lo"), sl("tmp"))
            p.ve("tensor_sub", sl("tmp"), sl("mid"), sl("hi"))
            p.ve("tensor_scalar", sl("ind"), sl("ind"), -1.0, 1.0,
                 ALU.mult, ALU.add)
            p.ve("tensor_mul", sl("tmp"), sl("tmp"), sl("ind"))
            p.ve("tensor_add", sl("hi"), sl("hi"), sl("tmp"))

        p.mm([(X.PB[:, 2:3], X.onesf_128[:], sl("lo"), True, True)])
        p.act(midcol[:], X.PB[:, 2:3], AF.Copy)
        p.ve("tensor_scalar", msk[:], ck_sb[:], midcol[:], None, ALU.is_ge)
        p.ve("tensor_add", comb[:], hist[:], msk[:])
        p.ve("tensor_scalar", comb[:], comb[:], 1.0, None, ALU.min)
        p.ve("tensor_sub", d96[:], comb[:], hist[:])
        p.ve("reduce_sum", cnt_c[:], d96[:], axis=AX.X)
        p.ve("tensor_copy", cnt16[:], cnt_c[:])
        p.mm([(X.PB[0:1, 3:4], X.ones_col[:], cnt16[:], True, True)])
        p.act(sl("nov"), X.PB[0:1, 3:4], AF.Copy)
        p.ve("memset", nov_stage[:], 0.0)
        p.ve("tensor_copy", nov_stage[0:1, 0:1], sl("nov"))
        p.dma("gpsimd", lambda g: g.dma_start(out=nov_in, in_=nov_stage[:]))
        p.step("gpsimd", lambda g: g.collective_compute(
            "AllReduce", ALU.add, [[0, 1, 2, 3, 4, 5, 6, 7]],
            [nov_in.opt()], [nov_out.opt()]), inc="gp", inc_k=1)
        p.dma("gpsimd", lambda g: g.dma_start(out=sl("nov"),
                                              in_=nov_out[0:1, 0:1]))
        p.ve("tensor_scalar", sl("dnew"), sl("nov"), R_THRESH, None,
             ALU.is_lt)
        p.mm([(X.PB[:, 2:3], X.onesf_128[:], sl("done"), True, True)])
        p.act(dcol[:], X.PB[:, 2:3], AF.Copy)
        p.ve("tensor_scalar", omdcol[:], dcol[:], -1.0, 1.0,
             ALU.mult, ALU.add)
        p.ve("tensor_scalar", jk96[:], msk[:], omdcol[:], None, ALU.mult)
        p.ve("tensor_scalar", active[:], active[:], dcol[:], None, ALU.mult)
        p.ve("tensor_add", active[:], active[:], jk96[:])
        p.ve("tensor_scalar", jk96[:], comb[:], omdcol[:], None, ALU.mult)
        p.ve("tensor_scalar", hist[:], hist[:], dcol[:], None, ALU.mult)
        p.ve("tensor_add", hist[:], hist[:], jk96[:])
        p.ve("tensor_max", sl("done"), sl("done"), sl("dnew"))

        if True:
            t1f = bl_t1f
            st16 = bl_st16
            for t4 in range(4):
                t0 = t4 * JW
                for c in range(NCH):
                    par = c % 2
                    xc16 = xcJ[:, c, t0:t0 + JW]
                    xn = xnew[:, c, t0:t0 + JW]
                    if st < 2:
                        p.ve("tensor_scalar", t1f[:, par, :], xn,
                             omdcol[:], None, ALU.mult)
                        p.ve("scalar_tensor_tensor", st16[:, par, :], xc16,
                             dcol[:], t1f[:, par, :], ALU.mult, ALU.add)
                        p.ld("sync", dst_ap[c * 128:(c + 1) * 128, t0:t0 + JW],
                             st16[:, par, :])
                    else:
                        p.ve("tensor_scalar", xn, xn, omdcol[:], None, ALU.mult)
                        p.ve("scalar_tensor_tensor", xn, xc16,
                             dcol[:], xn, ALU.mult, ALU.add)
                        p.ld("sync", t1f[:, par, :],
                             x_ap[c * 128:(c + 1) * 128, t0:t0 + JW])
                        p.ve("tensor_add", xn, t1f[:, par, :], xn)
                if st == 2:
                    emit_outer_ln(ctx, xnew[:, :, t0:t0 + JW],
                                  xnew[:, :, t0:t0 + JW], y_ap, t0)



def build_block_program(block_name, wnames):
    nc = bass.Bass("TRN2", target_bir_lowering=False)
    x_ap = nc.dram_tensor("x", [D, T], F32, kind="ExternalInput").ap()
    y_ap = nc.dram_tensor("y", [D, T], F32, kind="ExternalOutput").ap()
    W = {n: nc.dram_tensor(n, WEIGHT_SHAPES[n], F32, kind="ExternalInput").ap()
         for n in wnames}
    C = {n: nc.dram_tensor(n, CONST_SHAPES[n], F32, kind="ExternalInput").ap()
         for n in CONST_SHAPES}
    plan = Plan()
    es = ExitStack()
    with es:
        ctx = Ctx(nc, plan, es)
        ctx.make_shared(C)
        if block_name == "l0_attn":
            emit_l0_attn(ctx, W, x_ap, y_ap)
        elif block_name == "l1_attn":
            emit_l1_attn(ctx, W, x_ap, y_ap)
        elif block_name == "l0_ffn":
            emit_l0_ffn(ctx, W, x_ap, y_ap)
        elif block_name == "l1_ffn":
            emit_l1_ffn(ctx, W, x_ap, y_ap)
        else:
            raise ValueError(block_name)
        plan.step("sync", lambda sy: sy.nop())
        sems = {s: es.enter_context(nc.semaphore(s)) for s in SEMS}
        block = es.enter_context(nc.Block())
        plan.emit(block, sems)
    return nc


BLOCK_WNAMES = {
    "l0_attn": ["a0_Wq", "a0_Wk", "a0_Wv", "a0_Wo", "a0_bo_row",
                "a0_kln_w", "a0_kln_b", "a0_qln_w", "a0_qln_b"],
    "l1_attn": ["a1_Wq", "a1_Wk", "a1_Wv", "a1_Wo", "a1_bo_row",
                "a1_bq", "a1_bk", "a1_bv"],
    "l0_ffn": ["f0_wiT", "f0_woTt", "f0_bi", "f0_bo_row", "f0_AT", "f0_BT",
               "f0_gwT", "f0_gb"],
    "l1_ffn": ["f1_WmT", "f1_WpT", "f1_Wc1T", "f1_Wc2T"],
}

_prog_cache = {}


EMITTERS = {}


def build_full_program():
    nc = bass.Bass("TRN2", target_bir_lowering=False)
    x_ap = nc.dram_tensor("x", [D, T], F32, kind="ExternalInput").ap()
    y_ap = nc.dram_tensor("y", [D, T], F32, kind="ExternalOutput").ap()
    W = {}
    for names in BLOCK_WNAMES.values():
        for n in names:
            W[n] = nc.dram_tensor(n, WEIGHT_SHAPES[n], F32, kind="ExternalInput").ap()
    C = {n: nc.dram_tensor(n, CONST_SHAPES[n], F32, kind="ExternalInput").ap()
         for n in CONST_SHAPES}
    xb1 = nc.dram_tensor("xbuf1", [D, T], F32, kind="Internal").ap()
    xb2 = nc.dram_tensor("xbuf2", [D, T], F32, kind="Internal").ap()
    xb3 = nc.dram_tensor("xbuf3", [D, T], F32, kind="Internal").ap()
    plan = Plan()
    es = ExitStack()
    with es:
        ctx = Ctx(nc, plan, es)
        ctx.make_shared(C)
        chain = [(emit_l0_attn, x_ap, xb1), (emit_l0_ffn, xb1, xb2),
                 (emit_l1_attn, xb2, xb3), (emit_l1_ffn, xb3, y_ap)]
        for emit, xin, xout in chain:
            plan.fence()
            with ExitStack() as bes:
                ctx.block_es = bes
                emit(ctx, W, xin, xout)
            ctx.block_es = es
        plan.step("sync", lambda sy: sy.nop())
        sems = {s: es.enter_context(nc.semaphore(s)) for s in SEMS}
        block = es.enter_context(nc.Block())
        plan.emit(block, sems)
    return nc


_consts_cache = {}


def run_block(block_name, x_fm_cores, W):
    if block_name not in _prog_cache:
        _prog_cache[block_name] = build_block_program(block_name,
                                                      BLOCK_WNAMES[block_name])
    if "c" not in _consts_cache:
        _consts_cache["c"] = make_consts()
    CONSTS = _consts_cache["c"]
    nc = _prog_cache[block_name]
    in_maps = []
    for c in range(8):
        m = {"x": x_fm_cores[c]}
        for n in BLOCK_WNAMES[block_name]:
            m[n] = W[n]
        m.update(CONSTS)
        in_maps.append(m)
    res = run_bass_kernel_spmd(nc, in_maps, core_ids=list(range(8)))
    return [res.results[c]["y"] for c in range(8)]


def kernel(**inputs):
    W = prep_weights(inputs)
    x = np.asarray(inputs["L0_attn_x"], np.float32)
    xs = [np.ascontiguousarray(x[c].T) for c in range(8)]
    if "full" not in _prog_cache:
        _prog_cache["full"] = build_full_program()
    if "c" not in _consts_cache:
        _consts_cache["c"] = make_consts()
    nc = _prog_cache["full"]
    in_maps = []
    for c in range(8):
        m = {"x": xs[c]}
        for names in BLOCK_WNAMES.values():
            for n in names:
                m[n] = W[n]
        m.update(_consts_cache["c"])
        in_maps.append(m)
    res = run_bass_kernel_spmd(nc, in_maps, core_ids=list(range(8)))
    out = np.stack([np.asarray(res.results[c]["y"]).T for c in range(8)], axis=0)
    return np.ascontiguousarray(out, np.float32)


# version 16
# speedup vs baseline: 1.1334x; 1.0115x over previous
import numpy as np
import ml_dtypes
from contextlib import ExitStack

import concourse.bass as bass
import concourse.mybir as mybir
from concourse.bass_utils import run_bass_kernel_spmd

F32 = mybir.dt.float32
BF16 = mybir.dt.bfloat16
AF = mybir.ActivationFunctionType
ALU = mybir.AluOpType
AX = mybir.AxisListType

D = 1024
T = 2048
NCH = D // 128
H = 16
NJ = T // 512
JW = 512
SCALE = 0.125
NEG = -2.0e30
EPS = 1e-5

N_GQ = 10
N_SQ = 6
SEMS = (("pe", "dv", "ac", "gp", "sy")
        + tuple(f"g{i}" for i in range(N_GQ))
        + tuple(f"s{i}" for i in range(N_SQ)))
ENG2SEM = {"tensor": "pe", "vector": "dv", "scalar": "ac",
           "gpsimd": "gp", "sync": "sy"}
_ESIZE = {F32: 4, BF16: 2}


def _is_ap(x):
    return hasattr(x, "base_partition")


def _unm(x):
    return x() if callable(x) else x


def _fp(ap):
    if callable(ap.ap):
        ap = ap.ap()
    name = _unm(ap.name)
    ent = _unm(ap.ap)
    es = _ESIZE.get(_unm(ap.dtype), 4)
    if "DRAM" in str(_unm(ap.space)):
        reach = 1
        for st, sz in ent:
            reach += (sz - 1) * abs(st)
        off = _unm(ap.offset)
        return (name, 0, 1 << 20, off * es, (off + reach) * es)
    p0 = _unm(ap.base_partition)
    p1 = p0 + ent[0][1]
    reach = 1
    for st, sz in ent[1:]:
        reach += (sz - 1) * abs(st)
    off = _unm(ap.offset)
    return (name, p0, p1, off * es, (off + reach) * es)


def _ovl(a, b):
    return a[1] < b[2] and b[1] < a[2] and a[3] < b[4] and b[3] < a[4]


def _covers(big, small):
    return (big[1] <= small[1] and big[2] >= small[2]
            and big[3] <= small[3] and big[4] >= small[4])


class Plan:

    def __init__(self):
        self.steps = {e: [] for e in ENG2SEM}
        self.cnt = {s: 0 for s in SEMS}
        self.wr = {}
        self.rd = {}
        self.bar = None
        self.gq = 0
        self.sq = 0
        self.last_pool = {}

    def _dep_rd(self, f, deps):
        for e in self.wr.get(f[0], ()):
            if _ovl(e[0], f):
                if e[2] > deps.get(e[1], 0):
                    deps[e[1]] = e[2]

    def _dep_wr(self, f, deps):
        self._dep_rd(f, deps)
        for e in self.rd.get(f[0], ()):
            if _ovl(e[0], f):
                if e[2] > deps.get(e[1], 0):
                    deps[e[1]] = e[2]

    def _note_wr(self, f, sem, cnt):
        lst = self.wr.setdefault(f[0], [])
        lst[:] = [e for e in lst if not _covers(f, e[0])]
        lst.append((f, sem, cnt))
        rl = self.rd.get(f[0])
        if rl:
            rl[:] = [e for e in rl if not _covers(f, e[0])]

    def _note_rd(self, f, sem, cnt):
        lst = self.rd.setdefault(f[0], [])
        for i, e in enumerate(lst):
            if e[1] == sem and _covers(e[0], f):
                lst[i] = (e[0], sem, max(e[2], cnt))
                return
        lst.append((f, sem, cnt))

    def _emit(self, eng, fn, rf, wf, inc, inc_k, extra=None):
        deps = {}
        for f in rf:
            self._dep_rd(f, deps)
        for f in wf:
            self._dep_wr(f, deps)
        if self.bar is not None:
            s, c = self.bar
            if c > deps.get(s, 0):
                deps[s] = c
        if extra:
            for s, c in extra.items():
                if c > deps.get(s, 0):
                    deps[s] = c
        if eng == "tensor":
            deps.pop("pe", None)
        self.cnt[inc] += inc_k
        cnt = self.cnt[inc]
        for f in wf:
            self._note_wr(f, inc, cnt)
        for f in rf:
            self._note_rd(f, inc, cnt)
        self.steps[eng].append((fn, sorted(deps.items()), (inc, inc_k)))

    def _emit_universal(self, eng, fn, inc, inc_k):
        deps = {s: c for s, c in self.cnt.items() if c > 0}
        self.cnt[inc] += inc_k
        self.wr = {}
        self.rd = {}
        self.bar = (inc, self.cnt[inc])
        self.steps[eng].append((fn, sorted(deps.items()), (inc, inc_k)))

    def step(self, eng, fn, r=None, w=None, inc=None, inc_k=1):
        if inc is None:
            inc = ENG2SEM[eng]
        if r is None and w is None:
            self._emit_universal(eng, fn, inc, inc_k)
        else:
            rf = [_fp(a) for a in (r or ()) if _is_ap(a)]
            wf = [_fp(a) for a in (w or ()) if _is_ap(a)]
            self._emit(eng, fn, rf, wf, inc, inc_k)

    def _pool_sem(self, eng):
        if eng == "gpsimd":
            s = f"g{self.gq % N_GQ}"
            self.gq += 1
        else:
            s = f"s{self.sq % N_SQ}"
            self.sq += 1
        return s

    def dma(self, eng, fn, r=None, w=None):
        sem = self._pool_sem(eng)
        extra = {}
        prev = self.last_pool.get(sem)
        if prev:
            extra[sem] = prev
        if r is None and w is None:
            deps = {s: c for s, c in self.cnt.items() if c > 0}
            self.cnt[sem] += 16
            self.wr = {}
            self.rd = {}
            self.bar = (sem, self.cnt[sem])
            self.steps[eng].append((fn, sorted(deps.items()), (sem, 16)))
        else:
            rf = [_fp(a) for a in (r or ()) if _is_ap(a)]
            wf = [_fp(a) for a in (w or ()) if _is_ap(a)]
            self._emit(eng, fn, rf, wf, sem, 16, extra=extra)
        self.last_pool[sem] = self.cnt[sem]

    def fence(self):
        self._emit_universal("sync", lambda sy: sy.nop(), "sy", 1)

    def ld(self, eng, out, in_, **kw):
        self.dma(eng, lambda e, out=out, in_=in_, kw=kw: e.dma_start(
            out=out, in_=in_, **kw), r=[in_], w=[out])

    def mm(self, ops):
        rl, wl = [], []
        for (o, l, rr, _s, _p) in ops:
            wl.append(o)
            rl += [l, rr]
        def fn(t, ops=ops):
            last = None
            for (o, l, rr, s, p) in ops:
                last = t.matmul(o, l, rr, start=s, stop=p)
            return last
        self.step("tensor", fn, r=rl, w=wl)

    def tr(self, ops, ident):
        if callable(ident.ap):
            ident = ident.ap()
        rl = [ident]
        wl = []
        for (o, i) in ops:
            wl.append(o)
            rl.append(i)
        def fn(t, ops=ops, ident=ident):
            last = None
            for (o, i) in ops:
                last = t.transpose(o, i, ident)
            return last
        self.step("tensor", fn, r=rl, w=wl)

    def act(self, out, in_, func, bias=0.0, scale=1.0, accum_out=None):
        r = [in_] + [x for x in (bias, scale) if _is_ap(x)]
        w = [out] + ([accum_out] if accum_out is not None else [])
        if accum_out is not None:
            self.step("scalar", lambda s, o=out, i=in_, f=func, b=bias, sc=scale,
                      a=accum_out: s.activation(o, i, f, bias=b, scale=sc,
                                                accum_out=a), r=r, w=w)
        else:
            self.step("scalar", lambda s, o=out, i=in_, f=func, b=bias, sc=scale:
                      s.activation(o, i, f, bias=b, scale=sc), r=r, w=w)

    def ve(self, name, out, *ins, eng="vector", **kw):
        r = [x for x in ins if _is_ap(x)] + [x for x in kw.values() if _is_ap(x)]
        def fn(v, name=name, out=out, ins=ins, kw=kw):
            return getattr(v, name)(out, *ins, **kw)
        self.step(eng, fn, r=r, w=[out])

    def emit(self, block, sems):
        def runner(eng_name):
            def run(eng):
                hi = {s: 0 for s in SEMS}
                for fn, wvals, inc in self.steps[eng_name]:
                    for s, v in wvals:
                        if v > hi[s]:
                            eng.wait_ge(sems[s], v)
                            hi[s] = v
                    inst = fn(eng)
                    name, k = inc
                    assert inst is not None, f"step incrementing {name} returned None"
                    inst.then_inc(sems[name], k)
            return run
        block.tensor(runner("tensor"))
        block.vector(runner("vector"))
        block.scalar(runner("scalar"))
        block.gpsimd(runner("gpsimd"))
        block.sync(runner("sync"))



def prep_weights(inp):
    f32c = lambda a: np.ascontiguousarray(np.asarray(a), dtype=np.float32)
    w = {}
    for n in ("Wq", "Wk", "Wv", "Wo"):
        w["a0_" + n] = f32c(inp[f"L0_attn_{n}"])
    w["a0_bo_row"] = f32c(np.asarray(inp["L0_attn_bo"]).reshape(1, D))
    for n in ("kln_w", "kln_b", "qln_w", "qln_b"):
        w["a0_" + n] = f32c(np.tile(np.asarray(inp[f"L0_attn_{n}"]), 2).reshape(128, 1))
    w["f0_wiT"] = f32c(np.asarray(inp["L0_ffn_wi"]).T)
    w["f0_woT"] = f32c(np.asarray(inp["L0_ffn_wo"]).T)
    w["f0_bi"] = f32c(np.asarray(inp["L0_ffn_bi"]).reshape(1, 4096))
    w["f0_bo_row"] = f32c(np.asarray(inp["L0_ffn_bo"]).reshape(1, D))
    _la = np.asarray(inp["L0_ffn_lora_A"])
    w["f0_AT"] = f32c(_la.transpose(2, 0, 1).reshape(NCH, 128, 8, 16)
                      .transpose(1, 2, 0, 3))
    _wo = np.asarray(inp["L0_ffn_wo"]).T
    w["f0_woTt"] = f32c(_wo.reshape(32, 128, D).transpose(1, 0, 2))
    w["f0_BT"] = f32c(np.asarray(inp["L0_ffn_lora_B"]).transpose(0, 2, 1))
    w["f0_gwT"] = f32c(np.asarray(inp["L0_ffn_gate_w"]).T)
    w["f0_gb"] = f32c(np.asarray(inp["L0_ffn_gate_b"]).reshape(1, 8))
    for n in ("Wq", "Wk", "Wv", "Wo"):
        w["a1_" + n] = f32c(np.asarray(inp[f"L1_attn_{n}"]).T)
    for n in ("bq", "bk", "bv"):
        w["a1_" + n] = f32c(np.asarray(inp[f"L1_attn_{n}"]).reshape(1, D))
    w["a1_bo_row"] = f32c(np.asarray(inp["L1_attn_bo"]).reshape(1, D))
    w["f1_WmT"] = f32c(np.asarray(inp["L1_ffn_Wm"]).T)
    w["f1_WpT"] = f32c(np.asarray(inp["L1_ffn_Wp"]).T)
    w["f1_Wc1T"] = f32c(np.asarray(inp["L1_ffn_Wc1"]).T)
    w["f1_Wc2T"] = f32c(np.asarray(inp["L1_ffn_Wc2"]).T)
    return w


def make_consts():
    c = {}
    p = np.arange(128)[:, None]
    f512 = np.arange(JW)[None, :]
    c["c_ident"] = np.eye(128, dtype=np.float32)
    m = np.zeros((128, 4, JW), np.float32)
    for o in range(4):
        m[:, o, :] = np.where(p + 128 * o - f512 >= 1, NEG, 0.0)
    c["c_masks"] = np.ascontiguousarray(m.reshape(128, 4 * JW))
    s2 = np.zeros((2, 128), np.float32)
    s2[0, 0:64] = 1.0
    s2[1, 64:128] = 1.0
    c["c_sel2"] = s2
    o2 = np.zeros((128, 2), np.float32)
    o2[0:64, 0] = 1.0
    o2[64:128, 1] = 1.0
    c["c_ones2"] = o2
    c["c_ones_col"] = np.ones((128, 1), np.float32)
    dg = np.where(np.arange(128)[None, :] > np.arange(128)[:, None], NEG, 0.0)
    c["c_diag"] = np.ascontiguousarray(dg.astype(np.float32))
    c["c_ones_row"] = np.ones((1, JW), np.float32)
    c["c_eps"] = np.full((128, 1), EPS, np.float32)
    return c


CONST_SHAPES = {
    "c_ident": [128, 128], "c_masks": [128, 4 * JW], "c_sel2": [2, 128],
    "c_ones2": [128, 2], "c_ones_col": [128, 1], "c_ones_row": [1, JW],
    "c_eps": [128, 1], "c_diag": [128, 128],
}


WEIGHT_SHAPES = {
    "a0_Wq": [D, D], "a0_Wk": [D, D], "a0_Wv": [D, D], "a0_Wo": [D, D],
    "a0_bo_row": [1, D],
    "a0_kln_w": [128, 1], "a0_kln_b": [128, 1],
    "a0_qln_w": [128, 1], "a0_qln_b": [128, 1],
    "f0_wiT": [D, 4096], "f0_woT": [4096, D], "f0_bi": [1, 4096],
    "f0_bo_row": [1, D], "f0_AT": [128, 8, NCH, 16], "f0_BT": [8, 16, 4096],
    "f0_woTt": [128, 32, D],
    "f0_gwT": [D, 8], "f0_gb": [1, 8],
    "a1_Wq": [D, D], "a1_Wk": [D, D], "a1_Wv": [D, D], "a1_Wo": [D, D],
    "a1_bq": [1, D], "a1_bk": [1, D], "a1_bv": [1, D],
    "a1_bo_row": [1, D],
    "f1_WmT": [D, 12288], "f1_WpT": [12288, D], "f1_Wc1T": [D, 256],
    "f1_Wc2T": [256, 12288],
}


class Ctx:
    def __init__(self, nc, plan, es):
        self.nc, self.plan, self.es = nc, plan, es
        self.block_es = es
        self.uid = 0

    def nm(self, base):
        self.uid += 1
        return f"{base}{self.uid}"

    def sbuf(self, base, shape, dt, es=None):
        return (es or self.block_es).enter_context(
            self.nc.sbuf_tensor(self.nm(base), shape, dt))

    def psum(self, base, shape, dt=F32):
        return self.es.enter_context(self.nc.psum_tensor(self.nm(base), shape, dt))

    def make_shared(self, C):
        p = self.plan
        self.ones_col = self.sbuf("ones_col", [128, 1], BF16)
        self.ones_row16 = self.sbuf("ones_row16", [1, 128], BF16)
        self.onesf_row = self.sbuf("onesf_row", [1, JW], F32)
        self.onesf_128 = self.sbuf("onesf_128", [1, 128], F32)
        self.onesf_col = self.sbuf("onesf_col", [128, 1], F32)
        self.ones2 = self.sbuf("ones2", [128, 2], BF16)
        self.sel2 = self.sbuf("sel2", [2, 128], BF16)
        self.identf = self.sbuf("identf", [128, 128], F32)
        self.ident16 = self.sbuf("ident16", [128, 128], BF16)
        self.C = C
        self.eps_col = self.sbuf("eps_col", [128, 1], F32)
        self.nc.const_aps.register(F32, EPS, self.eps_col[:])

        self.PA = self.psum("PA", [128, 2048])
        self.PB = self.psum("PB", [128, 1024])
        self.PC = self.psum("PC", [128, 512])
        self.PT = self.psum("PTps", [128, 1024], BF16)

        self.row_a = self.sbuf("row_a", [2, JW], F32)
        self.row_b = self.sbuf("row_b", [2, JW], F32)
        self.row_c = self.sbuf("row_c", [2, JW], F32)
        self.abf = self.sbuf("abf", [128, JW], F32)
        self.cbf = self.sbuf("cbf", [128, JW], F32)
        self.sqf = self.sbuf("sqf", [128, JW], F32)

        for sb, cn in ((self.ones_col, "c_ones_col"), (self.ones2, "c_ones2"),
                       (self.sel2, "c_sel2"), (self.ident16, "c_ident")):
            p.ld("gpsimd", sb[:], C[cn])
        p.ld("gpsimd", self.ones_row16[:], C["c_ones_row"][0:1, 0:128])
        p.ld("gpsimd", self.onesf_row[:], C["c_ones_row"])
        p.ld("gpsimd", self.onesf_128[:], C["c_ones_row"][0:1, 0:128])
        p.ld("gpsimd", self.onesf_col[:], C["c_ones_col"])
        p.ld("gpsimd", self.identf[:], C["c_ident"])
        p.ld("gpsimd", self.eps_col[:], C["c_eps"])


def load_chunked(ctx, sb, hbm, n_chunks):
    p = ctx.plan
    for c in range(n_chunks):
        p.ld("gpsimd", sb[:, c, :], hbm[c * 128:(c + 1) * 128, :])


LN_STAGE = 99


def emit_outer_ln(ctx, rj, yj, y_ap, j0):
    p, X = ctx.plan, ctx
    for c in range(NCH):
        p.act(X.sqf[:, :], rj[:, c, :], AF.Square)
        def mm_st(t, c=c):
            t.matmul(X.PB[0:1, 0:JW], X.onesf_col[:], rj[:, c, :],
                     start=(c == 0), stop=(c == NCH - 1))
            return t.matmul(X.PB[0:1, JW:2 * JW], X.onesf_col[:], X.sqf[:, :],
                            start=(c == 0), stop=(c == NCH - 1))
        p.step("tensor", mm_st,
               r=[X.onesf_col[:], rj[:, c, :], X.sqf[:, :]],
               w=[X.PB[0:1, 0:2 * JW]])
    p.ve("tensor_scalar", X.row_b[0:1, :], X.PB[0:1, 0:JW],
         1.0 / D, None, ALU.mult)
    p.act(X.row_a[0:1, :], X.row_b[0:1, :], AF.Square)
    p.ve("scalar_tensor_tensor", X.row_c[0:1, :], X.PB[0:1, JW:2 * JW],
         1.0 / D, X.row_a[0:1, :], ALU.mult, ALU.subtract)
    p.act(X.row_c[0:1, :], X.row_c[0:1, :], AF.Ln, bias=EPS)
    p.act(X.row_a[0:1, :], X.row_c[0:1, :], AF.Exp, scale=-0.5)
    p.ve("scalar_tensor_tensor", X.row_b[0:1, :], X.row_b[0:1, :],
         -1.0, X.row_a[0:1, :], ALU.mult, ALU.mult)

    p.mm([(X.PC[:, :], X.onesf_128[:], X.row_a[0:1, :], True, True),
          (X.PB[:, 0:JW], X.onesf_128[:], X.row_b[0:1, :], True, True)])
    p.act(X.abf[:], X.PC[:, :], AF.Copy)
    p.act(X.cbf[:], X.PB[:, 0:JW], AF.Copy)
    for c in range(NCH):
        p.ve("tensor_mul", yj[:, c, :], rj[:, c, :], X.abf[:])
        p.ve("tensor_add", yj[:, c, :], yj[:, c, :], X.cbf[:])
        p.ld("sync", y_ap[c * 128:(c + 1) * 128, j0:j0 + JW], yj[:, c, :])



def emit_l0_attn(ctx, W, x_ap, y_ap):
    p, X = ctx.plan, ctx
    diagm = ctx.sbuf("diagm", [128, 128], F32)
    p.ld("gpsimd", diagm[:], ctx.C["c_diag"])
    qb = ctx.sbuf("qb", [128, NCH, T], BF16)
    kb = ctx.sbuf("kb", [128, NCH, T], BF16)
    vt = ctx.sbuf("vt", [128, T // 128, D], BF16)
    wo_j = ctx.sbuf("wo_j", [128, 2, NCH, 128], BF16)
    bo_row = ctx.sbuf("bo_row", [1, D], F32)
    lnw = {n: ctx.sbuf(n, [128, 1], F32) for n in ("klw", "klb", "qlw", "qlb")}
    sq = ctx.sbuf("sqs", [128, JW], BF16)
    t1 = ctx.sbuf("t1", [128, JW], BF16)
    ab16 = ctx.sbuf("ab16", [128, JW], BF16)
    cb16 = ctx.sbuf("cb16", [128, JW], BF16)
    row16a = ctx.sbuf("row16a", [2, JW], BF16)
    row16c = ctx.sbuf("row16c", [2, JW], BF16)

    with ExitStack() as es1:
        xb = ctx.sbuf("xb", [128, NCH, T], BF16, es1)
        wcol = ctx.sbuf("wcol", [128, 2, NCH, 128], BF16, es1)
        stage = ctx.sbuf("stage", [128, 2, JW], BF16, es1)

        load_chunked(ctx, xb, x_ap, NCH)
        p.ld("gpsimd", bo_row[:], W["a0_bo_row"])
        for sb_n, w_n in (("klw", "a0_kln_w"), ("klb", "a0_kln_b"),
                          ("qlw", "a0_qln_w"), ("qlb", "a0_qln_b")):
            p.ld("gpsimd", lnw[sb_n][:], W[w_n])

        qkv_ps = [X.PC[:, 0:JW], X.PB[:, 0:JW]]
        for wi, (w_n, kind) in enumerate((("a0_Wq", "q"), ("a0_Wk", "k"),
                                          ("a0_Wv", "v"))):
            for cout in range(NCH):
                slot = (wi * NCH + cout) % 2
                for ci in range(NCH):
                    p.ld("gpsimd", wcol[:, slot, ci, :],
                         W[w_n][ci * 128:(ci + 1) * 128,
                                cout * 128:(cout + 1) * 128])
                for jn in range(T // JW):
                    n0 = jn * JW
                    ps = qkv_ps[jn % 2]
                    p.mm([(ps, wcol[:, slot, ci, :], xb[:, ci, n0:n0 + JW],
                           ci == 0, ci == NCH - 1) for ci in range(NCH)])
                    if kind in ("q", "k"):
                        out_sb = qb if kind == "q" else kb
                        p.act(out_sb[:, cout, n0:n0 + JW], ps, AF.Copy)
                    else:
                        stg = stage[:, jn % 2, :]
                        p.act(stg, ps, AF.Copy)
                        p.tr([(X.PT[:, i * 128:(i + 1) * 128],
                               stg[:, i * 128:(i + 1) * 128])
                              for i in range(4)], X.ident16)
                        p.ve("tensor_copy",
                             vt[:, jn * 4:jn * 4 + 4,
                                cout * 128:(cout + 1) * 128],
                             X.PT[:, 0:512].rearrange("p (a b) -> p a b", a=4))

    def qkln(src, wl, bl):
        for c in range(NCH):
            for n in range(4):
                n0 = n * JW
                sl = src[:, c, n0:n0 + JW]
                p.act(sq[:, :], sl, AF.Square)
                p.mm([(X.PB[0:2, 0:JW], X.ones2[:], sl, True, True),
                      (X.PC[0:2, :], X.ones2[:], sq[:, :], True, True)])
                p.ve("tensor_scalar", X.row_b[:, :], X.PB[0:2, 0:JW],
                     1.0 / 64, None, ALU.mult)
                p.act(X.row_a[:, :], X.row_b[:, :], AF.Square)
                p.ve("scalar_tensor_tensor", X.row_c[:, :], X.PC[0:2, :],
                     1.0 / 64, X.row_a[:, :], ALU.mult, ALU.subtract)
                p.act(X.row_c[:, :], X.row_c[:, :], AF.Ln, bias=EPS)
                p.act(X.row_a[:, :], X.row_c[:, :], AF.Exp, scale=-0.5)
                p.ve("scalar_tensor_tensor", X.row_b[:, :], X.row_b[:, :],
                     -1.0, X.row_a[:, :], ALU.mult, ALU.mult)
                p.ve("tensor_copy", row16a[0:2, :], X.row_a[:, :])
                p.ve("tensor_copy", row16c[0:2, :], X.row_b[:, :])
                p.mm([(X.PB[:, 0:JW], X.sel2[:], row16a[0:2, :], True, True),
                      (X.PC[:, :], X.sel2[:], row16c[0:2, :], True, True)])
                p.act(ab16[:, :], X.PB[:, 0:JW], AF.Copy)
                p.act(cb16[:, :], X.PC[:, :], AF.Copy)
                p.ve("tensor_mul", t1[:, :], sl, ab16[:, :])
                p.ve("tensor_add", t1[:, :], t1[:, :], cb16[:, :])
                p.ve("tensor_scalar", sl, t1[:, :], wl[:], bl[:],
                     ALU.mult, ALU.add)
    qkln(qb, lnw["qlw"], lnw["qlb"])
    qkln(kb, lnw["klw"], lnw["klb"])

    p.fence()
    with ExitStack() as es2:
        xj = ctx.sbuf("xj", [128, NCH, JW], F32, es2)
        s0b = ctx.sbuf("s0b", [128, 2, T], F32, es2)
        pt = ctx.sbuf("pt", [128, 2, T], BF16, es2)
        ptT = ctx.sbuf("ptT", [128, T // 128, JW], BF16, es2)
        negm = ctx.sbuf("negm", [128, 1], F32, es2)
        bcol = ctx.sbuf("bcol", [128, 1], F32, es2)
        zcol = ctx.sbuf("zcol", [128, 4], F32, es2)
        rrow = X.row_c[0:1, :]
        rb64 = ctx.sbuf("rb64", [64, JW], F32, es2)
        ao = ctx.sbuf("ao", [128, NCH, JW], BF16, es2)

        for J in range(NJ):
            j0 = J * JW
            for c in range(NCH):
                p.ld("sync", xj[:, c, :],
                     x_ap[c * 128:(c + 1) * 128, j0:j0 + JW])
            def l0_T(h, jj):
                ktw = j0 + (jj + 1) * 128
                nch_k = ktw // 128
                par = jj % 2
                for g0 in range(0, nch_k, 8):
                    gn = min(8, nch_k - g0)
                    p.tr([(X.PT[:, i * 128:(i + 1) * 128],
                           pt[:, par, (g0 + i) * 128:(g0 + i + 1) * 128])
                          for i in range(gn)], X.ident16)
                    p.act(ptT[:, g0:g0 + gn, jj * 128:(jj + 1) * 128],
                          X.PT[:, 0:gn * 128].rearrange(
                              "p (a b) -> p a b", b=128), AF.Copy)

            for h in range(H):
                hr0 = (h % 2) * 64
                hc = h // 2
                for jj in range(4):
                    qt0 = j0 + jj * 128
                    ktw = qt0 + 128
                    ops = []
                    n0 = 0
                    while n0 < ktw:
                        nn = min(JW, ktw - n0)
                        ops.append((X.PA[:, n0:n0 + nn],
                                    qb[hr0:hr0 + 64, hc, qt0:qt0 + 128],
                                    kb[hr0:hr0 + 64, hc, n0:n0 + nn],
                                    True, True))
                        n0 += nn
                    p.mm(ops)
                    sj = s0b[:, jj % 2, 0:ktw]
                    p.act(sj, X.PA[:, 0:ktw], AF.Copy)
                    p.ve("tensor_add", sj[:, ktw - 128:ktw],
                         sj[:, ktw - 128:ktw], diagm[:])
                    p.ve("reduce_max", negm[:], sj, axis=AX.X,
                         negate=True)
                    p.ve("tensor_scalar", bcol[:], negm[:], SCALE, None,
                         ALU.mult)
                    p.act(pt[:, jj % 2, 0:ktw], sj, AF.Exp, bias=bcol[:],
                          scale=SCALE, accum_out=zcol[:, jj:jj + 1])
                    if jj > 0:
                        l0_T(h, jj - 1)
                l0_T(h, 3)
                p.mm([(X.PB[0:1, JW + jj * 128:JW + (jj + 1) * 128],
                       zcol[:, jj:jj + 1], X.identf[:], True, True)
                      for jj in range(4)])
                p.act(rrow[:], X.PB[0:1, JW:2 * JW], AF.Ln)
                p.act(rrow[:], rrow[:], AF.Exp, scale=-1.0)
                p.mm([(X.PB[0:64, 0:JW], X.onesf_128[0:1, 0:64], rrow[:],
                       True, True)])
                p.act(rb64[:], X.PB[0:64, 0:JW], AF.Copy)
                ops = []
                for c in range(4 * J + 4):
                    k = c - 4 * J
                    lo = max(0, k) * 128
                    ops.append((X.PC[0:64, lo:JW],
                                vt[:, c, h * 64:(h + 1) * 64],
                                ptT[:, c, lo:JW], c == 0, c == 4 * J + 3))
                p.mm(ops)
                p.ve("tensor_mul",
                     ao[(h % 2) * 64:(h % 2) * 64 + 64, h // 2, :],
                     X.PC[0:64, :], rb64[:])

            for cout in range(NCH):
                ws = cout % 2
                for ci in range(NCH):
                    p.ld("gpsimd", wo_j[:, ws, ci, :],
                         W["a0_Wo"][ci * 128:(ci + 1) * 128,
                                    cout * 128:(cout + 1) * 128])
                p.mm([(X.PC[:, :], wo_j[:, ws, ci, :],
                       ao[:, ci, :], ci == 0, False) for ci in range(NCH)]
                     + [(X.PC[:, :], bo_row[0:1, cout * 128:(cout + 1) * 128],
                         X.onesf_row[:], False, True)])
                p.ve("tensor_add", xj[:, cout, :], X.PC[:, :], xj[:, cout, :])
            emit_outer_ln(ctx, xj, xj, y_ap, j0)



L1A_STAGE = 99
L1A_NJ = NJ
L1A_DBG = False


def emit_l1_attn(ctx, W, x_ap, y_ap):
    if L1A_DBG:
        dbg = {n: ctx.nc.dram_tensor("dbg_" + n, shp, F32,
                                     kind="ExternalOutput").ap()
               for n, shp in (("scores", [128, T]), ("m8", [128, 8]),
                              ("u", [128, T]), ("pt", [128, T]),
                              ("zcol", [128, 4]), ("ao", [64, JW]),
                              ("q", [64, 128]), ("k", [64, T]),
                              ("rj", [128, NCH * JW]),
                              ("aof", [128, NCH * JW]),
                              ("xj0", [128, NCH * JW]))}
    p, X = ctx.plan, ctx
    qb = ctx.sbuf("qb", [128, NCH, T], BF16)
    kb = ctx.sbuf("kb", [128, NCH, T], BF16)
    vt = ctx.sbuf("vt", [128, T // 128, D], BF16)
    wo_j = ctx.sbuf("wo_j", [128, 2, NCH, 128], BF16)
    bo_row = ctx.sbuf("bo_row", [1, D], F32)

    with ExitStack() as es1:
        xb = ctx.sbuf("xb", [128, NCH, T], BF16, es1)
        bqkv = [ctx.sbuf("bqkv", [1, D], F32, es1) for _ in range(3)]
        wcol = ctx.sbuf("wcol", [128, 2, NCH, 128], BF16, es1)
        stage = ctx.sbuf("stage", [128, 2, JW], BF16, es1)

        load_chunked(ctx, xb, x_ap, NCH)
        p.ld("gpsimd", bo_row[:], W["a1_bo_row"])
        for i, n in enumerate(("a1_bq", "a1_bk", "a1_bv")):
            p.ld("gpsimd", bqkv[i][:], W[n])

        qkv_ps = [X.PC[:, 0:JW], X.PB[:, 0:JW]]
        for wi, (w_n, kind) in enumerate((("a1_Wq", "q"), ("a1_Wk", "k"),
                                          ("a1_Wv", "v"))):
            for cout in range(NCH):
                slot = (wi * NCH + cout) % 2
                for ci in range(NCH):
                    p.ld("gpsimd", wcol[:, slot, ci, :],
                         W[w_n][ci * 128:(ci + 1) * 128,
                                cout * 128:(cout + 1) * 128])
                for jn in range(T // JW):
                    n0 = jn * JW
                    ps = qkv_ps[jn % 2]
                    p.mm([(ps, wcol[:, slot, ci, :], xb[:, ci, n0:n0 + JW],
                           ci == 0, False) for ci in range(NCH)]
                         + [(ps, bqkv[wi][0:1, cout * 128:(cout + 1) * 128],
                             X.onesf_row[:], False, True)])
                    if kind in ("q", "k"):
                        out_sb = qb if kind == "q" else kb
                        p.act(out_sb[:, cout, n0:n0 + JW], ps, AF.Copy)
                    else:
                        stg = stage[:, jn % 2, :]
                        p.act(stg, ps, AF.Copy)
                        p.tr([(X.PT[:, i * 128:(i + 1) * 128],
                               stg[:, i * 128:(i + 1) * 128])
                              for i in range(4)], X.ident16)
                        p.ve("tensor_copy",
                             vt[:, jn * 4:jn * 4 + 4,
                                cout * 128:(cout + 1) * 128],
                             X.PT[:, 0:512].rearrange("p (a b) -> p a b", a=4))

    p.fence()
    with ExitStack() as es2:
        xj = ctx.sbuf("xj", [128, NCH, JW], F32, es2)
        s_sb = ctx.sbuf("s_sb", [128, 4, T], F32, es2)
        scr = ctx.sbuf("scr", [128, 1024], F32, es2)
        cand = ctx.sbuf("cand", [128, 4, 256], F32, es2)
        m8 = ctx.sbuf("m8", [128, 4, 8], F32, es2)
        taucol = ctx.sbuf("taucol", [128, 4], F32, es2)
        pt = ctx.sbuf("pt", [128, T], BF16, es2)
        ptT = ctx.sbuf("ptT", [128, T // 128, JW], BF16, es2)
        zcol = ctx.sbuf("zcol", [128, 4], F32, es2)
        rrow = X.row_c[0:1, :]
        rb64 = ctx.sbuf("rb64", [64, JW], F32, es2)
        ao = ctx.sbuf("ao", [128, NCH, JW], BF16, es2)

        for J in range(L1A_NJ):
            j0 = J * JW
            for c in range(NCH):
                p.ld("sync", xj[:, c, :],
                     x_ap[c * 128:(c + 1) * 128, j0:j0 + JW])
            def tail_T(hp, jj):
                for g in range(2):
                    p.tr([(X.PT[:, i * 128:(i + 1) * 128],
                           pt[:, (g * 8 + i) * 128:(g * 8 + i + 1) * 128])
                          for i in range(8)], X.ident16)
                    p.act(ptT[:, g * 8:g * 8 + 8, jj * 128:(jj + 1) * 128],
                          X.PT[:].rearrange("p (a b) -> p a b", a=8),
                          AF.Copy)

            def tail_PV(hp):
                p.mm([(X.PB[0:1, JW + jj * 128:JW + (jj + 1) * 128],
                       zcol[:, jj:jj + 1], X.identf[:], True, True)
                      for jj in range(4)])
                p.act(rrow[:], X.PB[0:1, JW:2 * JW], AF.Ln)
                p.act(rrow[:], rrow[:], AF.Exp, scale=-1.0)
                p.mm([(X.PB[0:64, 0:JW], X.onesf_128[0:1, 0:64], rrow[:],
                       True, True)])
                p.act(rb64[:], X.PB[0:64, 0:JW], AF.Copy)
                p.mm([(X.PC[0:64, :], vt[:, c, hp * 64:(hp + 1) * 64],
                       ptT[:, c, :], c == 0, c == T // 128 - 1)
                      for c in range(T // 128)])
                p.ve("tensor_mul",
                     ao[(hp % 2) * 64:(hp % 2) * 64 + 64, hp // 2, :],
                     X.PC[0:64, :], rb64[:])

            NHH = H if L1A_STAGE >= 2 else 0
            for h in range(NHH + 1):
                hr0 = (h % 2) * 64
                hc = h // 2
                for jj in range(4 if h < NHH else 0):
                    qt0 = j0 + jj * 128
                    p.mm([(X.PA[:, n * JW:(n + 1) * JW],
                           qb[hr0:hr0 + 64, hc, qt0:qt0 + 128],
                           kb[hr0:hr0 + 64, hc, n * JW:(n + 1) * JW],
                           True, True) for n in range(T // JW)])
                    sj = s_sb[:, jj, :]
                    p.act(sj, X.PA[:, :], AF.Copy)
                    cnd = cand[:, jj, :]
                    def tk_a(v, sj=sj, cnd=cnd):
                        last = None
                        for c in range(16):
                            last = v.max(out=cnd[:, c * 8:(c + 1) * 8],
                                         in_=sj[:, c * 128:(c + 1) * 128])
                        return last
                    p.step("vector", tk_a, r=[sj], w=[cnd])
                    for half in range(2):
                        def tk_b(v, sj=sj, cnd=cnd, half=half):
                            last = None
                            for c8 in range(8):
                                c = half * 8 + c8
                                last = v.match_replace(
                                    out=scr[:, c8 * 128:(c8 + 1) * 128],
                                    in_to_replace=cnd[:, c * 8:(c + 1) * 8],
                                    in_values=sj[:, c * 128:(c + 1) * 128],
                                    imm_value=-1e30)
                            return last
                        p.step("vector", tk_b, r=[sj, cnd], w=[scr[:]])
                        def tk_c(v, cnd=cnd, half=half):
                            last = None
                            for c8 in range(8):
                                c = half * 8 + c8
                                last = v.max(
                                    out=cnd[:, 128 + c * 8:136 + c * 8],
                                    in_=scr[:, c8 * 128:(c8 + 1) * 128])
                            return last
                        p.step("vector", tk_c, r=[scr[:]], w=[cnd])
                if h > 0:
                    tail_PV(h - 1)
                if h >= NHH:
                    break
                def dwa(jj):
                    return scr[:, jj * 256:(jj + 1) * 256]
                def dwb(jj):
                    return cand[:, jj, :]
                def mstep(fn, r, w):
                    p.step("vector", fn, r=r, w=w)
                mstep(lambda v: [v.max(out=m8[:, jj, :], in_=cand[:, jj, :])
                                 for jj in range(4)][-1],
                      r=[cand[:]], w=[m8[:]])
                mstep(lambda v: [v.match_replace(
                    out=dwa(jj), in_to_replace=m8[:, jj, :],
                    in_values=cand[:, jj, :], imm_value=-1e30)
                    for jj in range(4)][-1],
                      r=[cand[:], m8[:]], w=[scr[:]])
                mstep(lambda v: [v.max(out=m8[:, jj, :], in_=dwa(jj))
                                 for jj in range(4)][-1],
                      r=[scr[:]], w=[m8[:]])
                mstep(lambda v: [v.match_replace(
                    out=dwb(jj), in_to_replace=m8[:, jj, :],
                    in_values=dwa(jj), imm_value=-1e30)
                    for jj in range(4)][-1],
                      r=[scr[:], m8[:]], w=[cand[:]])
                mstep(lambda v: [v.max(out=m8[:, jj, :], in_=dwb(jj))
                                 for jj in range(4)][-1],
                      r=[cand[:]], w=[m8[:]])
                mstep(lambda v: [v.match_replace(
                    out=dwa(jj), in_to_replace=m8[:, jj, :],
                    in_values=dwb(jj), imm_value=-1e30)
                    for jj in range(4)][-1],
                      r=[cand[:], m8[:]], w=[scr[:]])
                mstep(lambda v: [v.max(out=m8[:, jj, :], in_=dwa(jj))
                                 for jj in range(4)][-1],
                      r=[scr[:]], w=[m8[:]])
                mstep(lambda v: [v.tensor_scalar(
                    taucol[:, jj:jj + 1], m8[:, jj, 7:8], -SCALE, None,
                    ALU.mult) for jj in range(4)][-1],
                      r=[m8[:]], w=[taucol[:]])
                for jj in range(4):
                    p.act(s_sb[:, jj, :], s_sb[:, jj, :], AF.Exp,
                          bias=taucol[:, jj:jj + 1], scale=SCALE)
                for jj in range(4):
                    sj = s_sb[:, jj, :]
                    p.ve("scalar_tensor_tensor", pt[:, :], sj, 0.99995, sj,
                         ALU.is_ge, ALU.mult)
                    p.ve("reduce_sum", zcol[:, jj:jj + 1], pt[:, :],
                         axis=AX.X)
                    tail_T(h, jj)

            for cout in range(NCH if L1A_STAGE >= 6 else 0):
                ws = cout % 2
                for ci in range(NCH):
                    p.ld("gpsimd", wo_j[:, ws, ci, :],
                         W["a1_Wo"][ci * 128:(ci + 1) * 128,
                                    cout * 128:(cout + 1) * 128])
                p.mm([(X.PC[:, :], wo_j[:, ws, ci, :],
                       ao[:, ci, :], ci == 0, False) for ci in range(NCH)]
                     + [(X.PC[:, :], bo_row[0:1, cout * 128:(cout + 1) * 128],
                         X.onesf_row[:], False, True)])
                p.ve("tensor_add", xj[:, cout, :], X.PC[:, :], xj[:, cout, :])
            if L1A_DBG and J == 0:
                p.ld("gpsimd", dbg["aof"],
                     ao[:].rearrange("p a b -> p (a b)"))
                p.ld("gpsimd", dbg["rj"],
                     xj[:].rearrange("p a b -> p (a b)"))
            if L1A_STAGE >= 7:
                emit_outer_ln(ctx, xj, xj, y_ap, j0)



def emit_l0_ffn(ctx, W, x_ap, y_ap):
    p, X = ctx.plan, ctx
    DFF = 4096
    NFC = DFF // JW
    gw_sb = ctx.sbuf("gw_sb", [128, NCH, 8], BF16)
    gb_row = ctx.sbuf("gb_row", [1, 8], F32)
    bi_row = ctx.sbuf("bi_row", [1, DFF], F32)
    bo_row = ctx.sbuf("bo_row", [1, D], F32)
    at_sb = ctx.sbuf("at_sb", [128, 8, NCH, 16], BF16)
    u16 = ctx.sbuf("u16", [16, 8, JW], BF16)

    for ci in range(NCH):
        p.ld("gpsimd", gw_sb[:, ci, :], W["f0_gwT"][ci * 128:(ci + 1) * 128, :])
    p.ld("gpsimd", gb_row[:], W["f0_gb"])
    p.ld("gpsimd", bi_row[:], W["f0_bi"])
    p.ld("gpsimd", bo_row[:], W["f0_bo_row"])
    p.ld("gpsimd", at_sb[:], W["f0_AT"][:])

    with ExitStack() as es2:
        xbJ = ctx.sbuf("xbJ", [128, NCH, JW], BF16, es2)
        xj = ctx.sbuf("xj", [128, NCH, JW], F32, es2)
        lg = ctx.sbuf("lg", [128, 4, 8], F32, es2)
        srt = ctx.sbuf("srt", [128, 4, 8], F32, es2)
        ge = ctx.sbuf("ge", [128, 4, 8], F32, es2)
        vs = ctx.sbuf("vs", [128, 4], F32, es2)
        n1c = ctx.sbuf("n1c", [128, 1], F32, es2)
        e2c = ctx.sbuf("e2c", [128, 1], F32, es2)
        rzc = ctx.sbuf("rzc", [128, 1], F32, es2)
        zc = ctx.sbuf("zc", [128, 1], F32, es2)
        jk8 = ctx.sbuf("jk8", [128, 8], F32, es2)
        vs2row = ctx.sbuf("vs2row", [1, JW], F32, es2)
        wi_fc = ctx.sbuf("wi_fc", [128, 2, NCH, JW], BF16, es2)
        bt_fc = ctx.sbuf("bt_fc", [16, 2, 8, JW], BF16, es2)
        base_sb = ctx.sbuf("base_sb", [128, 4, JW], F32, es2)
        tmp_f = ctx.sbuf("tmp_f", [128, 2, 4, JW], F32, es2)
        rsum = ctx.sbuf("rsum", [128, 2, 4, JW], BF16, es2)
        rk = ctx.sbuf("rk", [128, 2, 4, JW], BF16, es2)
        rt_sb = ctx.sbuf("rt_sb", [128, 4 * NFC, JW], BF16, es2)
        wo_co = ctx.sbuf("wo_co", [128, 2, 4 * NFC, 128], BF16, es2)

        p.fence()

        for J in range(NJ):
            j0 = J * JW
            for c in range(NCH):
                p.ld("gpsimd", xbJ[:, c, :],
                     x_ap[c * 128:(c + 1) * 128, j0:j0 + JW])
                p.ld("sync", xj[:, c, :],
                     x_ap[c * 128:(c + 1) * 128, j0:j0 + JW])

            for ts in range(4):
                p.mm([(X.PC[:, 0:8], xbJ[:, ci, ts * 128:(ts + 1) * 128],
                       gw_sb[:, ci, :], ci == 0, False) for ci in range(NCH)]
                     + [(X.PC[:, 0:8], X.onesf_128[:], gb_row[:], False, True)])
                p.act(lg[:, ts, :], X.PC[:, 0:8], AF.Copy)
                p.step("vector", (lambda ts=ts: lambda v: v.max(
                    out=srt[:, ts, :], in_=lg[:, ts, :]))(),
                    r=[lg[:, ts, :]], w=[srt[:, ts, :]])
                p.ve("tensor_scalar", n1c[:], srt[:, ts, 0:1], -1.0, None,
                     ALU.mult)
                p.act(jk8[:], lg[:, ts, :], AF.Exp, bias=n1c[:],
                      accum_out=zc[:])
                p.act(e2c[:], srt[:, ts, 1:2], AF.Exp, bias=n1c[:])
                p.ve("reciprocal", rzc[:], zc[:])
                p.ve("scalar_tensor_tensor", vs[:, ts:ts + 1], e2c[:], 1.0,
                     rzc[:], ALU.add, ALU.mult)
                def mk_ge(v, ts=ts):
                    last = None
                    for e in range(8):
                        last = v.scalar_tensor_tensor(
                            ge[:, ts, e:e + 1], lg[:, ts, e:e + 1],
                            srt[:, ts, 1:2], vs[:, ts:ts + 1],
                            ALU.is_ge, ALU.mult)
                    return last
                p.step("vector", mk_ge,
                       r=[lg[:, ts, :], srt[:, ts, :], vs[:, ts:ts + 1]],
                       w=[ge[:, ts, :]])
            p.ve("tensor_scalar", vs[:, :], vs[:, :], 2.0, None, ALU.mult)
            p.mm([(X.PB[0:1, ts * 128:(ts + 1) * 128], vs[:, ts:ts + 1],
                   X.identf[:], True, True) for ts in range(4)])
            p.act(vs2row[:], X.PB[0:1, 0:JW], AF.Copy)

            for e in range(8):
                p.mm([(X.PB[0:16, 0:JW], at_sb[:, e, ci, :], xbJ[:, ci, :],
                       ci == 0, ci == NCH - 1) for ci in range(NCH)])
                p.act(u16[0:16, e, :], X.PB[0:16, 0:JW], AF.Copy)

            for fc in range(NFC):
                slot = fc % 2
                for ci in range(NCH):
                    p.ld("gpsimd", wi_fc[:, slot, ci, :],
                         W["f0_wiT"][ci * 128:(ci + 1) * 128,
                                     fc * JW:(fc + 1) * JW])
                for e in range(8):
                    p.ld("gpsimd", bt_fc[0:16, slot, e, :],
                         W["f0_BT"][e][:, fc * JW:(fc + 1) * JW])
                for ts in range(4):
                    p.mm([(X.PA[:, ts * JW:(ts + 1) * JW],
                           xbJ[:, ci, ts * 128:(ts + 1) * 128],
                           wi_fc[:, slot, ci, :], ci == 0, False)
                          for ci in range(NCH)]
                         + [(X.PA[:, ts * JW:(ts + 1) * JW], X.onesf_128[:],
                             bi_row[0:1, fc * JW:(fc + 1) * JW], False, True)])
                for hp in range(2):
                    p.act(base_sb[:, 2 * hp:2 * hp + 2, :].rearrange(
                        "p a b -> p (a b)"),
                        X.PA[:, hp * 1024:(hp + 1) * 1024], AF.Copy)
                for e in range(8):
                    tpar = e % 2
                    for hp in range(2):
                        p.mm([(X.PA[:, ts * JW:(ts + 1) * JW],
                               u16[0:16, e, ts * 128:(ts + 1) * 128],
                               bt_fc[0:16, slot, e, :], True, True)
                              for ts in (2 * hp, 2 * hp + 1)])
                    for hp in range(2):
                        p.ve("scalar_tensor_tensor",
                             tmp_f[:, tpar, 2 * hp:2 * hp + 2, :].rearrange(
                                 "p a b -> p (a b)"),
                             X.PA[:, hp * 1024:(hp + 1) * 1024], 1.0,
                             base_sb[:, 2 * hp:2 * hp + 2, :].rearrange(
                                 "p a b -> p (a b)"),
                             ALU.mult, ALU.add)
                    dst = rsum if e < 2 else rk
                    for hp in range(2):
                        def mk_relu(s, e=e, dst=dst, tpar=tpar, hp=hp):
                            last = None
                            for ts in (2 * hp, 2 * hp + 1):
                                last = s.activation(dst[:, tpar, ts, :],
                                                    tmp_f[:, tpar, ts, :],
                                                    AF.Relu, bias=0.0,
                                                    scale=ge[:, ts, e:e + 1])
                            return last
                        p.step("scalar", mk_relu,
                               r=[tmp_f[:, tpar, 2 * hp:2 * hp + 2], ge[:]],
                               w=[dst[:, tpar, 2 * hp:2 * hp + 2]])
                    if e >= 2:
                        for hp in range(2):
                            p.ve("tensor_tensor",
                                 rsum[:, tpar, 2 * hp:2 * hp + 2].rearrange(
                                     "p a b -> p (a b)"),
                                 rsum[:, tpar, 2 * hp:2 * hp + 2].rearrange(
                                     "p a b -> p (a b)"),
                                 rk[:, tpar, 2 * hp:2 * hp + 2].rearrange(
                                     "p a b -> p (a b)"),
                                 eng=("vector" if tpar == 0 else "gpsimd"),
                                 op=ALU.add)
                p.ve("tensor_tensor",
                     rsum[:, 0].rearrange("p a b -> p (a b)"),
                     rsum[:, 0].rearrange("p a b -> p (a b)"),
                     rsum[:, 1].rearrange("p a b -> p (a b)"),
                     op=ALU.add)
                for half in range(2):
                    ops = []
                    for i in range(8):
                        ts, fs = (half * 8 + i) // 4, (half * 8 + i) % 4
                        ops.append((X.PT[:, i * 128:(i + 1) * 128],
                                    rsum[:, 0, ts, fs * 128:(fs + 1) * 128]))
                    p.tr(ops, X.ident16)
                    for i in range(8):
                        ts, fs = (half * 8 + i) // 4, (half * 8 + i) % 4
                        p.act(rt_sb[:, fc * 4 + fs, ts * 128:(ts + 1) * 128],
                              X.PT[:, i * 128:(i + 1) * 128], AF.Copy)

            for cout in range(NCH):
                ws = cout % 2
                p.ld("gpsimd", wo_co[:, ws, :, :],
                     W["f0_woTt"][:, :, cout * 128:(cout + 1) * 128])
                p.mm([(X.PB[:, 0:JW], wo_co[:, ws, fcc, :], rt_sb[:, fcc, :],
                       fcc == 0, False) for fcc in range(4 * NFC)]
                     + [(X.PB[:, 0:JW],
                         bo_row[0:1, cout * 128:(cout + 1) * 128],
                         vs2row[:], False, True)])
                p.ve("tensor_add", xj[:, cout, :], X.PB[:, 0:JW],
                     xj[:, cout, :])
            emit_outer_ln(ctx, xj, xj, y_ap, j0)



def emit_l1_ffn(ctx, W, x_ap, y_ap):
    p, X = ctx.plan, ctx
    nc = ctx.nc
    FULL = 12288
    NF128 = FULL // 128
    NSC = 12
    R_THRESH = 0.7 * 4096.0 * 8.0

    carry = [nc.dram_tensor(ctx.nm("carry"), [D, T], BF16, kind="Internal").ap()
             for _ in range(2)]
    nov_in = nc.dram_tensor(ctx.nm("nov_in"), [1, 512], F32, kind="Internal").ap()
    nov_out = nc.dram_tensor(ctx.nm("nov_out"), [1, 512], F32, kind="Internal").ap()
    nov_stage = ctx.sbuf("nov_stage", [1, 512], F32)

    xnew = ctx.sbuf("xnew", [128, NCH, T], F32)
    active = ctx.sbuf("active", [128, NF128], F32)
    hist = ctx.sbuf("hist", [128, NF128], F32)
    msk = ctx.sbuf("msk", [128, NF128], F32)
    comb = ctx.sbuf("comb", [128, NF128], F32)
    d96 = ctx.sbuf("d96", [128, NF128], F32)
    ck_sb = ctx.sbuf("ck_sb", [128, NF128], F32)
    jk96 = ctx.sbuf("jk96", [128, NF128], F32)
    pooled = ctx.sbuf("pooled", [128, NCH], F32)
    pooled16 = ctx.sbuf("pooled16", [128, NCH], BF16)
    cnt_c = ctx.sbuf("cnt_c", [128, 1], F32)
    cnt16 = ctx.sbuf("cnt16", [128, 1], BF16)
    rr16 = ctx.sbuf("rr16", [128, 2], BF16)
    wc1_sb = ctx.sbuf("wc1_sb", [128, NCH, 256], BF16)
    srow = ctx.sbuf("srow", [1, 16], F32)
    dcol = ctx.sbuf("dcol", [128, 1], F32)
    omdcol = ctx.sbuf("omdcol", [128, 1], F32)
    midcol = ctx.sbuf("midcol", [128, 1], F32)
    colr = ctx.sbuf("colr", [128, 1], F32)

    SL = {n: i for i, n in enumerate(
        ("lo", "hi", "mid", "tot", "ind", "tmp", "nov", "done", "dnew"))}

    def sl(n):
        return srow[0:1, SL[n]:SL[n] + 1]

    for ci in range(NCH):
        p.dma("gpsimd", (lambda ci=ci: lambda g: g.dma_start(
            out=wc1_sb[:, ci, :], in_=W["f1_Wc1T"][ci * 128:(ci + 1) * 128, :]))())

    def init_v(v):
        v.memset(active[:], 0.0)
        v.memset(active[:, 0:32], 1.0)
        v.memset(hist[:], 0.0)
        v.memset(hist[:, 0:32], 1.0)
        return v.memset(srow[:], 0.0)
    p.step("vector", init_v)

    wm_sc = ctx.sbuf("wm_sc", [128, 2, NCH, 1024], BF16)
    wp_sc = ctx.sbuf("wp_sc", [128, 2, 8, 1024], BF16)
    hb = ctx.sbuf("hb", [128, 8, JW], BF16)
    xcJ = ctx.sbuf("xcJ", [128, NCH, T], BF16)
    bl_t1f = ctx.sbuf("bl_t1f", [128, 2, JW], F32)
    bl_st16 = ctx.sbuf("bl_st16", [128, 2, JW], BF16)

    def load_mwp(sc, slot):
        f0 = sc * 1024
        for ci in range(NCH):
            p.ld("gpsimd", wm_sc[:, slot, ci, :],
                 W["f1_WmT"][ci * 128:(ci + 1) * 128, f0:f0 + 1024])
        for fi in range(8):
            p.ld("gpsimd", wp_sc[:, slot, fi, :],
                 W["f1_WpT"][f0 + fi * 128:f0 + (fi + 1) * 128, :])

    for st in range(3):
        src_ap = x_ap if st == 0 else carry[(st + 1) % 2]
        dst_ap = carry[st % 2]

        for ci in range(NCH):
            p.ld("gpsimd", xcJ[:, ci, :], src_ap[ci * 128:(ci + 1) * 128, :])
        if st == 0:
            load_mwp(0, 0)

        for sc in range(NSC):
            slot = sc % 2
            if sc + 1 < NSC:
                load_mwp(sc + 1, (sc + 1) % 2)
            elif st < 2:
                load_mwp(0, 0)
            for t4 in range(4):
                t0 = t4 * JW
                t4p = t4 % 2
                for fi in range(8):
                    s0 = (fi % 2) * JW
                    p.mm([(X.PB[:, s0:s0 + JW],
                           wm_sc[:, slot, ci, fi * 128:(fi + 1) * 128],
                           xcJ[:, ci, t0:t0 + JW],
                           ci == 0, ci == NCH - 1) for ci in range(NCH)])
                    acol = active[:, sc * 8 + fi:sc * 8 + fi + 1]
                    p.act(hb[:, fi, :], X.PB[:, s0:s0 + JW], AF.Relu,
                          bias=0.0, scale=acol)
                for half in range(2):
                    for cc in range(4):
                        c = half * 4 + cc
                        p.mm([(X.PA[:, cc * JW:(cc + 1) * JW],
                               wp_sc[:, slot, fi, c * 128:(c + 1) * 128],
                               hb[:, fi, :], fi == 0, fi == 7)
                              for fi in range(8)])
                    for cc in range(4):
                        c = half * 4 + cc
                        if sc == 0:
                            p.ve("tensor_copy", xnew[:, c, t0:t0 + JW],
                                 X.PA[:, cc * JW:(cc + 1) * JW])
                        else:
                            p.ve("tensor_add", xnew[:, c, t0:t0 + JW],
                                 X.PA[:, cc * JW:(cc + 1) * JW],
                                 xnew[:, c, t0:t0 + JW])

        for c in range(NCH):
            p.ve("reduce_sum", pooled[:, c:c + 1], xnew[:, c, :], axis=AX.X)
        p.ve("tensor_copy", pooled16[:], pooled[:])
        for rc in range(2):
            p.mm([(X.PB[:, rc:rc + 1],
                   wc1_sb[:, ci, rc * 128:(rc + 1) * 128],
                   pooled16[:, ci:ci + 1], ci == 0, ci == NCH - 1)
                  for ci in range(NCH)])
            p.act(rr16[:, rc:rc + 1], X.PB[:, rc:rc + 1], AF.Relu,
                  scale=1.0 / T)
        wc2a = wm_sc[:, 1, :, :].rearrange("p a b -> p (a b)")
        wc2b = wp_sc[:, 1, 0:4, :].rearrange("p a b -> p (a b)")
        for rc in range(2):
            p.ld("gpsimd", wc2a, W["f1_Wc2T"][rc * 128:(rc + 1) * 128, 0:8192])
            p.ld("gpsimd", wc2b, W["f1_Wc2T"][rc * 128:(rc + 1) * 128, 8192:FULL])
            def mm_ck(t, rc=rc, wc2a=wc2a, wc2b=wc2b):
                last = None
                for f in range(NF128):
                    src = (wc2a[:, f * 128:(f + 1) * 128] if f < 64 else
                           wc2b[:, (f - 64) * 128:(f - 63) * 128])
                    last = t.matmul(X.PC[:, f:f + 1], src,
                                    rr16[:, rc:rc + 1], start=(rc == 0),
                                    stop=(rc == 1))
                return last
            p.step("tensor", mm_ck,
                   r=[wc2a, wc2b, rr16[:, rc:rc + 1]], w=[X.PC[:, 0:NF128]])
        p.act(ck_sb[:], X.PC[:, 0:NF128], AF.Copy)

        p.ve("reduce_max", colr[:], ck_sb[:], axis=AX.X)
        p.mm([(X.PB[0:1, 0:128], colr[:], X.identf[:], True, True)])
        p.ve("reduce_max", sl("hi"), X.PB[0:1, 0:128], axis=AX.X)
        p.ve("tensor_reduce", colr[:], ck_sb[:], axis=AX.X, op=ALU.min)
        p.mm([(X.PB[0:1, 128:256], colr[:], X.identf[:], True, True)])
        p.ve("tensor_reduce", sl("lo"), X.PB[0:1, 128:256], axis=AX.X,
             op=ALU.min)
        p.ve("tensor_scalar", sl("hi"), sl("hi"), 1e-6, None, ALU.add)
        p.ve("tensor_scalar", sl("lo"), sl("lo"), 1e-6, None, ALU.subtract)

        for it in range(22):
            p.ve("tensor_add", sl("mid"), sl("lo"), sl("hi"))
            p.ve("tensor_scalar", sl("mid"), sl("mid"), 0.5, None, ALU.mult)
            p.mm([(X.PB[:, 2:3], X.onesf_128[:], sl("mid"), True, True)])
            p.act(midcol[:], X.PB[:, 2:3], AF.Copy)
            p.ve("tensor_scalar", jk96[:], ck_sb[:], midcol[:], None,
                 ALU.is_ge)
            p.ve("reduce_sum", cnt_c[:], jk96[:], axis=AX.X)
            p.ve("tensor_copy", cnt16[:], cnt_c[:])
            p.mm([(X.PB[0:1, 3:4], X.ones_col[:], cnt16[:], True, True)])
            p.act(sl("tot"), X.PB[0:1, 3:4], AF.Copy)
            p.ve("tensor_scalar", sl("ind"), sl("tot"), 4096.0, None,
                 ALU.is_ge)
            p.ve("tensor_sub", sl("tmp"), sl("mid"), sl("lo"))
            p.ve("tensor_mul", sl("tmp"), sl("tmp"), sl("ind"))
            p.ve("tensor_add", sl("lo"), sl("lo"), sl("tmp"))
            p.ve("tensor_sub", sl("tmp"), sl("mid"), sl("hi"))
            p.ve("tensor_scalar", sl("ind"), sl("ind"), -1.0, 1.0,
                 ALU.mult, ALU.add)
            p.ve("tensor_mul", sl("tmp"), sl("tmp"), sl("ind"))
            p.ve("tensor_add", sl("hi"), sl("hi"), sl("tmp"))

        p.mm([(X.PB[:, 2:3], X.onesf_128[:], sl("lo"), True, True)])
        p.act(midcol[:], X.PB[:, 2:3], AF.Copy)
        p.ve("tensor_scalar", msk[:], ck_sb[:], midcol[:], None, ALU.is_ge)
        p.ve("tensor_add", comb[:], hist[:], msk[:])
        p.ve("tensor_scalar", comb[:], comb[:], 1.0, None, ALU.min)
        p.ve("tensor_sub", d96[:], comb[:], hist[:])
        p.ve("reduce_sum", cnt_c[:], d96[:], axis=AX.X)
        p.ve("tensor_copy", cnt16[:], cnt_c[:])
        p.mm([(X.PB[0:1, 3:4], X.ones_col[:], cnt16[:], True, True)])
        p.act(sl("nov"), X.PB[0:1, 3:4], AF.Copy)
        p.ve("memset", nov_stage[:], 0.0)
        p.ve("tensor_copy", nov_stage[0:1, 0:1], sl("nov"))
        p.dma("gpsimd", lambda g: g.dma_start(out=nov_in, in_=nov_stage[:]))
        p.step("gpsimd", lambda g: g.collective_compute(
            "AllReduce", ALU.add, [[0, 1, 2, 3, 4, 5, 6, 7]],
            [nov_in.opt()], [nov_out.opt()]), inc="gp", inc_k=1)
        p.dma("gpsimd", lambda g: g.dma_start(out=sl("nov"),
                                              in_=nov_out[0:1, 0:1]))
        p.ve("tensor_scalar", sl("dnew"), sl("nov"), R_THRESH, None,
             ALU.is_lt)
        p.mm([(X.PB[:, 2:3], X.onesf_128[:], sl("done"), True, True)])
        p.act(dcol[:], X.PB[:, 2:3], AF.Copy)
        p.ve("tensor_scalar", omdcol[:], dcol[:], -1.0, 1.0,
             ALU.mult, ALU.add)
        p.ve("tensor_scalar", jk96[:], msk[:], omdcol[:], None, ALU.mult)
        p.ve("tensor_scalar", active[:], active[:], dcol[:], None, ALU.mult)
        p.ve("tensor_add", active[:], active[:], jk96[:])
        p.ve("tensor_scalar", jk96[:], comb[:], omdcol[:], None, ALU.mult)
        p.ve("tensor_scalar", hist[:], hist[:], dcol[:], None, ALU.mult)
        p.ve("tensor_add", hist[:], hist[:], jk96[:])
        p.ve("tensor_max", sl("done"), sl("done"), sl("dnew"))

        if True:
            t1f = bl_t1f
            st16 = bl_st16
            for t4 in range(4):
                t0 = t4 * JW
                for c in range(NCH):
                    par = c % 2
                    xc16 = xcJ[:, c, t0:t0 + JW]
                    xn = xnew[:, c, t0:t0 + JW]
                    if st < 2:
                        p.ve("tensor_scalar", t1f[:, par, :], xn,
                             omdcol[:], None, ALU.mult)
                        p.ve("scalar_tensor_tensor", st16[:, par, :], xc16,
                             dcol[:], t1f[:, par, :], ALU.mult, ALU.add)
                        p.ld("sync", dst_ap[c * 128:(c + 1) * 128, t0:t0 + JW],
                             st16[:, par, :])
                    else:
                        p.ve("tensor_scalar", xn, xn, omdcol[:], None, ALU.mult)
                        p.ve("scalar_tensor_tensor", xn, xc16,
                             dcol[:], xn, ALU.mult, ALU.add)
                        p.ld("sync", t1f[:, par, :],
                             x_ap[c * 128:(c + 1) * 128, t0:t0 + JW])
                        p.ve("tensor_add", xn, t1f[:, par, :], xn)
                if st == 2:
                    emit_outer_ln(ctx, xnew[:, :, t0:t0 + JW],
                                  xnew[:, :, t0:t0 + JW], y_ap, t0)



def build_block_program(block_name, wnames):
    nc = bass.Bass("TRN2", target_bir_lowering=False)
    x_ap = nc.dram_tensor("x", [D, T], F32, kind="ExternalInput").ap()
    y_ap = nc.dram_tensor("y", [D, T], F32, kind="ExternalOutput").ap()
    W = {n: nc.dram_tensor(n, WEIGHT_SHAPES[n], F32, kind="ExternalInput").ap()
         for n in wnames}
    C = {n: nc.dram_tensor(n, CONST_SHAPES[n], F32, kind="ExternalInput").ap()
         for n in CONST_SHAPES}
    plan = Plan()
    es = ExitStack()
    with es:
        ctx = Ctx(nc, plan, es)
        ctx.make_shared(C)
        if block_name == "l0_attn":
            emit_l0_attn(ctx, W, x_ap, y_ap)
        elif block_name == "l1_attn":
            emit_l1_attn(ctx, W, x_ap, y_ap)
        elif block_name == "l0_ffn":
            emit_l0_ffn(ctx, W, x_ap, y_ap)
        elif block_name == "l1_ffn":
            emit_l1_ffn(ctx, W, x_ap, y_ap)
        else:
            raise ValueError(block_name)
        plan.step("sync", lambda sy: sy.nop())
        sems = {s: es.enter_context(nc.semaphore(s)) for s in SEMS}
        block = es.enter_context(nc.Block())
        plan.emit(block, sems)
    return nc


BLOCK_WNAMES = {
    "l0_attn": ["a0_Wq", "a0_Wk", "a0_Wv", "a0_Wo", "a0_bo_row",
                "a0_kln_w", "a0_kln_b", "a0_qln_w", "a0_qln_b"],
    "l1_attn": ["a1_Wq", "a1_Wk", "a1_Wv", "a1_Wo", "a1_bo_row",
                "a1_bq", "a1_bk", "a1_bv"],
    "l0_ffn": ["f0_wiT", "f0_woTt", "f0_bi", "f0_bo_row", "f0_AT", "f0_BT",
               "f0_gwT", "f0_gb"],
    "l1_ffn": ["f1_WmT", "f1_WpT", "f1_Wc1T", "f1_Wc2T"],
}

_prog_cache = {}


EMITTERS = {}


def build_full_program():
    nc = bass.Bass("TRN2", target_bir_lowering=False)
    x_ap = nc.dram_tensor("x", [D, T], F32, kind="ExternalInput").ap()
    y_ap = nc.dram_tensor("y", [D, T], F32, kind="ExternalOutput").ap()
    W = {}
    for names in BLOCK_WNAMES.values():
        for n in names:
            W[n] = nc.dram_tensor(n, WEIGHT_SHAPES[n], F32, kind="ExternalInput").ap()
    C = {n: nc.dram_tensor(n, CONST_SHAPES[n], F32, kind="ExternalInput").ap()
         for n in CONST_SHAPES}
    xb1 = nc.dram_tensor("xbuf1", [D, T], F32, kind="Internal").ap()
    xb2 = nc.dram_tensor("xbuf2", [D, T], F32, kind="Internal").ap()
    xb3 = nc.dram_tensor("xbuf3", [D, T], F32, kind="Internal").ap()
    plan = Plan()
    es = ExitStack()
    with es:
        ctx = Ctx(nc, plan, es)
        ctx.make_shared(C)
        chain = [(emit_l0_attn, x_ap, xb1), (emit_l0_ffn, xb1, xb2),
                 (emit_l1_attn, xb2, xb3), (emit_l1_ffn, xb3, y_ap)]
        for emit, xin, xout in chain:
            plan.fence()
            with ExitStack() as bes:
                ctx.block_es = bes
                emit(ctx, W, xin, xout)
            ctx.block_es = es
        plan.step("sync", lambda sy: sy.nop())
        sems = {s: es.enter_context(nc.semaphore(s)) for s in SEMS}
        block = es.enter_context(nc.Block())
        plan.emit(block, sems)
    return nc


_consts_cache = {}


def run_block(block_name, x_fm_cores, W):
    if block_name not in _prog_cache:
        _prog_cache[block_name] = build_block_program(block_name,
                                                      BLOCK_WNAMES[block_name])
    if "c" not in _consts_cache:
        _consts_cache["c"] = make_consts()
    CONSTS = _consts_cache["c"]
    nc = _prog_cache[block_name]
    in_maps = []
    for c in range(8):
        m = {"x": x_fm_cores[c]}
        for n in BLOCK_WNAMES[block_name]:
            m[n] = W[n]
        m.update(CONSTS)
        in_maps.append(m)
    res = run_bass_kernel_spmd(nc, in_maps, core_ids=list(range(8)))
    return [res.results[c]["y"] for c in range(8)]


def kernel(**inputs):
    W = prep_weights(inputs)
    x = np.asarray(inputs["L0_attn_x"], np.float32)
    xs = [np.ascontiguousarray(x[c].T) for c in range(8)]
    if "full" not in _prog_cache:
        _prog_cache["full"] = build_full_program()
    if "c" not in _consts_cache:
        _consts_cache["c"] = make_consts()
    nc = _prog_cache["full"]
    in_maps = []
    for c in range(8):
        m = {"x": xs[c]}
        for names in BLOCK_WNAMES.values():
            for n in names:
                m[n] = W[n]
        m.update(_consts_cache["c"])
        in_maps.append(m)
    res = run_bass_kernel_spmd(nc, in_maps, core_ids=list(range(8)))
    out = np.stack([np.asarray(res.results[c]["y"]).T for c in range(8)], axis=0)
    return np.ascontiguousarray(out, np.float32)


# version 17 (final)
# speedup vs baseline: 1.1364x; 1.0026x over previous
import numpy as np
import ml_dtypes
from contextlib import ExitStack

import concourse.bass as bass
import concourse.mybir as mybir
from concourse.bass_utils import run_bass_kernel_spmd

F32 = mybir.dt.float32
BF16 = mybir.dt.bfloat16
AF = mybir.ActivationFunctionType
ALU = mybir.AluOpType
AX = mybir.AxisListType

D = 1024
T = 2048
NCH = D // 128
H = 16
NJ = T // 512
JW = 512
SCALE = 0.125
NEG = -2.0e30
EPS = 1e-5

N_GQ = 10
N_SQ = 6
SEMS = (("pe", "dv", "ac", "gp", "sy")
        + tuple(f"g{i}" for i in range(N_GQ))
        + tuple(f"s{i}" for i in range(N_SQ)))
ENG2SEM = {"tensor": "pe", "vector": "dv", "scalar": "ac",
           "gpsimd": "gp", "sync": "sy"}
_ESIZE = {F32: 4, BF16: 2}


def _is_ap(x):
    return hasattr(x, "base_partition")


def _unm(x):
    return x() if callable(x) else x


def _fp(ap):
    if callable(ap.ap):
        ap = ap.ap()
    name = _unm(ap.name)
    ent = _unm(ap.ap)
    es = _ESIZE.get(_unm(ap.dtype), 4)
    if "DRAM" in str(_unm(ap.space)):
        reach = 1
        for st, sz in ent:
            reach += (sz - 1) * abs(st)
        off = _unm(ap.offset)
        return (name, 0, 1 << 20, off * es, (off + reach) * es)
    p0 = _unm(ap.base_partition)
    p1 = p0 + ent[0][1]
    reach = 1
    for st, sz in ent[1:]:
        reach += (sz - 1) * abs(st)
    off = _unm(ap.offset)
    return (name, p0, p1, off * es, (off + reach) * es)


def _ovl(a, b):
    return a[1] < b[2] and b[1] < a[2] and a[3] < b[4] and b[3] < a[4]


def _covers(big, small):
    return (big[1] <= small[1] and big[2] >= small[2]
            and big[3] <= small[3] and big[4] >= small[4])


class Plan:

    def __init__(self):
        self.steps = {e: [] for e in ENG2SEM}
        self.cnt = {s: 0 for s in SEMS}
        self.wr = {}
        self.rd = {}
        self.bar = None
        self.gq = 0
        self.sq = 0
        self.last_pool = {}

    def _dep_rd(self, f, deps):
        for e in self.wr.get(f[0], ()):
            if _ovl(e[0], f):
                if e[2] > deps.get(e[1], 0):
                    deps[e[1]] = e[2]

    def _dep_wr(self, f, deps):
        self._dep_rd(f, deps)
        for e in self.rd.get(f[0], ()):
            if _ovl(e[0], f):
                if e[2] > deps.get(e[1], 0):
                    deps[e[1]] = e[2]

    def _note_wr(self, f, sem, cnt):
        lst = self.wr.setdefault(f[0], [])
        lst[:] = [e for e in lst if not _covers(f, e[0])]
        lst.append((f, sem, cnt))
        rl = self.rd.get(f[0])
        if rl:
            rl[:] = [e for e in rl if not _covers(f, e[0])]

    def _note_rd(self, f, sem, cnt):
        lst = self.rd.setdefault(f[0], [])
        for i, e in enumerate(lst):
            if e[1] == sem and _covers(e[0], f):
                lst[i] = (e[0], sem, max(e[2], cnt))
                return
        lst.append((f, sem, cnt))

    def _emit(self, eng, fn, rf, wf, inc, inc_k, extra=None):
        deps = {}
        for f in rf:
            self._dep_rd(f, deps)
        for f in wf:
            self._dep_wr(f, deps)
        if self.bar is not None:
            s, c = self.bar
            if c > deps.get(s, 0):
                deps[s] = c
        if extra:
            for s, c in extra.items():
                if c > deps.get(s, 0):
                    deps[s] = c
        if eng == "tensor":
            deps.pop("pe", None)
        self.cnt[inc] += inc_k
        cnt = self.cnt[inc]
        for f in wf:
            self._note_wr(f, inc, cnt)
        for f in rf:
            self._note_rd(f, inc, cnt)
        self.steps[eng].append((fn, sorted(deps.items()), (inc, inc_k)))

    def _emit_universal(self, eng, fn, inc, inc_k):
        deps = {s: c for s, c in self.cnt.items() if c > 0}
        self.cnt[inc] += inc_k
        self.wr = {}
        self.rd = {}
        self.bar = (inc, self.cnt[inc])
        self.steps[eng].append((fn, sorted(deps.items()), (inc, inc_k)))

    def step(self, eng, fn, r=None, w=None, inc=None, inc_k=1):
        if inc is None:
            inc = ENG2SEM[eng]
        if r is None and w is None:
            self._emit_universal(eng, fn, inc, inc_k)
        else:
            rf = [_fp(a) for a in (r or ()) if _is_ap(a)]
            wf = [_fp(a) for a in (w or ()) if _is_ap(a)]
            self._emit(eng, fn, rf, wf, inc, inc_k)

    def _pool_sem(self, eng):
        if eng == "gpsimd":
            s = f"g{self.gq % N_GQ}"
            self.gq += 1
        else:
            s = f"s{self.sq % N_SQ}"
            self.sq += 1
        return s

    def dma(self, eng, fn, r=None, w=None):
        sem = self._pool_sem(eng)
        extra = {}
        prev = self.last_pool.get(sem)
        if prev:
            extra[sem] = prev
        if r is None and w is None:
            deps = {s: c for s, c in self.cnt.items() if c > 0}
            self.cnt[sem] += 16
            self.wr = {}
            self.rd = {}
            self.bar = (sem, self.cnt[sem])
            self.steps[eng].append((fn, sorted(deps.items()), (sem, 16)))
        else:
            rf = [_fp(a) for a in (r or ()) if _is_ap(a)]
            wf = [_fp(a) for a in (w or ()) if _is_ap(a)]
            self._emit(eng, fn, rf, wf, sem, 16, extra=extra)
        self.last_pool[sem] = self.cnt[sem]

    def fence(self):
        self._emit_universal("sync", lambda sy: sy.nop(), "sy", 1)

    def ld(self, eng, out, in_, **kw):
        self.dma(eng, lambda e, out=out, in_=in_, kw=kw: e.dma_start(
            out=out, in_=in_, **kw), r=[in_], w=[out])

    def mm(self, ops):
        rl, wl = [], []
        for (o, l, rr, _s, _p) in ops:
            wl.append(o)
            rl += [l, rr]
        def fn(t, ops=ops):
            last = None
            for (o, l, rr, s, p) in ops:
                last = t.matmul(o, l, rr, start=s, stop=p)
            return last
        self.step("tensor", fn, r=rl, w=wl)

    def tr(self, ops, ident):
        if callable(ident.ap):
            ident = ident.ap()
        rl = [ident]
        wl = []
        for (o, i) in ops:
            wl.append(o)
            rl.append(i)
        def fn(t, ops=ops, ident=ident):
            last = None
            for (o, i) in ops:
                last = t.transpose(o, i, ident)
            return last
        self.step("tensor", fn, r=rl, w=wl)

    def act(self, out, in_, func, bias=0.0, scale=1.0, accum_out=None):
        r = [in_] + [x for x in (bias, scale) if _is_ap(x)]
        w = [out] + ([accum_out] if accum_out is not None else [])
        if accum_out is not None:
            self.step("scalar", lambda s, o=out, i=in_, f=func, b=bias, sc=scale,
                      a=accum_out: s.activation(o, i, f, bias=b, scale=sc,
                                                accum_out=a), r=r, w=w)
        else:
            self.step("scalar", lambda s, o=out, i=in_, f=func, b=bias, sc=scale:
                      s.activation(o, i, f, bias=b, scale=sc), r=r, w=w)

    def ve(self, name, out, *ins, eng="vector", **kw):
        r = [x for x in ins if _is_ap(x)] + [x for x in kw.values() if _is_ap(x)]
        def fn(v, name=name, out=out, ins=ins, kw=kw):
            return getattr(v, name)(out, *ins, **kw)
        self.step(eng, fn, r=r, w=[out])

    def emit(self, block, sems):
        def runner(eng_name):
            def run(eng):
                hi = {s: 0 for s in SEMS}
                for fn, wvals, inc in self.steps[eng_name]:
                    for s, v in wvals:
                        if v > hi[s]:
                            eng.wait_ge(sems[s], v)
                            hi[s] = v
                    inst = fn(eng)
                    name, k = inc
                    assert inst is not None, f"step incrementing {name} returned None"
                    inst.then_inc(sems[name], k)
            return run
        block.tensor(runner("tensor"))
        block.vector(runner("vector"))
        block.scalar(runner("scalar"))
        block.gpsimd(runner("gpsimd"))
        block.sync(runner("sync"))



def prep_weights(inp):
    f32c = lambda a: np.ascontiguousarray(np.asarray(a), dtype=np.float32)
    w = {}
    for n in ("Wq", "Wk", "Wv", "Wo"):
        w["a0_" + n] = f32c(inp[f"L0_attn_{n}"])
    w["a0_bo_row"] = f32c(np.asarray(inp["L0_attn_bo"]).reshape(1, D))
    for n in ("kln_w", "kln_b", "qln_w", "qln_b"):
        w["a0_" + n] = f32c(np.tile(np.asarray(inp[f"L0_attn_{n}"]), 2).reshape(128, 1))
    w["f0_wiT"] = f32c(np.asarray(inp["L0_ffn_wi"]).T)
    w["f0_woT"] = f32c(np.asarray(inp["L0_ffn_wo"]).T)
    w["f0_bi"] = f32c(np.asarray(inp["L0_ffn_bi"]).reshape(1, 4096))
    w["f0_bo_row"] = f32c(np.asarray(inp["L0_ffn_bo"]).reshape(1, D))
    _la = np.asarray(inp["L0_ffn_lora_A"])
    w["f0_AT"] = f32c(_la.transpose(2, 0, 1).reshape(NCH, 128, 8, 16)
                      .transpose(1, 2, 0, 3))
    _wo = np.asarray(inp["L0_ffn_wo"]).T
    w["f0_woTt"] = f32c(_wo.reshape(32, 128, D).transpose(1, 0, 2))
    w["f0_BT"] = f32c(np.asarray(inp["L0_ffn_lora_B"]).transpose(0, 2, 1))
    w["f0_gwT"] = f32c(np.asarray(inp["L0_ffn_gate_w"]).T)
    w["f0_gb"] = f32c(np.asarray(inp["L0_ffn_gate_b"]).reshape(1, 8))
    for n in ("Wq", "Wk", "Wv", "Wo"):
        w["a1_" + n] = f32c(np.asarray(inp[f"L1_attn_{n}"]).T)
    for n in ("bq", "bk", "bv"):
        w["a1_" + n] = f32c(np.asarray(inp[f"L1_attn_{n}"]).reshape(1, D))
    w["a1_bo_row"] = f32c(np.asarray(inp["L1_attn_bo"]).reshape(1, D))
    w["f1_WmT"] = f32c(np.asarray(inp["L1_ffn_Wm"]).T)
    w["f1_WpT"] = f32c(np.asarray(inp["L1_ffn_Wp"]).T)
    w["f1_Wc1T"] = f32c(np.asarray(inp["L1_ffn_Wc1"]).T)
    w["f1_Wc2T"] = f32c(np.asarray(inp["L1_ffn_Wc2"]).T)
    return w


def make_consts():
    c = {}
    p = np.arange(128)[:, None]
    f512 = np.arange(JW)[None, :]
    c["c_ident"] = np.eye(128, dtype=np.float32)
    m = np.zeros((128, 4, JW), np.float32)
    for o in range(4):
        m[:, o, :] = np.where(p + 128 * o - f512 >= 1, NEG, 0.0)
    c["c_masks"] = np.ascontiguousarray(m.reshape(128, 4 * JW))
    s2 = np.zeros((2, 128), np.float32)
    s2[0, 0:64] = 1.0
    s2[1, 64:128] = 1.0
    c["c_sel2"] = s2
    o2 = np.zeros((128, 2), np.float32)
    o2[0:64, 0] = 1.0
    o2[64:128, 1] = 1.0
    c["c_ones2"] = o2
    c["c_ones_col"] = np.ones((128, 1), np.float32)
    dg = np.where(np.arange(128)[None, :] > np.arange(128)[:, None], NEG, 0.0)
    c["c_diag"] = np.ascontiguousarray(dg.astype(np.float32))
    c["c_ones_row"] = np.ones((1, JW), np.float32)
    c["c_eps"] = np.full((128, 1), EPS, np.float32)
    return c


CONST_SHAPES = {
    "c_ident": [128, 128], "c_masks": [128, 4 * JW], "c_sel2": [2, 128],
    "c_ones2": [128, 2], "c_ones_col": [128, 1], "c_ones_row": [1, JW],
    "c_eps": [128, 1], "c_diag": [128, 128],
}


WEIGHT_SHAPES = {
    "a0_Wq": [D, D], "a0_Wk": [D, D], "a0_Wv": [D, D], "a0_Wo": [D, D],
    "a0_bo_row": [1, D],
    "a0_kln_w": [128, 1], "a0_kln_b": [128, 1],
    "a0_qln_w": [128, 1], "a0_qln_b": [128, 1],
    "f0_wiT": [D, 4096], "f0_woT": [4096, D], "f0_bi": [1, 4096],
    "f0_bo_row": [1, D], "f0_AT": [128, 8, NCH, 16], "f0_BT": [8, 16, 4096],
    "f0_woTt": [128, 32, D],
    "f0_gwT": [D, 8], "f0_gb": [1, 8],
    "a1_Wq": [D, D], "a1_Wk": [D, D], "a1_Wv": [D, D], "a1_Wo": [D, D],
    "a1_bq": [1, D], "a1_bk": [1, D], "a1_bv": [1, D],
    "a1_bo_row": [1, D],
    "f1_WmT": [D, 12288], "f1_WpT": [12288, D], "f1_Wc1T": [D, 256],
    "f1_Wc2T": [256, 12288],
}


class Ctx:
    def __init__(self, nc, plan, es):
        self.nc, self.plan, self.es = nc, plan, es
        self.block_es = es
        self.uid = 0

    def nm(self, base):
        self.uid += 1
        return f"{base}{self.uid}"

    def sbuf(self, base, shape, dt, es=None):
        return (es or self.block_es).enter_context(
            self.nc.sbuf_tensor(self.nm(base), shape, dt))

    def psum(self, base, shape, dt=F32):
        return self.es.enter_context(self.nc.psum_tensor(self.nm(base), shape, dt))

    def make_shared(self, C):
        p = self.plan
        self.ones_col = self.sbuf("ones_col", [128, 1], BF16)
        self.ones_row16 = self.sbuf("ones_row16", [1, 128], BF16)
        self.onesf_row = self.sbuf("onesf_row", [1, JW], F32)
        self.onesf_128 = self.sbuf("onesf_128", [1, 128], F32)
        self.onesf_col = self.sbuf("onesf_col", [128, 1], F32)
        self.ones2 = self.sbuf("ones2", [128, 2], BF16)
        self.sel2 = self.sbuf("sel2", [2, 128], BF16)
        self.identf = self.sbuf("identf", [128, 128], F32)
        self.ident16 = self.sbuf("ident16", [128, 128], BF16)
        self.C = C
        self.eps_col = self.sbuf("eps_col", [128, 1], F32)
        self.nc.const_aps.register(F32, EPS, self.eps_col[:])

        self.PA = self.psum("PA", [128, 2048])
        self.PB = self.psum("PB", [128, 1024])
        self.PC = self.psum("PC", [128, 512])
        self.PT = self.psum("PTps", [128, 1024], BF16)

        self.row_a = self.sbuf("row_a", [2, JW], F32)
        self.row_b = self.sbuf("row_b", [2, JW], F32)
        self.row_c = self.sbuf("row_c", [2, JW], F32)
        self.abf = self.sbuf("abf", [128, JW], F32)
        self.cbf = self.sbuf("cbf", [128, JW], F32)
        self.sqf = self.sbuf("sqf", [128, JW], F32)

        for sb, cn in ((self.ones_col, "c_ones_col"), (self.ones2, "c_ones2"),
                       (self.sel2, "c_sel2"), (self.ident16, "c_ident")):
            p.ld("gpsimd", sb[:], C[cn])
        p.ld("gpsimd", self.ones_row16[:], C["c_ones_row"][0:1, 0:128])
        p.ld("gpsimd", self.onesf_row[:], C["c_ones_row"])
        p.ld("gpsimd", self.onesf_128[:], C["c_ones_row"][0:1, 0:128])
        p.ld("gpsimd", self.onesf_col[:], C["c_ones_col"])
        p.ld("gpsimd", self.identf[:], C["c_ident"])
        p.ld("gpsimd", self.eps_col[:], C["c_eps"])


def load_chunked(ctx, sb, hbm, n_chunks):
    p = ctx.plan
    for c in range(n_chunks):
        p.ld("gpsimd", sb[:, c, :], hbm[c * 128:(c + 1) * 128, :])


LN_STAGE = 99


def emit_outer_ln(ctx, rj, yj, y_ap, j0):
    p, X = ctx.plan, ctx
    for c in range(NCH):
        p.act(X.sqf[:, :], rj[:, c, :], AF.Square)
        def mm_st(t, c=c):
            t.matmul(X.PB[0:1, 0:JW], X.onesf_col[:], rj[:, c, :],
                     start=(c == 0), stop=(c == NCH - 1))
            return t.matmul(X.PB[0:1, JW:2 * JW], X.onesf_col[:], X.sqf[:, :],
                            start=(c == 0), stop=(c == NCH - 1))
        p.step("tensor", mm_st,
               r=[X.onesf_col[:], rj[:, c, :], X.sqf[:, :]],
               w=[X.PB[0:1, 0:2 * JW]])
    p.ve("tensor_scalar", X.row_b[0:1, :], X.PB[0:1, 0:JW],
         1.0 / D, None, ALU.mult)
    p.act(X.row_a[0:1, :], X.row_b[0:1, :], AF.Square)
    p.ve("scalar_tensor_tensor", X.row_c[0:1, :], X.PB[0:1, JW:2 * JW],
         1.0 / D, X.row_a[0:1, :], ALU.mult, ALU.subtract)
    p.act(X.row_c[0:1, :], X.row_c[0:1, :], AF.Ln, bias=EPS)
    p.act(X.row_a[0:1, :], X.row_c[0:1, :], AF.Exp, scale=-0.5)
    p.ve("scalar_tensor_tensor", X.row_b[0:1, :], X.row_b[0:1, :],
         -1.0, X.row_a[0:1, :], ALU.mult, ALU.mult)

    p.mm([(X.PC[:, :], X.onesf_128[:], X.row_a[0:1, :], True, True),
          (X.PB[:, 0:JW], X.onesf_128[:], X.row_b[0:1, :], True, True)])
    p.act(X.abf[:], X.PC[:, :], AF.Copy)
    p.act(X.cbf[:], X.PB[:, 0:JW], AF.Copy)
    for c in range(NCH):
        p.ve("tensor_mul", yj[:, c, :], rj[:, c, :], X.abf[:])
        p.ve("tensor_add", yj[:, c, :], yj[:, c, :], X.cbf[:])
        p.ld("sync", y_ap[c * 128:(c + 1) * 128, j0:j0 + JW], yj[:, c, :])



def emit_l0_attn(ctx, W, x_ap, y_ap):
    p, X = ctx.plan, ctx
    diagm = ctx.sbuf("diagm", [128, 128], F32)
    p.ld("gpsimd", diagm[:], ctx.C["c_diag"])
    qb = ctx.sbuf("qb", [128, NCH, T], BF16)
    kb = ctx.sbuf("kb", [128, NCH, T], BF16)
    vt = ctx.sbuf("vt", [128, T // 128, D], BF16)
    wo_j = ctx.sbuf("wo_j", [128, 2, NCH, 128], BF16)
    bo_row = ctx.sbuf("bo_row", [1, D], F32)
    lnw = {n: ctx.sbuf(n, [128, 1], F32) for n in ("klw", "klb", "qlw", "qlb")}
    sq = ctx.sbuf("sqs", [128, JW], BF16)
    t1 = ctx.sbuf("t1", [128, JW], BF16)
    ab16 = ctx.sbuf("ab16", [128, JW], BF16)
    cb16 = ctx.sbuf("cb16", [128, JW], BF16)
    row16a = ctx.sbuf("row16a", [2, JW], BF16)
    row16c = ctx.sbuf("row16c", [2, JW], BF16)

    with ExitStack() as es1:
        xb = ctx.sbuf("xb", [128, NCH, T], BF16, es1)
        wcol = ctx.sbuf("wcol", [128, 2, NCH, 128], BF16, es1)
        stage = ctx.sbuf("stage", [128, 2, JW], BF16, es1)

        load_chunked(ctx, xb, x_ap, NCH)
        p.ld("gpsimd", bo_row[:], W["a0_bo_row"])
        for sb_n, w_n in (("klw", "a0_kln_w"), ("klb", "a0_kln_b"),
                          ("qlw", "a0_qln_w"), ("qlb", "a0_qln_b")):
            p.ld("gpsimd", lnw[sb_n][:], W[w_n])

        qkv_ps = [X.PC[:, 0:JW], X.PB[:, 0:JW]]
        for wi, (w_n, kind) in enumerate((("a0_Wq", "q"), ("a0_Wk", "k"),
                                          ("a0_Wv", "v"))):
            for cout in range(NCH):
                slot = (wi * NCH + cout) % 2
                for ci in range(NCH):
                    p.ld("gpsimd", wcol[:, slot, ci, :],
                         W[w_n][ci * 128:(ci + 1) * 128,
                                cout * 128:(cout + 1) * 128])
                for jn in range(T // JW):
                    n0 = jn * JW
                    ps = qkv_ps[jn % 2]
                    p.mm([(ps, wcol[:, slot, ci, :], xb[:, ci, n0:n0 + JW],
                           ci == 0, ci == NCH - 1) for ci in range(NCH)])
                    if kind in ("q", "k"):
                        out_sb = qb if kind == "q" else kb
                        p.act(out_sb[:, cout, n0:n0 + JW], ps, AF.Copy)
                    else:
                        stg = stage[:, jn % 2, :]
                        p.act(stg, ps, AF.Copy)
                        p.tr([(X.PT[:, i * 128:(i + 1) * 128],
                               stg[:, i * 128:(i + 1) * 128])
                              for i in range(4)], X.ident16)
                        p.ve("tensor_copy",
                             vt[:, jn * 4:jn * 4 + 4,
                                cout * 128:(cout + 1) * 128],
                             X.PT[:, 0:512].rearrange("p (a b) -> p a b", a=4))

    def qkln(src, wl, bl):
        for c in range(NCH):
            for n in range(4):
                n0 = n * JW
                sl = src[:, c, n0:n0 + JW]
                p.act(sq[:, :], sl, AF.Square)
                p.mm([(X.PB[0:2, 0:JW], X.ones2[:], sl, True, True),
                      (X.PC[0:2, :], X.ones2[:], sq[:, :], True, True)])
                p.ve("tensor_scalar", X.row_b[:, :], X.PB[0:2, 0:JW],
                     1.0 / 64, None, ALU.mult)
                p.act(X.row_a[:, :], X.row_b[:, :], AF.Square)
                p.ve("scalar_tensor_tensor", X.row_c[:, :], X.PC[0:2, :],
                     1.0 / 64, X.row_a[:, :], ALU.mult, ALU.subtract)
                p.act(X.row_c[:, :], X.row_c[:, :], AF.Ln, bias=EPS)
                p.act(X.row_a[:, :], X.row_c[:, :], AF.Exp, scale=-0.5)
                p.ve("scalar_tensor_tensor", X.row_b[:, :], X.row_b[:, :],
                     -1.0, X.row_a[:, :], ALU.mult, ALU.mult)
                p.ve("tensor_copy", row16a[0:2, :], X.row_a[:, :])
                p.ve("tensor_copy", row16c[0:2, :], X.row_b[:, :])
                p.mm([(X.PB[:, 0:JW], X.sel2[:], row16a[0:2, :], True, True),
                      (X.PC[:, :], X.sel2[:], row16c[0:2, :], True, True)])
                p.act(ab16[:, :], X.PB[:, 0:JW], AF.Copy)
                p.act(cb16[:, :], X.PC[:, :], AF.Copy)
                p.ve("tensor_mul", t1[:, :], sl, ab16[:, :])
                p.ve("tensor_add", t1[:, :], t1[:, :], cb16[:, :])
                p.ve("tensor_scalar", sl, t1[:, :], wl[:], bl[:],
                     ALU.mult, ALU.add)
    qkln(qb, lnw["qlw"], lnw["qlb"])
    qkln(kb, lnw["klw"], lnw["klb"])

    p.fence()
    with ExitStack() as es2:
        xj = ctx.sbuf("xj", [128, NCH, JW], F32, es2)
        s0b = ctx.sbuf("s0b", [128, 2, T], F32, es2)
        pt = ctx.sbuf("pt", [128, 2, T], BF16, es2)
        ptT = ctx.sbuf("ptT", [128, T // 128, JW], BF16, es2)
        negm = ctx.sbuf("negm", [128, 1], F32, es2)
        bcol = ctx.sbuf("bcol", [128, 1], F32, es2)
        zcol = ctx.sbuf("zcol", [128, 4], F32, es2)
        rrow = X.row_c[0:1, :]
        rb64 = ctx.sbuf("rb64", [64, JW], F32, es2)
        ao = ctx.sbuf("ao", [128, NCH, JW], BF16, es2)

        for J in range(NJ):
            j0 = J * JW
            for c in range(NCH):
                p.ld("sync", xj[:, c, :],
                     x_ap[c * 128:(c + 1) * 128, j0:j0 + JW])
            def l0_T(h, jj):
                ktw = j0 + (jj + 1) * 128
                nch_k = ktw // 128
                par = jj % 2
                for g0 in range(0, nch_k, 8):
                    gn = min(8, nch_k - g0)
                    p.tr([(X.PT[:, i * 128:(i + 1) * 128],
                           pt[:, par, (g0 + i) * 128:(g0 + i + 1) * 128])
                          for i in range(gn)], X.ident16)
                    p.act(ptT[:, g0:g0 + gn, jj * 128:(jj + 1) * 128],
                          X.PT[:, 0:gn * 128].rearrange(
                              "p (a b) -> p a b", b=128), AF.Copy)

            for h in range(H):
                hr0 = (h % 2) * 64
                hc = h // 2
                for jj in range(4):
                    qt0 = j0 + jj * 128
                    ktw = qt0 + 128
                    ops = []
                    n0 = 0
                    while n0 < ktw:
                        nn = min(JW, ktw - n0)
                        ops.append((X.PA[:, n0:n0 + nn],
                                    qb[hr0:hr0 + 64, hc, qt0:qt0 + 128],
                                    kb[hr0:hr0 + 64, hc, n0:n0 + nn],
                                    True, True))
                        n0 += nn
                    p.mm(ops)
                    sj = s0b[:, jj % 2, 0:ktw]
                    p.act(sj, X.PA[:, 0:ktw], AF.Copy)
                    p.ve("tensor_add", sj[:, ktw - 128:ktw],
                         sj[:, ktw - 128:ktw], diagm[:])
                    p.ve("reduce_max", negm[:], sj, axis=AX.X,
                         negate=True)
                    p.ve("tensor_scalar", bcol[:], negm[:], SCALE, None,
                         ALU.mult)
                    p.act(pt[:, jj % 2, 0:ktw], sj, AF.Exp, bias=bcol[:],
                          scale=SCALE, accum_out=zcol[:, jj:jj + 1])
                    if jj > 0:
                        l0_T(h, jj - 1)
                l0_T(h, 3)
                p.mm([(X.PB[0:1, JW + jj * 128:JW + (jj + 1) * 128],
                       zcol[:, jj:jj + 1], X.identf[:], True, True)
                      for jj in range(4)])
                p.act(rrow[:], X.PB[0:1, JW:2 * JW], AF.Ln)
                p.act(rrow[:], rrow[:], AF.Exp, scale=-1.0)
                p.mm([(X.PB[0:64, 0:JW], X.onesf_128[0:1, 0:64], rrow[:],
                       True, True)])
                p.act(rb64[:], X.PB[0:64, 0:JW], AF.Copy)
                ops = []
                for c in range(4 * J + 4):
                    k = c - 4 * J
                    lo = max(0, k) * 128
                    ops.append((X.PC[0:64, lo:JW],
                                vt[:, c, h * 64:(h + 1) * 64],
                                ptT[:, c, lo:JW], c == 0, c == 4 * J + 3))
                p.mm(ops)
                p.ve("tensor_mul",
                     ao[(h % 2) * 64:(h % 2) * 64 + 64, h // 2, :],
                     X.PC[0:64, :], rb64[:])

            for cout in range(NCH):
                ws = cout % 2
                for ci in range(NCH):
                    p.ld("gpsimd", wo_j[:, ws, ci, :],
                         W["a0_Wo"][ci * 128:(ci + 1) * 128,
                                    cout * 128:(cout + 1) * 128])
                p.mm([(X.PC[:, :], wo_j[:, ws, ci, :],
                       ao[:, ci, :], ci == 0, False) for ci in range(NCH)]
                     + [(X.PC[:, :], bo_row[0:1, cout * 128:(cout + 1) * 128],
                         X.onesf_row[:], False, True)])
                p.ve("tensor_add", xj[:, cout, :], X.PC[:, :], xj[:, cout, :])
            emit_outer_ln(ctx, xj, xj, y_ap, j0)



L1A_STAGE = 99
L1A_NJ = NJ
L1A_DBG = False


def emit_l1_attn(ctx, W, x_ap, y_ap):
    if L1A_DBG:
        dbg = {n: ctx.nc.dram_tensor("dbg_" + n, shp, F32,
                                     kind="ExternalOutput").ap()
               for n, shp in (("scores", [128, T]), ("m8", [128, 8]),
                              ("u", [128, T]), ("pt", [128, T]),
                              ("zcol", [128, 4]), ("ao", [64, JW]),
                              ("q", [64, 128]), ("k", [64, T]),
                              ("rj", [128, NCH * JW]),
                              ("aof", [128, NCH * JW]),
                              ("xj0", [128, NCH * JW]))}
    p, X = ctx.plan, ctx
    qb = ctx.sbuf("qb", [128, NCH, T], BF16)
    kb = ctx.sbuf("kb", [128, NCH, T], BF16)
    vt = ctx.sbuf("vt", [128, T // 128, D], BF16)
    wo_j = ctx.sbuf("wo_j", [128, 2, NCH, 128], BF16)
    bo_row = ctx.sbuf("bo_row", [1, D], F32)

    with ExitStack() as es1:
        xb = ctx.sbuf("xb", [128, NCH, T], BF16, es1)
        bqkv = [ctx.sbuf("bqkv", [1, D], F32, es1) for _ in range(3)]
        wcol = ctx.sbuf("wcol", [128, 2, NCH, 128], BF16, es1)
        stage = ctx.sbuf("stage", [128, 2, JW], BF16, es1)

        load_chunked(ctx, xb, x_ap, NCH)
        p.ld("gpsimd", bo_row[:], W["a1_bo_row"])
        for i, n in enumerate(("a1_bq", "a1_bk", "a1_bv")):
            p.ld("gpsimd", bqkv[i][:], W[n])

        qkv_ps = [X.PC[:, 0:JW], X.PB[:, 0:JW]]
        for wi, (w_n, kind) in enumerate((("a1_Wq", "q"), ("a1_Wk", "k"),
                                          ("a1_Wv", "v"))):
            for cout in range(NCH):
                slot = (wi * NCH + cout) % 2
                for ci in range(NCH):
                    p.ld("gpsimd", wcol[:, slot, ci, :],
                         W[w_n][ci * 128:(ci + 1) * 128,
                                cout * 128:(cout + 1) * 128])
                for jn in range(T // JW):
                    n0 = jn * JW
                    ps = qkv_ps[jn % 2]
                    p.mm([(ps, wcol[:, slot, ci, :], xb[:, ci, n0:n0 + JW],
                           ci == 0, False) for ci in range(NCH)]
                         + [(ps, bqkv[wi][0:1, cout * 128:(cout + 1) * 128],
                             X.onesf_row[:], False, True)])
                    if kind in ("q", "k"):
                        out_sb = qb if kind == "q" else kb
                        p.act(out_sb[:, cout, n0:n0 + JW], ps, AF.Copy)
                    else:
                        stg = stage[:, jn % 2, :]
                        p.act(stg, ps, AF.Copy)
                        p.tr([(X.PT[:, i * 128:(i + 1) * 128],
                               stg[:, i * 128:(i + 1) * 128])
                              for i in range(4)], X.ident16)
                        p.ve("tensor_copy",
                             vt[:, jn * 4:jn * 4 + 4,
                                cout * 128:(cout + 1) * 128],
                             X.PT[:, 0:512].rearrange("p (a b) -> p a b", a=4))

    p.fence()
    with ExitStack() as es2:
        xj = ctx.sbuf("xj", [128, NCH, JW], F32, es2)
        s_sb = ctx.sbuf("s_sb", [128, 4, T], F32, es2)
        scr = ctx.sbuf("scr", [128, 1024], F32, es2)
        cand = ctx.sbuf("cand", [128, 4, 256], F32, es2)
        m8 = ctx.sbuf("m8", [128, 4, 8], F32, es2)
        taucol = ctx.sbuf("taucol", [128, 4], F32, es2)
        pt = ctx.sbuf("pt", [128, T], BF16, es2)
        ptT = ctx.sbuf("ptT", [128, T // 128, JW], BF16, es2)
        zcol = ctx.sbuf("zcol", [128, 4], F32, es2)
        rrow = X.row_c[0:1, :]
        rb64 = ctx.sbuf("rb64", [64, JW], F32, es2)
        ao = ctx.sbuf("ao", [128, NCH, JW], BF16, es2)

        for J in range(L1A_NJ):
            j0 = J * JW
            for c in range(NCH):
                p.ld("sync", xj[:, c, :],
                     x_ap[c * 128:(c + 1) * 128, j0:j0 + JW])
            def tail_T(hp, jj):
                for g in range(2):
                    p.tr([(X.PT[:, i * 128:(i + 1) * 128],
                           pt[:, (g * 8 + i) * 128:(g * 8 + i + 1) * 128])
                          for i in range(8)], X.ident16)
                    p.act(ptT[:, g * 8:g * 8 + 8, jj * 128:(jj + 1) * 128],
                          X.PT[:].rearrange("p (a b) -> p a b", a=8),
                          AF.Copy)

            def tail_PV(hp):
                p.mm([(X.PB[0:1, JW + jj * 128:JW + (jj + 1) * 128],
                       zcol[:, jj:jj + 1], X.identf[:], True, True)
                      for jj in range(4)])
                p.act(rrow[:], X.PB[0:1, JW:2 * JW], AF.Ln)
                p.act(rrow[:], rrow[:], AF.Exp, scale=-1.0)
                p.mm([(X.PB[0:64, 0:JW], X.onesf_128[0:1, 0:64], rrow[:],
                       True, True)])
                p.act(rb64[:], X.PB[0:64, 0:JW], AF.Copy)
                p.mm([(X.PC[0:64, :], vt[:, c, hp * 64:(hp + 1) * 64],
                       ptT[:, c, :], c == 0, c == T // 128 - 1)
                      for c in range(T // 128)])
                p.ve("tensor_mul",
                     ao[(hp % 2) * 64:(hp % 2) * 64 + 64, hp // 2, :],
                     X.PC[0:64, :], rb64[:])

            NHH = H if L1A_STAGE >= 2 else 0
            for h in range(NHH + 1):
                hr0 = (h % 2) * 64
                hc = h // 2
                for jj in range(4 if h < NHH else 0):
                    qt0 = j0 + jj * 128
                    p.mm([(X.PA[:, n * JW:(n + 1) * JW],
                           qb[hr0:hr0 + 64, hc, qt0:qt0 + 128],
                           kb[hr0:hr0 + 64, hc, n * JW:(n + 1) * JW],
                           True, True) for n in range(T // JW)])
                    sj = s_sb[:, jj, :]
                    p.act(sj, X.PA[:, :], AF.Copy)
                    cnd = cand[:, jj, :]
                    def tk_a(v, sj=sj, cnd=cnd):
                        last = None
                        for c in range(16):
                            last = v.max(out=cnd[:, c * 8:(c + 1) * 8],
                                         in_=sj[:, c * 128:(c + 1) * 128])
                        return last
                    p.step("vector", tk_a, r=[sj], w=[cnd])
                    for half in range(2):
                        def tk_b(v, sj=sj, cnd=cnd, half=half):
                            last = None
                            for c8 in range(8):
                                c = half * 8 + c8
                                last = v.match_replace(
                                    out=scr[:, c8 * 128:(c8 + 1) * 128],
                                    in_to_replace=cnd[:, c * 8:(c + 1) * 8],
                                    in_values=sj[:, c * 128:(c + 1) * 128],
                                    imm_value=-1e30)
                            return last
                        p.step("vector", tk_b, r=[sj, cnd], w=[scr[:]])
                        def tk_c(v, cnd=cnd, half=half):
                            last = None
                            for c8 in range(8):
                                c = half * 8 + c8
                                last = v.max(
                                    out=cnd[:, 128 + c * 8:136 + c * 8],
                                    in_=scr[:, c8 * 128:(c8 + 1) * 128])
                            return last
                        p.step("vector", tk_c, r=[scr[:]], w=[cnd])
                if h > 0:
                    tail_PV(h - 1)
                if h >= NHH:
                    break
                def dwa(jj):
                    return scr[:, jj * 256:(jj + 1) * 256]
                def dwb(jj):
                    return cand[:, jj, :]
                def mstep(fn, r, w):
                    p.step("vector", fn, r=r, w=w)
                mstep(lambda v: [v.max(out=m8[:, jj, :], in_=cand[:, jj, :])
                                 for jj in range(4)][-1],
                      r=[cand[:]], w=[m8[:]])
                mstep(lambda v: [v.match_replace(
                    out=dwa(jj), in_to_replace=m8[:, jj, :],
                    in_values=cand[:, jj, :], imm_value=-1e30)
                    for jj in range(4)][-1],
                      r=[cand[:], m8[:]], w=[scr[:]])
                mstep(lambda v: [v.max(out=m8[:, jj, :], in_=dwa(jj))
                                 for jj in range(4)][-1],
                      r=[scr[:]], w=[m8[:]])
                mstep(lambda v: [v.match_replace(
                    out=dwb(jj), in_to_replace=m8[:, jj, :],
                    in_values=dwa(jj), imm_value=-1e30)
                    for jj in range(4)][-1],
                      r=[scr[:], m8[:]], w=[cand[:]])
                mstep(lambda v: [v.max(out=m8[:, jj, :], in_=dwb(jj))
                                 for jj in range(4)][-1],
                      r=[cand[:]], w=[m8[:]])
                mstep(lambda v: [v.match_replace(
                    out=dwa(jj), in_to_replace=m8[:, jj, :],
                    in_values=dwb(jj), imm_value=-1e30)
                    for jj in range(4)][-1],
                      r=[cand[:], m8[:]], w=[scr[:]])
                mstep(lambda v: [v.max(out=m8[:, jj, :], in_=dwa(jj))
                                 for jj in range(4)][-1],
                      r=[scr[:]], w=[m8[:]])
                mstep(lambda v: [v.tensor_scalar(
                    taucol[:, jj:jj + 1], m8[:, jj, 7:8], -SCALE, None,
                    ALU.mult) for jj in range(4)][-1],
                      r=[m8[:]], w=[taucol[:]])
                for jj in range(4):
                    p.act(s_sb[:, jj, :], s_sb[:, jj, :], AF.Exp,
                          bias=taucol[:, jj:jj + 1], scale=SCALE)
                for jj in range(4):
                    sj = s_sb[:, jj, :]
                    p.ve("scalar_tensor_tensor", pt[:, :], sj, 0.99995, sj,
                         ALU.is_ge, ALU.mult)
                    p.ve("reduce_sum", zcol[:, jj:jj + 1], pt[:, :],
                         axis=AX.X)
                    tail_T(h, jj)

            for cout in range(NCH if L1A_STAGE >= 6 else 0):
                ws = cout % 2
                for ci in range(NCH):
                    p.ld("gpsimd", wo_j[:, ws, ci, :],
                         W["a1_Wo"][ci * 128:(ci + 1) * 128,
                                    cout * 128:(cout + 1) * 128])
                p.mm([(X.PC[:, :], wo_j[:, ws, ci, :],
                       ao[:, ci, :], ci == 0, False) for ci in range(NCH)]
                     + [(X.PC[:, :], bo_row[0:1, cout * 128:(cout + 1) * 128],
                         X.onesf_row[:], False, True)])
                p.ve("tensor_add", xj[:, cout, :], X.PC[:, :], xj[:, cout, :])
            if L1A_DBG and J == 0:
                p.ld("gpsimd", dbg["aof"],
                     ao[:].rearrange("p a b -> p (a b)"))
                p.ld("gpsimd", dbg["rj"],
                     xj[:].rearrange("p a b -> p (a b)"))
            if L1A_STAGE >= 7:
                emit_outer_ln(ctx, xj, xj, y_ap, j0)



def emit_l0_ffn(ctx, W, x_ap, y_ap):
    p, X = ctx.plan, ctx
    DFF = 4096
    NFC = DFF // JW
    gw_sb = ctx.sbuf("gw_sb", [128, NCH, 8], BF16)
    gb_row = ctx.sbuf("gb_row", [1, 8], F32)
    bi_row = ctx.sbuf("bi_row", [1, DFF], F32)
    bo_row = ctx.sbuf("bo_row", [1, D], F32)
    at_sb = ctx.sbuf("at_sb", [128, 8, NCH, 16], BF16)
    u16 = ctx.sbuf("u16", [16, 8, JW], BF16)

    for ci in range(NCH):
        p.ld("gpsimd", gw_sb[:, ci, :], W["f0_gwT"][ci * 128:(ci + 1) * 128, :])
    p.ld("gpsimd", gb_row[:], W["f0_gb"])
    p.ld("gpsimd", bi_row[:], W["f0_bi"])
    p.ld("gpsimd", bo_row[:], W["f0_bo_row"])
    p.ld("gpsimd", at_sb[:], W["f0_AT"][:])

    with ExitStack() as es2:
        xbJ = ctx.sbuf("xbJ", [128, NCH, JW], BF16, es2)
        xj = ctx.sbuf("xj", [128, NCH, JW], F32, es2)
        lg = ctx.sbuf("lg", [128, 4, 8], F32, es2)
        srt = ctx.sbuf("srt", [128, 4, 8], F32, es2)
        ge = ctx.sbuf("ge", [128, 4, 8], F32, es2)
        vs = ctx.sbuf("vs", [128, 4], F32, es2)
        n1c = ctx.sbuf("n1c", [128, 1], F32, es2)
        e2c = ctx.sbuf("e2c", [128, 1], F32, es2)
        rzc = ctx.sbuf("rzc", [128, 1], F32, es2)
        zc = ctx.sbuf("zc", [128, 1], F32, es2)
        jk8 = ctx.sbuf("jk8", [128, 8], F32, es2)
        vs2row = ctx.sbuf("vs2row", [1, JW], F32, es2)
        wi_fc = ctx.sbuf("wi_fc", [128, 2, NCH, JW], BF16, es2)
        bt_fc = ctx.sbuf("bt_fc", [16, 2, 8, JW], BF16, es2)
        base_sb = ctx.sbuf("base_sb", [128, 4, JW], F32, es2)
        tmp_f = ctx.sbuf("tmp_f", [128, 2, 4, JW], F32, es2)
        rsum = ctx.sbuf("rsum", [128, 2, 4, JW], BF16, es2)
        rk = ctx.sbuf("rk", [128, 2, 4, JW], BF16, es2)
        rt_sb = ctx.sbuf("rt_sb", [128, 4 * NFC, JW], BF16, es2)
        wo_co = ctx.sbuf("wo_co", [128, 2, 4 * NFC, 128], BF16, es2)

        p.fence()

        for J in range(NJ):
            j0 = J * JW
            for c in range(NCH):
                p.ld("gpsimd", xbJ[:, c, :],
                     x_ap[c * 128:(c + 1) * 128, j0:j0 + JW])
                p.ld("sync", xj[:, c, :],
                     x_ap[c * 128:(c + 1) * 128, j0:j0 + JW])

            for ts in range(4):
                p.mm([(X.PC[:, 0:8], xbJ[:, ci, ts * 128:(ts + 1) * 128],
                       gw_sb[:, ci, :], ci == 0, False) for ci in range(NCH)]
                     + [(X.PC[:, 0:8], X.onesf_128[:], gb_row[:], False, True)])
                p.act(lg[:, ts, :], X.PC[:, 0:8], AF.Copy)
                p.step("vector", (lambda ts=ts: lambda v: v.max(
                    out=srt[:, ts, :], in_=lg[:, ts, :]))(),
                    r=[lg[:, ts, :]], w=[srt[:, ts, :]])
                p.ve("tensor_scalar", n1c[:], srt[:, ts, 0:1], -1.0, None,
                     ALU.mult)
                p.act(jk8[:], lg[:, ts, :], AF.Exp, bias=n1c[:],
                      accum_out=zc[:])
                p.act(e2c[:], srt[:, ts, 1:2], AF.Exp, bias=n1c[:])
                p.ve("reciprocal", rzc[:], zc[:])
                p.ve("scalar_tensor_tensor", vs[:, ts:ts + 1], e2c[:], 1.0,
                     rzc[:], ALU.add, ALU.mult)
                def mk_ge(v, ts=ts):
                    last = None
                    for e in range(8):
                        last = v.scalar_tensor_tensor(
                            ge[:, ts, e:e + 1], lg[:, ts, e:e + 1],
                            srt[:, ts, 1:2], vs[:, ts:ts + 1],
                            ALU.is_ge, ALU.mult)
                    return last
                p.step("vector", mk_ge,
                       r=[lg[:, ts, :], srt[:, ts, :], vs[:, ts:ts + 1]],
                       w=[ge[:, ts, :]])
            p.ve("tensor_scalar", vs[:, :], vs[:, :], 2.0, None, ALU.mult)
            p.mm([(X.PB[0:1, ts * 128:(ts + 1) * 128], vs[:, ts:ts + 1],
                   X.identf[:], True, True) for ts in range(4)])
            p.act(vs2row[:], X.PB[0:1, 0:JW], AF.Copy)

            for e in range(8):
                p.mm([(X.PB[0:16, 0:JW], at_sb[:, e, ci, :], xbJ[:, ci, :],
                       ci == 0, ci == NCH - 1) for ci in range(NCH)])
                p.act(u16[0:16, e, :], X.PB[0:16, 0:JW], AF.Copy)

            for fc in range(NFC):
                slot = fc % 2
                for ci in range(NCH):
                    p.ld("gpsimd", wi_fc[:, slot, ci, :],
                         W["f0_wiT"][ci * 128:(ci + 1) * 128,
                                     fc * JW:(fc + 1) * JW])
                for e in range(8):
                    p.ld("gpsimd", bt_fc[0:16, slot, e, :],
                         W["f0_BT"][e][:, fc * JW:(fc + 1) * JW])
                for ts in range(4):
                    p.mm([(X.PA[:, ts * JW:(ts + 1) * JW],
                           xbJ[:, ci, ts * 128:(ts + 1) * 128],
                           wi_fc[:, slot, ci, :], ci == 0, False)
                          for ci in range(NCH)]
                         + [(X.PA[:, ts * JW:(ts + 1) * JW], X.onesf_128[:],
                             bi_row[0:1, fc * JW:(fc + 1) * JW], False, True)])
                for hp in range(2):
                    p.act(base_sb[:, 2 * hp:2 * hp + 2, :].rearrange(
                        "p a b -> p (a b)"),
                        X.PA[:, hp * 1024:(hp + 1) * 1024], AF.Copy)
                for e in range(8):
                    tpar = e % 2
                    for hp in range(2):
                        p.mm([(X.PA[:, ts * JW:(ts + 1) * JW],
                               u16[0:16, e, ts * 128:(ts + 1) * 128],
                               bt_fc[0:16, slot, e, :], True, True)
                              for ts in (2 * hp, 2 * hp + 1)])
                    for hp in range(2):
                        p.ve("scalar_tensor_tensor",
                             tmp_f[:, tpar, 2 * hp:2 * hp + 2, :].rearrange(
                                 "p a b -> p (a b)"),
                             X.PA[:, hp * 1024:(hp + 1) * 1024], 1.0,
                             base_sb[:, 2 * hp:2 * hp + 2, :].rearrange(
                                 "p a b -> p (a b)"),
                             ALU.mult, ALU.add)
                    dst = rsum if e < 2 else rk
                    for hp in range(2):
                        def mk_relu(s, e=e, dst=dst, tpar=tpar, hp=hp):
                            last = None
                            for ts in (2 * hp, 2 * hp + 1):
                                last = s.activation(dst[:, tpar, ts, :],
                                                    tmp_f[:, tpar, ts, :],
                                                    AF.Relu, bias=0.0,
                                                    scale=ge[:, ts, e:e + 1])
                            return last
                        p.step("scalar", mk_relu,
                               r=[tmp_f[:, tpar, 2 * hp:2 * hp + 2], ge[:]],
                               w=[dst[:, tpar, 2 * hp:2 * hp + 2]])
                    if e >= 2:
                        for hp in range(2):
                            p.ve("tensor_tensor",
                                 rsum[:, tpar, 2 * hp:2 * hp + 2].rearrange(
                                     "p a b -> p (a b)"),
                                 rsum[:, tpar, 2 * hp:2 * hp + 2].rearrange(
                                     "p a b -> p (a b)"),
                                 rk[:, tpar, 2 * hp:2 * hp + 2].rearrange(
                                     "p a b -> p (a b)"),
                                 eng=("vector" if tpar == 0 else "gpsimd"),
                                 op=ALU.add)
                p.ve("tensor_tensor",
                     rsum[:, 0].rearrange("p a b -> p (a b)"),
                     rsum[:, 0].rearrange("p a b -> p (a b)"),
                     rsum[:, 1].rearrange("p a b -> p (a b)"),
                     op=ALU.add)
                for half in range(2):
                    ops = []
                    for i in range(8):
                        ts, fs = (half * 8 + i) // 4, (half * 8 + i) % 4
                        ops.append((X.PT[:, i * 128:(i + 1) * 128],
                                    rsum[:, 0, ts, fs * 128:(fs + 1) * 128]))
                    p.tr(ops, X.ident16)
                    for i in range(8):
                        ts, fs = (half * 8 + i) // 4, (half * 8 + i) % 4
                        p.act(rt_sb[:, fc * 4 + fs, ts * 128:(ts + 1) * 128],
                              X.PT[:, i * 128:(i + 1) * 128], AF.Copy)

            for cout in range(NCH):
                ws = cout % 2
                p.ld("gpsimd", wo_co[:, ws, :, :],
                     W["f0_woTt"][:, :, cout * 128:(cout + 1) * 128])
                p.mm([(X.PB[:, 0:JW], wo_co[:, ws, fcc, :], rt_sb[:, fcc, :],
                       fcc == 0, False) for fcc in range(4 * NFC)]
                     + [(X.PB[:, 0:JW],
                         bo_row[0:1, cout * 128:(cout + 1) * 128],
                         vs2row[:], False, True)])
                p.ve("tensor_add", xj[:, cout, :], X.PB[:, 0:JW],
                     xj[:, cout, :])
            emit_outer_ln(ctx, xj, xj, y_ap, j0)



def emit_l1_ffn(ctx, W, x_ap, y_ap):
    p, X = ctx.plan, ctx
    nc = ctx.nc
    FULL = 12288
    NF128 = FULL // 128
    NSC = 12
    R_THRESH = 0.7 * 4096.0 * 8.0

    carry = [nc.dram_tensor(ctx.nm("carry"), [D, T], BF16, kind="Internal").ap()
             for _ in range(2)]
    nov_in = nc.dram_tensor(ctx.nm("nov_in"), [1, 512], F32, kind="Internal").ap()
    nov_out = nc.dram_tensor(ctx.nm("nov_out"), [1, 512], F32, kind="Internal").ap()
    nov_stage = ctx.sbuf("nov_stage", [1, 512], F32)

    xnew = ctx.sbuf("xnew", [128, NCH, T], F32)
    active = ctx.sbuf("active", [128, NF128], F32)
    hist = ctx.sbuf("hist", [128, NF128], F32)
    msk = ctx.sbuf("msk", [128, NF128], F32)
    comb = ctx.sbuf("comb", [128, NF128], F32)
    d96 = ctx.sbuf("d96", [128, NF128], F32)
    ck_sb = ctx.sbuf("ck_sb", [128, NF128], F32)
    jk96 = ctx.sbuf("jk96", [128, NF128], F32)
    pooled = ctx.sbuf("pooled", [128, NCH], F32)
    pooled16 = ctx.sbuf("pooled16", [128, NCH], BF16)
    cnt_c = ctx.sbuf("cnt_c", [128, 1], F32)
    cnt16 = ctx.sbuf("cnt16", [128, 1], BF16)
    rr16 = ctx.sbuf("rr16", [128, 2], BF16)
    wc1_sb = ctx.sbuf("wc1_sb", [128, NCH, 256], BF16)
    srow = ctx.sbuf("srow", [1, 16], F32)
    dcol = ctx.sbuf("dcol", [128, 1], F32)
    omdcol = ctx.sbuf("omdcol", [128, 1], F32)
    midcol = ctx.sbuf("midcol", [128, 1], F32)
    colr = ctx.sbuf("colr", [128, 1], F32)

    SL = {n: i for i, n in enumerate(
        ("lo", "hi", "mid", "tot", "ind", "tmp", "nov", "done", "dnew"))}

    def sl(n):
        return srow[0:1, SL[n]:SL[n] + 1]

    for ci in range(NCH):
        p.dma("gpsimd", (lambda ci=ci: lambda g: g.dma_start(
            out=wc1_sb[:, ci, :], in_=W["f1_Wc1T"][ci * 128:(ci + 1) * 128, :]))())

    p.ve("memset", active[:], 0.0)
    p.ve("memset", active[:, 0:32], 1.0)
    p.ve("memset", hist[:], 0.0)
    p.ve("memset", hist[:, 0:32], 1.0)
    p.ve("memset", srow[:], 0.0)

    wm_sc = ctx.sbuf("wm_sc", [128, 2, NCH, 1024], BF16)
    wp_sc = ctx.sbuf("wp_sc", [128, 2, 8, 1024], BF16)
    hb = ctx.sbuf("hb", [128, 8, JW], BF16)
    xcJ = ctx.sbuf("xcJ", [128, NCH, T], BF16)
    bl_t1f = ctx.sbuf("bl_t1f", [128, 2, JW], F32)
    bl_st16 = ctx.sbuf("bl_st16", [128, 2, JW], BF16)

    def load_mwp(sc, slot):
        f0 = sc * 1024
        for ci in range(NCH):
            p.ld("gpsimd", wm_sc[:, slot, ci, :],
                 W["f1_WmT"][ci * 128:(ci + 1) * 128, f0:f0 + 1024])
        for fi in range(8):
            p.ld("gpsimd", wp_sc[:, slot, fi, :],
                 W["f1_WpT"][f0 + fi * 128:f0 + (fi + 1) * 128, :])

    for st in range(3):
        src_ap = x_ap if st == 0 else carry[(st + 1) % 2]
        dst_ap = carry[st % 2]

        for ci in range(NCH):
            p.ld("gpsimd", xcJ[:, ci, :], src_ap[ci * 128:(ci + 1) * 128, :])
        if st == 0:
            load_mwp(0, 0)

        for sc in range(NSC):
            slot = sc % 2
            if sc + 1 < NSC:
                load_mwp(sc + 1, (sc + 1) % 2)
            elif st < 2:
                load_mwp(0, 0)
            for t4 in range(4):
                t0 = t4 * JW
                t4p = t4 % 2
                for fi in range(8):
                    s0 = (fi % 2) * JW
                    p.mm([(X.PB[:, s0:s0 + JW],
                           wm_sc[:, slot, ci, fi * 128:(fi + 1) * 128],
                           xcJ[:, ci, t0:t0 + JW],
                           ci == 0, ci == NCH - 1) for ci in range(NCH)])
                    acol = active[:, sc * 8 + fi:sc * 8 + fi + 1]
                    p.act(hb[:, fi, :], X.PB[:, s0:s0 + JW], AF.Relu,
                          bias=0.0, scale=acol)
                for half in range(2):
                    for cc in range(4):
                        c = half * 4 + cc
                        p.mm([(X.PA[:, cc * JW:(cc + 1) * JW],
                               wp_sc[:, slot, fi, c * 128:(c + 1) * 128],
                               hb[:, fi, :], fi == 0, fi == 7)
                              for fi in range(8)])
                    for cc in range(4):
                        c = half * 4 + cc
                        if sc == 0:
                            p.ve("tensor_copy", xnew[:, c, t0:t0 + JW],
                                 X.PA[:, cc * JW:(cc + 1) * JW])
                        else:
                            p.ve("tensor_add", xnew[:, c, t0:t0 + JW],
                                 X.PA[:, cc * JW:(cc + 1) * JW],
                                 xnew[:, c, t0:t0 + JW])

        for c in range(NCH):
            p.ve("reduce_sum", pooled[:, c:c + 1], xnew[:, c, :], axis=AX.X)
        p.ve("tensor_copy", pooled16[:], pooled[:])
        for rc in range(2):
            p.mm([(X.PB[:, rc:rc + 1],
                   wc1_sb[:, ci, rc * 128:(rc + 1) * 128],
                   pooled16[:, ci:ci + 1], ci == 0, ci == NCH - 1)
                  for ci in range(NCH)])
            p.act(rr16[:, rc:rc + 1], X.PB[:, rc:rc + 1], AF.Relu,
                  scale=1.0 / T)
        wc2a = wm_sc[:, 1, :, :].rearrange("p a b -> p (a b)")
        wc2b = wp_sc[:, 1, 0:4, :].rearrange("p a b -> p (a b)")
        for rc in range(2):
            p.ld("gpsimd", wc2a, W["f1_Wc2T"][rc * 128:(rc + 1) * 128, 0:8192])
            p.ld("gpsimd", wc2b, W["f1_Wc2T"][rc * 128:(rc + 1) * 128, 8192:FULL])
            def mm_ck(t, rc=rc, wc2a=wc2a, wc2b=wc2b):
                last = None
                for f in range(NF128):
                    src = (wc2a[:, f * 128:(f + 1) * 128] if f < 64 else
                           wc2b[:, (f - 64) * 128:(f - 63) * 128])
                    last = t.matmul(X.PC[:, f:f + 1], src,
                                    rr16[:, rc:rc + 1], start=(rc == 0),
                                    stop=(rc == 1))
                return last
            p.step("tensor", mm_ck,
                   r=[wc2a, wc2b, rr16[:, rc:rc + 1]], w=[X.PC[:, 0:NF128]])
        p.act(ck_sb[:], X.PC[:, 0:NF128], AF.Copy)

        p.ve("reduce_max", colr[:], ck_sb[:], axis=AX.X)
        p.mm([(X.PB[0:1, 0:128], colr[:], X.identf[:], True, True)])
        p.ve("reduce_max", sl("hi"), X.PB[0:1, 0:128], axis=AX.X)
        p.ve("tensor_reduce", colr[:], ck_sb[:], axis=AX.X, op=ALU.min)
        p.mm([(X.PB[0:1, 128:256], colr[:], X.identf[:], True, True)])
        p.ve("tensor_reduce", sl("lo"), X.PB[0:1, 128:256], axis=AX.X,
             op=ALU.min)
        p.ve("tensor_scalar", sl("hi"), sl("hi"), 1e-6, None, ALU.add)
        p.ve("tensor_scalar", sl("lo"), sl("lo"), 1e-6, None, ALU.subtract)

        for it in range(22):
            p.ve("tensor_add", sl("mid"), sl("lo"), sl("hi"))
            p.ve("tensor_scalar", sl("mid"), sl("mid"), 0.5, None, ALU.mult)
            p.mm([(X.PB[:, 2:3], X.onesf_128[:], sl("mid"), True, True)])
            p.act(midcol[:], X.PB[:, 2:3], AF.Copy)
            p.ve("tensor_scalar", jk96[:], ck_sb[:], midcol[:], None,
                 ALU.is_ge)
            p.ve("reduce_sum", cnt_c[:], jk96[:], axis=AX.X)
            p.ve("tensor_copy", cnt16[:], cnt_c[:])
            p.mm([(X.PB[0:1, 3:4], X.ones_col[:], cnt16[:], True, True)])
            p.act(sl("tot"), X.PB[0:1, 3:4], AF.Copy)
            p.ve("tensor_scalar", sl("ind"), sl("tot"), 4096.0, None,
                 ALU.is_ge)
            p.ve("tensor_sub", sl("tmp"), sl("mid"), sl("lo"))
            p.ve("tensor_mul", sl("tmp"), sl("tmp"), sl("ind"))
            p.ve("tensor_add", sl("lo"), sl("lo"), sl("tmp"))
            p.ve("tensor_sub", sl("tmp"), sl("mid"), sl("hi"))
            p.ve("tensor_scalar", sl("ind"), sl("ind"), -1.0, 1.0,
                 ALU.mult, ALU.add)
            p.ve("tensor_mul", sl("tmp"), sl("tmp"), sl("ind"))
            p.ve("tensor_add", sl("hi"), sl("hi"), sl("tmp"))

        p.mm([(X.PB[:, 2:3], X.onesf_128[:], sl("lo"), True, True)])
        p.act(midcol[:], X.PB[:, 2:3], AF.Copy)
        p.ve("tensor_scalar", msk[:], ck_sb[:], midcol[:], None, ALU.is_ge)
        p.ve("tensor_add", comb[:], hist[:], msk[:])
        p.ve("tensor_scalar", comb[:], comb[:], 1.0, None, ALU.min)
        p.ve("tensor_sub", d96[:], comb[:], hist[:])
        p.ve("reduce_sum", cnt_c[:], d96[:], axis=AX.X)
        p.ve("tensor_copy", cnt16[:], cnt_c[:])
        p.mm([(X.PB[0:1, 3:4], X.ones_col[:], cnt16[:], True, True)])
        p.act(sl("nov"), X.PB[0:1, 3:4], AF.Copy)
        p.ve("memset", nov_stage[:], 0.0)
        p.ve("tensor_copy", nov_stage[0:1, 0:1], sl("nov"))
        p.dma("gpsimd", lambda g: g.dma_start(out=nov_in, in_=nov_stage[:]))
        p.step("gpsimd", lambda g: g.collective_compute(
            "AllReduce", ALU.add, [[0, 1, 2, 3, 4, 5, 6, 7]],
            [nov_in.opt()], [nov_out.opt()]), inc="gp", inc_k=1)
        p.dma("gpsimd", lambda g: g.dma_start(out=sl("nov"),
                                              in_=nov_out[0:1, 0:1]))
        p.ve("tensor_scalar", sl("dnew"), sl("nov"), R_THRESH, None,
             ALU.is_lt)
        p.mm([(X.PB[:, 2:3], X.onesf_128[:], sl("done"), True, True)])
        p.act(dcol[:], X.PB[:, 2:3], AF.Copy)
        p.ve("tensor_scalar", omdcol[:], dcol[:], -1.0, 1.0,
             ALU.mult, ALU.add)
        p.ve("tensor_scalar", jk96[:], msk[:], omdcol[:], None, ALU.mult)
        p.ve("tensor_scalar", active[:], active[:], dcol[:], None, ALU.mult)
        p.ve("tensor_add", active[:], active[:], jk96[:])
        p.ve("tensor_scalar", jk96[:], comb[:], omdcol[:], None, ALU.mult)
        p.ve("tensor_scalar", hist[:], hist[:], dcol[:], None, ALU.mult)
        p.ve("tensor_add", hist[:], hist[:], jk96[:])
        p.ve("tensor_max", sl("done"), sl("done"), sl("dnew"))

        if True:
            t1f = bl_t1f
            st16 = bl_st16
            for t4 in range(4):
                t0 = t4 * JW
                for c in range(NCH):
                    par = c % 2
                    xc16 = xcJ[:, c, t0:t0 + JW]
                    xn = xnew[:, c, t0:t0 + JW]
                    if st < 2:
                        p.ve("tensor_scalar", t1f[:, par, :], xn,
                             omdcol[:], None, ALU.mult)
                        p.ve("scalar_tensor_tensor", st16[:, par, :], xc16,
                             dcol[:], t1f[:, par, :], ALU.mult, ALU.add)
                        p.ld("sync", dst_ap[c * 128:(c + 1) * 128, t0:t0 + JW],
                             st16[:, par, :])
                    else:
                        p.ve("tensor_scalar", xn, xn, omdcol[:], None, ALU.mult)
                        p.ve("scalar_tensor_tensor", xn, xc16,
                             dcol[:], xn, ALU.mult, ALU.add)
                        p.ld("sync", t1f[:, par, :],
                             x_ap[c * 128:(c + 1) * 128, t0:t0 + JW])
                        p.ve("tensor_add", xn, t1f[:, par, :], xn)
                if st == 2:
                    emit_outer_ln(ctx, xnew[:, :, t0:t0 + JW],
                                  xnew[:, :, t0:t0 + JW], y_ap, t0)



def build_block_program(block_name, wnames):
    nc = bass.Bass("TRN2", target_bir_lowering=False)
    x_ap = nc.dram_tensor("x", [D, T], F32, kind="ExternalInput").ap()
    y_ap = nc.dram_tensor("y", [D, T], F32, kind="ExternalOutput").ap()
    W = {n: nc.dram_tensor(n, WEIGHT_SHAPES[n], F32, kind="ExternalInput").ap()
         for n in wnames}
    C = {n: nc.dram_tensor(n, CONST_SHAPES[n], F32, kind="ExternalInput").ap()
         for n in CONST_SHAPES}
    plan = Plan()
    es = ExitStack()
    with es:
        ctx = Ctx(nc, plan, es)
        ctx.make_shared(C)
        if block_name == "l0_attn":
            emit_l0_attn(ctx, W, x_ap, y_ap)
        elif block_name == "l1_attn":
            emit_l1_attn(ctx, W, x_ap, y_ap)
        elif block_name == "l0_ffn":
            emit_l0_ffn(ctx, W, x_ap, y_ap)
        elif block_name == "l1_ffn":
            emit_l1_ffn(ctx, W, x_ap, y_ap)
        else:
            raise ValueError(block_name)
        plan.step("sync", lambda sy: sy.nop())
        sems = {s: es.enter_context(nc.semaphore(s)) for s in SEMS}
        block = es.enter_context(nc.Block())
        plan.emit(block, sems)
    return nc


BLOCK_WNAMES = {
    "l0_attn": ["a0_Wq", "a0_Wk", "a0_Wv", "a0_Wo", "a0_bo_row",
                "a0_kln_w", "a0_kln_b", "a0_qln_w", "a0_qln_b"],
    "l1_attn": ["a1_Wq", "a1_Wk", "a1_Wv", "a1_Wo", "a1_bo_row",
                "a1_bq", "a1_bk", "a1_bv"],
    "l0_ffn": ["f0_wiT", "f0_woTt", "f0_bi", "f0_bo_row", "f0_AT", "f0_BT",
               "f0_gwT", "f0_gb"],
    "l1_ffn": ["f1_WmT", "f1_WpT", "f1_Wc1T", "f1_Wc2T"],
}

_prog_cache = {}


EMITTERS = {}


def build_full_program():
    nc = bass.Bass("TRN2", target_bir_lowering=False)
    x_ap = nc.dram_tensor("x", [D, T], F32, kind="ExternalInput").ap()
    y_ap = nc.dram_tensor("y", [D, T], F32, kind="ExternalOutput").ap()
    W = {}
    for names in BLOCK_WNAMES.values():
        for n in names:
            W[n] = nc.dram_tensor(n, WEIGHT_SHAPES[n], F32, kind="ExternalInput").ap()
    C = {n: nc.dram_tensor(n, CONST_SHAPES[n], F32, kind="ExternalInput").ap()
         for n in CONST_SHAPES}
    xb1 = nc.dram_tensor("xbuf1", [D, T], F32, kind="Internal").ap()
    xb2 = nc.dram_tensor("xbuf2", [D, T], F32, kind="Internal").ap()
    xb3 = nc.dram_tensor("xbuf3", [D, T], F32, kind="Internal").ap()
    plan = Plan()
    es = ExitStack()
    with es:
        ctx = Ctx(nc, plan, es)
        ctx.make_shared(C)
        chain = [(emit_l0_attn, x_ap, xb1), (emit_l0_ffn, xb1, xb2),
                 (emit_l1_attn, xb2, xb3), (emit_l1_ffn, xb3, y_ap)]
        for emit, xin, xout in chain:
            plan.fence()
            with ExitStack() as bes:
                ctx.block_es = bes
                emit(ctx, W, xin, xout)
            ctx.block_es = es
        plan.step("sync", lambda sy: sy.nop())
        sems = {s: es.enter_context(nc.semaphore(s)) for s in SEMS}
        block = es.enter_context(nc.Block())
        plan.emit(block, sems)
    return nc


_consts_cache = {}


def run_block(block_name, x_fm_cores, W):
    if block_name not in _prog_cache:
        _prog_cache[block_name] = build_block_program(block_name,
                                                      BLOCK_WNAMES[block_name])
    if "c" not in _consts_cache:
        _consts_cache["c"] = make_consts()
    CONSTS = _consts_cache["c"]
    nc = _prog_cache[block_name]
    in_maps = []
    for c in range(8):
        m = {"x": x_fm_cores[c]}
        for n in BLOCK_WNAMES[block_name]:
            m[n] = W[n]
        m.update(CONSTS)
        in_maps.append(m)
    res = run_bass_kernel_spmd(nc, in_maps, core_ids=list(range(8)))
    return [res.results[c]["y"] for c in range(8)]


def kernel(**inputs):
    W = prep_weights(inputs)
    x = np.asarray(inputs["L0_attn_x"], np.float32)
    xs = [np.ascontiguousarray(x[c].T) for c in range(8)]
    if "full" not in _prog_cache:
        _prog_cache["full"] = build_full_program()
    if "c" not in _consts_cache:
        _consts_cache["c"] = make_consts()
    nc = _prog_cache["full"]
    in_maps = []
    for c in range(8):
        m = {"x": xs[c]}
        for names in BLOCK_WNAMES.values():
            for n in names:
                m[n] = W[n]
        m.update(_consts_cache["c"])
        in_maps.append(m)
    res = run_bass_kernel_spmd(nc, in_maps, core_ids=list(range(8)))
    out = np.stack([np.asarray(res.results[c]["y"]).T for c in range(8)], axis=0)
    return np.ascontiguousarray(out, np.float32)
